# version 14
# baseline (speedup 1.0000x reference)
import numpy as np
import ml_dtypes
import concourse.bass as bass
import concourse.mybir as mybir
from concourse.alu_op_type import AluOpType as ALU
from concourse.bass_utils import run_bass_kernel_spmd

F32 = mybir.dt.float32
BF16 = mybir.dt.bfloat16
U32 = mybir.dt.uint32
AF = mybir.ActivationFunctionType
AX = mybir.AxisListType
BF = ml_dtypes.bfloat16

NCORES = 8
S = 4096
D = 1024
PC = 128
NDC = D // PC
NB = 512
HS = S // 2
ENGINES = ("sync", "tensor", "vector", "scalar", "gpsimd")


class _Sem:
    __slots__ = ("h", "count", "name")


class Rot:

    def __init__(self, psums):
        self.slots = [[p, None, 0] for p in psums]
        self.i = -1
        self.pending = []

    def take(self):
        self.i = (self.i + 1) % len(self.slots)
        p, sem, cnt = self.slots[self.i]
        w = [(sem, cnt)] if sem is not None and cnt > 0 else []
        self.pending.append(self.i)
        return p, w

    def settle(self, sem):
        i = self.pending.pop(0)
        self.slots[i][1] = sem
        self.slots[i][2] = sem.count

    def seed(self, sem):
        assert not self.pending
        for s in self.slots:
            s[1] = sem
            s[2] = sem.count


class Builder:
    def __init__(self, nc):
        self.nc = nc
        self.steps = {e: [] for e in ENGINES}
        self._ctxs = []
        self.sb = {}
        self.esem = {}

    def set_engine_sem(self, eng, sem):
        self.esem[eng] = sem

    def sem(self, name):
        ctx = self.nc.semaphore(name)
        h = ctx.__enter__()
        self._ctxs.append((name, ctx))
        s = _Sem()
        s.h, s.count, s.name = h, 0, name
        return s

    _uniq = {}

    def _uname(self, name):
        n = Builder._uniq.get(name, 0)
        Builder._uniq[name] = n + 1
        return name if n == 0 else f"{name}__{n}"

    def sbuf(self, name, shape, dtype=F32):
        ctx = self.nc.sbuf_tensor(self._uname(name), list(shape), dtype)
        h = ctx.__enter__()
        self._ctxs.append((name, ctx))
        self.sb[name] = h
        return h

    def psum(self, name, shape, dtype=F32):
        ctx = self.nc.psum_tensor(self._uname(name), list(shape), dtype)
        h = ctx.__enter__()
        self._ctxs.append((name, ctx))
        self.sb[name] = h
        return h

    def free(self, *names):
        for name in names:
            nm, ctx = self._ctxs.pop()
            assert nm == name, f"free order mismatch: {nm} != {name}"
            ctx.__exit__(None, None, None)
            self.sb.pop(nm, None)

    def on(self, eng, fn, waits=(), inc=None, inc_n=1):
        es = self.esem.get(eng)
        wl = []
        if es is not None and es.count > 0:
            wl.append((es.h, es.count))
        wl += [(s.h, int(v)) for (s, v) in waits if v and v > 0]
        if inc is None and es is not None:
            inc = es
        tgt = None
        if inc is not None:
            inc.count += inc_n
            tgt = (inc.h, inc_n)

        def step(q, _wl=wl, _fn=fn, _tgt=tgt):
            for h, v in _wl:
                q.wait_ge(h, v)
            r = _fn(q)
            if _tgt is not None:
                insts = r if isinstance(r, (list, tuple)) else [r]
                insts[-1].then_inc(*_tgt)
        self.steps[eng].append(step)
        return inc.count if inc is not None else None

    def chain(self, eng, fns, waits=(), inc=None):
        n = len(fns)
        for i, fn in enumerate(fns):
            self.on(eng, fn,
                    waits=(waits if i == 0 else ()),
                    inc=(inc if (i == n - 1 and inc is not None) else None))

    def dma(self, qeng, out, in_, sem, waits=()):
        return self.on(qeng, lambda q, _o=out, _i=in_: q.dma_start(out=_o, in_=_i),
                       waits=waits, inc=sem, inc_n=16)

    def wait(self, eng, waits):
        wl = [(s.h, int(v)) for (s, v) in waits if v and v > 0]
        if not wl:
            return

        def step(q, _wl=wl):
            for h, v in _wl:
                q.wait_ge(h, v)
        self.steps[eng].append(step)

    def mm(self, dst, pairs, waits=(), inc=None, tile_position=None):
        def fn(q, _dst=dst, _pairs=tuple(pairs), _tp=tile_position):
            insts = []
            n = len(_pairs)
            for i, (l, r) in enumerate(_pairs):
                insts.append(self.nc.tensor.matmul(
                    _dst, l, r, start=(i == 0), stop=(i == n - 1),
                    tile_position=_tp))
            return insts
        return self.on("tensor", fn, waits=waits, inc=inc)

    def finish(self):
        nc = self.nc
        with nc.Block() as block:
            for e in ENGINES:
                if not self.steps[e]:
                    continue

                def body(q, _steps=tuple(self.steps[e])):
                    seen = {}

                    class QW:
                        def __getattr__(self, a):
                            return getattr(q, a)

                        def wait_ge(self, h, v, _seen=seen):
                            k = id(h)
                            if _seen.get(k, -1) >= v:
                                return
                            _seen[k] = v
                            q.wait_ge(h, v)
                    qw = QW()
                    for st in _steps:
                        st(qw)
                getattr(block, e)(body)
        for nm, c in reversed(self._ctxs):
            c.__exit__(None, None, None)
        self._ctxs = []




def _tbf(nc, out, in_, idn):
    with nc.allow_low_precision(reason="bf16 PE transpose (no accumulation)"):
        return nc.tensor.transpose(out, in_, idn)

def _ld3(dram2d, sb2d, cols):
    return (sb2d.rearrange("p (c f) -> p c f", f=cols),
            dram2d.rearrange("(c p) f -> p c f", p=PC))

def make_consts():
    c = {}
    N = 1024
    n = np.arange(N)[:, None]
    f_re = np.arange(513)[None, :]
    f_im = np.arange(1, 512)[None, :]
    PK = np.concatenate([np.cos(2 * np.pi * n * f_re / N),
                         -np.sin(2 * np.pi * n * f_im / N)], axis=1)
    c["c_PK"] = PK.astype(BF)
    nn = np.arange(N)[None, :]
    ff_re = np.arange(513)[:, None]
    ff_im = np.arange(1, 512)[:, None]
    coef = np.concatenate([np.ones(1), 2 * np.ones(511), np.ones(1)])[:, None]
    IW_top = coef * np.cos(2 * np.pi * ff_re * nn / N) / N
    IW_bot = -2 * np.sin(2 * np.pi * ff_im * nn / N) / N
    c["c_IW"] = np.concatenate([IW_top, IW_bot], axis=0).astype(BF)
    W = np.full((PC, NDC), 2.0, np.float32)
    W[0, 0] = 1.0
    W[0, 4] = 1.0
    c["c_nw"] = W.astype(BF)
    c["c_ones_row"] = np.ones((1, PC), np.float32)
    c["c_identity"] = np.eye(PC, dtype=np.float32).astype(BF)
    c["c_identity_f32"] = np.eye(PC, dtype=np.float32)
    c["c_tric"] = np.tril(np.ones((PC, PC), np.float32)).T.astype(BF)
    c["c_triu_neg"] = (np.triu(np.ones((PC, PC), np.float32), 1) * -1e4).astype(BF)
    c["c_iota8"] = np.broadcast_to(np.arange(8, dtype=np.float32), (PC, 8)).copy()
    inv = 1.0 / 10000.0 ** (np.arange(0, 64, 2, dtype=np.float64) / 64.0)
    ang = np.arange(S, dtype=np.float64)[None, :] * inv[:, None]
    cosr = np.cos(ang)
    sinr = np.sin(ang)
    c["c_cosP"] = np.tile(cosr, (4, 1)).astype(BF)
    c["c_sinP"] = np.tile(sinr, (4, 1)).astype(BF)
    return c



def make_sems(b):
    keys = ["ld", "mm", "ev", "sc", "st", "gl", "xr", "qh", "kh", "vh", "cc", "gr", "av"]
    sems = {k: b.sem("s_" + k) for k in keys}
    b.set_engine_sem("vector", sems["ev"])
    b.set_engine_sem("scalar", sems["sc"])
    return sems


def emit_prelude(nc, b, io, sems):
    b.psum("p_A", (PC, NB))
    b.psum("p_B", (PC, NB))
    b.psum("p_C", (PC, NB))
    b.psum("p_R1", (PC, NB))
    b.psum("p_R2", (PC, NB))
    b.sbuf("g_onescol", (PC, 1), F32)
    b.sbuf("g_onescolb", (PC, 1), BF16)
    b.sbuf("g_onesr", (1, PC), F32)
    b.sbuf("g_onesrb", (1, PC), BF16)
    b.sbuf("g_idn", (PC, PC), BF16)
    b.sbuf("g_idnf", (PC, PC), F32)
    b.on("vector", lambda q: [nc.vector.memset(b.sb["g_onescol"][:], 1.0),
                              nc.vector.memset(b.sb["g_onescolb"][:], 1.0),
                              nc.vector.memset(b.sb["g_onesrb"][:], 1.0)], inc=sems["ev"])
    b.dma("sync", b.sb["g_onesr"][:], io["c_ones_row"], sems["ld"])
    b.dma("sync", b.sb["g_idn"][:], io["c_identity"], sems["ld"])
    b.dma("sync", b.sb["g_idnf"][:], io["c_identity_f32"], sems["ld"])
    b.wait("tensor", [(sems["ld"], sems["ld"].count)])


def alloc_ln_region(b):
    b.sbuf("ln_xld", (PC, NDC * NB), F32)
    b.sbuf("ln_sq", (PC, NDC * NB), BF16)
    b.sbuf("ln_tmp", (PC, NB), F32)
    b.sbuf("ln_rowm", (1, NB), F32)
    b.sbuf("ln_rowv", (1, NB), F32)
    b.sbuf("ln_rown", (1, NB), F32)


def free_ln_region(b):
    b.free("ln_rown", "ln_rowv", "ln_rowm", "ln_tmp", "ln_sq", "ln_xld")


def emit_ln(nc, b, xA, xn, sems, t0=0, t1=S):
    s_ld, s_mm, s_ev, s_sc = sems["ld"], sems["mm"], sems["ev"], sems["sc"]
    sb = b.sb
    xld, sq, tmp = sb["ln_xld"], sb["ln_sq"], sb["ln_tmp"]
    rowm, rowv, rown = sb["ln_rowm"], sb["ln_rowv"], sb["ln_rown"]
    onescol, onescolb, onesr = sb["g_onescol"], sb["g_onescolb"], sb["g_onesr"]
    pR1, pR2, pA, pB = sb["p_R1"], sb["p_R2"], sb["p_A"], sb["p_B"]
    pR1 = pR1[0:1, :]
    pR2 = pR2[0:1, :]

    s_st = sems["st"]
    for n0 in range(t0, t1, NB):
        for dc in range(NDC):
            b.dma("sync", xld[:, dc * NB:(dc + 1) * NB],
                  xA[dc * PC:(dc + 1) * PC, n0:n0 + NB], s_ld,
                  waits=[(s_ev, s_ev.count), (s_sc, s_sc.count), (s_mm, s_mm.count),
                         (s_st, s_st.count), (s_ld, s_ld.count)] if dc == 0 else ())
        ld_here = s_ld.count
        for dc in range(NDC):
            xc = xld[:, dc * NB:(dc + 1) * NB]
            sqc = sq[:, dc * NB:(dc + 1) * NB]
            b.on("scalar", lambda q, _x=xc, _s=sqc: nc.scalar.activation(_s, _x, AF.Square),
                 waits=[(s_ld, ld_here), (s_mm, s_mm.count)], inc=s_sc)
            sc_here = s_sc.count
            b.on("tensor", lambda q, _dc=dc, _s=sqc: nc.tensor.matmul(
                pR2[:], onescolb[:], _s, start=(_dc == 0), stop=(_dc == NDC - 1)),
                waits=[(s_sc, sc_here)] + ([(s_ev, s_ev.count)] if dc == 0 else []),
                inc=(s_mm if dc == NDC - 1 else None))
            b.on("tensor", lambda q, _dc=dc, _x=xc: nc.tensor.matmul(
                pR1[:], onescol[:], _x, start=(_dc == 0), stop=(_dc == NDC - 1)),
                waits=[(s_ld, ld_here)] + ([(s_ev, s_ev.count)] if dc == 0 else []),
                inc=(s_mm if dc == NDC - 1 else None))
        mm_stats = s_mm.count

        b.chain("vector", [
            lambda q: nc.vector.tensor_scalar(rowm[:], pR1[:], 1.0 / D, None, op0=ALU.mult),
            lambda q: nc.vector.tensor_scalar(rowv[:], pR2[:], 1.0 / D, None, op0=ALU.mult),
            lambda q: nc.vector.tensor_tensor(rown[:], rowm[:], rowm[:], op=ALU.mult),
            lambda q: nc.vector.tensor_tensor(rowv[:], rowv[:], rown[:], op=ALU.subtract),
            lambda q: nc.vector.tensor_scalar(rowv[:], rowv[:], 1e-5, None, op0=ALU.add),
        ], waits=[(s_mm, mm_stats)])
        b.on("scalar", lambda q: nc.scalar.activation(rowv[:], rowv[:], AF.Sqrt),
             waits=[(s_ev, s_ev.count)], inc=s_sc)

        b.chain("vector", [
            lambda q: nc.vector.reciprocal(rowv[:], rowv[:]),
            lambda q: nc.vector.tensor_tensor(rown[:], rowm[:], rowv[:], op=ALU.mult),
            lambda q: nc.vector.tensor_scalar(rown[:], rown[:], -1.0, None, op0=ALU.mult),
        ], waits=[(s_sc, s_sc.count)])
        b.on("tensor", lambda q: [
            nc.tensor.matmul(pA[:], onesr[:], rowv[:], start=True, stop=True),
            nc.tensor.matmul(pB[:], onesr[:], rown[:], start=True, stop=True)],
            waits=[(s_ev, s_ev.count)], inc=s_mm)
        mm_bc = s_mm.count
        for dc in range(NDC):
            xc = xld[:, dc * NB:(dc + 1) * NB]
            xo = xn[:, dc * S + n0: dc * S + n0 + NB]

            b.chain("vector", [
                lambda q, _x=xc: nc.vector.tensor_tensor(tmp[:], _x, pA[:], op=ALU.mult),
                lambda q, _o=xo: nc.vector.tensor_tensor(_o, tmp[:], pB[:], op=ALU.add),
            ], waits=[(s_mm, mm_bc)])


def build_l0_attn(nc, b, io, sems):
    s_ld, s_mm, s_ev, s_sc, s_st = sems["ld"], sems["mm"], sems["ev"], sems["sc"], sems["st"]
    s_gl, s_xr = sems["gl"], sems["xr"]
    b.psum("p_N", (1, NB))
    pD = b.psum("p_D", (PC, NB))
    rotA = Rot([b.sb["p_A"], b.sb["p_B"], pD, b.sb["p_R1"], b.sb["p_R2"]])
    xn = b.sbuf("a_xn", (PC, NDC * S), BF16)
    wpk = b.sbuf("a_wpk", (PC, NDC * 1024), BF16)
    wprime = [b.sbuf(f"a_wp{i}", (PC, NDC * 1024), BF16) for i in range(3)]
    bvec = b.sbuf("a_bvec", (PC, NDC * 3), BF16)
    bcol = b.sbuf("a_bcol", (PC, 3 * NDC), F32)
    nw = b.sbuf("a_nw", (PC, NDC), BF16)
    nrm2row = b.sbuf("a_nrm2", (1, S), F32)
    scl = nrm2row
    carry = b.sbuf("a_carry", (PC, NDC * 2), F32)

    pA, pB, pC = b.sb["p_A"], b.sb["p_B"], b.sb["p_C"]
    pN = b.sb["p_N"]
    onesr = b.sb["g_onesr"]

    xA = io["x"]
    b.dma("sync", *_ld3(io["c_PK"], wpk[:], 1024), s_ld,
          waits=[(s_ld, s_ld.count)])
    b.dma("sync", nw[:], io["c_nw"], s_ld)
    b.wait("tensor", [(s_ld, s_ld.count)])
    b.wait("scalar", [(s_ld, s_ld.count)])

    alloc_ln_region(b)
    emit_ln(nc, b, xA, xn, sems)
    free_ln_region(b)

    wW = b.sbuf("a_wW", (PC, NDC * 1024), BF16)
    for wi, (wname, bname) in enumerate([("Wk", "bk"), ("Wv", "bv"), ("Wq", "bq")]):
        b.dma("gpsimd", bvec[:, wi * NDC:(wi + 1) * NDC], io[bname], s_gl,
              waits=[(s_mm, s_mm.count)] + ([(s_ev, s_ev.count), (s_sc, s_sc.count)]
                                            if wi == 0 else []))
        b.dma("gpsimd", *_ld3(io[wname], wW[:], 1024), s_gl)
        ld_w = s_gl.count
        for mc in range(NDC):
            for nb_ in range(2):
                dstp = pA if (nb_ == 0) else pB
                b.mm(dstp[:],
                     [(wW[:, kc * 1024 + mc * PC: kc * 1024 + (mc + 1) * PC],
                       wpk[:, kc * 1024 + nb_ * NB: kc * 1024 + (nb_ + 1) * NB])
                      for kc in range(NDC)],
                     waits=[(s_gl, ld_w), (s_ev, s_ev.count)], inc=s_mm)
                mm_w = s_mm.count
                b.on("vector", lambda q, _wi=wi, _mc=mc, _nb=nb_, _p=dstp: nc.vector.tensor_copy(
                    wprime[_wi][:, _mc * 1024 + _nb * NB: _mc * 1024 + (_nb + 1) * NB], _p[:]),
                    waits=[(s_mm, mm_w)], inc=s_ev)
        ld_b = s_gl.count
        for mc in range(NDC):
            b.mm(pC[:, 0:1],
                 [(wpk[:, kc * 1024 + mc * PC: kc * 1024 + mc * PC + PC],
                   bvec[:, wi * NDC + kc: wi * NDC + kc + 1]) for kc in range(NDC)],
                 waits=[(s_gl, ld_b), (s_ev, s_ev.count)], inc=s_mm)
            mm_b = s_mm.count
            b.on("vector", lambda q, _wi=wi, _mc=mc: nc.vector.tensor_copy(
                bcol[:, _wi * NDC + _mc: _wi * NDC + _mc + 1], pC[:, 0:1]),
                waits=[(s_mm, mm_b)], inc=s_ev)

    rotA.seed(s_ev)
    b.free("a_wW")
    kr = b.sbuf("a_kr", (PC, HS), BF16)
    ki = b.sbuf("a_ki", (PC, HS), BF16)
    vr = b.sbuf("a_vr", (PC, HS), BF16)
    vi = b.sbuf("a_vi", (PC, HS), BF16)
    kvr = b.sbuf("a_kvr", (PC, HS), BF16)
    kvi = b.sbuf("a_kvi", (PC, HS), BF16)
    sqs = b.sbuf("a_sqs", (PC, HS), BF16)

    for h in range(2):
        th0 = h * HS
        for pc_ in range(4):
            mc_re, mc_im = pc_, pc_ + 4
            for out_t, wi, mc in [(kr, 0, mc_re), (ki, 0, mc_im),
                                  (vr, 1, mc_re), (vi, 1, mc_im)]:
                for nb_ in range(HS // NB):
                    n0 = th0 + nb_ * NB
                    dstp, rw_ = rotA.take()
                    b.mm(dstp[:],
                         [(wprime[wi][:, kc * 1024 + mc * PC: kc * 1024 + (mc + 1) * PC],
                           xn[:, kc * S + n0: kc * S + n0 + NB]) for kc in range(NDC)],
                         waits=rw_, inc=s_mm)
                    mm_p = s_mm.count
                    b.on("vector", lambda q, _o=out_t, _wi=wi, _mc=mc, _nb=nb_, _p=dstp: nc.vector.tensor_scalar(
                        _o[:, _nb * NB:(_nb + 1) * NB], _p[:],
                        bcol[:, _wi * NDC + _mc: _wi * NDC + _mc + 1], None, op0=ALU.add),
                        waits=[(s_mm, mm_p)], inc=s_ev)
                    rotA.settle(s_ev)
            ev_kv_in = s_ev.count

            b.chain("vector", [
                lambda q: nc.vector.tensor_tensor(kvr[:], kr[:], vr[:], op=ALU.mult),
                lambda q: nc.vector.tensor_tensor(sqs[:], ki[:], vi[:], op=ALU.mult),
                lambda q: nc.vector.tensor_tensor(kvr[:], kvr[:], sqs[:], op=ALU.subtract),
                lambda q: nc.vector.tensor_tensor(kvi[:], kr[:], vi[:], op=ALU.mult),
                lambda q: nc.vector.tensor_tensor(sqs[:], ki[:], vr[:], op=ALU.mult),
                lambda q: nc.vector.tensor_tensor(kvi[:], kvi[:], sqs[:], op=ALU.add),
            ], waits=[(s_ev, ev_kv_in), (s_st, s_st.count)])
            if pc_ == 0:
                b.chain("vector", [
                    lambda q: nc.vector.tensor_tensor(kvr[0:1, :], kr[0:1, :], vr[0:1, :], op=ALU.mult),
                    lambda q: nc.vector.tensor_tensor(kvi[0:1, :], ki[0:1, :], vi[0:1, :], op=ALU.mult),
                ])
            c_re = carry[:, pc_ * 2: pc_ * 2 + 1]
            c_im = carry[:, pc_ * 2 + 1: pc_ * 2 + 2]

            b.chain("vector", [
                lambda q, _c=c_re, _h=h: nc.vector.tensor_tensor_scan(
                    vr[:], kvr[:], kvr[:], (0.0 if _h == 0 else _c), op0=ALU.add, op1=ALU.bypass),
                lambda q, _c=c_im, _h=h: nc.vector.tensor_tensor_scan(
                    vi[:], kvi[:], kvi[:], (0.0 if _h == 0 else _c), op0=ALU.add, op1=ALU.bypass),
                lambda q, _c=c_re: nc.vector.tensor_copy(_c, vr[:, HS - 1:HS]),
                lambda q, _c=c_im: nc.vector.tensor_copy(_c, vi[:, HS - 1:HS]),
            ])
            ev_scan = s_ev.count
            for out_t, mc in [(kr, mc_re), (ki, mc_im)]:
                for nb_ in range(HS // NB):
                    n0 = th0 + nb_ * NB
                    dstp, rw_ = rotA.take()
                    b.mm(dstp[:],
                         [(wprime[2][:, kc * 1024 + mc * PC: kc * 1024 + (mc + 1) * PC],
                           xn[:, kc * S + n0: kc * S + n0 + NB]) for kc in range(NDC)],
                         waits=rw_ + [(s_ev, ev_scan)], inc=s_mm)
                    mm_p = s_mm.count
                    b.on("vector", lambda q, _o=out_t, _mc=mc, _nb=nb_, _p=dstp: nc.vector.tensor_scalar(
                        _o[:, _nb * NB:(_nb + 1) * NB], _p[:],
                        bcol[:, 2 * NDC + _mc: 2 * NDC + _mc + 1], None, op0=ALU.add),
                        waits=[(s_mm, mm_p)], inc=s_ev)
                    rotA.settle(s_ev)
            ev_q = s_ev.count

            b.chain("vector", [
                lambda q: nc.vector.tensor_tensor(sqs[:], vr[:], kr[:], op=ALU.mult),
                lambda q: nc.vector.tensor_tensor(kvr[:], vi[:], ki[:], op=ALU.mult),
                lambda q: nc.vector.tensor_tensor(kvr[:], kvr[:], sqs[:], op=ALU.add),
                lambda q: nc.vector.tensor_tensor(sqs[:], vi[:], kr[:], op=ALU.mult),
                lambda q: nc.vector.tensor_tensor(kvi[:], vr[:], ki[:], op=ALU.mult),
                lambda q: nc.vector.scalar_tensor_tensor(kvi[:], kvi[:], -1.0, sqs[:], op0=ALU.mult, op1=ALU.add),
            ], waits=[(s_ev, ev_q)])
            if pc_ == 0:
                b.chain("vector", [
                    lambda q: nc.vector.tensor_tensor(kvr[0:1, :], vr[0:1, :], kr[0:1, :], op=ALU.mult),
                    lambda q: nc.vector.tensor_tensor(kvi[0:1, :], vi[0:1, :], ki[0:1, :], op=ALU.mult),
                ])
            ev_u = s_ev.count
            b.dma("gpsimd", io["qscr"][mc_re * PC:(mc_re + 1) * PC, th0:th0 + HS], kvr[:], s_st,
                  waits=[(s_ev, ev_u)])
            b.dma("gpsimd", io["qscr"][mc_im * PC:(mc_im + 1) * PC, th0:th0 + HS], kvi[:], s_st,
                  waits=[(s_ev, ev_u)])
            for slc in range(HS // NB):
                n0 = th0 + slc * NB
                sl = slice(slc * NB, (slc + 1) * NB)
                b.on("scalar", lambda q, _sl=sl: nc.scalar.activation(
                    sqs[:, _sl], kvr[:, _sl], AF.Square),
                    waits=[(s_ev, ev_u), (s_mm, s_mm.count)], inc=s_sc)
                sc1 = s_sc.count
                b.on("tensor", lambda q, _wc=mc_re, _sl=sl: nc.tensor.matmul(
                    pN[:], nw[:, _wc:_wc + 1], sqs[:, _sl], start=True, stop=False),
                    waits=[(s_sc, sc1), (s_ev, s_ev.count)], inc=s_mm)
                mm1 = s_mm.count
                b.on("scalar", lambda q, _sl=sl: nc.scalar.activation(
                    sqs[:, _sl], kvi[:, _sl], AF.Square),
                    waits=[(s_mm, mm1)], inc=s_sc)
                sc2 = s_sc.count
                b.on("tensor", lambda q, _wc=mc_im, _sl=sl: nc.tensor.matmul(
                    pN[:], nw[:, _wc:_wc + 1], sqs[:, _sl], start=False, stop=True),
                    waits=[(s_sc, sc2)], inc=s_mm)
                mm2 = s_mm.count
                b.on("vector", lambda q, _n0=n0, _first=(pc_ == 0): (
                    nc.vector.tensor_copy(nrm2row[:, _n0:_n0 + NB], pN[:]) if _first
                    else nc.vector.tensor_tensor(nrm2row[:, _n0:_n0 + NB],
                                                 nrm2row[:, _n0:_n0 + NB], pN[:], op=ALU.add)),
                    waits=[(s_mm, mm2)], inc=s_ev)

    b.on("scalar", lambda q: nc.scalar.activation(nrm2row[:], nrm2row[:], AF.Sqrt),
         waits=[(s_ev, s_ev.count)], inc=s_sc)

    b.chain("vector", [
        lambda q: nc.vector.tensor_scalar(scl[:], nrm2row[:], 1e-9, 32.0, op0=ALU.add, op1=ALU.mult),
        lambda q: nc.vector.reciprocal(scl[:], scl[:]),
    ], waits=[(s_sc, s_sc.count)])
    ev_scale = s_ev.count

    b.free("a_sqs", "a_kvi", "a_kvr", "a_vi", "a_vr", "a_ki", "a_kr")
    qld = b.sbuf("a_qld", (PC, NDC * NB), BF16)
    xres = b.sbuf("a_xres", (PC, NB), F32)
    oev = b.sbuf("a_oev", (PC, NB), F32)
    sclb = b.sbuf("a_sclb", (PC, NB), F32)

    b.dma("gpsimd", *_ld3(io["c_IW"], wpk[:], 1024), s_gl,
          waits=[(s_mm, s_mm.count)])
    ld_iw = s_gl.count
    for nb_ in range(S // NB):
        n0 = nb_ * NB
        for kc in range(NDC):
            b.dma("sync", qld[:, kc * NB:(kc + 1) * NB],
                  io["qscr"][kc * PC:(kc + 1) * PC, n0:n0 + NB], s_ld,
                  waits=[(s_st, s_st.count), (s_mm, s_mm.count), (s_ev, s_ev.count),
                         (s_sc, s_sc.count)] if kc == 0 else ())
        ld_q = s_ld.count
        b.on("tensor", lambda q, _n0=n0: nc.tensor.matmul(
            pC[:], onesr[:], scl[:, _n0:_n0 + NB], start=True, stop=True),
            waits=[(s_ev, s_ev.count)], inc=s_mm)
        mm_s = s_mm.count
        b.on("vector", lambda q: nc.vector.tensor_copy(sclb[:], pC[:]),
             waits=[(s_mm, mm_s)], inc=s_ev)
        ev_sb = s_ev.count
        for mc in range(NDC):
            dstp, rw_ = rotA.take()
            b.mm(dstp[:],
                 [(wpk[:, kc * 1024 + mc * PC: kc * 1024 + (mc + 1) * PC],
                   qld[:, kc * NB:(kc + 1) * NB]) for kc in range(NDC)],
                 waits=rw_ + [(s_ld, ld_q), (s_gl, ld_iw)], inc=s_mm)
            mm_i = s_mm.count
            b.dma("sync", xres[:], xA[mc * PC:(mc + 1) * PC, n0:n0 + NB], s_xr,
                  waits=[(s_ev, s_ev.count)])
            ld_x = s_xr.count

            b.chain("vector", [
                lambda q, _p=dstp: nc.vector.tensor_tensor(oev[:], _p[:], sclb[:], op=ALU.mult),
                lambda q: nc.vector.tensor_tensor(oev[:], oev[:], xres[:], op=ALU.add),
            ], waits=[(s_mm, mm_i), (s_xr, ld_x), (s_ev, ev_sb)])
            rotA.settle(s_ev)
            b.dma("gpsimd", io["x1"][mc * PC:(mc + 1) * PC, n0:n0 + NB], oev[:], s_st,
                  waits=[(s_ev, s_ev.count)])
            b.wait("vector", [(s_st, s_st.count)])
    b.wait("gpsimd", [(s_st, s_st.count)])
    b.free("a_sclb", "a_oev", "a_xres", "a_qld")
    b.free("a_carry", "a_nrm2", "a_nw", "a_bcol", "a_bvec",
           "a_wp2", "a_wp1", "a_wp0", "a_wpk", "a_xn", "p_D", "p_N")



def build_l0_ffn(nc, b, io, sems):
    s_ld, s_mm, s_ev, s_sc, s_st = sems["ld"], sems["mm"], sems["ev"], sems["sc"], sems["st"]
    s_gl, s_xr = sems["gl"], sems["xr"]
    NSL = 64
    pS1 = b.psum("p_S1", (NSL, NB))
    pS2 = b.psum("p_S2", (NSL, NB))
    pT = b.psum("p_T", (PC, PC), BF16)
    rotF = Rot([b.sb["p_A"], b.sb["p_B"], b.sb["p_R1"], b.sb["p_R2"]])
    pA, pB, pC = b.sb["p_A"], b.sb["p_B"], b.sb["p_C"]
    pR1, pR2 = b.sb["p_R1"], b.sb["p_R2"]
    pR1 = pR1[0:1, :]
    pR2 = pR2[0:1, :]
    onesrb = b.sb["g_onesrb"]
    idn = b.sb["g_idn"]
    idnf = b.sb["g_idnf"]

    xn = b.sbuf("f_xn", (PC, NDC * S), BF16)
    w1x = b.sbuf("f_w1x", (PC, NDC * NB), BF16)
    w1s = b.sbuf("f_w1s", (PC, NDC * NB), BF16)
    w2 = b.sbuf("f_w2", (PC, 4 * NSL), BF16)
    wq = b.sbuf("f_wq", (PC, NDC * 256), BF16)
    wk = b.sbuf("f_wk", (PC, NDC * 256), BF16)
    wv = b.sbuf("f_wv", (PC, NDC * 1024), BF16)
    wvp = b.sbuf("f_wvp", (PC, NDC * 1024), BF16)
    wo = b.sbuf("f_wo", (PC, NDC * 1024), BF16)
    slotT = b.sbuf("f_slotT", (PC, NDC * NSL), BF16)
    smcol = b.sbuf("f_smcol", (PC, NDC), BF16)
    smcolf = b.sbuf("f_smcolf", (PC, NDC), F32)
    hcol = b.sbuf("f_hcol", (PC, 4), F32)
    b1c = b.sbuf("f_b1c", (PC, 4), F32)
    b2c = b.sbuf("f_b2c", (NSL, 1), F32)
    invt = b.sbuf("f_invt", (NSL, 1), F32)
    tau1 = b.sbuf("f_tau1", (1, 1), F32)
    s8 = b.sbuf("f_s8", (PC, 8), F32)
    ecol = b.sbuf("f_ecol", (PC, 1), F32)
    etile = b.sbuf("f_etile", (PC, NSL), F32)
    emsk = b.sbuf("f_emsk", (PC, NSL), BF16)
    atok = b.sbuf("f_atok", (PC, 32 * NSL), BF16)
    afm = b.sbuf("f_afm", (NSL, S), BF16)
    dcol = b.sbuf("f_dcol", (NSL, 1), F32)
    bihc = b.sbuf("f_bihc", (PC, 24), F32)
    bhhc = b.sbuf("f_bhhc", (PC, 24), F32)
    snT = b.sbuf("f_snT", (PC, NDC * NSL), BF16)
    snTf = b.sbuf("f_snTf", (PC, NDC * NSL), F32)
    qT = b.sbuf("f_qT", (PC, 2 * NSL), BF16)
    kT = b.sbuf("f_kT", (PC, 2 * NSL), BF16)
    bqc = b.sbuf("f_bqc", (PC, 2), F32)
    bkc = b.sbuf("f_bkc", (PC, 2), F32)
    vtok = b.sbuf("f_vtok", (NSL, 1024), BF16)
    bvrow = b.sbuf("f_bvrow", (1, 1024), F32)
    bvrowb = b.sbuf("f_bvrowb", (1, 1024), BF16)
    bvprow = b.sbuf("f_bvprow", (1, 1024), F32)
    bvprowb = b.sbuf("f_bvprowb", (1, 1024), BF16)
    sc64 = b.sbuf("f_sc64", (NSL, NSL), F32)
    at64 = b.sbuf("f_at64", (NSL, NSL), BF16)
    atT = b.sbuf("f_atT", (NSL, NSL), BF16)
    rowm = b.sbuf("f_rowm", (1, NSL), F32)
    rowv = b.sbuf("f_rowv", (1, NSL), F32)
    rown = b.sbuf("f_rown", (1, NSL), F32)
    lgc = b.sbuf("f_lgc", (PC, NDC), F32)
    lbc = b.sbuf("f_lbc", (PC, NDC), F32)
    svtok = b.sbuf("f_svtok", (NSL, 1024), BF16)
    boc = b.sbuf("f_boc", (PC, NDC), F32)
    ynT = b.sbuf("f_ynT", (PC, NDC * NSL), BF16)

    xA = io["x1"]
    b.dma("gpsimd", *_ld3(io["w1x"], w1x[:], NB), s_gl,
          waits=[(s_mm, s_mm.count), (s_ev, s_ev.count), (s_sc, s_sc.count),
                 (s_st, s_st.count)])
    b.dma("gpsimd", *_ld3(io["w1s"], w1s[:], NB), s_gl)
    b.dma("gpsimd", *_ld3(io["w2"], w2[:], NSL), s_gl)
    b.dma("gpsimd", *_ld3(io["wq"], wq[:], 256), s_gl)
    b.dma("gpsimd", *_ld3(io["wk"], wk[:], 256), s_gl)
    b.dma("gpsimd", *_ld3(io["wv"], wv[:], 1024), s_gl)
    b.dma("gpsimd", *_ld3(io["wvp"], wvp[:], 1024), s_gl)
    b.dma("gpsimd", *_ld3(io["wo"], wo[:], 1024), s_gl)
    b.dma("gpsimd", *_ld3(io["slotT"], slotT[:], NSL), s_gl)
    b.dma("sync", b1c[:], io["b1c"], s_ld,
          waits=[(s_ld, s_ld.count), (s_mm, s_mm.count), (s_ev, s_ev.count),
                 (s_sc, s_sc.count), (s_st, s_st.count)])
    b.dma("sync", b2c[:], io["b2c"], s_ld)
    b.dma("sync", tau1[:], io["tau"], s_ld)
    b.dma("sync", bihc[:], io["bihc"], s_ld)
    b.dma("sync", bhhc[:], io["bhhc"], s_ld)
    b.dma("sync", bqc[:], io["bqc"], s_ld)
    b.dma("sync", bkc[:], io["bkc"], s_ld)
    b.dma("sync", bvrow[:], io["bvrow"], s_ld)
    b.dma("sync", bvprow[:], io["bvprow"], s_ld)
    b.dma("sync", lgc[:], io["lngc"], s_ld)
    b.dma("sync", lbc[:], io["lnbc"], s_ld)
    b.dma("sync", boc[:], io["boc"], s_ld)
    ld0 = s_ld.count
    ld0g = s_gl.count

    alloc_ln_region(b)
    emit_ln(nc, b, xA, xn, sems)
    free_ln_region(b)

    hb = b.sbuf("f_hb", (PC, 4 * NB), BF16)
    rlfm = b.sbuf("f_rlfm", (NSL, S), F32)
    rltok = b.sbuf("f_rltok", (PC, 32 * NSL), F32)

    b.chain("vector",
            [lambda q, _dc=dc: nc.vector.tensor_reduce(
                smcolf[:, _dc:_dc + 1], slotT[:, _dc * NSL:(_dc + 1) * NSL], axis=AX.X, op=ALU.add)
             for dc in range(NDC)] +
            [lambda q: nc.vector.tensor_scalar(smcol[:], smcolf[:], 1.0 / NSL, None, op0=ALU.mult)],
            waits=[(s_ld, s_ld.count), (s_gl, s_gl.count)])
    b.on("scalar", lambda q: nc.scalar.activation(tau1[:], tau1[:], AF.Abs),
         waits=[(s_ld, s_ld.count)], inc=s_sc)

    b.chain("vector", [
        lambda q: nc.vector.tensor_scalar(tau1[:], tau1[:], 0.1, None, op0=ALU.add),
        lambda q: nc.vector.reciprocal(tau1[:], tau1[:]),
    ], waits=[(s_sc, s_sc.count)])
    ev_tau = s_ev.count
    b.mm(pC[:NSL, 0:1], [(b.sb["g_onesr"][:, 0:NSL], tau1[:])],
         waits=[(s_ev, ev_tau)], inc=s_mm)
    b.on("vector", lambda q: nc.vector.tensor_copy(invt[:], pC[:NSL, 0:1]),
         waits=[(s_mm, s_mm.count)], inc=s_ev)
    for mc in range(4):
        b.mm(pC[:, 0:1],
             [(w1s[:, kc * NB + mc * PC: kc * NB + (mc + 1) * PC],
               smcol[:, kc:kc + 1]) for kc in range(NDC)],
             waits=[(s_ev, s_ev.count)], inc=s_mm)
        mm_h = s_mm.count
        b.on("vector", lambda q, _mc=mc: nc.vector.scalar_tensor_tensor(
            hcol[:, _mc:_mc + 1], pC[:, 0:1], 1.0, b1c[:, _mc:_mc + 1],
            op0=ALU.mult, op1=ALU.add),
            waits=[(s_mm, mm_h)], inc=s_ev)
    ev_hcol = s_ev.count
    b.on("vector", lambda q: [nc.vector.tensor_copy(bvrowb[:], bvrow[:]),
                              nc.vector.tensor_copy(bvprowb[:], bvprow[:])],
         waits=[(s_ld, s_ld.count)], inc=s_ev)

    rotF.seed(s_ev)
    for nb_ in range(S // NB):
        n0 = nb_ * NB
        for mc in range(4):
            dstp, rw_ = rotF.take()
            b.mm(dstp[:],
                 [(w1x[:, kc * NB + mc * PC: kc * NB + (mc + 1) * PC],
                   xn[:, kc * S + n0: kc * S + n0 + NB]) for kc in range(NDC)],
                 waits=rw_ + [(s_gl, ld0g)], inc=s_mm)
            mm_h = s_mm.count
            b.on("scalar", lambda q, _mc=mc, _p=dstp: nc.scalar.activation(
                hb[:, _mc * NB:(_mc + 1) * NB], _p[:], AF.Gelu, bias=hcol[:, _mc:_mc + 1]),
                waits=[(s_mm, mm_h), (s_ev, ev_hcol)], inc=s_sc)
            rotF.settle(s_sc)
        sc_h = s_sc.count
        b.mm(pC[:NSL, :],
             [(w2[:, mc * NSL:(mc + 1) * NSL], hb[:, mc * NB:(mc + 1) * NB])
              for mc in range(4)],
             waits=[(s_sc, sc_h), (s_ev, s_ev.count)], inc=s_mm)
        mm_rl = s_mm.count
        b.on("vector", lambda q, _n0=n0: nc.vector.tensor_scalar(
            rlfm[:, _n0:_n0 + NB], pC[:NSL, :], b2c[:], invt[:],
            op0=ALU.add, op1=ALU.mult),
            waits=[(s_mm, mm_rl)], inc=s_ev)
    ev_rl = s_ev.count

    for tc in range(32):
        t0 = tc * PC
        b.on("tensor", lambda q, _t0=t0: nc.tensor.transpose(
            pA[:, 0:NSL], rlfm[:, _t0:_t0 + PC], idnf[0:NSL, 0:NSL]),
            waits=[(s_ev, s_ev.count)], inc=s_mm)
        mm_t = s_mm.count
        b.on("vector", lambda q, _tc=tc: nc.vector.tensor_copy(
            rltok[:, _tc * NSL:(_tc + 1) * NSL], pA[:, 0:NSL]),
            waits=[(s_mm, mm_t)], inc=s_ev)
        ev_t = s_ev.count
        rlc = rltok[:, tc * NSL:(tc + 1) * NSL]

        b.chain("vector", [
            lambda q, _rl=rlc: nc.vector.max(s8[:], _rl),
            lambda q: nc.vector.tensor_scalar(ecol[:], s8[:, 0:1], -1.0, None, op0=ALU.mult),
        ], waits=[(s_ev, ev_t)])
        ev_r = s_ev.count
        b.on("scalar", lambda q, _rl=rlc: nc.scalar.activation(
            etile[:], _rl, AF.Exp, bias=ecol[:]),
            waits=[(s_ev, ev_r)], inc=s_sc)
        sc_e = s_sc.count

        b.chain("vector", [
            lambda q, _rl=rlc: nc.vector.tensor_scalar(emsk[:], _rl, s8[:, 7:8], None, op0=ALU.is_ge),
            lambda q: nc.vector.tensor_tensor(etile[:], etile[:], emsk[:], op=ALU.mult),
            lambda q: nc.vector.tensor_reduce(ecol[:], etile[:], axis=AX.X, op=ALU.add),
            lambda q: nc.vector.reciprocal(ecol[:], ecol[:]),
            lambda q, _tc=tc: nc.vector.tensor_scalar(atok[:, _tc * NSL:(_tc + 1) * NSL], etile[:],
                                                      ecol[:], None, op0=ALU.mult),
        ], waits=[(s_sc, sc_e)])
        ev_a = s_ev.count
        b.on("tensor", lambda q, _tc=tc: _tbf(nc, pT[0:NSL, :], atok[:, _tc * NSL:(_tc + 1) * NSL], idn[:]),
            waits=[(s_ev, ev_a)], inc=s_mm)
        mm_at = s_mm.count
        b.on("vector", lambda q, _tc=tc: nc.vector.tensor_copy(
            afm[:, _tc * PC:(_tc + 1) * PC], pT[0:NSL, :]),
            waits=[(s_mm, mm_at)], inc=s_ev)
    ev_alpha = s_ev.count
    b.free("f_rltok", "f_rlfm", "f_hb")
    xtok = b.sbuf("f_xtok", (PC, NDC * PC), BF16)
    sit = b.sbuf("f_sit", (PC, NDC * NSL), BF16)
    siev = b.sbuf("f_siev", (NSL, 1024), BF16)
    git = b.sbuf("f_git", (PC, 24 * NSL), F32)
    ght = b.sbuf("f_ght", (PC, 24 * NSL), F32)
    grs = b.sbuf("f_grs", (PC, 8 * NSL), F32)
    gzs = b.sbuf("f_gzs", (PC, 8 * NSL), F32)
    gns = b.sbuf("f_gns", (PC, 8 * NSL), F32)
    yT = b.sbuf("f_yT", (PC, NDC * NSL), F32)
    y2 = b.sbuf("f_y2", (PC, NDC * NSL), BF16)
    wstage = b.sbuf("f_wstage", (PC, 2 * 1024), BF16)

    b.chain("vector", [
        lambda q: nc.vector.tensor_reduce(dcol[:], afm[:], axis=AX.X, op=ALU.add),
        lambda q: nc.vector.tensor_scalar(dcol[:], dcol[:], 1e-8, None, op0=ALU.add),
        lambda q: nc.vector.reciprocal(dcol[:], dcol[:]),
    ], waits=[(s_ev, ev_alpha)])
    ev_den = s_ev.count
    for tc in range(32):
        t0 = tc * PC
        for dc in range(NDC):
            b.on("tensor", lambda q, _dc=dc, _t0=t0: _tbf(nc, pT[:], xn[:, _dc * S + _t0: _dc * S + _t0 + PC], idn[:]),
                waits=[(s_ev, s_ev.count)], inc=s_mm)
            mm_x = s_mm.count
            b.on("vector", lambda q, _dc=dc: nc.vector.tensor_copy(
                xtok[:, _dc * PC:(_dc + 1) * PC], pT[:]),
                waits=[(s_mm, mm_x)], inc=s_ev)
        ev_x = s_ev.count
        for half, pS in [(0, pS1), (1, pS2)]:
            b.on("tensor", lambda q, _tc=tc, _h=half, _p=pS: nc.tensor.matmul(
                _p[:], atok[:, _tc * NSL:(_tc + 1) * NSL],
                xtok[:, _h * NB:(_h + 1) * NB],
                start=(_tc == 0), stop=(_tc == 31)),
                waits=[(s_ev, ev_x)],
                inc=(s_mm if tc == 31 else None))
    mm_si = s_mm.count
    b.on("vector", lambda q: [
        nc.vector.tensor_scalar(siev[:, 0:NB], pS1[:], dcol[:], None, op0=ALU.mult),
        nc.vector.tensor_scalar(siev[:, NB:1024], pS2[:], dcol[:], None, op0=ALU.mult)],
        waits=[(s_mm, mm_si), (s_ev, ev_den)], inc=s_ev)
    ev_si = s_ev.count
    for dc in range(NDC):
        b.on("tensor", lambda q, _dc=dc: _tbf(nc, pT[:, 0:NSL], siev[:, _dc * PC:(_dc + 1) * PC], idn[0:NSL, 0:NSL]),
            waits=[(s_ev, s_ev.count)], inc=s_mm)
        mm_s = s_mm.count
        b.on("vector", lambda q, _dc=dc: nc.vector.tensor_copy(
            sit[:, _dc * NSL:(_dc + 1) * NSL], pT[:, 0:NSL] if False else pT[0:PC, 0:NSL]),
            waits=[(s_mm, mm_s)], inc=s_ev)
    ev_sit = s_ev.count

    for gname, wsrc, dst, bcols in [("ih", "WihT", git, bihc), ("hh", "WhhT", ght, bhhc)]:
        rhs = sit if gname == "ih" else slotT
        for mc in range(24):
            buf = wstage[:, (mc % 2) * 1024:(mc % 2) * 1024 + 1024]
            b.dma("gpsimd", *_ld3(io[wsrc][:, mc * PC:(mc + 1) * PC], buf, PC), s_gl,
                  waits=[(s_mm, s_mm.count), (s_ev, s_ev.count)])
            ld_w = s_gl.count
            b.mm(pC[:, 0:NSL],
                 [(buf[:, kc * PC:(kc + 1) * PC], rhs[:, kc * NSL:(kc + 1) * NSL])
                  for kc in range(NDC)],
                 waits=[(s_gl, ld_w), (s_ev, s_ev.count)], inc=s_mm)
            mm_g = s_mm.count
            b.on("vector", lambda q, _dst=dst, _mc=mc, _bc=bcols: nc.vector.tensor_scalar(
                _dst[:, _mc * NSL:(_mc + 1) * NSL], pC[:, 0:NSL], _bc[:, _mc:_mc + 1], None, op0=ALU.add),
                waits=[(s_mm, mm_g)], inc=s_ev)
    ev_g = s_ev.count

    for dc in range(NDC):
        sl = slice(dc * NSL, (dc + 1) * NSL)
        slz = slice((NDC + dc) * NSL, (NDC + dc + 1) * NSL)
        sln = slice((2 * NDC + dc) * NSL, (2 * NDC + dc + 1) * NSL)

        def gru1(q, _sl=sl, _slz=slz, _sln=sln, _dc=dc):
            i1 = nc.vector.tensor_tensor(grs[:, _sl], git[:, _sl], ght[:, _sl], op=ALU.add)
            i2 = nc.vector.tensor_tensor(gzs[:, _sl], git[:, _slz], ght[:, _slz], op=ALU.add)
            return [i1, i2]
        b.on("vector", gru1, waits=[(s_ev, ev_g)], inc=s_ev)
        b.on("scalar", lambda q, _sl=sl: nc.scalar.activation(grs[:, _sl], grs[:, _sl], AF.Sigmoid),
             waits=[(s_ev, s_ev.count)], inc=s_sc)
        b.on("scalar", lambda q, _sl=sl: nc.scalar.activation(gzs[:, _sl], gzs[:, _sl], AF.Sigmoid),
             waits=[(s_ev, s_ev.count)], inc=s_sc)
        sc_rz = s_sc.count

        b.chain("vector", [
            lambda q, _sl=sl, _sln=sln: nc.vector.tensor_tensor(gns[:, _sl], grs[:, _sl], ght[:, _sln], op=ALU.mult),
            lambda q, _sl=sl, _sln=sln: nc.vector.tensor_tensor(gns[:, _sl], gns[:, _sl], git[:, _sln], op=ALU.add),
        ], waits=[(s_sc, sc_rz)])
        b.on("scalar", lambda q, _sl=sl: nc.scalar.activation(gns[:, _sl], gns[:, _sl], AF.Tanh),
             waits=[(s_ev, s_ev.count)], inc=s_sc)
        sc_n = s_sc.count

        b.chain("vector", [
            lambda q, _sl=sl: nc.vector.tensor_tensor(grs[:, _sl], slotT[:, _sl], gns[:, _sl], op=ALU.subtract),
            lambda q, _sl=sl: nc.vector.tensor_tensor(grs[:, _sl], grs[:, _sl], gzs[:, _sl], op=ALU.mult),
            lambda q, _sl=sl: nc.vector.tensor_tensor(snTf[:, _sl], grs[:, _sl], gns[:, _sl], op=ALU.add),
            lambda q, _sl=sl: nc.vector.tensor_copy(snT[:, _sl], snTf[:, _sl]),
        ], waits=[(s_sc, sc_n)])
    ev_sn = s_ev.count

    for dst, wsb, bcc, width in [(qT, wq, bqc, 256), (kT, wk, bkc, 256)]:
        for mc in range(width // PC):
            b.mm(pC[:, 0:NSL],
                 [(wsb[:, kc * width + mc * PC: kc * width + (mc + 1) * PC],
                   snT[:, kc * NSL:(kc + 1) * NSL]) for kc in range(NDC)],
                 waits=[(s_ev, s_ev.count)], inc=s_mm)
            mm_q = s_mm.count
            b.on("vector", lambda q, _dst=dst, _mc=mc, _bc=bcc: nc.vector.tensor_scalar(
                _dst[:, _mc * NSL:(_mc + 1) * NSL], pC[:, 0:NSL], _bc[:, _mc:_mc + 1], None, op0=ALU.add),
                waits=[(s_mm, mm_q)], inc=s_ev)
    for nb_ in range(2):
        b.mm(pS1[:, :] if nb_ == 0 else pS2[:, :],
             [(snT[:, kc * NSL:(kc + 1) * NSL],
               wv[:, kc * 1024 + nb_ * NB: kc * 1024 + (nb_ + 1) * NB]) for kc in range(NDC)] +
             [(onesrb[:, 0:NSL], bvrowb[:, nb_ * NB:(nb_ + 1) * NB])],
             waits=[(s_ev, s_ev.count)], inc=s_mm)
        mm_v = s_mm.count
        b.on("vector", lambda q, _nb=nb_: nc.vector.tensor_copy(
            vtok[:, _nb * NB:(_nb + 1) * NB], (pS1 if _nb == 0 else pS2)[:, :]),
            waits=[(s_mm, mm_v)], inc=s_ev)
    ev_v = s_ev.count

    b.mm(pC[:NSL, 0:NSL],
         [(qT[:, kc * NSL:(kc + 1) * NSL], kT[:, kc * NSL:(kc + 1) * NSL]) for kc in range(2)],
         waits=[(s_ev, s_ev.count)], inc=s_mm)
    mm_sc64 = s_mm.count
    b.on("vector", lambda q: nc.vector.tensor_scalar(
        sc64[:], pC[:NSL, 0:NSL], 0.0625, None, op0=ALU.mult),
        waits=[(s_mm, mm_sc64)], inc=s_ev)

    b.chain("vector", [
        lambda q: nc.vector.tensor_reduce(dcol[:], sc64[:], axis=AX.X, op=ALU.max),
        lambda q: nc.vector.tensor_scalar(dcol[:], dcol[:], -1.0, None, op0=ALU.mult),
    ], waits=[(s_ev, s_ev.count)])
    b.on("scalar", lambda q: nc.scalar.activation(sc64[:], sc64[:], AF.Exp, bias=dcol[:]),
         waits=[(s_ev, s_ev.count)], inc=s_sc)

    b.chain("vector", [
        lambda q: nc.vector.tensor_reduce(dcol[:], sc64[:], axis=AX.X, op=ALU.add),
        lambda q: nc.vector.reciprocal(dcol[:], dcol[:]),
        lambda q: nc.vector.tensor_scalar(at64[:], sc64[:], dcol[:], None, op0=ALU.mult),
    ], waits=[(s_sc, s_sc.count)])
    ev_at = s_ev.count
    b.on("tensor", lambda q: _tbf(nc, pT[0:NSL, 0:NSL], at64[:], idn[0:NSL, 0:NSL]),
         waits=[(s_ev, ev_at)], inc=s_mm)
    b.on("vector", lambda q: nc.vector.tensor_copy(atT[:], pT[0:NSL, 0:NSL]),
         waits=[(s_mm, s_mm.count)], inc=s_ev)
    ev_atT = s_ev.count
    for dc in range(NDC):
        b.mm(pC[:, 0:NSL],
             [(vtok[:NSL, dc * PC:(dc + 1) * PC], atT[:, :])],
             waits=[(s_ev, s_ev.count)], inc=s_mm)
        mm_y = s_mm.count
        b.on("vector", lambda q, _dc=dc: nc.vector.scalar_tensor_tensor(
            yT[:, _dc * NSL:(_dc + 1) * NSL], pC[:, 0:NSL], 1.0, snTf[:, _dc * NSL:(_dc + 1) * NSL],
            op0=ALU.mult, op1=ALU.add),
            waits=[(s_mm, mm_y)], inc=s_ev)
    ev_y = s_ev.count

    for dc in range(NDC):
        b.on("scalar", lambda q, _dc=dc: nc.scalar.activation(
            y2[:, _dc * NSL:(_dc + 1) * NSL], yT[:, _dc * NSL:(_dc + 1) * NSL], AF.Square),
            waits=[(s_ev, ev_y), (s_mm, s_mm.count)], inc=s_sc)
        sc_y = s_sc.count
        b.on("tensor", lambda q, _dc=dc: nc.tensor.matmul(
            pR2[:, 0:NSL], b.sb["g_onescolb"][:], y2[:, _dc * NSL:(_dc + 1) * NSL],
            start=(_dc == 0), stop=(_dc == NDC - 1)),
            waits=[(s_sc, sc_y)] + ([(s_ev, s_ev.count)] if dc == 0 else []),
            inc=(s_mm if dc == NDC - 1 else None))
        b.on("tensor", lambda q, _dc=dc: nc.tensor.matmul(
            pR1[:, 0:NSL], b.sb["g_onescol"][:], yT[:, _dc * NSL:(_dc + 1) * NSL],
            start=(_dc == 0), stop=(_dc == NDC - 1)),
            waits=[(s_ev, ev_y)] if dc == 0 else (),
            inc=(s_mm if dc == NDC - 1 else None))
    mm_ys = s_mm.count

    b.chain("vector", [
        lambda q: nc.vector.tensor_scalar(rowm[:], pR1[:, 0:NSL], 1.0 / D, None, op0=ALU.mult),
        lambda q: nc.vector.tensor_scalar(rowv[:], pR2[:, 0:NSL], 1.0 / D, None, op0=ALU.mult),
        lambda q: nc.vector.tensor_tensor(rown[:], rowm[:], rowm[:], op=ALU.mult),
        lambda q: nc.vector.tensor_tensor(rowv[:], rowv[:], rown[:], op=ALU.subtract),
        lambda q: nc.vector.tensor_scalar(rowv[:], rowv[:], 1e-5, None, op0=ALU.add),
    ], waits=[(s_mm, mm_ys)])
    b.on("scalar", lambda q: nc.scalar.activation(rowv[:], rowv[:], AF.Sqrt),
         waits=[(s_ev, s_ev.count)], inc=s_sc)

    b.chain("vector", [
        lambda q: nc.vector.reciprocal(rowv[:], rowv[:]),
        lambda q: nc.vector.tensor_tensor(rown[:], rowm[:], rowv[:], op=ALU.mult),
        lambda q: nc.vector.tensor_scalar(rown[:], rown[:], -1.0, None, op0=ALU.mult),
    ], waits=[(s_sc, s_sc.count)])
    b.on("tensor", lambda q: [
        nc.tensor.matmul(pA[:, 0:NSL], b.sb["g_onesr"][:], rowv[:], start=True, stop=True),
        nc.tensor.matmul(pB[:, 0:NSL], b.sb["g_onesr"][:], rown[:], start=True, stop=True)],
        waits=[(s_ev, s_ev.count)], inc=s_mm)
    mm_bc = s_mm.count
    for dc in range(NDC):
        sl_ = slice(dc * NSL, (dc + 1) * NSL)
        b.chain("vector", [
            lambda q, _sl=sl_: nc.vector.tensor_tensor(yT[:, _sl], yT[:, _sl], pA[:, 0:NSL], op=ALU.mult),
            lambda q, _sl=sl_: nc.vector.tensor_tensor(yT[:, _sl], yT[:, _sl], pB[:, 0:NSL], op=ALU.add),
            lambda q, _sl=sl_, _dc=dc: nc.vector.tensor_scalar(ynT[:, _sl], yT[:, _sl], lgc[:, _dc:_dc + 1],
                                                               lbc[:, _dc:_dc + 1], op0=ALU.mult, op1=ALU.add),
        ], waits=[(s_mm, mm_bc)])
    ev_yn = s_ev.count

    for nb_ in range(2):
        b.mm(pS1[:, :] if nb_ == 0 else pS2[:, :],
             [(ynT[:, kc * NSL:(kc + 1) * NSL],
               wvp[:, kc * 1024 + nb_ * NB: kc * 1024 + (nb_ + 1) * NB]) for kc in range(NDC)] +
             [(onesrb[:, 0:NSL], bvprowb[:, nb_ * NB:(nb_ + 1) * NB])],
             waits=[(s_ev, ev_yn)], inc=s_mm)
        mm_sv = s_mm.count
        b.on("vector", lambda q, _nb=nb_: nc.vector.tensor_copy(
            svtok[:, _nb * NB:(_nb + 1) * NB], (pS1 if _nb == 0 else pS2)[:, :]),
            waits=[(s_mm, mm_sv)], inc=s_ev)
    ev_sv = s_ev.count
    b.free("f_wstage", "f_y2", "f_yT", "f_gns", "f_gzs", "f_grs", "f_ght",
           "f_git", "f_siev", "f_sit", "f_xtok")
    outb = b.sbuf("f_outb", (PC, NDC * NB), BF16)
    xres = b.sbuf("f_xres", (PC, NB), F32)
    oev = b.sbuf("f_oev", (PC, NB), F32)

    rotF.seed(s_ev)
    for nb_ in range(S // NB):
        n0 = nb_ * NB
        for mc in range(NDC):
            dstp, rw_ = rotF.take()
            b.mm(dstp[:],
                 [(svtok[:NSL, mc * PC:(mc + 1) * PC], afm[:, n0:n0 + NB])],
                 waits=rw_ + [(s_ev, ev_sv)], inc=s_mm)
            mm_o = s_mm.count
            b.on("vector", lambda q, _mc=mc, _p=dstp: nc.vector.tensor_copy(
                outb[:, _mc * NB:(_mc + 1) * NB], _p[:]),
                waits=[(s_mm, mm_o)], inc=s_ev)
            rotF.settle(s_ev)
        ev_o = s_ev.count
        for mc in range(NDC):
            dstp, rw_ = rotF.take()
            b.mm(dstp[:],
                 [(wo[:, kc * 1024 + mc * PC: kc * 1024 + (mc + 1) * PC],
                   outb[:, kc * NB:(kc + 1) * NB]) for kc in range(NDC)],
                 waits=rw_ + [(s_ev, s_ev.count)], inc=s_mm)
            mm_f = s_mm.count
            b.dma("sync", xres[:], xA[mc * PC:(mc + 1) * PC, n0:n0 + NB], s_xr,
                  waits=[(s_ev, s_ev.count)])
            ld_x = s_xr.count

            b.chain("vector", [
                lambda q, _mc=mc, _p=dstp: nc.vector.tensor_scalar(
                    oev[:], _p[:], boc[:, _mc:_mc + 1], None, op0=ALU.add),
                lambda q: nc.vector.tensor_tensor(oev[:], oev[:], xres[:], op=ALU.add),
            ], waits=[(s_mm, mm_f), (s_xr, ld_x)])
            rotF.settle(s_ev)
            b.dma("gpsimd", io["x2"][mc * PC:(mc + 1) * PC, n0:n0 + NB], oev[:], s_st,
                  waits=[(s_ev, s_ev.count)])
            b.wait("vector", [(s_st, s_st.count)])
    b.wait("gpsimd", [(s_st, s_st.count)])
    b.free("f_oev", "f_xres", "f_outb",
           "f_ynT", "f_boc", "f_svtok", "f_lbc", "f_lgc", "f_rown", "f_rowv", "f_rowm",
           "f_atT", "f_at64", "f_sc64", "f_bvprowb", "f_bvprow", "f_bvrowb", "f_bvrow",
           "f_vtok", "f_bkc", "f_bqc", "f_kT", "f_qT", "f_snTf", "f_snT", "f_bhhc",
           "f_bihc", "f_dcol", "f_afm", "f_atok", "f_emsk", "f_etile", "f_ecol", "f_s8",
           "f_tau1", "f_invt", "f_b2c", "f_b1c", "f_hcol", "f_smcolf", "f_smcol",
           "f_slotT", "f_wo", "f_wvp", "f_wv", "f_wk", "f_wq", "f_w2", "f_w1s", "f_w1x",
           "f_xn", "p_T", "p_S2", "p_S1")




def build_l1_attn(nc, b, io, sems):
    s_ld, s_mm, s_ev, s_sc, s_st = sems["ld"], sems["mm"], sems["ev"], sems["sc"], sems["st"]
    s_gl, s_xr = sems["gl"], sems["xr"]
    s_qh, s_kh, s_vh = sems["qh"], sems["kh"], sems["vh"]
    s_mmav = sems["av"]
    H = 16
    pAV = b.psum("p_AV", (65, NB))
    pZ = b.psum("p_Z", (64, NB))
    pT = b.psum("p_T", (PC, PC), BF16)
    pA, pB, pC = b.sb["p_A"], b.sb["p_B"], b.sb["p_C"]
    rotT = Rot([pA, pB, pC, b.sb["p_R1"], b.sb["p_R2"]])
    idn = b.sb["g_idn"]
    trneg = b.sbuf("t_trneg", (PC, PC), BF16)
    iota8 = b.sbuf("t_iota8", (PC, 8), F32)

    ki = b.sbuf("t_ki", (64, S), BF16)
    wiT = b.sbuf("t_wiT", (PC, 32 * 4), F32)
    xn = b.sbuf("t_xn", (PC, NDC * S), BF16)

    xA = io["x2"]
    b.dma("sync", trneg[:], io["c_triu_neg"], s_xr,
          waits=[(s_ev, s_ev.count), (s_sc, s_sc.count), (s_mm, s_mm.count),
                 (sems["st"], sems["st"].count)])
    b.dma("sync", iota8[:], io["c_iota8"], s_xr)
    ld_tn = s_xr.count

    alloc_ln_region(b)
    emit_ln(nc, b, xA, xn, sems)
    free_ln_region(b)

    wst = b.sbuf("t_wst", (PC, 2 * NDC * PC), BF16)
    qc1 = b.sbuf("t_qc1", (PC, S), BF16)
    qc2 = b.sbuf("t_qc2", (PC, S), BF16)
    rsc = b.sbuf("t_rsc", (PC, S), BF16)
    rsc2 = b.sbuf("t_rsc2", (PC, S), BF16)
    cosP = b.sbuf("t_cosP", (PC, S), BF16)
    sinP = b.sbuf("t_sinP", (PC, S), BF16)
    wif = b.sbuf("t_wif", (4, S), F32)

    b.dma("sync", cosP[:], io["c_cosP"], s_xr,
          waits=[(s_ev, s_ev.count), (s_sc, s_sc.count), (s_mm, s_mm.count)])
    b.dma("sync", sinP[:], io["c_sinP"], s_xr)
    ld_cs = s_xr.count

    b.dma("gpsimd", *_ld3(io["WwiT"], wst[:, 0:NDC * 4], 4), s_gl,
          waits=[(s_ev, s_ev.count), (s_sc, s_sc.count), (s_mm, s_mm.count),
                 (s_st, s_st.count)])
    ld_wi = s_gl.count
    for nb_ in range(S // NB):
        n0 = nb_ * NB
        b.mm(pC[0:4, :],
             [(wst[:, kc * 4:(kc + 1) * 4], xn[:, kc * S + n0: kc * S + n0 + NB])
              for kc in range(NDC)],
             waits=[(s_gl, ld_wi), (s_ev, s_ev.count), (s_sc, s_sc.count)], inc=s_mm)
        mm_w = s_mm.count
        b.on("vector", lambda q, _n0=n0: nc.vector.tensor_copy(wif[:, _n0:_n0 + NB], pC[0:4, :]),
             waits=[(s_mm, mm_w)])
    idnf = b.sb["g_idnf"]
    for tc in range(32):
        b.on("tensor", lambda q, _tc=tc: nc.tensor.transpose(
            pC[:, 0:4], wif[:, _tc * PC:(_tc + 1) * PC], idnf[0:4, 0:4]),
            waits=[(s_ev, s_ev.count), (s_ld, s_ld.count)], inc=s_mm)
        mm_t = s_mm.count
        b.on("vector", lambda q, _tc=tc: nc.vector.tensor_copy(
            wiT[:, _tc * 4:(_tc + 1) * 4], pC[:, 0:4]),
            waits=[(s_mm, mm_t)])

    b.dma("gpsimd", *_ld3(io["WkiT"], wst[:, 0:NDC * 64], 64), s_gl,
          waits=[(s_mm, s_mm.count)])
    ld_ki = s_gl.count
    for nb_ in range(S // NB):
        n0 = nb_ * NB
        b.mm(pC[0:64, :],
             [(wst[:, kc * 64:(kc + 1) * 64], xn[:, kc * S + n0: kc * S + n0 + NB])
              for kc in range(NDC)],
             waits=[(s_gl, ld_ki), (s_ev, s_ev.count)], inc=s_mm)
        mm_k = s_mm.count
        b.on("vector", lambda q, _n0=n0: nc.vector.tensor_copy(ki[:, _n0:_n0 + NB], pC[0:64, :]),
             waits=[(s_mm, mm_k)])

    rotT.seed(s_ev)
    for ch in range(2):
        buf = ch % 2
        b.dma("gpsimd", *_ld3(io["WqiT"][:, ch * PC:(ch + 1) * PC], wst[:, buf * NDC * PC:(buf + 1) * NDC * PC], PC), s_gl,
              waits=[(s_mm, s_mm.count)])
        ld_q = s_gl.count
        for nb_ in range(S // NB):
            n0 = nb_ * NB
            dstp, rw_ = rotT.take()
            b.mm(dstp[:],
                 [(wst[:, buf * NDC * PC + kc * PC:buf * NDC * PC + (kc + 1) * PC],
                   xn[:, kc * S + n0: kc * S + n0 + NB]) for kc in range(NDC)],
                 waits=rw_ + [(s_gl, ld_q)], inc=s_mm)
            mm_q = s_mm.count
            b.on("vector", lambda q, _n0=n0, _p=dstp: nc.vector.tensor_copy(
                qc1[:, _n0:_n0 + NB], _p[:]),
                waits=[(s_mm, mm_q)])
            rotT.settle(s_ev)
        b.dma("gpsimd", io["qid"][ch * PC:(ch + 1) * PC, :], qc1[:], s_st,
              waits=[(s_ev, s_ev.count)])
        b.wait("vector", [(s_st, s_st.count)])

    for ti, (wname, dio) in enumerate([("WqT", "qd"), ("WkT", "kd")]):
        for c in range(4):
            for half, dst in [(0, qc1), (1, qc2)]:
                mc = c + 4 * half
                buf = (c * 2 + half) % 2
                b.dma("gpsimd", *_ld3(io[wname][:, mc * PC:(mc + 1) * PC],
                                      wst[:, buf * NDC * PC:(buf + 1) * NDC * PC], PC), s_gl,
                      waits=[(s_mm, s_mm.count)])
            ld_w = s_gl.count
            for half, dst in [(0, qc1), (1, qc2)]:
                buf = (c * 2 + half) % 2
                for nb_ in range(S // NB):
                    n0 = nb_ * NB
                    dstp, rw_ = rotT.take()
                    b.mm(dstp[:],
                         [(wst[:, buf * NDC * PC + kc * PC:buf * NDC * PC + (kc + 1) * PC],
                           xn[:, kc * S + n0: kc * S + n0 + NB]) for kc in range(NDC)],
                         waits=rw_ + [(s_gl, ld_w)], inc=s_mm)
                    mm_q = s_mm.count
                    b.on("vector", lambda q, _n0=n0, _p=dstp, _d=dst: nc.vector.tensor_copy(
                        _d[:, _n0:_n0 + NB], _p[:]),
                        waits=[(s_mm, mm_q)])
                    rotT.settle(s_ev)
            b.wait("vector", [(s_xr, ld_cs)])
            emit_rope(nc, b, qc1, qc2, rsc, rsc2, cosP, sinP)
            ev_rope = s_ev.count
            for h_loc in range(4):
                h = c * 4 + h_loc
                b.dma("gpsimd", io[dio][h, 0:32, :], qc1[h_loc * 32:(h_loc + 1) * 32, :], s_st,
                      waits=[(s_ev, ev_rope)])
                b.dma("gpsimd", io[dio][h, 32:64, :], qc2[h_loc * 32:(h_loc + 1) * 32, :], s_st,
                      waits=[(s_ev, ev_rope)])
            b.wait("vector", [(s_st, s_st.count)])

    vtile = qc1
    wv2 = b.sbuf("t_wv2", (PC, NDC * 1024), BF16)
    b.dma("gpsimd", *_ld3(io["WvT"], wv2[:], 1024), s_gl, waits=[(s_mm, s_mm.count)])
    ld_v = s_gl.count
    b.on("vector", lambda q: nc.vector.memset(vtile[:, 0:1040], 0.0))
    b.on("vector", lambda q: nc.vector.memset(
        vtile[:, 0:1040].rearrange("p (h d) -> p h d", d=65)[:, :, 64:65], 1.0))
    for sc in range(32):
        t0 = sc * PC
        for half in range(2):
            dstp, rw_ = rotT.take()
            b.mm(dstp[:],
                 [(xn[:, kc * S + t0: kc * S + t0 + PC],
                   wv2[:, kc * 1024 + half * NB: kc * 1024 + (half + 1) * NB]) for kc in range(NDC)],
                 waits=rw_ + [(s_gl, ld_v), (s_st, s_st.count)], inc=s_mm)
            mm_v = s_mm.count
            b.on("vector", lambda q, _h=half, _p=dstp: nc.vector.tensor_copy(
                vtile[:, 0:1040].rearrange("p (h d) -> p h d", d=65)[:, _h * 8:(_h + 1) * 8, 0:64],
                _p[:].rearrange("p (h d) -> p h d", d=64)),
                waits=[(s_mm, mm_v)])
            rotT.settle(s_ev)
        b.dma("gpsimd", io["vd"][t0:t0 + PC, :], vtile[:, 0:1040], s_st,
              waits=[(s_ev, s_ev.count)])
        b.wait("vector", [(s_st, s_st.count)])
    st_proj = s_st.count

    b.free("t_wv2", "t_wif", "t_sinP", "t_cosP", "t_rsc2", "t_rsc", "t_qc2", "t_qc1", "t_wst", "t_xn")
    idx = b.sbuf("u_idx", (PC, S), F32)
    scr32 = b.sbuf("u_scr32", (PC, S), F32)
    sgn0 = b.sbuf("u_sgn0", (PC, S), BF16)
    sgnB = b.sbuf("u_sgnB", (PC, S), BF16)
    msk = b.sbuf("u_msk", (PC, S), BF16)
    mTs = b.sbuf("u_mTs", (PC, 2 * 32 * NB), BF16)
    rl4 = b.sbuf("u_rl4", (PC, NB), BF16)
    qisb = b.sbuf("u_qisb", (64, 4 * NB), BF16)
    qh = b.sbuf("u_qh", (64, 2 * NB), BF16)
    kh = b.sbuf("u_kh", (64, 2 * S), BF16)
    vh = b.sbuf("u_vh", (PC, 2 * 32 * 65), BF16)
    un = b.sbuf("u_un", (PC, 4 * NB), BF16)
    acc = b.sbuf("u_acc", (65, NB), F32)
    hbuf = b.sbuf("u_hbuf", (64, NB), BF16)
    rhsb = b.sbuf("u_rhsb", (PC, NDC * NB), BF16)
    zrow = b.sbuf("u_zrow", (1, NB), F32)
    wo2 = b.sbuf("u_wo2", (PC, NDC * 1024), BF16)
    xres = b.sbuf("u_xres", (PC, NB), F32)
    oev = b.sbuf("u_oev", (PC, NB), F32)
    cols = {}
    for cn in ["poscnt", "nzcnt", "zcnt", "lo", "hi", "mid", "nmid", "ccol", "b01",
               "tmpa", "tmpb", "isP", "thr", "chi", "mcol", "tie", "quota", "rmin", "rmax",
               "snk"]:
        cols[cn] = b.sbuf("u_c_" + cn, (PC, 1), F32)

    b.dma("gpsimd", *_ld3(io["WoT"], wo2[:], 1024), s_gl,
          waits=[(s_mm, s_mm.count), (s_ev, s_ev.count), (s_sc, s_sc.count),
                 (s_st, s_st.count)])
    ld_wo = s_gl.count
    qh_slot = [0, 0]
    kh_slot = [0, 0]
    vh_slot = [0, 0]
    un_slot = [0, 0, 0, 0]
    rotT.seed(s_ev)

    def sel_gen(sp):
        t0 = sp * NB
        mT0 = (sp % 2) * 32 * NB
        for hi in range(4):
            b.dma("sync", qisb[:, hi * NB:(hi + 1) * NB],
                  io["qid"][hi * 64:(hi + 1) * 64, t0:t0 + NB], s_ld,
                  waits=([(s_st, s_st.count), (s_ev, s_ev.count), (s_mm, s_mm.count)]
                         if hi == 0 else ()))
        ld_qi = s_ld.count
        b.on("vector", lambda q, _sp=sp: nc.vector.memset(
            mTs[:, mT0 + _sp * 4 * NB: mT0 + (_sp * 4 + 4) * NB], -98304.0),
            waits=[(s_mm, s_mm.count)])
        yield
        if True:
            for tl in range(4):
                tcg = sp * 4 + tl
                sr = (tcg + 1) * PC
                nw = (sr + NB - 1) // NB
                first_mac = True
                for hi in range(4):
                    lq = qisb[:, hi * NB + tl * PC: hi * NB + (tl + 1) * PC]
                    for w in range(nw):
                        wid = min(NB, sr - w * NB)
                        dstp, rw_ = rotT.take()
                        b.on("tensor", lambda q, _l=lq, _w=w, _wd=wid, _p=dstp: nc.tensor.matmul(
                            _p[:, 0:_wd], _l, ki[:, _w * NB:_w * NB + _wd], start=True, stop=True),
                            waits=rw_ + [(s_ld, ld_qi)], inc=s_mm)
                        mm_d = s_mm.count
                        b.on("scalar", lambda q, _wd=wid, _p=dstp: nc.scalar.activation(
                            rl4[:, 0:_wd], _p[:, 0:_wd], AF.Relu),
                            waits=[(s_mm, mm_d), (s_ev, s_ev.count)])
                        sc_r = s_sc.count
                        rotT.settle(s_sc)
                        if first_mac:
                            b.on("vector", lambda q, _w=w, _wd=wid, _tc=tcg, _hi=hi: nc.vector.tensor_scalar(
                                idx[:, _w * NB:_w * NB + _wd], rl4[:, 0:_wd],
                                wiT[:, _tc * 4 + _hi:_tc * 4 + _hi + 1], None, op0=ALU.mult),
                                waits=[(s_sc, sc_r)])
                        else:
                            b.on("vector", lambda q, _w=w, _wd=wid, _tc=tcg, _hi=hi: nc.vector.scalar_tensor_tensor(
                                idx[:, _w * NB:_w * NB + _wd], rl4[:, 0:_wd],
                                wiT[:, _tc * 4 + _hi:_tc * 4 + _hi + 1],
                                idx[:, _w * NB:_w * NB + _wd], op0=ALU.mult, op1=ALU.add),
                                waits=[(s_sc, sc_r)])
                    if hi == 0:
                        first_mac = False
                    yield
                b.on("vector", lambda q, _tc=tcg: nc.vector.tensor_tensor(
                    idx[:, _tc * PC:(_tc + 1) * PC], idx[:, _tc * PC:(_tc + 1) * PC],
                    trneg[:], op=ALU.add), waits=[(s_xr, ld_tn)])
                b.on("scalar", lambda q, _sr=sr: nc.scalar.activation(
                    sgn0[:, 0:_sr], idx[:, 0:_sr], AF.Sign, accum_out=cols["poscnt"][:]),
                    waits=[(s_ev, s_ev.count)])
                b.on("scalar", lambda q, _sr=sr: nc.scalar.activation(
                    sgnB[:, 0:_sr], sgn0[:, 0:_sr], AF.Abs, accum_out=cols["nzcnt"][:]))
                b.on("scalar", lambda q: nc.scalar.copy(cols["snk"][:], cols["poscnt"][:]))
                b.on("scalar", lambda q: nc.scalar.copy(cols["snk"][:], cols["nzcnt"][:]))
                sc_cnt = s_sc.count
                yield
                b.chain("vector", [
                    lambda q: nc.vector.tensor_tensor(cols["tmpa"][:], cols["poscnt"][:], cols["nzcnt"][:], op=ALU.add),
                    lambda q: nc.vector.tensor_scalar(cols["poscnt"][:], cols["tmpa"][:], 0.5, None, op0=ALU.mult),
                    lambda q, _sr=sr: nc.vector.tensor_scalar(cols["zcnt"][:], cols["nzcnt"][:], -1.0, float(_sr), op0=ALU.mult, op1=ALU.add),
                    lambda q, _sr=sr: nc.vector.tensor_reduce(cols["rmin"][:], idx[:, 0:_sr - PC], axis=AX.X, op=ALU.min),
                    lambda q, _sr=sr: nc.vector.tensor_reduce(cols["rmax"][:], idx[:, 0:_sr], axis=AX.X, op=ALU.max),
                    lambda q: nc.vector.tensor_scalar(cols["isP"][:], cols["poscnt"][:], 512.0, None, op0=ALU.is_ge),
                    lambda q: nc.vector.tensor_scalar(cols["tmpa"][:], cols["rmin"][:], -1e-3, None, op0=ALU.add),
                    lambda q: nc.vector.tensor_tensor(cols["tmpb"][:], cols["tmpa"][:], cols["isP"][:], op=ALU.mult),
                    lambda q: nc.vector.tensor_tensor(cols["lo"][:], cols["tmpa"][:], cols["tmpb"][:], op=ALU.subtract),
                    lambda q: nc.vector.tensor_scalar(cols["tmpb"][:], cols["isP"][:], 2.0 ** -30, None, op0=ALU.mult),
                    lambda q: nc.vector.tensor_tensor(cols["lo"][:], cols["lo"][:], cols["tmpb"][:], op=ALU.add),
                    lambda q: nc.vector.tensor_scalar(cols["tmpa"][:], cols["rmax"][:], 1.0, None, op0=ALU.add),
                    lambda q: nc.vector.tensor_tensor(cols["tmpa"][:], cols["tmpa"][:], cols["isP"][:], op=ALU.mult),
                    lambda q: nc.vector.tensor_scalar(cols["tmpb"][:], cols["isP"][:], -1.0, 1.0, op0=ALU.mult, op1=ALU.add),
                    lambda q: nc.vector.tensor_scalar(cols["tmpb"][:], cols["tmpb"][:], 2.0 ** -30, None, op0=ALU.mult),
                    lambda q: nc.vector.tensor_tensor(cols["hi"][:], cols["tmpa"][:], cols["tmpb"][:], op=ALU.add),
                ], waits=[(s_sc, sc_cnt)])
                yield
                for it in range(18):
                    b.chain("vector", [
                        lambda q: nc.vector.tensor_tensor(cols["mid"][:], cols["lo"][:], cols["hi"][:], op=ALU.add),
                        lambda q: nc.vector.tensor_scalar(cols["mid"][:], cols["mid"][:], 0.5, None, op0=ALU.mult),
                        lambda q, _sr=sr: nc.vector.tensor_scalar(sgnB[:, 0:_sr], idx[:, 0:_sr], cols["mid"][:], 0.0, op0=ALU.is_ge, op1=ALU.add, accum_out=cols["ccol"][:]),
                        lambda q: nc.vector.tensor_scalar(cols["b01"][:], cols["ccol"][:], 512.0, None, op0=ALU.is_ge),
                        lambda q: nc.vector.tensor_tensor(cols["tmpa"][:], cols["mid"][:], cols["lo"][:], op=ALU.subtract),
                        lambda q: nc.vector.scalar_tensor_tensor(cols["lo"][:], cols["tmpa"][:], cols["b01"][:], cols["lo"][:], op0=ALU.mult, op1=ALU.add),
                        lambda q: nc.vector.tensor_tensor(cols["tmpa"][:], cols["mid"][:], cols["hi"][:], op=ALU.subtract),
                        lambda q: nc.vector.tensor_scalar(cols["tmpb"][:], cols["b01"][:], -1.0, 1.0, op0=ALU.mult, op1=ALU.add),
                        lambda q: nc.vector.scalar_tensor_tensor(cols["hi"][:], cols["tmpa"][:], cols["tmpb"][:], cols["hi"][:], op0=ALU.mult, op1=ALU.add),
                    ])
                    if it % 2 == 1:
                        yield
                b.chain("vector", [
                    lambda q, _sr=sr: nc.vector.tensor_scalar(sgnB[:, 0:_sr], idx[:, 0:_sr], cols["hi"][:], 0.0, op0=ALU.is_ge, op1=ALU.add, accum_out=cols["chi"][:]),
                    lambda q: nc.vector.tensor_scalar(cols["mcol"][:], cols["chi"][:], -1.0, 512.0, op0=ALU.mult, op1=ALU.add),
                    lambda q: nc.vector.tensor_scalar(cols["mcol"][:], cols["mcol"][:], 1.0, 8.0, op0=ALU.max, op1=ALU.min),
                    lambda q: nc.vector.tensor_scalar(cols["mcol"][:], cols["mcol"][:], -1.0, None, op0=ALU.add),
                    lambda q, _sr=sr: nc.vector.tensor_scalar(sgnB[:, 0:_sr], idx[:, 0:_sr], cols["hi"][:], -1e6, op0=ALU.is_ge, op1=ALU.mult),
                    lambda q, _sr=sr: nc.vector.tensor_tensor(scr32[:, 0:_sr], idx[:, 0:_sr], sgnB[:, 0:_sr], op=ALU.add),
                    lambda q, _sr=sr: nc.vector.max(rl4[:, 0:8], scr32[:, 0:_sr]),
                    lambda q: nc.vector.tensor_scalar(sgnB[:, 0:8], iota8[:], cols["mcol"][:], None, op0=ALU.is_equal),
                    lambda q: nc.vector.tensor_tensor(sgnB[:, 8:16], rl4[:, 0:8], sgnB[:, 0:8], op=ALU.mult),
                    lambda q: nc.vector.tensor_reduce(cols["thr"][:], sgnB[:, 8:16], axis=AX.X, op=ALU.add),
                    lambda q: nc.vector.tensor_scalar(cols["tie"][:], cols["poscnt"][:], 512.0, None, op0=ALU.is_lt),
                    lambda q: nc.vector.tensor_tensor(cols["tmpa"][:], cols["poscnt"][:], cols["zcnt"][:], op=ALU.add),
                    lambda q: nc.vector.tensor_scalar(cols["tmpa"][:], cols["tmpa"][:], 512.0, None, op0=ALU.is_ge),
                    lambda q: nc.vector.tensor_tensor(cols["tie"][:], cols["tie"][:], cols["tmpa"][:], op=ALU.mult),
                    lambda q: nc.vector.tensor_tensor(cols["tmpa"][:], cols["thr"][:], cols["tie"][:], op=ALU.mult),
                    lambda q: nc.vector.tensor_tensor(cols["thr"][:], cols["thr"][:], cols["tmpa"][:], op=ALU.subtract),
                    lambda q: nc.vector.tensor_scalar(cols["tmpa"][:], cols["tie"][:], 2.0 ** -100, None, op0=ALU.mult),
                    lambda q: nc.vector.tensor_tensor(cols["thr"][:], cols["thr"][:], cols["tmpa"][:], op=ALU.add),
                    lambda q: nc.vector.tensor_scalar(cols["quota"][:], cols["poscnt"][:], -1.0, 512.0, op0=ALU.mult, op1=ALU.add),
                    lambda q: nc.vector.tensor_tensor(cols["quota"][:], cols["quota"][:], cols["tie"][:], op=ALU.mult),
                    lambda q: nc.vector.tensor_scalar(cols["tmpa"][:], cols["tie"][:], -1.0, 1.0, op0=ALU.mult, op1=ALU.add),
                    lambda q: nc.vector.tensor_tensor(cols["quota"][:], cols["quota"][:], cols["tmpa"][:], op=ALU.subtract),
                    lambda q, _sr=sr: nc.vector.tensor_scalar(sgnB[:, 0:_sr], sgn0[:, 0:_sr], 0.0, None, op0=ALU.is_equal),
                    lambda q, _sr=sr: nc.vector.tensor_tensor_scan(scr32[:, 0:_sr], sgnB[:, 0:_sr], sgnB[:, 0:_sr], 0.0, op0=ALU.add, op1=ALU.bypass),
                    lambda q, _sr=sr: nc.vector.tensor_scalar(scr32[:, 0:_sr], scr32[:, 0:_sr], cols["quota"][:], None, op0=ALU.is_le),
                    lambda q, _sr=sr: nc.vector.tensor_tensor(sgnB[:, 0:_sr], sgnB[:, 0:_sr], scr32[:, 0:_sr], op=ALU.mult),
                    lambda q, _sr=sr: nc.vector.tensor_scalar(msk[:, 0:_sr], idx[:, 0:_sr], cols["thr"][:], None, op0=ALU.is_ge),
                    lambda q, _sr=sr: nc.vector.tensor_tensor(msk[:, 0:_sr], msk[:, 0:_sr], sgnB[:, 0:_sr], op=ALU.add),
                    lambda q, _sr=sr: nc.vector.tensor_scalar(msk[:, 0:_sr], msk[:, 0:_sr], 98304.0, -98304.0, op0=ALU.mult, op1=ALU.add),
                ], waits=[(s_xr, ld_tn)])
                ev_msk = s_ev.count
                yield
                for sc in range(tcg + 1):
                    dstp2, rw2 = rotT.take()
                    b.on("tensor", lambda q, _sc=sc, _p=dstp2: nc.tensor.matmul(
                        _p[0:PC, 0:PC], msk[:, _sc * PC:(_sc + 1) * PC], idn[:],
                        start=True, stop=True),
                        waits=rw2 + [(s_ev, ev_msk)], inc=s_mm)
                    mm_mt = s_mm.count
                    b.on("scalar", lambda q, _sc=sc, _tl=tl, _p=dstp2: nc.scalar.copy(
                        mTs[:, mT0 + _sc * NB + _tl * PC: mT0 + _sc * NB + (_tl + 1) * PC],
                        _p[0:PC, 0:PC]),
                        waits=[(s_mm, mm_mt)])
                    rotT.settle(s_sc)
                    if sc % 8 == 7:
                        yield
        yield



    sel_done = {}
    gen = None
    for sp in range(8):
        t0 = sp * NB
        srng = min(S, (sp + 1) * NB)
        nsc = srng // PC
        if sp == 0:
            gen = sel_gen(1)
        for h in range(H):
            hs = h % 2
            qhs = qh[:, hs * NB:(hs + 1) * NB]
            khs = kh[:, hs * S:hs * S + srng]
            vhs = vh[:, hs * 32 * 65:hs * 32 * 65 + nsc * 65]
            b.dma("sync", qhs, io["qd"][h, :, t0:t0 + NB], s_qh,
                  waits=[(s_mm, qh_slot[hs]), (s_qh, s_qh.count)]
                        + ([(s_ev, s_ev.count), (s_st, s_st.count),
                            (s_mm, s_mm.count)] if (h < 2) else []))
            ld_qh = s_qh.count
            b.dma("sync", khs, io["kd"][h, :, 0:srng], s_kh,
                  waits=[(s_mm, kh_slot[hs]), (s_kh, s_kh.count)])
            ld_kh = s_kh.count
            b.dma("sync", vhs.rearrange("p (c f) -> p c f", f=65),
                  io["vd"][0:srng, h * 65:(h + 1) * 65].rearrange("(c p) f -> p c f", p=PC),
                  s_vh, waits=[(s_mmav, vh_slot[hs]), (s_vh, s_vh.count)])
            ld_vh = s_vh.count
            for g0 in range(0, nsc, 4):
                grp = list(range(g0, min(g0 + 4, nsc)))
                ginfo = []
                for sc in grp:
                    dstp, rw_ = rotT.take()
                    if sp > 0:
                        b.on("tensor", lambda q, _sc=sc, _p=dstp, _k=khs, _q=qhs: nc.tensor.matmul(
                            _p[:], _k[:, _sc * PC:(_sc + 1) * PC], _q, start=True, stop=False),
                            waits=rw_ + [(s_kh, ld_kh), (s_qh, ld_qh)],
                            inc=s_mm)
                        sdv, sds = sel_done.get(sp, (0, 0))
                        b.on("tensor", lambda q, _sc=sc, _p=dstp, _m0=(sp % 2) * 32 * NB: nc.tensor.matmul(
                            _p[:], idn[:], mTs[:, _m0 + _sc * NB: _m0 + (_sc + 1) * NB],
                            start=False, stop=True),
                            waits=[(s_ev, sdv), (s_sc, sds)], inc=s_mm)
                    else:
                        b.on("tensor", lambda q, _sc=sc, _p=dstp, _k=khs, _q=qhs: nc.tensor.matmul(
                            _p[:], _k[:, _sc * PC:(_sc + 1) * PC], _q, start=True, stop=True),
                            waits=rw_ + [(s_kh, ld_kh), (s_qh, ld_qh)],
                            inc=s_mm)
                    ginfo.append((sc, dstp, s_mm.count))
                ubs = {}
                for sc, dstp, mm_s in ginfo:
                    usl = sc % 4
                    ub = un[:, usl * NB:(usl + 1) * NB]
                    ubs[sc] = ub
                    b.on("scalar", lambda q, _p=dstp, _u=ub: nc.scalar.activation(
                        _u, _p[:], AF.Exp, scale=0.125),
                        waits=[(s_mm, mm_s), (s_mmav, un_slot[usl])])
                    rotT.settle(s_sc)
                ng = len(ginfo)
                for i, (sc, dstp, mm_s) in enumerate(ginfo):
                    avw = [(s_sc, s_sc.count - (ng - 1 - i)), (s_vh, ld_vh)]
                    if sc == 0:
                        avw = avw + [(s_ev, s_ev.count)]
                    usl = sc % 4
                    b.on("tensor", lambda q, _sc=sc, _u=ubs[sc], _first=(sc == 0), _last=(sc == nsc - 1), _v=vhs: nc.tensor.matmul(
                        pAV[:], _v[:, _sc * 65:(_sc + 1) * 65], _u,
                        start=_first, stop=_last),
                        waits=avw, inc=s_mmav)
                    un_slot[usl] = s_mmav.count
            qh_slot[hs] = s_mm.count
            kh_slot[hs] = s_mm.count
            vh_slot[hs] = s_mmav.count
            if gen is not None:
                for _ in range(7):
                    try:
                        next(gen)
                    except StopIteration:
                        gen = None
                        break
            mm_av = s_mmav.count
            b.chain("vector", [
                lambda q: nc.vector.tensor_copy(zrow[:], pAV[64:65, :]),
                lambda q: nc.vector.reciprocal(zrow[:], zrow[:]),
            ], waits=[(s_mmav, mm_av)])
            ev_z = s_ev.count
            b.mm(pZ[:], [(b.sb["g_onesr"][:, 0:64], zrow[:])],
                 waits=[(s_ev, ev_z)], inc=s_mm)
            mm_z = s_mm.count
            b.on("vector", lambda q: nc.vector.tensor_copy(acc[0:64, :], pAV[0:64, :]),
                 waits=[(s_mm, mm_z)])
            b.on("vector", lambda q: nc.vector.tensor_tensor(hbuf[:], acc[0:64, :], pZ[:], op=ALU.mult))
            ev_h = s_ev.count
            b.dma("gpsimd", rhsb[(h % 2) * 64:(h % 2) * 64 + 64, (h // 2) * NB:(h // 2 + 1) * NB],
                  hbuf[:], s_st, waits=[(s_ev, ev_h)])
            b.wait("vector", [(s_st, s_st.count)])
        if gen is not None:
            for _ in gen:
                pass
            gen = None
        if sp + 1 < 8:
            sel_done[sp + 1] = (s_ev.count, s_sc.count)
            gen = sel_gen(sp + 2) if sp + 2 < 8 else None
        st_heads = s_st.count
        for mc in range(NDC):
            dstp, rw_ = rotT.take()
            b.mm(dstp[:],
                 [(wo2[:, kc * 1024 + mc * PC: kc * 1024 + (mc + 1) * PC],
                   rhsb[:, kc * NB:(kc + 1) * NB]) for kc in range(NDC)],
                 waits=rw_ + [(s_st, st_heads), (s_gl, ld_wo)], inc=s_mm)
            mm_o = s_mm.count
            b.dma("sync", xres[:], xA[mc * PC:(mc + 1) * PC, t0:t0 + NB], s_xr,
                  waits=[(s_ev, s_ev.count)])
            ld_x = s_xr.count
            b.on("vector", lambda q, _p=dstp: nc.vector.tensor_tensor(oev[:], _p[:], xres[:], op=ALU.add),
                 waits=[(s_mm, mm_o), (s_xr, ld_x)])
            rotT.settle(s_ev)
            b.dma("gpsimd", io["x3"][mc * PC:(mc + 1) * PC, t0:t0 + NB], oev[:], s_st,
                  waits=[(s_ev, s_ev.count)])
            b.wait("vector", [(s_st, s_st.count)])
    b.wait("gpsimd", [(s_st, s_st.count)])
    for cn in reversed(["poscnt", "nzcnt", "zcnt", "lo", "hi", "mid", "nmid", "ccol",
                        "b01", "tmpa", "tmpb", "isP", "thr", "chi", "mcol", "tie",
                        "quota", "rmin", "rmax", "snk"]):
        b.free("u_c_" + cn)
    b.free("u_oev", "u_xres", "u_wo2", "u_zrow", "u_rhsb", "u_hbuf", "u_acc", "u_un",
           "u_vh", "u_kh", "u_qh", "u_qisb", "u_rl4", "u_mTs", "u_msk", "u_sgnB",
           "u_sgn0", "u_scr32", "u_idx",
           "t_wiT", "t_ki", "t_iota8", "t_trneg", "p_T", "p_Z", "p_AV")


def emit_rope(nc, b, q1, q2, tmp, tmp2, cosP, sinP):
    nv = nc.vector
    ALU_ = ALU
    b.chain("vector", [
        lambda q: nv.tensor_tensor(tmp[:], q2[:], sinP[:], op=ALU_.mult),
        lambda q: nv.tensor_tensor(tmp2[:], q1[:], sinP[:], op=ALU_.mult),
        lambda q: nv.tensor_tensor(q2[:], q2[:], cosP[:], op=ALU_.mult),
        lambda q: nv.tensor_tensor(q1[:], q1[:], cosP[:], op=ALU_.mult),
        lambda q: nv.tensor_tensor(q1[:], q1[:], tmp[:], op=ALU_.subtract),
        lambda q: nv.tensor_tensor(q2[:], q2[:], tmp2[:], op=ALU_.add),
    ])

def build_l1_ffn(nc, b, io, sems, core_id=None):
    s_ld, s_mm, s_ev, s_sc, s_st = sems["ld"], sems["mm"], sems["ev"], sems["sc"], sems["st"]
    s_gl, s_xr, s_cc, s_gr = sems["gl"], sems["xr"], sems["cc"], sems["gr"]
    NU = 12
    CAP = 5120.0
    pTw = b.psum("p_T", (PC, NB))
    pT = pTw[:, 0:PC]
    pD = b.psum("p_D", (PC, NB))
    pE = b.psum("p_E", (PC, NB))
    pA, pB, pC = b.sb["p_A"], b.sb["p_B"], b.sb["p_C"]
    pR1, pR2 = b.sb["p_R1"], b.sb["p_R2"]
    pR1 = pR1[0:1, :]
    pR2 = pR2[0:1, :]
    rot4 = Rot([pA, pB, pD, pE, b.sb["p_R1"], b.sb["p_R2"]])
    rotC = Rot([pC, pTw])
    onesrb = b.sb["g_onesrb"]
    idnf = b.sb["g_idnf"]
    tric = b.sbuf("m_tric", (PC, PC), BF16)

    xn = b.sbuf("m_xn", (PC, NDC * S), BF16)
    gwT = b.sbuf("m_gwT", (PC, NDC * 9), BF16)
    rlf = b.sbuf("m_rlf", (9, S), F32)
    fltok = b.sbuf("m_fltok", (PC, 32 * 9), F32)
    rntok = b.sbuf("m_rntok", (PC, 32 * 8), F32)
    mtok = b.sbuf("m_mtok", (PC, 32 * 8), BF16)
    mcaptok = b.sbuf("m_mcap", (PC, 32 * 8), BF16)
    glwtok = b.sbuf("m_glwtok", (PC, 32 * 9), F32)
    glwfm = b.sbuf("m_glwfm", (9, S), F32)
    srt8 = b.sbuf("m_srt8", (PC, 8), F32)
    absdl = b.sbuf("m_absdl", (PC, 8), F32)
    absrt = b.sbuf("m_absrt", (PC, 8), F32)
    dsc = b.sbuf("m_dsc", (PC, 8), F32)
    fsc = b.sbuf("m_fsc", (PC, 8), F32)
    jms = b.sbuf("m_jms", (PC, 8), F32)
    als = b.sbuf("m_als", (PC, 8), F32)
    ets = b.sbuf("m_ets", (PC, 8), F32)
    rwt = b.sbuf("m_rwt", (PC, 8), F32)
    m9 = b.sbuf("m_m9", (PC, 9), F32)
    e9 = b.sbuf("m_e9", (PC, 9), F32)
    ncol = b.sbuf("m_ncol", (PC, 1), F32)
    zcol = b.sbuf("m_zcol", (PC, 1), F32)
    kcol = b.sbuf("m_kcol", (PC, 1), F32)
    wcol = b.sbuf("m_wcol", (PC, 1), F32)
    runrow = b.sbuf("m_runrow", (1, 8), F32)
    cntrow = b.sbuf("m_cntrow", (1, 8), F32)
    ccs = b.sbuf("m_ccs", (PC, PC), F32)
    ccsel = b.sbuf("m_ccsel", (1, PC), F32)
    prefc = b.sbuf("m_prefc", (8, 1), F32)
    ccld = b.sbuf("m_ccld", (8, 8), F32)

    xA = io["x3"]
    b.dma("sync", tric[:], io["c_tric"], s_ld,
          waits=[(s_ev, s_ev.count), (s_sc, s_sc.count), (s_mm, s_mm.count),
                 (s_st, s_st.count), (s_ld, s_ld.count)])
    b.dma("sync", ccsel[:], io["cc_sel"], s_ld)
    b.dma("sync", prefc[:], io["cc_pref"], s_ld)
    b.dma("gpsimd", *_ld3(io["gwT"], gwT[:], 9), s_gl,
          waits=[(s_st, s_st.count), (s_ev, s_ev.count), (s_sc, s_sc.count),
                 (s_mm, s_mm.count)])

    alloc_ln_region(b)
    emit_ln(nc, b, xA, xn, sems)
    free_ln_region(b)

    for nb_ in range(S // NB):
        n0 = nb_ * NB
        b.mm(pC[0:9, :],
             [(gwT[:, kc * 9:(kc + 1) * 9], xn[:, kc * S + n0: kc * S + n0 + NB])
              for kc in range(NDC)],
             waits=[(s_ev, s_ev.count), (s_gl, s_gl.count)], inc=s_mm)
        mm_l = s_mm.count
        b.on("vector", lambda q, _n0=n0: nc.vector.tensor_copy(rlf[:, _n0:_n0 + NB], pC[0:9, :]),
             waits=[(s_mm, mm_l)])

    for tc in range(32):
        t0 = tc * PC
        b.on("tensor", lambda q, _t0=t0: nc.tensor.transpose(
            pT[:, 0:9], rlf[:, _t0:_t0 + PC], idnf[0:9, 0:9]),
            waits=[(s_ev, s_ev.count)], inc=s_mm)
        mm_t = s_mm.count
        b.on("vector", lambda q, _tc=tc: nc.vector.tensor_copy(
            fltok[:, _tc * 9:(_tc + 1) * 9], pT[:, 0:9]),
            waits=[(s_mm, mm_t)])
        dl = fltok[:, tc * 9: tc * 9 + 8]
        b.chain("vector", [
            lambda q, _dl=dl: nc.vector.max(srt8[:], _dl),
            lambda q: nc.vector.memset(rwt[:], 0.0),
            lambda q: nc.vector.tensor_scalar(ncol[:], srt8[:, 0:1], -1.0, None, op0=ALU.mult),
        ])
        b.on("scalar", lambda q: nc.scalar.activation(ets[:], srt8[:], AF.Exp, bias=ncol[:]),
             waits=[(s_ev, s_ev.count)])
        b.chain("vector", [
            lambda q: nc.vector.tensor_reduce(zcol[:], ets[:], axis=AX.X, op=ALU.add),
            lambda q: nc.vector.reciprocal(zcol[:], zcol[:]),
            lambda q: nc.vector.tensor_scalar(ets[:], ets[:], zcol[:], None, op0=ALU.mult),
            lambda q: nc.vector.tensor_tensor_scan(ets[:], ets[:], ets[:], 0.0, op0=ALU.add, op1=ALU.bypass),
            lambda q: nc.vector.tensor_scalar(ets[:], ets[:], 0.75, None, op0=ALU.is_lt),
            lambda q: nc.vector.tensor_reduce(kcol[:], ets[:], axis=AX.X, op=ALU.add),
            lambda q: nc.vector.tensor_scalar(kcol[:], kcol[:], 1.0, 4.0, op0=ALU.add, op1=ALU.min),
        ], waits=[(s_sc, s_sc.count)])
        b.on("scalar", lambda q, _dl=dl: nc.scalar.activation(absdl[:], _dl, AF.Abs),
             waits=[(s_ev, s_ev.count)])
        b.on("scalar", lambda q: nc.scalar.activation(absrt[:], srt8[:], AF.Abs),
             waits=[(s_ev, s_ev.count)])
        sc_abs = s_sc.count
        b.wait("vector", [(s_sc, sc_abs)])
        for j in range(4):
            thr = srt8[:, j:j + 1]
            b.chain("vector", [
                lambda q, _dl=dl, _t=thr: nc.vector.tensor_scalar(dsc[:], _dl, _t, -1.0, op0=ALU.subtract, op1=ALU.mult),
                lambda q, _j=j: nc.vector.tensor_scalar(fsc[:], absdl[:], absrt[:, _j:_j + 1], 0.2, op0=ALU.max, op1=ALU.mult),
                lambda q: nc.vector.tensor_tensor(jms[:], dsc[:], fsc[:], op=ALU.is_gt),
                lambda q, _dl=dl, _t=thr: nc.vector.tensor_scalar(als[:], _dl, _t, None, op0=ALU.is_le),
                lambda q: nc.vector.tensor_tensor(als[:], als[:], jms[:], op=ALU.subtract),
                lambda q, _t=thr: nc.vector.tensor_scalar(ncol[:], _t, -1.0, None, op0=ALU.mult),
            ])
            b.on("scalar", lambda q, _dl=dl: nc.scalar.activation(e9[:, 0:8], _dl, AF.Exp, bias=ncol[:]),
                 waits=[(s_ev, s_ev.count)])
            b.chain("vector", [
                lambda q: nc.vector.tensor_tensor(e9[:, 0:8], e9[:, 0:8], als[:], op=ALU.mult),
                lambda q: nc.vector.tensor_reduce(zcol[:], e9[:, 0:8], axis=AX.X, op=ALU.add),
                lambda q: nc.vector.reciprocal(zcol[:], zcol[:]),
                lambda q, _j=j: nc.vector.tensor_scalar(wcol[:], kcol[:], float(_j), None, op0=ALU.is_gt),
                lambda q: nc.vector.tensor_tensor(wcol[:], wcol[:], zcol[:], op=ALU.mult),
                lambda q, _dl=dl, _t=thr: nc.vector.tensor_scalar(ets[:], _dl, _t, None, op0=ALU.is_equal),
                lambda q: nc.vector.scalar_tensor_tensor(rwt[:], ets[:], wcol[:], rwt[:], op0=ALU.mult, op1=ALU.add),
            ], waits=[(s_sc, s_sc.count)])
        b.chain("vector", [
            lambda q: nc.vector.tensor_reduce(zcol[:], rwt[:], axis=AX.X, op=ALU.add),
            lambda q: nc.vector.reciprocal(zcol[:], zcol[:]),
            lambda q, _tc=tc: nc.vector.tensor_scalar(rntok[:, _tc * 8:(_tc + 1) * 8], rwt[:], zcol[:], None, op0=ALU.mult),
            lambda q, _tc=tc: nc.vector.tensor_scalar(mtok[:, _tc * 8:(_tc + 1) * 8],
                                                      rntok[:, _tc * 8:(_tc + 1) * 8], 0.0, None, op0=ALU.is_gt),
        ])
    ev_routing = s_ev.count

    for tc in range(32):
        b.on("tensor", lambda q, _tc=tc: nc.tensor.matmul(
            pR1[:, 0:8], b.sb["g_onescolb"][:], mtok[:, _tc * 8:(_tc + 1) * 8],
            start=(_tc == 0), stop=(_tc == 31)),
            waits=[(s_ev, ev_routing)] if tc == 0 else (),
            inc=(s_mm if tc == 31 else None))
    mm_cnt = s_mm.count
    b.on("vector", lambda q: nc.vector.tensor_copy(cntrow[:], pR1[:, 0:8]),
         waits=[(s_mm, mm_cnt)])
    ev_cnt = s_ev.count
    b.on("vector", lambda q: nc.vector.memset(ccs[:], 0.0))
    b.mm(pT[:, 0:8], [(ccsel[:], cntrow[:])], waits=[(s_ev, s_ev.count)], inc=s_mm)
    b.on("vector", lambda q: nc.vector.tensor_copy(ccs[:, 0:8], pT[:, 0:8]),
         waits=[(s_mm, s_mm.count)])
    b.dma("gpsimd", io["cc_in"], ccs[:], s_st, waits=[(s_ev, s_ev.count)])

    def cc_fn(q):
        return q.collective_compute(
            "AllReduce", mybir.AluOpType.add,
            replica_groups=[list(range(NCORES))],
            ins=[io["cc_in"].opt()],
            outs=[io["cc_out"].opt()],
        )
    b.on("gpsimd", cc_fn, waits=[(s_st, s_st.count)], inc=s_cc, inc_n=1)
    b.dma("sync", ccld[:], io["cc_out"][0:8, 0:8], s_ld, waits=[(s_cc, s_cc.count)])
    b.mm(pR2[0:1, 0:8], [(prefc[:], ccld[:])], waits=[(s_ld, s_ld.count)], inc=s_mm)
    b.on("vector", lambda q: nc.vector.tensor_copy(runrow[:], pR2[0:1, 0:8]),
         waits=[(s_mm, s_mm.count)])

    for tc in range(32):
        mslice = mtok[:, tc * 8:(tc + 1) * 8]
        b.mm(pT[:, 0:8], [(tric[:], mslice)], waits=[(s_ev, s_ev.count)], inc=s_mm)
        mm_tri = s_mm.count
        b.mm(pC[:, 0:8], [(b.sb["g_onesr"][:], runrow[:])], inc=s_mm)
        mm_off = s_mm.count
        b.chain("vector", [
            lambda q: nc.vector.tensor_tensor(m9[:, 0:8], pT[:, 0:8], pC[:, 0:8], op=ALU.add) if False else nc.vector.tensor_copy(m9[:, 0:8], pT[:, 0:8]),
            lambda q: nc.vector.tensor_tensor(m9[:, 0:8], m9[:, 0:8], pC[:, 0:8], op=ALU.add),
            lambda q: nc.vector.tensor_scalar(m9[:, 0:8], m9[:, 0:8], CAP, None, op0=ALU.is_le),
            lambda q, _tc=tc, _m=mslice: nc.vector.tensor_tensor(
                mcaptok[:, _tc * 8:(_tc + 1) * 8], _m, m9[:, 0:8], op=ALU.mult),
        ], waits=[(s_mm, mm_off)])
        b.on("tensor", lambda q, _m=mslice: nc.tensor.matmul(
            pR1[:, 0:8], b.sb["g_onescolb"][:], _m, start=True, stop=True),
            waits=[(s_ev, s_ev.count)], inc=s_mm)
        mm_c = s_mm.count
        b.chain("vector", [
            lambda q: nc.vector.tensor_copy(cntrow[:], pR1[:, 0:8]),
            lambda q: nc.vector.tensor_tensor(runrow[:], runrow[:], cntrow[:], op=ALU.add),
        ], waits=[(s_mm, mm_c)])
    ev_cap = s_ev.count

    for tc in range(32):
        fl9 = fltok[:, tc * 9:(tc + 1) * 9]
        mc8 = mcaptok[:, tc * 8:(tc + 1) * 8]
        b.chain("vector", [
            lambda q, _m=mc8: nc.vector.tensor_copy(m9[:, 0:8], _m),
            lambda q: nc.vector.memset(m9[:, 8:9], 1.0),
            lambda q: nc.vector.tensor_scalar(e9[:], m9[:], 1.0, 1e30, op0=ALU.subtract, op1=ALU.mult),
            lambda q, _f=fl9: nc.vector.tensor_tensor(e9[:], e9[:], _f, op=ALU.add),
            lambda q: nc.vector.tensor_reduce(ncol[:], e9[:], axis=AX.X, op=ALU.max),
            lambda q: nc.vector.tensor_scalar(ncol[:], ncol[:], -1.0, None, op0=ALU.mult),
        ])
        b.on("scalar", lambda q: nc.scalar.activation(e9[:], e9[:], AF.Exp, bias=ncol[:]),
             waits=[(s_ev, s_ev.count)])
        b.chain("vector", [
            lambda q: nc.vector.tensor_reduce(zcol[:], e9[:], axis=AX.X, op=ALU.add),
            lambda q: nc.vector.reciprocal(zcol[:], zcol[:]),
            lambda q: nc.vector.tensor_scalar(e9[:], e9[:], zcol[:], None, op0=ALU.mult),
            lambda q: nc.vector.tensor_reduce(ncol[:], e9[:, 0:8], axis=AX.X, op=ALU.add),
            lambda q, _tc=tc: nc.vector.tensor_scalar(
                glwtok[:, _tc * 9:(_tc + 1) * 9 - 1], rntok[:, _tc * 8:(_tc + 1) * 8],
                ncol[:], None, op0=ALU.mult),
            lambda q, _tc=tc, _m=mc8: nc.vector.tensor_tensor(
                glwtok[:, _tc * 9:(_tc + 1) * 9 - 1], glwtok[:, _tc * 9:(_tc + 1) * 9 - 1],
                _m, op=ALU.mult),
            lambda q, _tc=tc: nc.vector.tensor_copy(glwtok[:, _tc * 9 + 8:(_tc + 1) * 9], e9[:, 8:9]),
        ], waits=[(s_sc, s_sc.count)])
        b.on("tensor", lambda q, _tc=tc: nc.tensor.transpose(
            pT[0:9, :], glwtok[:, _tc * 9:(_tc + 1) * 9], idnf[:]),
            waits=[(s_ev, s_ev.count)], inc=s_mm)
        mm_g = s_mm.count
        b.on("vector", lambda q, _tc=tc: nc.vector.tensor_copy(
            glwfm[:, _tc * PC:(_tc + 1) * PC], pT[0:9, :]),
            waits=[(s_mm, mm_g)])
    ev_glw = s_ev.count
    b.dma("gpsimd", io["glwd"], glwfm[:], s_st, waits=[(s_ev, ev_glw)])
    st_glw = s_st.count

    wg = b.sbuf("m_wg", (PC, 2 * NDC * NB), BF16)
    wu = b.sbuf("m_wu", (PC, 2 * NDC * NB), BF16)
    sact = b.sbuf("m_sact", (PC, 2 * NB), F32)
    aout = b.sbuf("m_aout", (PC, 2 * NB), BF16)
    glwrow = b.sbuf("m_glwrow", (1, 2 * NB), BF16)
    rot4.seed(s_ev)
    rotC.seed(s_ev)
    slot_ev = [0, 0]
    slot_sc = [0, 0]
    git_ = 0
    for u in range(NU):
        buf = u % 2
        wgs = wg[:, buf * NDC * NB:(buf + 1) * NDC * NB]
        wus = wu[:, buf * NDC * NB:(buf + 1) * NDC * NB]
        e_u = u if u < 8 else 8
        if u < 8:
            gsrc = io["dgT"][u * D:(u + 1) * D, :]
            usrc = io["duT"][u * D:(u + 1) * D, :]
        else:
            sh0 = (u - 8) * NB
            gsrc = io["shgT"][:, sh0:sh0 + NB]
            usrc = io["shuT"][:, sh0:sh0 + NB]
        b.dma("gpsimd", *_ld3(gsrc, wgs, NB), s_gl,
              waits=[(s_mm, s_mm.count)] + ([(s_ev, s_ev.count), (s_sc, s_sc.count)]
                                            if u == 0 else []))
        b.dma("gpsimd", *_ld3(usrc, wus, NB), s_gl)
        ld_w = s_gl.count
        for nb_ in range(S // NB):
            n0 = nb_ * NB
            gsl = (u * (S // NB) + nb_) % 2
            b.dma("gpsimd", glwrow[:, gsl * NB:(gsl + 1) * NB], io["glwd"][e_u:e_u + 1, n0:n0 + NB], s_gr,
                  waits=[(s_st, st_glw), (s_mm, s_mm.count)])
            ld_gr = s_gr.count
            pCb, rwC = rotC.take()
            b.mm(pCb[:], [(onesrb[:], glwrow[:, gsl * NB:(gsl + 1) * NB])],
                 waits=rwC + [(s_gr, ld_gr)], inc=s_mm)
            for mc4 in range(4):
                asl = git_ % 2
                git_ += 1
                pG, rwG = rot4.take()
                b.mm(pG[:],
                     [(wgs[:, kc * NB + mc4 * PC: kc * NB + (mc4 + 1) * PC],
                       xn[:, kc * S + n0: kc * S + n0 + NB]) for kc in range(NDC)],
                     waits=rwG + [(s_gl, ld_w)], inc=s_mm)
                pU, rwU = rot4.take()
                b.mm(pU[:],
                     [(wus[:, kc * NB + mc4 * PC: kc * NB + (mc4 + 1) * PC],
                       xn[:, kc * S + n0: kc * S + n0 + NB]) for kc in range(NDC)],
                     waits=rwU, inc=s_mm)
                mm_gu = s_mm.count
                sa = sact[:, asl * NB:(asl + 1) * NB]
                ao = aout[:, asl * NB:(asl + 1) * NB]
                b.on("scalar", lambda q, _p=pG, _s=sa: nc.scalar.activation(_s, _p[:], AF.Silu),
                     waits=[(s_mm, mm_gu), (s_ev, slot_sc[asl])])
                sc_si = s_sc.count
                rot4.settle(s_sc)
                b.chain("vector", [
                    lambda q, _s=sa, _p=pU: nc.vector.tensor_tensor(_s, _s, _p[:], op=ALU.mult),
                    lambda q, _s=sa, _a=ao, _c=pCb: nc.vector.tensor_tensor(_a, _s, _c[:], op=ALU.mult),
                ], waits=[(s_sc, sc_si), (s_st, slot_ev[asl])])
                rot4.settle(s_ev)
                slot_sc[asl] = s_ev.count
                ev_a = s_ev.count
                b.dma("gpsimd", io["actscr"][u * NB + mc4 * PC: u * NB + (mc4 + 1) * PC, n0:n0 + NB],
                      ao, s_st, waits=[(s_ev, ev_a), (s_st, s_st.count)])
                slot_ev[asl] = s_st.count
            rotC.settle(s_ev)
    b.free("m_glwrow", "m_aout", "m_sact", "m_wu", "m_wg",
           "m_ccld", "m_prefc", "m_ccsel", "m_ccs", "m_cntrow", "m_runrow",
           "m_wcol", "m_kcol", "m_zcol", "m_ncol", "m_e9", "m_m9", "m_rwt",
           "m_ets", "m_als", "m_jms", "m_fsc", "m_dsc", "m_absrt", "m_absdl",
           "m_srt8", "m_glwfm", "m_glwtok", "m_mcap", "m_mtok", "m_rntok",
           "m_fltok", "m_rlf", "m_gwT", "m_xn")
    dall = b.sbuf("m_dall", (PC, 48 * 1024), BF16)
    actn = b.sbuf("m_actn", (PC, 48 * NB), BF16)
    xres = b.sbuf("m_xres", (PC, 2 * NB), F32)
    oev = b.sbuf("m_oev", (PC, 2 * NB), F32)
    oev_slot = [0, 0]
    b.dma("gpsimd", *_ld3(io["dallT"], dall[:], 1024), s_gl,
          waits=[(s_mm, s_mm.count), (s_ev, s_ev.count), (s_sc, s_sc.count),
                 (s_st, s_st.count)])
    ld_dall = s_gl.count
    for nb_ in range(S // NB):
        n0 = nb_ * NB
        for u in range(NU):
            b.dma("sync", actn[:, u * 4 * NB:(u + 1) * 4 * NB].rearrange("p (c f) -> p c f", f=NB),
                  io["actscr"][u * NB:(u + 1) * NB, n0:n0 + NB].rearrange("(c p) f -> p c f", p=PC),
                  s_ld, waits=([(s_mm, s_mm.count), (s_ev, s_ev.count),
                                (s_sc, s_sc.count), (s_st, s_st.count)] if u == 0 else ()))
        ld_act = s_ld.count
        for mc in range(NDC):
            dstp, rw_ = rot4.take()
            osl = mc % 2
            ov = oev[:, osl * NB:(osl + 1) * NB]
            b.mm(dstp[:],
                 [(dall[:, kc48 * 1024 + mc * PC: kc48 * 1024 + (mc + 1) * PC],
                   actn[:, kc48 * NB:(kc48 + 1) * NB]) for kc48 in range(48)],
                 waits=rw_ + [(s_ld, ld_act), (s_gl, ld_dall)], inc=s_mm)
            mm_f = s_mm.count
            xsl = mc % 2
            xv = xres[:, xsl * NB:(xsl + 1) * NB]
            b.dma("sync", xv, xA[mc * PC:(mc + 1) * PC, n0:n0 + NB], s_xr,
                  waits=[(s_ev, s_ev.count)])
            ld_x = s_xr.count
            b.chain("vector", [
                lambda q, _p=dstp, _o=ov, _x=xv: nc.vector.tensor_tensor(_o, _p[:], _x, op=ALU.add),
            ], waits=[(s_mm, mm_f), (s_xr, ld_x), (s_st, oev_slot[osl])])
            rot4.settle(s_ev)
            b.dma("gpsimd", io["x4"][mc * PC:(mc + 1) * PC, n0:n0 + NB], ov, s_st,
                  waits=[(s_ev, s_ev.count), (s_st, s_st.count)])
            oev_slot[osl] = s_st.count
    b.wait("gpsimd", [(s_st, s_st.count)])
    b.free("m_oev", "m_xres", "m_actn", "m_dall", "m_tric", "p_E", "p_D", "p_T")


_CACHE = {}


def _declare_consts(nc, io, consts):
    for nm, arr in consts.items():
        dt = BF16 if arr.dtype == BF else F32
        io[nm] = nc.dram_tensor(nm, list(arr.shape), dt, kind="ExternalInput").ap()


def _build_l0a_nc(consts):
    nc = bass.Bass("TRN2", target_bir_lowering=False, name="l0attn")
    io = {}
    io["x"] = nc.dram_tensor("x", [D, S], F32, kind="ExternalInput").ap()
    io["x1"] = nc.dram_tensor("x1", [D, S], F32, kind="ExternalOutput").ap()
    io["qscr"] = nc.dram_tensor("qscr", [D, S], BF16, kind="Internal").ap()
    for nm in ["Wk", "Wv", "Wq"]:
        io[nm] = nc.dram_tensor(nm, [D, D], F32, kind="ExternalInput").ap()
    for nm in ["bk", "bv", "bq"]:
        io[nm] = nc.dram_tensor(nm, [PC, NDC], F32, kind="ExternalInput").ap()
    _declare_consts(nc, io, consts)
    b = Builder(nc)
    sems = make_sems(b)
    emit_prelude(nc, b, io, sems)
    build_l0_attn(nc, b, io, sems)
    b.finish()
    return nc


def _build_l0f_nc(consts):
    nc = bass.Bass("TRN2", target_bir_lowering=False, name="l0ffn")
    io = {}
    io["x1"] = nc.dram_tensor("x1", [D, S], F32, kind="ExternalInput").ap()
    io["x2"] = nc.dram_tensor("x2", [D, S], F32, kind="ExternalOutput").ap()
    shapes = dict(slotT=[D, 64], w1x=[D, NB], w1s=[D, NB], w2=[NB, 64],
                  WihT=[D, 3072], WhhT=[D, 3072],
                  wq=[D, 256], wk=[D, 256], wv=[D, D], wvp=[D, D], wo=[D, D],
                  b1c=[PC, 4], b2c=[64, 1], tau=[1, 1],
                  bihc=[PC, 24], bhhc=[PC, 24], bqc=[PC, 2], bkc=[PC, 2],
                  bvrow=[1, D], bvprow=[1, D], lngc=[PC, NDC], lnbc=[PC, NDC],
                  boc=[PC, NDC])
    for nm, shp in shapes.items():
        io[nm] = nc.dram_tensor(nm, shp, F32, kind="ExternalInput").ap()
    _declare_consts(nc, io, consts)
    b = Builder(nc)
    sems = make_sems(b)
    emit_prelude(nc, b, io, sems)
    build_l0_ffn(nc, b, io, sems)
    b.finish()
    return nc


def _qk_perm():
    perm = np.zeros(1024, np.int64)
    for parity in range(2):
        for c in range(4):
            for h_loc in range(4):
                for i in range(32):
                    new = parity * 512 + c * 128 + h_loc * 32 + i
                    old = (c * 4 + h_loc) * 64 + 2 * i + parity
                    perm[new] = old
    return perm


def _build_l1a_nc(consts):
    nc = bass.Bass("TRN2", target_bir_lowering=False, name="l1attn")
    io = {}
    io["x2"] = nc.dram_tensor("x2", [D, S], F32, kind="ExternalInput").ap()
    io["x3"] = nc.dram_tensor("x3", [D, S], F32, kind="ExternalOutput").ap()
    io["qd"] = nc.dram_tensor("qd", [16, 64, S], BF16, kind="Internal").ap()
    io["kd"] = nc.dram_tensor("kd", [16, 64, S], BF16, kind="Internal").ap()
    io["vd"] = nc.dram_tensor("vd", [S, 16 * 65], BF16, kind="Internal").ap()
    io["qid"] = nc.dram_tensor("qid", [256, S], BF16, kind="Internal").ap()
    shapes = dict(WqT=[D, D], WkT=[D, D], WvT=[D, D], WqiT=[D, 256],
                  WkiT=[D, 64], WwiT=[D, 4], WoT=[D, D])
    for nm, shp in shapes.items():
        io[nm] = nc.dram_tensor(nm, shp, F32, kind="ExternalInput").ap()
    _declare_consts(nc, io, consts)
    b = Builder(nc)
    sems = make_sems(b)
    emit_prelude(nc, b, io, sems)
    build_l1_attn(nc, b, io, sems)
    b.finish()
    return nc


def _l1a_inputs(inputs):
    wqkv = np.asarray(inputs["L1_attn_w_qkv"], np.float32)
    perm = _qk_perm()
    return {
        "WqT": np.ascontiguousarray(wqkv[0:1024][perm].T),
        "WkT": np.ascontiguousarray(wqkv[1024:2048][perm].T),
        "WvT": np.ascontiguousarray(wqkv[2048:3072].T),
        "WqiT": np.ascontiguousarray(np.asarray(inputs["L1_attn_w_qi"], np.float32).T),
        "WkiT": np.ascontiguousarray(np.asarray(inputs["L1_attn_w_ki"], np.float32).T),
        "WwiT": np.ascontiguousarray(np.asarray(inputs["L1_attn_w_wi"], np.float32).T),
        "WoT": np.ascontiguousarray(np.asarray(inputs["L1_attn_w_o"], np.float32).T),
    }


def _build_l1f_nc(consts, core_id):
    nc = bass.Bass("TRN2", target_bir_lowering=False, name="l1ffn")
    io = {}
    io["x3"] = nc.dram_tensor("x3", [D, S], F32, kind="ExternalInput").ap()
    io["x4"] = nc.dram_tensor("x4", [D, S], F32, kind="ExternalOutput").ap()
    io["cc_in"] = nc.dram_tensor("cc_in", [PC, PC], F32).ap()
    io["cc_out"] = nc.dram_tensor("cc_out", [PC, PC], F32).ap()
    io["actscr"] = nc.dram_tensor("actscr", [12 * NB, S], BF16, kind="Internal").ap()
    io["glwd"] = nc.dram_tensor("glwd", [9, S], F32, kind="Internal").ap()
    shapes = dict(gwT=[D, 9], dgT=[8 * D, NB], duT=[8 * D, NB],
                  shgT=[D, 2048], shuT=[D, 2048], dallT=[6144, D],
                  cc_sel=[1, PC], cc_pref=[8, 1])
    for nm, shp in shapes.items():
        io[nm] = nc.dram_tensor(nm, shp, F32, kind="ExternalInput").ap()
    _declare_consts(nc, io, consts)
    b = Builder(nc)
    sems = make_sems(b)
    emit_prelude(nc, b, io, sems)
    build_l1_ffn(nc, b, io, sems, core_id)
    b.finish()
    return nc


def _l1f_inputs(inputs, core_id):
    t = lambda a: np.ascontiguousarray(np.asarray(a, np.float32).T)
    dg = np.asarray(inputs["L1_ffn_dyn_gate_w"], np.float32)
    du = np.asarray(inputs["L1_ffn_dyn_up_w"], np.float32)
    dd = np.asarray(inputs["L1_ffn_dyn_down_w"], np.float32)
    sel = np.zeros((1, PC), np.float32)
    sel[0, core_id] = 1.0
    pref = np.zeros((8, 1), np.float32)
    pref[:core_id, 0] = 1.0
    return {
        "gwT": t(inputs["L1_ffn_gate_w"]),
        "dgT": np.ascontiguousarray(np.concatenate([dg[e].T for e in range(8)], axis=0)),
        "duT": np.ascontiguousarray(np.concatenate([du[e].T for e in range(8)], axis=0)),
        "shgT": t(inputs["L1_ffn_sh_gate_w"][0]),
        "shuT": t(inputs["L1_ffn_sh_up_w"][0]),
        "dallT": np.ascontiguousarray(np.concatenate(
            [dd[e].T for e in range(8)] + [np.asarray(inputs["L1_ffn_sh_down_w"], np.float32)[0].T], axis=0)),
        "cc_sel": sel,
        "cc_pref": pref,
    }


def _col(v, nchunk):
    return np.ascontiguousarray(np.asarray(v, np.float32).reshape(nchunk, PC).T)


def _l0f_inputs(inputs):
    t = lambda a: np.ascontiguousarray(np.asarray(a, np.float32).T)
    w1 = np.asarray(inputs["L0_ffn_w1"], np.float32)
    return {
        "slotT": t(inputs["L0_ffn_slot_init"]),
        "w1x": np.ascontiguousarray(w1[:D]),
        "w1s": np.ascontiguousarray(w1[D:]),
        "w2": np.asarray(inputs["L0_ffn_w2"], np.float32),
        "WihT": t(inputs["L0_ffn_Wih"]),
        "WhhT": t(inputs["L0_ffn_Whh"]),
        "wq": np.asarray(inputs["L0_ffn_wq"], np.float32),
        "wk": np.asarray(inputs["L0_ffn_wk"], np.float32),
        "wv": np.asarray(inputs["L0_ffn_wv"], np.float32),
        "wvp": np.asarray(inputs["L0_ffn_wvp"], np.float32),
        "wo": np.asarray(inputs["L0_ffn_wo"], np.float32),
        "b1c": _col(inputs["L0_ffn_b1"], 4),
        "b2c": np.asarray(inputs["L0_ffn_b2"], np.float32).reshape(64, 1),
        "tau": np.asarray(inputs["L0_ffn_tau"], np.float32).reshape(1, 1),
        "bihc": _col(inputs["L0_ffn_bih"], 24),
        "bhhc": _col(inputs["L0_ffn_bhh"], 24),
        "bqc": _col(inputs["L0_ffn_bq"], 2),
        "bkc": _col(inputs["L0_ffn_bk"], 2),
        "bvrow": np.asarray(inputs["L0_ffn_bv"], np.float32).reshape(1, D),
        "bvprow": np.asarray(inputs["L0_ffn_bvp"], np.float32).reshape(1, D),
        "lngc": _col(inputs["L0_ffn_ln_g"], NDC),
        "lnbc": _col(inputs["L0_ffn_ln_b"], NDC),
        "boc": _col(inputs["L0_ffn_bo"], NDC),
    }


def _build_fused_nc(consts):
    nc = bass.Bass("TRN2", target_bir_lowering=False, name="mk_fused")
    io = {}
    io["x"] = nc.dram_tensor("x", [D, S], F32, kind="ExternalInput").ap()
    io["x4"] = nc.dram_tensor("x4", [D, S], F32, kind="ExternalOutput").ap()
    for nm in ["x1", "x2", "x3"]:
        io[nm] = nc.dram_tensor(nm, [D, S], F32, kind="Internal").ap()
    io["qscr"] = nc.dram_tensor("qscr", [D, S], BF16, kind="Internal").ap()
    io["qd"] = nc.dram_tensor("qd", [16, 64, S], BF16, kind="Internal").ap()
    io["kd"] = nc.dram_tensor("kd", [16, 64, S], BF16, kind="Internal").ap()
    io["vd"] = nc.dram_tensor("vd", [S, 16 * 65], BF16, kind="Internal").ap()
    io["qid"] = nc.dram_tensor("qid", [256, S], BF16, kind="Internal").ap()
    io["actscr"] = nc.dram_tensor("actscr", [12 * NB, S], BF16, kind="Internal").ap()
    io["glwd"] = nc.dram_tensor("glwd", [9, S], F32, kind="Internal").ap()
    io["cc_in"] = nc.dram_tensor("cc_in", [PC, PC], F32).ap()
    io["cc_out"] = nc.dram_tensor("cc_out", [PC, PC], F32).ap()
    for nm in ["Wk", "Wv", "Wq"]:
        io[nm] = nc.dram_tensor(nm, [D, D], F32, kind="ExternalInput").ap()
    for nm in ["bk", "bv", "bq"]:
        io[nm] = nc.dram_tensor(nm, [PC, NDC], F32, kind="ExternalInput").ap()
    shapes = dict(slotT=[D, 64], w1x=[D, NB], w1s=[D, NB], w2=[NB, 64],
                  WihT=[D, 3072], WhhT=[D, 3072],
                  wq=[D, 256], wk=[D, 256], wv=[D, D], wvp=[D, D], wo=[D, D],
                  b1c=[PC, 4], b2c=[64, 1], tau=[1, 1],
                  bihc=[PC, 24], bhhc=[PC, 24], bqc=[PC, 2], bkc=[PC, 2],
                  bvrow=[1, D], bvprow=[1, D], lngc=[PC, NDC], lnbc=[PC, NDC],
                  boc=[PC, NDC],
                  WqT=[D, D], WkT=[D, D], WvT=[D, D], WqiT=[D, 256],
                  WkiT=[D, 64], WwiT=[D, 4], WoT=[D, D],
                  gwT=[D, 9], dgT=[8 * D, NB], duT=[8 * D, NB],
                  shgT=[D, 2048], shuT=[D, 2048], dallT=[6144, D],
                  cc_sel=[1, PC], cc_pref=[8, 1])
    for nm, shp in shapes.items():
        io[nm] = nc.dram_tensor(nm, shp, F32, kind="ExternalInput").ap()
    _declare_consts(nc, io, consts)
    b = Builder(nc)
    sems = make_sems(b)
    emit_prelude(nc, b, io, sems)
    build_l0_attn(nc, b, dict(io, x=io["x"], x1=io["x1"]), sems)
    build_l0_ffn(nc, b, io, sems)
    build_l1_attn(nc, b, io, sems)
    build_l1_ffn(nc, b, io, sems)
    b.finish()
    return nc


def kernel(**inputs):
    B = 8
    consts = make_consts()
    x = np.asarray(inputs["L0_attn_x"])
    x_fm = [np.ascontiguousarray(x[c].T) for c in range(B)]
    nc = _build_fused_nc(consts)
    base = {
        "Wk": np.asarray(inputs["L0_attn_Wk"]),
        "Wv": np.asarray(inputs["L0_attn_Wv"]),
        "Wq": np.asarray(inputs["L0_attn_Wq"]),
        "bk": np.ascontiguousarray(np.asarray(inputs["L0_attn_bk"]).reshape(NDC, PC).T),
        "bv": np.ascontiguousarray(np.asarray(inputs["L0_attn_bv"]).reshape(NDC, PC).T),
        "bq": np.ascontiguousarray(np.asarray(inputs["L0_attn_bq"]).reshape(NDC, PC).T),
    }
    base.update(_l0f_inputs(inputs))
    base.update(_l1a_inputs(inputs))
    base.update(consts)
    percore = [_l1f_inputs(inputs, c) for c in range(B)]
    in_maps = []
    for c in range(B):
        m = dict(base)
        m.update(percore[c])
        m["x"] = x_fm[c]
        in_maps.append(m)
    res = run_bass_kernel_spmd(nc, in_maps, core_ids=list(range(B)))
    x4 = np.stack([np.asarray(r["x4"]).T for r in res.results])
    return x4.astype(np.float32)


# version 15
# speedup vs baseline: 1.0296x; 1.0296x over previous
import numpy as np
import ml_dtypes
import concourse.bass as bass
import concourse.mybir as mybir
from concourse.alu_op_type import AluOpType as ALU
from concourse.bass_utils import run_bass_kernel_spmd

F32 = mybir.dt.float32
BF16 = mybir.dt.bfloat16
U32 = mybir.dt.uint32
AF = mybir.ActivationFunctionType
AX = mybir.AxisListType
BF = ml_dtypes.bfloat16

NCORES = 8
S = 4096
D = 1024
PC = 128
NDC = D // PC
NB = 512
HS = S // 2
ENGINES = ("sync", "tensor", "vector", "scalar", "gpsimd")


class _Sem:
    __slots__ = ("h", "count", "name")


class Rot:

    def __init__(self, psums):
        self.slots = [[p, None, 0] for p in psums]
        self.i = -1
        self.pending = []

    def take(self):
        self.i = (self.i + 1) % len(self.slots)
        p, sem, cnt = self.slots[self.i]
        w = [(sem, cnt)] if sem is not None and cnt > 0 else []
        self.pending.append(self.i)
        return p, w

    def settle(self, sem):
        i = self.pending.pop(0)
        self.slots[i][1] = sem
        self.slots[i][2] = sem.count

    def seed(self, sem):
        assert not self.pending
        for s in self.slots:
            s[1] = sem
            s[2] = sem.count


class Builder:
    def __init__(self, nc):
        self.nc = nc
        self.steps = {e: [] for e in ENGINES}
        self._ctxs = []
        self.sb = {}
        self.esem = {}

    def set_engine_sem(self, eng, sem):
        self.esem[eng] = sem

    def sem(self, name):
        ctx = self.nc.semaphore(name)
        h = ctx.__enter__()
        self._ctxs.append((name, ctx))
        s = _Sem()
        s.h, s.count, s.name = h, 0, name
        return s

    _uniq = {}

    def _uname(self, name):
        n = Builder._uniq.get(name, 0)
        Builder._uniq[name] = n + 1
        return name if n == 0 else f"{name}__{n}"

    def sbuf(self, name, shape, dtype=F32):
        ctx = self.nc.sbuf_tensor(self._uname(name), list(shape), dtype)
        h = ctx.__enter__()
        self._ctxs.append((name, ctx))
        self.sb[name] = h
        return h

    def psum(self, name, shape, dtype=F32):
        ctx = self.nc.psum_tensor(self._uname(name), list(shape), dtype)
        h = ctx.__enter__()
        self._ctxs.append((name, ctx))
        self.sb[name] = h
        return h

    def free(self, *names):
        for name in names:
            nm, ctx = self._ctxs.pop()
            assert nm == name, f"free order mismatch: {nm} != {name}"
            ctx.__exit__(None, None, None)
            self.sb.pop(nm, None)

    def on(self, eng, fn, waits=(), inc=None, inc_n=1):
        es = self.esem.get(eng)
        wl = []
        if es is not None and es.count > 0:
            wl.append((es.h, es.count))
        wl += [(s.h, int(v)) for (s, v) in waits if v and v > 0]
        if inc is None and es is not None:
            inc = es
        tgt = None
        if inc is not None:
            inc.count += inc_n
            tgt = (inc.h, inc_n)

        def step(q, _wl=wl, _fn=fn, _tgt=tgt):
            for h, v in _wl:
                q.wait_ge(h, v)
            r = _fn(q)
            if _tgt is not None:
                insts = r if isinstance(r, (list, tuple)) else [r]
                insts[-1].then_inc(*_tgt)
        self.steps[eng].append(step)
        return inc.count if inc is not None else None

    def chain(self, eng, fns, waits=(), inc=None):
        n = len(fns)
        for i, fn in enumerate(fns):
            self.on(eng, fn,
                    waits=(waits if i == 0 else ()),
                    inc=(inc if (i == n - 1 and inc is not None) else None))

    def dma(self, qeng, out, in_, sem, waits=()):
        return self.on(qeng, lambda q, _o=out, _i=in_: q.dma_start(out=_o, in_=_i),
                       waits=waits, inc=sem, inc_n=16)

    def wait(self, eng, waits):
        wl = [(s.h, int(v)) for (s, v) in waits if v and v > 0]
        if not wl:
            return

        def step(q, _wl=wl):
            for h, v in _wl:
                q.wait_ge(h, v)
        self.steps[eng].append(step)

    def mm(self, dst, pairs, waits=(), inc=None, tile_position=None):
        def fn(q, _dst=dst, _pairs=tuple(pairs), _tp=tile_position):
            insts = []
            n = len(_pairs)
            for i, (l, r) in enumerate(_pairs):
                insts.append(self.nc.tensor.matmul(
                    _dst, l, r, start=(i == 0), stop=(i == n - 1),
                    tile_position=_tp))
            return insts
        return self.on("tensor", fn, waits=waits, inc=inc)

    def finish(self):
        nc = self.nc
        with nc.Block() as block:
            for e in ENGINES:
                if not self.steps[e]:
                    continue

                def body(q, _steps=tuple(self.steps[e])):
                    seen = {}

                    class QW:
                        def __getattr__(self, a):
                            return getattr(q, a)

                        def wait_ge(self, h, v, _seen=seen):
                            k = id(h)
                            if _seen.get(k, -1) >= v:
                                return
                            _seen[k] = v
                            q.wait_ge(h, v)
                    qw = QW()
                    for st in _steps:
                        st(qw)
                getattr(block, e)(body)
        for nm, c in reversed(self._ctxs):
            c.__exit__(None, None, None)
        self._ctxs = []




def _tbf(nc, out, in_, idn):
    with nc.allow_low_precision(reason="bf16 PE transpose (no accumulation)"):
        return nc.tensor.transpose(out, in_, idn)

def _ld3(dram2d, sb2d, cols):
    return (sb2d.rearrange("p (c f) -> p c f", f=cols),
            dram2d.rearrange("(c p) f -> p c f", p=PC))

def make_consts():
    c = {}
    N = 1024
    n = np.arange(N)[:, None]
    f_re = np.arange(513)[None, :]
    f_im = np.arange(1, 512)[None, :]
    PK = np.concatenate([np.cos(2 * np.pi * n * f_re / N),
                         -np.sin(2 * np.pi * n * f_im / N)], axis=1)
    c["c_PK"] = PK.astype(BF)
    nn = np.arange(N)[None, :]
    ff_re = np.arange(513)[:, None]
    ff_im = np.arange(1, 512)[:, None]
    coef = np.concatenate([np.ones(1), 2 * np.ones(511), np.ones(1)])[:, None]
    IW_top = coef * np.cos(2 * np.pi * ff_re * nn / N) / N
    IW_bot = -2 * np.sin(2 * np.pi * ff_im * nn / N) / N
    c["c_IW"] = np.concatenate([IW_top, IW_bot], axis=0).astype(BF)
    W = np.full((PC, NDC), 2.0, np.float32)
    W[0, 0] = 1.0
    W[0, 4] = 1.0
    c["c_nw"] = W.astype(BF)
    c["c_ones_row"] = np.ones((1, PC), np.float32)
    c["c_identity"] = np.eye(PC, dtype=np.float32).astype(BF)
    c["c_identity_f32"] = np.eye(PC, dtype=np.float32)
    c["c_tric"] = np.tril(np.ones((PC, PC), np.float32)).T.astype(BF)
    c["c_triu_neg"] = (np.triu(np.ones((PC, PC), np.float32), 1) * -1e4).astype(BF)
    c["c_iota8"] = np.broadcast_to(np.arange(8, dtype=np.float32), (PC, 8)).copy()
    inv = 1.0 / 10000.0 ** (np.arange(0, 64, 2, dtype=np.float64) / 64.0)
    ang = np.arange(S, dtype=np.float64)[None, :] * inv[:, None]
    cosr = np.cos(ang)
    sinr = np.sin(ang)
    c["c_cosP"] = np.tile(cosr, (4, 1)).astype(BF)
    c["c_sinP"] = np.tile(sinr, (4, 1)).astype(BF)
    return c



def make_sems(b):
    keys = ["ld", "mm", "ev", "sc", "st", "gl", "xr", "qh", "kh", "vh", "cc", "gr", "av"]
    sems = {k: b.sem("s_" + k) for k in keys}
    b.set_engine_sem("vector", sems["ev"])
    b.set_engine_sem("scalar", sems["sc"])
    return sems


def emit_prelude(nc, b, io, sems):
    b.psum("p_A", (PC, NB))
    b.psum("p_B", (PC, NB))
    b.psum("p_C", (PC, NB))
    b.psum("p_R1", (PC, NB))
    b.psum("p_R2", (PC, NB))
    b.sbuf("g_onescol", (PC, 1), F32)
    b.sbuf("g_onescolb", (PC, 1), BF16)
    b.sbuf("g_onesr", (1, PC), F32)
    b.sbuf("g_onesrb", (1, PC), BF16)
    b.sbuf("g_idn", (PC, PC), BF16)
    b.sbuf("g_idnf", (PC, PC), F32)
    b.on("vector", lambda q: [nc.vector.memset(b.sb["g_onescol"][:], 1.0),
                              nc.vector.memset(b.sb["g_onescolb"][:], 1.0),
                              nc.vector.memset(b.sb["g_onesrb"][:], 1.0)], inc=sems["ev"])
    b.dma("sync", b.sb["g_onesr"][:], io["c_ones_row"], sems["ld"])
    b.dma("sync", b.sb["g_idn"][:], io["c_identity"], sems["ld"])
    b.dma("sync", b.sb["g_idnf"][:], io["c_identity_f32"], sems["ld"])
    b.wait("tensor", [(sems["ld"], sems["ld"].count)])


def alloc_ln_region(b):
    b.sbuf("ln_xld", (PC, NDC * NB), F32)
    b.sbuf("ln_sq", (PC, NDC * NB), BF16)
    b.sbuf("ln_tmp", (PC, NB), F32)
    b.sbuf("ln_rowm", (1, NB), F32)
    b.sbuf("ln_rowv", (1, NB), F32)
    b.sbuf("ln_rown", (1, NB), F32)


def free_ln_region(b):
    b.free("ln_rown", "ln_rowv", "ln_rowm", "ln_tmp", "ln_sq", "ln_xld")


def emit_ln(nc, b, xA, xn, sems, t0=0, t1=S):
    s_ld, s_mm, s_ev, s_sc = sems["ld"], sems["mm"], sems["ev"], sems["sc"]
    sb = b.sb
    xld, sq, tmp = sb["ln_xld"], sb["ln_sq"], sb["ln_tmp"]
    rowm, rowv, rown = sb["ln_rowm"], sb["ln_rowv"], sb["ln_rown"]
    onescol, onescolb, onesr = sb["g_onescol"], sb["g_onescolb"], sb["g_onesr"]
    pR1, pR2, pA, pB = sb["p_R1"], sb["p_R2"], sb["p_A"], sb["p_B"]
    pR1 = pR1[0:1, :]
    pR2 = pR2[0:1, :]

    s_st = sems["st"]
    for n0 in range(t0, t1, NB):
        for dc in range(NDC):
            b.dma("sync", xld[:, dc * NB:(dc + 1) * NB],
                  xA[dc * PC:(dc + 1) * PC, n0:n0 + NB], s_ld,
                  waits=[(s_ev, s_ev.count), (s_sc, s_sc.count), (s_mm, s_mm.count),
                         (s_st, s_st.count), (s_ld, s_ld.count)] if dc == 0 else ())
        ld_here = s_ld.count
        for dc in range(NDC):
            xc = xld[:, dc * NB:(dc + 1) * NB]
            sqc = sq[:, dc * NB:(dc + 1) * NB]
            b.on("scalar", lambda q, _x=xc, _s=sqc: nc.scalar.activation(_s, _x, AF.Square),
                 waits=[(s_ld, ld_here), (s_mm, s_mm.count)], inc=s_sc)
            sc_here = s_sc.count
            b.on("tensor", lambda q, _dc=dc, _s=sqc: nc.tensor.matmul(
                pR2[:], onescolb[:], _s, start=(_dc == 0), stop=(_dc == NDC - 1)),
                waits=[(s_sc, sc_here)] + ([(s_ev, s_ev.count)] if dc == 0 else []),
                inc=(s_mm if dc == NDC - 1 else None))
            b.on("tensor", lambda q, _dc=dc, _x=xc: nc.tensor.matmul(
                pR1[:], onescol[:], _x, start=(_dc == 0), stop=(_dc == NDC - 1)),
                waits=[(s_ld, ld_here)] + ([(s_ev, s_ev.count)] if dc == 0 else []),
                inc=(s_mm if dc == NDC - 1 else None))
        mm_stats = s_mm.count

        b.chain("vector", [
            lambda q: nc.vector.tensor_scalar(rowm[:], pR1[:], 1.0 / D, None, op0=ALU.mult),
            lambda q: nc.vector.tensor_scalar(rowv[:], pR2[:], 1.0 / D, None, op0=ALU.mult),
            lambda q: nc.vector.tensor_tensor(rown[:], rowm[:], rowm[:], op=ALU.mult),
            lambda q: nc.vector.tensor_tensor(rowv[:], rowv[:], rown[:], op=ALU.subtract),
            lambda q: nc.vector.tensor_scalar(rowv[:], rowv[:], 1e-5, None, op0=ALU.add),
        ], waits=[(s_mm, mm_stats)])
        b.on("scalar", lambda q: nc.scalar.activation(rowv[:], rowv[:], AF.Sqrt),
             waits=[(s_ev, s_ev.count)], inc=s_sc)

        b.chain("vector", [
            lambda q: nc.vector.reciprocal(rowv[:], rowv[:]),
            lambda q: nc.vector.tensor_tensor(rown[:], rowm[:], rowv[:], op=ALU.mult),
            lambda q: nc.vector.tensor_scalar(rown[:], rown[:], -1.0, None, op0=ALU.mult),
        ], waits=[(s_sc, s_sc.count)])
        b.on("tensor", lambda q: [
            nc.tensor.matmul(pA[:], onesr[:], rowv[:], start=True, stop=True),
            nc.tensor.matmul(pB[:], onesr[:], rown[:], start=True, stop=True)],
            waits=[(s_ev, s_ev.count)], inc=s_mm)
        mm_bc = s_mm.count
        for dc in range(NDC):
            xc = xld[:, dc * NB:(dc + 1) * NB]
            xo = xn[:, dc * S + n0: dc * S + n0 + NB]

            b.chain("vector", [
                lambda q, _x=xc: nc.vector.tensor_tensor(tmp[:], _x, pA[:], op=ALU.mult),
                lambda q, _o=xo: nc.vector.tensor_tensor(_o, tmp[:], pB[:], op=ALU.add),
            ], waits=[(s_mm, mm_bc)])


def build_l0_attn(nc, b, io, sems):
    s_ld, s_mm, s_ev, s_sc, s_st = sems["ld"], sems["mm"], sems["ev"], sems["sc"], sems["st"]
    s_gl, s_xr = sems["gl"], sems["xr"]
    b.psum("p_N", (1, NB))
    pD = b.psum("p_D", (PC, NB))
    rotA = Rot([b.sb["p_A"], b.sb["p_B"], pD, b.sb["p_R1"], b.sb["p_R2"]])
    xn = b.sbuf("a_xn", (PC, NDC * S), BF16)
    wpk = b.sbuf("a_wpk", (PC, NDC * 1024), BF16)
    wprime = [b.sbuf(f"a_wp{i}", (PC, NDC * 1024), BF16) for i in range(3)]
    bvec = b.sbuf("a_bvec", (PC, NDC * 3), BF16)
    bcol = b.sbuf("a_bcol", (PC, 3 * NDC), F32)
    nw = b.sbuf("a_nw", (PC, NDC), BF16)
    nrm2row = b.sbuf("a_nrm2", (1, S), F32)
    scl = nrm2row
    carry = b.sbuf("a_carry", (PC, NDC * 2), F32)

    pA, pB, pC = b.sb["p_A"], b.sb["p_B"], b.sb["p_C"]
    pN = b.sb["p_N"]
    onesr = b.sb["g_onesr"]

    xA = io["x"]
    b.dma("sync", *_ld3(io["c_PK"], wpk[:], 1024), s_ld,
          waits=[(s_ld, s_ld.count)])
    b.dma("sync", nw[:], io["c_nw"], s_ld)
    b.wait("tensor", [(s_ld, s_ld.count)])
    b.wait("scalar", [(s_ld, s_ld.count)])

    alloc_ln_region(b)
    emit_ln(nc, b, xA, xn, sems)
    free_ln_region(b)

    wW = b.sbuf("a_wW", (PC, NDC * 1024), BF16)
    for wi, (wname, bname) in enumerate([("Wk", "bk"), ("Wv", "bv"), ("Wq", "bq")]):
        b.dma("gpsimd", bvec[:, wi * NDC:(wi + 1) * NDC], io[bname], s_gl,
              waits=[(s_mm, s_mm.count)] + ([(s_ev, s_ev.count), (s_sc, s_sc.count)]
                                            if wi == 0 else []))
        b.dma("gpsimd", *_ld3(io[wname], wW[:], 1024), s_gl)
        ld_w = s_gl.count
        for mc in range(NDC):
            for nb_ in range(2):
                dstp = pA if (nb_ == 0) else pB
                b.mm(dstp[:],
                     [(wW[:, kc * 1024 + mc * PC: kc * 1024 + (mc + 1) * PC],
                       wpk[:, kc * 1024 + nb_ * NB: kc * 1024 + (nb_ + 1) * NB])
                      for kc in range(NDC)],
                     waits=[(s_gl, ld_w), (s_ev, s_ev.count)], inc=s_mm)
                mm_w = s_mm.count
                b.on("vector", lambda q, _wi=wi, _mc=mc, _nb=nb_, _p=dstp: nc.vector.tensor_copy(
                    wprime[_wi][:, _mc * 1024 + _nb * NB: _mc * 1024 + (_nb + 1) * NB], _p[:]),
                    waits=[(s_mm, mm_w)], inc=s_ev)
        ld_b = s_gl.count
        for mc in range(NDC):
            b.mm(pC[:, 0:1],
                 [(wpk[:, kc * 1024 + mc * PC: kc * 1024 + mc * PC + PC],
                   bvec[:, wi * NDC + kc: wi * NDC + kc + 1]) for kc in range(NDC)],
                 waits=[(s_gl, ld_b), (s_ev, s_ev.count)], inc=s_mm)
            mm_b = s_mm.count
            b.on("vector", lambda q, _wi=wi, _mc=mc: nc.vector.tensor_copy(
                bcol[:, _wi * NDC + _mc: _wi * NDC + _mc + 1], pC[:, 0:1]),
                waits=[(s_mm, mm_b)], inc=s_ev)

    rotA.seed(s_ev)
    b.free("a_wW")
    kr = b.sbuf("a_kr", (PC, HS), BF16)
    ki = b.sbuf("a_ki", (PC, HS), BF16)
    vr = b.sbuf("a_vr", (PC, HS), BF16)
    vi = b.sbuf("a_vi", (PC, HS), BF16)
    kvr = b.sbuf("a_kvr", (PC, HS), BF16)
    kvi = b.sbuf("a_kvi", (PC, HS), BF16)
    sqs = b.sbuf("a_sqs", (PC, HS), BF16)

    for h in range(2):
        th0 = h * HS
        for pc_ in range(4):
            mc_re, mc_im = pc_, pc_ + 4
            for out_t, wi, mc in [(kr, 0, mc_re), (ki, 0, mc_im),
                                  (vr, 1, mc_re), (vi, 1, mc_im)]:
                for nb_ in range(HS // NB):
                    n0 = th0 + nb_ * NB
                    dstp, rw_ = rotA.take()
                    b.mm(dstp[:],
                         [(wprime[wi][:, kc * 1024 + mc * PC: kc * 1024 + (mc + 1) * PC],
                           xn[:, kc * S + n0: kc * S + n0 + NB]) for kc in range(NDC)],
                         waits=rw_, inc=s_mm)
                    mm_p = s_mm.count
                    b.on("vector", lambda q, _o=out_t, _wi=wi, _mc=mc, _nb=nb_, _p=dstp: nc.vector.tensor_scalar(
                        _o[:, _nb * NB:(_nb + 1) * NB], _p[:],
                        bcol[:, _wi * NDC + _mc: _wi * NDC + _mc + 1], None, op0=ALU.add),
                        waits=[(s_mm, mm_p)], inc=s_ev)
                    rotA.settle(s_ev)
            ev_kv_in = s_ev.count

            b.chain("vector", [
                lambda q: nc.vector.tensor_tensor(kvr[:], kr[:], vr[:], op=ALU.mult),
                lambda q: nc.vector.tensor_tensor(sqs[:], ki[:], vi[:], op=ALU.mult),
                lambda q: nc.vector.tensor_tensor(kvr[:], kvr[:], sqs[:], op=ALU.subtract),
                lambda q: nc.vector.tensor_tensor(kvi[:], kr[:], vi[:], op=ALU.mult),
                lambda q: nc.vector.tensor_tensor(sqs[:], ki[:], vr[:], op=ALU.mult),
                lambda q: nc.vector.tensor_tensor(kvi[:], kvi[:], sqs[:], op=ALU.add),
            ], waits=[(s_ev, ev_kv_in), (s_st, s_st.count)])
            if pc_ == 0:
                b.chain("vector", [
                    lambda q: nc.vector.tensor_tensor(kvr[0:1, :], kr[0:1, :], vr[0:1, :], op=ALU.mult),
                    lambda q: nc.vector.tensor_tensor(kvi[0:1, :], ki[0:1, :], vi[0:1, :], op=ALU.mult),
                ])
            c_re = carry[:, pc_ * 2: pc_ * 2 + 1]
            c_im = carry[:, pc_ * 2 + 1: pc_ * 2 + 2]

            b.chain("vector", [
                lambda q, _c=c_re, _h=h: nc.vector.tensor_tensor_scan(
                    vr[:], kvr[:], kvr[:], (0.0 if _h == 0 else _c), op0=ALU.add, op1=ALU.bypass),
                lambda q, _c=c_im, _h=h: nc.vector.tensor_tensor_scan(
                    vi[:], kvi[:], kvi[:], (0.0 if _h == 0 else _c), op0=ALU.add, op1=ALU.bypass),
                lambda q, _c=c_re: nc.vector.tensor_copy(_c, vr[:, HS - 1:HS]),
                lambda q, _c=c_im: nc.vector.tensor_copy(_c, vi[:, HS - 1:HS]),
            ])
            ev_scan = s_ev.count
            for out_t, mc in [(kr, mc_re), (ki, mc_im)]:
                for nb_ in range(HS // NB):
                    n0 = th0 + nb_ * NB
                    dstp, rw_ = rotA.take()
                    b.mm(dstp[:],
                         [(wprime[2][:, kc * 1024 + mc * PC: kc * 1024 + (mc + 1) * PC],
                           xn[:, kc * S + n0: kc * S + n0 + NB]) for kc in range(NDC)],
                         waits=rw_ + [(s_ev, ev_scan)], inc=s_mm)
                    mm_p = s_mm.count
                    b.on("vector", lambda q, _o=out_t, _mc=mc, _nb=nb_, _p=dstp: nc.vector.tensor_scalar(
                        _o[:, _nb * NB:(_nb + 1) * NB], _p[:],
                        bcol[:, 2 * NDC + _mc: 2 * NDC + _mc + 1], None, op0=ALU.add),
                        waits=[(s_mm, mm_p)], inc=s_ev)
                    rotA.settle(s_ev)
            ev_q = s_ev.count

            b.chain("vector", [
                lambda q: nc.vector.tensor_tensor(sqs[:], vr[:], kr[:], op=ALU.mult),
                lambda q: nc.vector.tensor_tensor(kvr[:], vi[:], ki[:], op=ALU.mult),
                lambda q: nc.vector.tensor_tensor(kvr[:], kvr[:], sqs[:], op=ALU.add),
                lambda q: nc.vector.tensor_tensor(sqs[:], vi[:], kr[:], op=ALU.mult),
                lambda q: nc.vector.tensor_tensor(kvi[:], vr[:], ki[:], op=ALU.mult),
                lambda q: nc.vector.scalar_tensor_tensor(kvi[:], kvi[:], -1.0, sqs[:], op0=ALU.mult, op1=ALU.add),
            ], waits=[(s_ev, ev_q)])
            if pc_ == 0:
                b.chain("vector", [
                    lambda q: nc.vector.tensor_tensor(kvr[0:1, :], vr[0:1, :], kr[0:1, :], op=ALU.mult),
                    lambda q: nc.vector.tensor_tensor(kvi[0:1, :], vi[0:1, :], ki[0:1, :], op=ALU.mult),
                ])
            ev_u = s_ev.count
            b.dma("gpsimd", io["qscr"][mc_re * PC:(mc_re + 1) * PC, th0:th0 + HS], kvr[:], s_st,
                  waits=[(s_ev, ev_u)])
            b.dma("gpsimd", io["qscr"][mc_im * PC:(mc_im + 1) * PC, th0:th0 + HS], kvi[:], s_st,
                  waits=[(s_ev, ev_u)])
            for slc in range(HS // NB):
                n0 = th0 + slc * NB
                sl = slice(slc * NB, (slc + 1) * NB)
                b.on("scalar", lambda q, _sl=sl: nc.scalar.activation(
                    sqs[:, _sl], kvr[:, _sl], AF.Square),
                    waits=[(s_ev, ev_u), (s_mm, s_mm.count)], inc=s_sc)
                sc1 = s_sc.count
                b.on("tensor", lambda q, _wc=mc_re, _sl=sl: nc.tensor.matmul(
                    pN[:], nw[:, _wc:_wc + 1], sqs[:, _sl], start=True, stop=False),
                    waits=[(s_sc, sc1), (s_ev, s_ev.count)], inc=s_mm)
                mm1 = s_mm.count
                b.on("scalar", lambda q, _sl=sl: nc.scalar.activation(
                    sqs[:, _sl], kvi[:, _sl], AF.Square),
                    waits=[(s_mm, mm1)], inc=s_sc)
                sc2 = s_sc.count
                b.on("tensor", lambda q, _wc=mc_im, _sl=sl: nc.tensor.matmul(
                    pN[:], nw[:, _wc:_wc + 1], sqs[:, _sl], start=False, stop=True),
                    waits=[(s_sc, sc2)], inc=s_mm)
                mm2 = s_mm.count
                b.on("vector", lambda q, _n0=n0, _first=(pc_ == 0): (
                    nc.vector.tensor_copy(nrm2row[:, _n0:_n0 + NB], pN[:]) if _first
                    else nc.vector.tensor_tensor(nrm2row[:, _n0:_n0 + NB],
                                                 nrm2row[:, _n0:_n0 + NB], pN[:], op=ALU.add)),
                    waits=[(s_mm, mm2)], inc=s_ev)

    b.on("scalar", lambda q: nc.scalar.activation(nrm2row[:], nrm2row[:], AF.Sqrt),
         waits=[(s_ev, s_ev.count)], inc=s_sc)

    b.chain("vector", [
        lambda q: nc.vector.tensor_scalar(scl[:], nrm2row[:], 1e-9, 32.0, op0=ALU.add, op1=ALU.mult),
        lambda q: nc.vector.reciprocal(scl[:], scl[:]),
    ], waits=[(s_sc, s_sc.count)])
    ev_scale = s_ev.count

    b.free("a_sqs", "a_kvi", "a_kvr", "a_vi", "a_vr", "a_ki", "a_kr")
    qld = b.sbuf("a_qld", (PC, NDC * NB), BF16)
    xres = b.sbuf("a_xres", (PC, NB), F32)
    oev = b.sbuf("a_oev", (PC, NB), F32)
    sclb = b.sbuf("a_sclb", (PC, NB), F32)

    b.dma("gpsimd", *_ld3(io["c_IW"], wpk[:], 1024), s_gl,
          waits=[(s_mm, s_mm.count)])
    ld_iw = s_gl.count
    for nb_ in range(S // NB):
        n0 = nb_ * NB
        for kc in range(NDC):
            b.dma("sync", qld[:, kc * NB:(kc + 1) * NB],
                  io["qscr"][kc * PC:(kc + 1) * PC, n0:n0 + NB], s_ld,
                  waits=[(s_st, s_st.count), (s_mm, s_mm.count), (s_ev, s_ev.count),
                         (s_sc, s_sc.count)] if kc == 0 else ())
        ld_q = s_ld.count
        b.on("tensor", lambda q, _n0=n0: nc.tensor.matmul(
            pC[:], onesr[:], scl[:, _n0:_n0 + NB], start=True, stop=True),
            waits=[(s_ev, s_ev.count)], inc=s_mm)
        mm_s = s_mm.count
        b.on("vector", lambda q: nc.vector.tensor_copy(sclb[:], pC[:]),
             waits=[(s_mm, mm_s)], inc=s_ev)
        ev_sb = s_ev.count
        for mc in range(NDC):
            dstp, rw_ = rotA.take()
            b.mm(dstp[:],
                 [(wpk[:, kc * 1024 + mc * PC: kc * 1024 + (mc + 1) * PC],
                   qld[:, kc * NB:(kc + 1) * NB]) for kc in range(NDC)],
                 waits=rw_ + [(s_ld, ld_q), (s_gl, ld_iw)], inc=s_mm)
            mm_i = s_mm.count
            b.dma("sync", xres[:], xA[mc * PC:(mc + 1) * PC, n0:n0 + NB], s_xr,
                  waits=[(s_ev, s_ev.count)])
            ld_x = s_xr.count

            b.chain("vector", [
                lambda q, _p=dstp: nc.vector.tensor_tensor(oev[:], _p[:], sclb[:], op=ALU.mult),
                lambda q: nc.vector.tensor_tensor(oev[:], oev[:], xres[:], op=ALU.add),
            ], waits=[(s_mm, mm_i), (s_xr, ld_x), (s_ev, ev_sb)])
            rotA.settle(s_ev)
            b.dma("gpsimd", io["x1"][mc * PC:(mc + 1) * PC, n0:n0 + NB], oev[:], s_st,
                  waits=[(s_ev, s_ev.count)])
            b.wait("vector", [(s_st, s_st.count)])
    b.wait("gpsimd", [(s_st, s_st.count)])
    b.free("a_sclb", "a_oev", "a_xres", "a_qld")
    b.free("a_carry", "a_nrm2", "a_nw", "a_bcol", "a_bvec",
           "a_wp2", "a_wp1", "a_wp0", "a_wpk", "a_xn", "p_D", "p_N")



def build_l0_ffn(nc, b, io, sems):
    s_ld, s_mm, s_ev, s_sc, s_st = sems["ld"], sems["mm"], sems["ev"], sems["sc"], sems["st"]
    s_gl, s_xr = sems["gl"], sems["xr"]
    NSL = 64
    pS1 = b.psum("p_S1", (NSL, NB))
    pS2 = b.psum("p_S2", (NSL, NB))
    pT = b.psum("p_T", (PC, PC), BF16)
    rotF = Rot([b.sb["p_A"], b.sb["p_B"], b.sb["p_R1"], b.sb["p_R2"]])
    pA, pB, pC = b.sb["p_A"], b.sb["p_B"], b.sb["p_C"]
    pR1, pR2 = b.sb["p_R1"], b.sb["p_R2"]
    pR1 = pR1[0:1, :]
    pR2 = pR2[0:1, :]
    onesrb = b.sb["g_onesrb"]
    idn = b.sb["g_idn"]
    idnf = b.sb["g_idnf"]

    xn = b.sbuf("f_xn", (PC, NDC * S), BF16)
    w1x = b.sbuf("f_w1x", (PC, NDC * NB), BF16)
    w1s = b.sbuf("f_w1s", (PC, NDC * NB), BF16)
    w2 = b.sbuf("f_w2", (PC, 4 * NSL), BF16)
    wq = b.sbuf("f_wq", (PC, NDC * 256), BF16)
    wk = b.sbuf("f_wk", (PC, NDC * 256), BF16)
    wv = b.sbuf("f_wv", (PC, NDC * 1024), BF16)
    wvp = b.sbuf("f_wvp", (PC, NDC * 1024), BF16)
    wo = b.sbuf("f_wo", (PC, NDC * 1024), BF16)
    slotT = b.sbuf("f_slotT", (PC, NDC * NSL), BF16)
    smcol = b.sbuf("f_smcol", (PC, NDC), BF16)
    smcolf = b.sbuf("f_smcolf", (PC, NDC), F32)
    hcol = b.sbuf("f_hcol", (PC, 4), F32)
    b1c = b.sbuf("f_b1c", (PC, 4), F32)
    b2c = b.sbuf("f_b2c", (NSL, 1), F32)
    invt = b.sbuf("f_invt", (NSL, 1), F32)
    tau1 = b.sbuf("f_tau1", (1, 1), F32)
    s8 = b.sbuf("f_s8", (PC, 8), F32)
    ecol = b.sbuf("f_ecol", (PC, 1), F32)
    etile = b.sbuf("f_etile", (PC, NSL), F32)
    emsk = b.sbuf("f_emsk", (PC, NSL), BF16)
    atok = b.sbuf("f_atok", (PC, 32 * NSL), BF16)
    afm = b.sbuf("f_afm", (NSL, S), BF16)
    dcol = b.sbuf("f_dcol", (NSL, 1), F32)
    bihc = b.sbuf("f_bihc", (PC, 24), F32)
    bhhc = b.sbuf("f_bhhc", (PC, 24), F32)
    snT = b.sbuf("f_snT", (PC, NDC * NSL), BF16)
    snTf = b.sbuf("f_snTf", (PC, NDC * NSL), F32)
    qT = b.sbuf("f_qT", (PC, 2 * NSL), BF16)
    kT = b.sbuf("f_kT", (PC, 2 * NSL), BF16)
    bqc = b.sbuf("f_bqc", (PC, 2), F32)
    bkc = b.sbuf("f_bkc", (PC, 2), F32)
    vtok = b.sbuf("f_vtok", (NSL, 1024), BF16)
    bvrow = b.sbuf("f_bvrow", (1, 1024), F32)
    bvrowb = b.sbuf("f_bvrowb", (1, 1024), BF16)
    bvprow = b.sbuf("f_bvprow", (1, 1024), F32)
    bvprowb = b.sbuf("f_bvprowb", (1, 1024), BF16)
    sc64 = b.sbuf("f_sc64", (NSL, NSL), F32)
    at64 = b.sbuf("f_at64", (NSL, NSL), BF16)
    atT = b.sbuf("f_atT", (NSL, NSL), BF16)
    rowm = b.sbuf("f_rowm", (1, NSL), F32)
    rowv = b.sbuf("f_rowv", (1, NSL), F32)
    rown = b.sbuf("f_rown", (1, NSL), F32)
    lgc = b.sbuf("f_lgc", (PC, NDC), F32)
    lbc = b.sbuf("f_lbc", (PC, NDC), F32)
    svtok = b.sbuf("f_svtok", (NSL, 1024), BF16)
    boc = b.sbuf("f_boc", (PC, NDC), F32)
    ynT = b.sbuf("f_ynT", (PC, NDC * NSL), BF16)

    xA = io["x1"]
    b.dma("gpsimd", *_ld3(io["w1x"], w1x[:], NB), s_gl,
          waits=[(s_mm, s_mm.count), (s_ev, s_ev.count), (s_sc, s_sc.count),
                 (s_st, s_st.count)])
    b.dma("gpsimd", *_ld3(io["w1s"], w1s[:], NB), s_gl)
    b.dma("gpsimd", *_ld3(io["w2"], w2[:], NSL), s_gl)
    b.dma("gpsimd", *_ld3(io["wq"], wq[:], 256), s_gl)
    b.dma("gpsimd", *_ld3(io["wk"], wk[:], 256), s_gl)
    b.dma("gpsimd", *_ld3(io["wv"], wv[:], 1024), s_gl)
    b.dma("gpsimd", *_ld3(io["wvp"], wvp[:], 1024), s_gl)
    b.dma("gpsimd", *_ld3(io["wo"], wo[:], 1024), s_gl)
    b.dma("gpsimd", *_ld3(io["slotT"], slotT[:], NSL), s_gl)
    b.dma("sync", b1c[:], io["b1c"], s_ld,
          waits=[(s_ld, s_ld.count), (s_mm, s_mm.count), (s_ev, s_ev.count),
                 (s_sc, s_sc.count), (s_st, s_st.count)])
    b.dma("sync", b2c[:], io["b2c"], s_ld)
    b.dma("sync", tau1[:], io["tau"], s_ld)
    b.dma("sync", bihc[:], io["bihc"], s_ld)
    b.dma("sync", bhhc[:], io["bhhc"], s_ld)
    b.dma("sync", bqc[:], io["bqc"], s_ld)
    b.dma("sync", bkc[:], io["bkc"], s_ld)
    b.dma("sync", bvrow[:], io["bvrow"], s_ld)
    b.dma("sync", bvprow[:], io["bvprow"], s_ld)
    b.dma("sync", lgc[:], io["lngc"], s_ld)
    b.dma("sync", lbc[:], io["lnbc"], s_ld)
    b.dma("sync", boc[:], io["boc"], s_ld)
    ld0 = s_ld.count
    ld0g = s_gl.count

    alloc_ln_region(b)
    emit_ln(nc, b, xA, xn, sems)
    free_ln_region(b)

    hb = b.sbuf("f_hb", (PC, 4 * NB), BF16)
    rlfm = b.sbuf("f_rlfm", (NSL, S), F32)
    rltok = b.sbuf("f_rltok", (PC, 32 * NSL), F32)

    b.chain("vector",
            [lambda q, _dc=dc: nc.vector.tensor_reduce(
                smcolf[:, _dc:_dc + 1], slotT[:, _dc * NSL:(_dc + 1) * NSL], axis=AX.X, op=ALU.add)
             for dc in range(NDC)] +
            [lambda q: nc.vector.tensor_scalar(smcol[:], smcolf[:], 1.0 / NSL, None, op0=ALU.mult)],
            waits=[(s_ld, s_ld.count), (s_gl, s_gl.count)])
    b.on("scalar", lambda q: nc.scalar.activation(tau1[:], tau1[:], AF.Abs),
         waits=[(s_ld, s_ld.count)], inc=s_sc)

    b.chain("vector", [
        lambda q: nc.vector.tensor_scalar(tau1[:], tau1[:], 0.1, None, op0=ALU.add),
        lambda q: nc.vector.reciprocal(tau1[:], tau1[:]),
    ], waits=[(s_sc, s_sc.count)])
    ev_tau = s_ev.count
    b.mm(pC[:NSL, 0:1], [(b.sb["g_onesr"][:, 0:NSL], tau1[:])],
         waits=[(s_ev, ev_tau)], inc=s_mm)
    b.on("vector", lambda q: nc.vector.tensor_copy(invt[:], pC[:NSL, 0:1]),
         waits=[(s_mm, s_mm.count)], inc=s_ev)
    for mc in range(4):
        b.mm(pC[:, 0:1],
             [(w1s[:, kc * NB + mc * PC: kc * NB + (mc + 1) * PC],
               smcol[:, kc:kc + 1]) for kc in range(NDC)],
             waits=[(s_ev, s_ev.count)], inc=s_mm)
        mm_h = s_mm.count
        b.on("vector", lambda q, _mc=mc: nc.vector.scalar_tensor_tensor(
            hcol[:, _mc:_mc + 1], pC[:, 0:1], 1.0, b1c[:, _mc:_mc + 1],
            op0=ALU.mult, op1=ALU.add),
            waits=[(s_mm, mm_h)], inc=s_ev)
    ev_hcol = s_ev.count
    b.on("vector", lambda q: [nc.vector.tensor_copy(bvrowb[:], bvrow[:]),
                              nc.vector.tensor_copy(bvprowb[:], bvprow[:])],
         waits=[(s_ld, s_ld.count)], inc=s_ev)

    rotF.seed(s_ev)
    for nb_ in range(S // NB):
        n0 = nb_ * NB
        for mc in range(4):
            dstp, rw_ = rotF.take()
            b.mm(dstp[:],
                 [(w1x[:, kc * NB + mc * PC: kc * NB + (mc + 1) * PC],
                   xn[:, kc * S + n0: kc * S + n0 + NB]) for kc in range(NDC)],
                 waits=rw_ + [(s_gl, ld0g)], inc=s_mm)
            mm_h = s_mm.count
            b.on("scalar", lambda q, _mc=mc, _p=dstp: nc.scalar.activation(
                hb[:, _mc * NB:(_mc + 1) * NB], _p[:], AF.Gelu, bias=hcol[:, _mc:_mc + 1]),
                waits=[(s_mm, mm_h), (s_ev, ev_hcol)], inc=s_sc)
            rotF.settle(s_sc)
        sc_h = s_sc.count
        b.mm(pC[:NSL, :],
             [(w2[:, mc * NSL:(mc + 1) * NSL], hb[:, mc * NB:(mc + 1) * NB])
              for mc in range(4)],
             waits=[(s_sc, sc_h), (s_ev, s_ev.count)], inc=s_mm)
        mm_rl = s_mm.count
        b.on("vector", lambda q, _n0=n0: nc.vector.tensor_scalar(
            rlfm[:, _n0:_n0 + NB], pC[:NSL, :], b2c[:], invt[:],
            op0=ALU.add, op1=ALU.mult),
            waits=[(s_mm, mm_rl)], inc=s_ev)
    ev_rl = s_ev.count

    for tc in range(32):
        t0 = tc * PC
        b.on("tensor", lambda q, _t0=t0: nc.tensor.transpose(
            pA[:, 0:NSL], rlfm[:, _t0:_t0 + PC], idnf[0:NSL, 0:NSL]),
            waits=[(s_ev, s_ev.count)], inc=s_mm)
        mm_t = s_mm.count
        b.on("vector", lambda q, _tc=tc: nc.vector.tensor_copy(
            rltok[:, _tc * NSL:(_tc + 1) * NSL], pA[:, 0:NSL]),
            waits=[(s_mm, mm_t)], inc=s_ev)
        ev_t = s_ev.count
        rlc = rltok[:, tc * NSL:(tc + 1) * NSL]

        b.chain("vector", [
            lambda q, _rl=rlc: nc.vector.max(s8[:], _rl),
            lambda q: nc.vector.tensor_scalar(ecol[:], s8[:, 0:1], -1.0, None, op0=ALU.mult),
        ], waits=[(s_ev, ev_t)])
        ev_r = s_ev.count
        b.on("scalar", lambda q, _rl=rlc: nc.scalar.activation(
            etile[:], _rl, AF.Exp, bias=ecol[:]),
            waits=[(s_ev, ev_r)], inc=s_sc)
        sc_e = s_sc.count

        b.chain("vector", [
            lambda q, _rl=rlc: nc.vector.tensor_scalar(emsk[:], _rl, s8[:, 7:8], None, op0=ALU.is_ge),
            lambda q: nc.vector.tensor_tensor(etile[:], etile[:], emsk[:], op=ALU.mult),
            lambda q: nc.vector.tensor_reduce(ecol[:], etile[:], axis=AX.X, op=ALU.add),
            lambda q: nc.vector.reciprocal(ecol[:], ecol[:]),
            lambda q, _tc=tc: nc.vector.tensor_scalar(atok[:, _tc * NSL:(_tc + 1) * NSL], etile[:],
                                                      ecol[:], None, op0=ALU.mult),
        ], waits=[(s_sc, sc_e)])
        ev_a = s_ev.count
        b.on("tensor", lambda q, _tc=tc: _tbf(nc, pT[0:NSL, :], atok[:, _tc * NSL:(_tc + 1) * NSL], idn[:]),
            waits=[(s_ev, ev_a)], inc=s_mm)
        mm_at = s_mm.count
        b.on("vector", lambda q, _tc=tc: nc.vector.tensor_copy(
            afm[:, _tc * PC:(_tc + 1) * PC], pT[0:NSL, :]),
            waits=[(s_mm, mm_at)], inc=s_ev)
    ev_alpha = s_ev.count
    b.free("f_rltok", "f_rlfm", "f_hb")
    xtok = b.sbuf("f_xtok", (PC, NDC * PC), BF16)
    sit = b.sbuf("f_sit", (PC, NDC * NSL), BF16)
    siev = b.sbuf("f_siev", (NSL, 1024), BF16)
    git = b.sbuf("f_git", (PC, 24 * NSL), F32)
    ght = b.sbuf("f_ght", (PC, 24 * NSL), F32)
    grs = b.sbuf("f_grs", (PC, 8 * NSL), F32)
    gzs = b.sbuf("f_gzs", (PC, 8 * NSL), F32)
    gns = b.sbuf("f_gns", (PC, 8 * NSL), F32)
    yT = b.sbuf("f_yT", (PC, NDC * NSL), F32)
    y2 = b.sbuf("f_y2", (PC, NDC * NSL), BF16)
    wstage = b.sbuf("f_wstage", (PC, 2 * 1024), BF16)

    b.chain("vector", [
        lambda q: nc.vector.tensor_reduce(dcol[:], afm[:], axis=AX.X, op=ALU.add),
        lambda q: nc.vector.tensor_scalar(dcol[:], dcol[:], 1e-8, None, op0=ALU.add),
        lambda q: nc.vector.reciprocal(dcol[:], dcol[:]),
    ], waits=[(s_ev, ev_alpha)])
    ev_den = s_ev.count
    for tc in range(32):
        t0 = tc * PC
        for dc in range(NDC):
            b.on("tensor", lambda q, _dc=dc, _t0=t0: _tbf(nc, pT[:], xn[:, _dc * S + _t0: _dc * S + _t0 + PC], idn[:]),
                waits=[(s_ev, s_ev.count)], inc=s_mm)
            mm_x = s_mm.count
            b.on("vector", lambda q, _dc=dc: nc.vector.tensor_copy(
                xtok[:, _dc * PC:(_dc + 1) * PC], pT[:]),
                waits=[(s_mm, mm_x)], inc=s_ev)
        ev_x = s_ev.count
        for half, pS in [(0, pS1), (1, pS2)]:
            b.on("tensor", lambda q, _tc=tc, _h=half, _p=pS: nc.tensor.matmul(
                _p[:], atok[:, _tc * NSL:(_tc + 1) * NSL],
                xtok[:, _h * NB:(_h + 1) * NB],
                start=(_tc == 0), stop=(_tc == 31)),
                waits=[(s_ev, ev_x)],
                inc=(s_mm if tc == 31 else None))
    mm_si = s_mm.count
    b.on("vector", lambda q: [
        nc.vector.tensor_scalar(siev[:, 0:NB], pS1[:], dcol[:], None, op0=ALU.mult),
        nc.vector.tensor_scalar(siev[:, NB:1024], pS2[:], dcol[:], None, op0=ALU.mult)],
        waits=[(s_mm, mm_si), (s_ev, ev_den)], inc=s_ev)
    ev_si = s_ev.count
    for dc in range(NDC):
        b.on("tensor", lambda q, _dc=dc: _tbf(nc, pT[:, 0:NSL], siev[:, _dc * PC:(_dc + 1) * PC], idn[0:NSL, 0:NSL]),
            waits=[(s_ev, s_ev.count)], inc=s_mm)
        mm_s = s_mm.count
        b.on("vector", lambda q, _dc=dc: nc.vector.tensor_copy(
            sit[:, _dc * NSL:(_dc + 1) * NSL], pT[:, 0:NSL] if False else pT[0:PC, 0:NSL]),
            waits=[(s_mm, mm_s)], inc=s_ev)
    ev_sit = s_ev.count

    for gname, wsrc, dst, bcols in [("ih", "WihT", git, bihc), ("hh", "WhhT", ght, bhhc)]:
        rhs = sit if gname == "ih" else slotT
        for mc in range(24):
            buf = wstage[:, (mc % 2) * 1024:(mc % 2) * 1024 + 1024]
            b.dma("gpsimd", *_ld3(io[wsrc][:, mc * PC:(mc + 1) * PC], buf, PC), s_gl,
                  waits=[(s_mm, s_mm.count), (s_ev, s_ev.count)])
            ld_w = s_gl.count
            b.mm(pC[:, 0:NSL],
                 [(buf[:, kc * PC:(kc + 1) * PC], rhs[:, kc * NSL:(kc + 1) * NSL])
                  for kc in range(NDC)],
                 waits=[(s_gl, ld_w), (s_ev, s_ev.count)], inc=s_mm)
            mm_g = s_mm.count
            b.on("vector", lambda q, _dst=dst, _mc=mc, _bc=bcols: nc.vector.tensor_scalar(
                _dst[:, _mc * NSL:(_mc + 1) * NSL], pC[:, 0:NSL], _bc[:, _mc:_mc + 1], None, op0=ALU.add),
                waits=[(s_mm, mm_g)], inc=s_ev)
    ev_g = s_ev.count

    for dc in range(NDC):
        sl = slice(dc * NSL, (dc + 1) * NSL)
        slz = slice((NDC + dc) * NSL, (NDC + dc + 1) * NSL)
        sln = slice((2 * NDC + dc) * NSL, (2 * NDC + dc + 1) * NSL)

        def gru1(q, _sl=sl, _slz=slz, _sln=sln, _dc=dc):
            i1 = nc.vector.tensor_tensor(grs[:, _sl], git[:, _sl], ght[:, _sl], op=ALU.add)
            i2 = nc.vector.tensor_tensor(gzs[:, _sl], git[:, _slz], ght[:, _slz], op=ALU.add)
            return [i1, i2]
        b.on("vector", gru1, waits=[(s_ev, ev_g)], inc=s_ev)
        b.on("scalar", lambda q, _sl=sl: nc.scalar.activation(grs[:, _sl], grs[:, _sl], AF.Sigmoid),
             waits=[(s_ev, s_ev.count)], inc=s_sc)
        b.on("scalar", lambda q, _sl=sl: nc.scalar.activation(gzs[:, _sl], gzs[:, _sl], AF.Sigmoid),
             waits=[(s_ev, s_ev.count)], inc=s_sc)
        sc_rz = s_sc.count

        b.chain("vector", [
            lambda q, _sl=sl, _sln=sln: nc.vector.tensor_tensor(gns[:, _sl], grs[:, _sl], ght[:, _sln], op=ALU.mult),
            lambda q, _sl=sl, _sln=sln: nc.vector.tensor_tensor(gns[:, _sl], gns[:, _sl], git[:, _sln], op=ALU.add),
        ], waits=[(s_sc, sc_rz)])
        b.on("scalar", lambda q, _sl=sl: nc.scalar.activation(gns[:, _sl], gns[:, _sl], AF.Tanh),
             waits=[(s_ev, s_ev.count)], inc=s_sc)
        sc_n = s_sc.count

        b.chain("vector", [
            lambda q, _sl=sl: nc.vector.tensor_tensor(grs[:, _sl], slotT[:, _sl], gns[:, _sl], op=ALU.subtract),
            lambda q, _sl=sl: nc.vector.tensor_tensor(grs[:, _sl], grs[:, _sl], gzs[:, _sl], op=ALU.mult),
            lambda q, _sl=sl: nc.vector.tensor_tensor(snTf[:, _sl], grs[:, _sl], gns[:, _sl], op=ALU.add),
            lambda q, _sl=sl: nc.vector.tensor_copy(snT[:, _sl], snTf[:, _sl]),
        ], waits=[(s_sc, sc_n)])
    ev_sn = s_ev.count

    for dst, wsb, bcc, width in [(qT, wq, bqc, 256), (kT, wk, bkc, 256)]:
        for mc in range(width // PC):
            b.mm(pC[:, 0:NSL],
                 [(wsb[:, kc * width + mc * PC: kc * width + (mc + 1) * PC],
                   snT[:, kc * NSL:(kc + 1) * NSL]) for kc in range(NDC)],
                 waits=[(s_ev, s_ev.count)], inc=s_mm)
            mm_q = s_mm.count
            b.on("vector", lambda q, _dst=dst, _mc=mc, _bc=bcc: nc.vector.tensor_scalar(
                _dst[:, _mc * NSL:(_mc + 1) * NSL], pC[:, 0:NSL], _bc[:, _mc:_mc + 1], None, op0=ALU.add),
                waits=[(s_mm, mm_q)], inc=s_ev)
    for nb_ in range(2):
        b.mm(pS1[:, :] if nb_ == 0 else pS2[:, :],
             [(snT[:, kc * NSL:(kc + 1) * NSL],
               wv[:, kc * 1024 + nb_ * NB: kc * 1024 + (nb_ + 1) * NB]) for kc in range(NDC)] +
             [(onesrb[:, 0:NSL], bvrowb[:, nb_ * NB:(nb_ + 1) * NB])],
             waits=[(s_ev, s_ev.count)], inc=s_mm)
        mm_v = s_mm.count
        b.on("vector", lambda q, _nb=nb_: nc.vector.tensor_copy(
            vtok[:, _nb * NB:(_nb + 1) * NB], (pS1 if _nb == 0 else pS2)[:, :]),
            waits=[(s_mm, mm_v)], inc=s_ev)
    ev_v = s_ev.count

    b.mm(pC[:NSL, 0:NSL],
         [(qT[:, kc * NSL:(kc + 1) * NSL], kT[:, kc * NSL:(kc + 1) * NSL]) for kc in range(2)],
         waits=[(s_ev, s_ev.count)], inc=s_mm)
    mm_sc64 = s_mm.count
    b.on("vector", lambda q: nc.vector.tensor_scalar(
        sc64[:], pC[:NSL, 0:NSL], 0.0625, None, op0=ALU.mult),
        waits=[(s_mm, mm_sc64)], inc=s_ev)

    b.chain("vector", [
        lambda q: nc.vector.tensor_reduce(dcol[:], sc64[:], axis=AX.X, op=ALU.max),
        lambda q: nc.vector.tensor_scalar(dcol[:], dcol[:], -1.0, None, op0=ALU.mult),
    ], waits=[(s_ev, s_ev.count)])
    b.on("scalar", lambda q: nc.scalar.activation(sc64[:], sc64[:], AF.Exp, bias=dcol[:]),
         waits=[(s_ev, s_ev.count)], inc=s_sc)

    b.chain("vector", [
        lambda q: nc.vector.tensor_reduce(dcol[:], sc64[:], axis=AX.X, op=ALU.add),
        lambda q: nc.vector.reciprocal(dcol[:], dcol[:]),
        lambda q: nc.vector.tensor_scalar(at64[:], sc64[:], dcol[:], None, op0=ALU.mult),
    ], waits=[(s_sc, s_sc.count)])
    ev_at = s_ev.count
    b.on("tensor", lambda q: _tbf(nc, pT[0:NSL, 0:NSL], at64[:], idn[0:NSL, 0:NSL]),
         waits=[(s_ev, ev_at)], inc=s_mm)
    b.on("vector", lambda q: nc.vector.tensor_copy(atT[:], pT[0:NSL, 0:NSL]),
         waits=[(s_mm, s_mm.count)], inc=s_ev)
    ev_atT = s_ev.count
    for dc in range(NDC):
        b.mm(pC[:, 0:NSL],
             [(vtok[:NSL, dc * PC:(dc + 1) * PC], atT[:, :])],
             waits=[(s_ev, s_ev.count)], inc=s_mm)
        mm_y = s_mm.count
        b.on("vector", lambda q, _dc=dc: nc.vector.scalar_tensor_tensor(
            yT[:, _dc * NSL:(_dc + 1) * NSL], pC[:, 0:NSL], 1.0, snTf[:, _dc * NSL:(_dc + 1) * NSL],
            op0=ALU.mult, op1=ALU.add),
            waits=[(s_mm, mm_y)], inc=s_ev)
    ev_y = s_ev.count

    for dc in range(NDC):
        b.on("scalar", lambda q, _dc=dc: nc.scalar.activation(
            y2[:, _dc * NSL:(_dc + 1) * NSL], yT[:, _dc * NSL:(_dc + 1) * NSL], AF.Square),
            waits=[(s_ev, ev_y), (s_mm, s_mm.count)], inc=s_sc)
        sc_y = s_sc.count
        b.on("tensor", lambda q, _dc=dc: nc.tensor.matmul(
            pR2[:, 0:NSL], b.sb["g_onescolb"][:], y2[:, _dc * NSL:(_dc + 1) * NSL],
            start=(_dc == 0), stop=(_dc == NDC - 1)),
            waits=[(s_sc, sc_y)] + ([(s_ev, s_ev.count)] if dc == 0 else []),
            inc=(s_mm if dc == NDC - 1 else None))
        b.on("tensor", lambda q, _dc=dc: nc.tensor.matmul(
            pR1[:, 0:NSL], b.sb["g_onescol"][:], yT[:, _dc * NSL:(_dc + 1) * NSL],
            start=(_dc == 0), stop=(_dc == NDC - 1)),
            waits=[(s_ev, ev_y)] if dc == 0 else (),
            inc=(s_mm if dc == NDC - 1 else None))
    mm_ys = s_mm.count

    b.chain("vector", [
        lambda q: nc.vector.tensor_scalar(rowm[:], pR1[:, 0:NSL], 1.0 / D, None, op0=ALU.mult),
        lambda q: nc.vector.tensor_scalar(rowv[:], pR2[:, 0:NSL], 1.0 / D, None, op0=ALU.mult),
        lambda q: nc.vector.tensor_tensor(rown[:], rowm[:], rowm[:], op=ALU.mult),
        lambda q: nc.vector.tensor_tensor(rowv[:], rowv[:], rown[:], op=ALU.subtract),
        lambda q: nc.vector.tensor_scalar(rowv[:], rowv[:], 1e-5, None, op0=ALU.add),
    ], waits=[(s_mm, mm_ys)])
    b.on("scalar", lambda q: nc.scalar.activation(rowv[:], rowv[:], AF.Sqrt),
         waits=[(s_ev, s_ev.count)], inc=s_sc)

    b.chain("vector", [
        lambda q: nc.vector.reciprocal(rowv[:], rowv[:]),
        lambda q: nc.vector.tensor_tensor(rown[:], rowm[:], rowv[:], op=ALU.mult),
        lambda q: nc.vector.tensor_scalar(rown[:], rown[:], -1.0, None, op0=ALU.mult),
    ], waits=[(s_sc, s_sc.count)])
    b.on("tensor", lambda q: [
        nc.tensor.matmul(pA[:, 0:NSL], b.sb["g_onesr"][:], rowv[:], start=True, stop=True),
        nc.tensor.matmul(pB[:, 0:NSL], b.sb["g_onesr"][:], rown[:], start=True, stop=True)],
        waits=[(s_ev, s_ev.count)], inc=s_mm)
    mm_bc = s_mm.count
    for dc in range(NDC):
        sl_ = slice(dc * NSL, (dc + 1) * NSL)
        b.chain("vector", [
            lambda q, _sl=sl_: nc.vector.tensor_tensor(yT[:, _sl], yT[:, _sl], pA[:, 0:NSL], op=ALU.mult),
            lambda q, _sl=sl_: nc.vector.tensor_tensor(yT[:, _sl], yT[:, _sl], pB[:, 0:NSL], op=ALU.add),
            lambda q, _sl=sl_, _dc=dc: nc.vector.tensor_scalar(ynT[:, _sl], yT[:, _sl], lgc[:, _dc:_dc + 1],
                                                               lbc[:, _dc:_dc + 1], op0=ALU.mult, op1=ALU.add),
        ], waits=[(s_mm, mm_bc)])
    ev_yn = s_ev.count

    for nb_ in range(2):
        b.mm(pS1[:, :] if nb_ == 0 else pS2[:, :],
             [(ynT[:, kc * NSL:(kc + 1) * NSL],
               wvp[:, kc * 1024 + nb_ * NB: kc * 1024 + (nb_ + 1) * NB]) for kc in range(NDC)] +
             [(onesrb[:, 0:NSL], bvprowb[:, nb_ * NB:(nb_ + 1) * NB])],
             waits=[(s_ev, ev_yn)], inc=s_mm)
        mm_sv = s_mm.count
        b.on("vector", lambda q, _nb=nb_: nc.vector.tensor_copy(
            svtok[:, _nb * NB:(_nb + 1) * NB], (pS1 if _nb == 0 else pS2)[:, :]),
            waits=[(s_mm, mm_sv)], inc=s_ev)
    ev_sv = s_ev.count
    b.free("f_wstage", "f_y2", "f_yT", "f_gns", "f_gzs", "f_grs", "f_ght",
           "f_git", "f_siev", "f_sit", "f_xtok")
    outb = b.sbuf("f_outb", (PC, NDC * NB), BF16)
    xres = b.sbuf("f_xres", (PC, NB), F32)
    oev = b.sbuf("f_oev", (PC, NB), F32)

    rotF.seed(s_ev)
    for nb_ in range(S // NB):
        n0 = nb_ * NB
        for mc in range(NDC):
            dstp, rw_ = rotF.take()
            b.mm(dstp[:],
                 [(svtok[:NSL, mc * PC:(mc + 1) * PC], afm[:, n0:n0 + NB])],
                 waits=rw_ + [(s_ev, ev_sv)], inc=s_mm)
            mm_o = s_mm.count
            b.on("vector", lambda q, _mc=mc, _p=dstp: nc.vector.tensor_copy(
                outb[:, _mc * NB:(_mc + 1) * NB], _p[:]),
                waits=[(s_mm, mm_o)], inc=s_ev)
            rotF.settle(s_ev)
        ev_o = s_ev.count
        for mc in range(NDC):
            dstp, rw_ = rotF.take()
            b.mm(dstp[:],
                 [(wo[:, kc * 1024 + mc * PC: kc * 1024 + (mc + 1) * PC],
                   outb[:, kc * NB:(kc + 1) * NB]) for kc in range(NDC)],
                 waits=rw_ + [(s_ev, s_ev.count)], inc=s_mm)
            mm_f = s_mm.count
            b.dma("sync", xres[:], xA[mc * PC:(mc + 1) * PC, n0:n0 + NB], s_xr,
                  waits=[(s_ev, s_ev.count)])
            ld_x = s_xr.count

            b.chain("vector", [
                lambda q, _mc=mc, _p=dstp: nc.vector.tensor_scalar(
                    oev[:], _p[:], boc[:, _mc:_mc + 1], None, op0=ALU.add),
                lambda q: nc.vector.tensor_tensor(oev[:], oev[:], xres[:], op=ALU.add),
            ], waits=[(s_mm, mm_f), (s_xr, ld_x)])
            rotF.settle(s_ev)
            b.dma("gpsimd", io["x2"][mc * PC:(mc + 1) * PC, n0:n0 + NB], oev[:], s_st,
                  waits=[(s_ev, s_ev.count)])
            b.wait("vector", [(s_st, s_st.count)])
    b.wait("gpsimd", [(s_st, s_st.count)])
    b.free("f_oev", "f_xres", "f_outb",
           "f_ynT", "f_boc", "f_svtok", "f_lbc", "f_lgc", "f_rown", "f_rowv", "f_rowm",
           "f_atT", "f_at64", "f_sc64", "f_bvprowb", "f_bvprow", "f_bvrowb", "f_bvrow",
           "f_vtok", "f_bkc", "f_bqc", "f_kT", "f_qT", "f_snTf", "f_snT", "f_bhhc",
           "f_bihc", "f_dcol", "f_afm", "f_atok", "f_emsk", "f_etile", "f_ecol", "f_s8",
           "f_tau1", "f_invt", "f_b2c", "f_b1c", "f_hcol", "f_smcolf", "f_smcol",
           "f_slotT", "f_wo", "f_wvp", "f_wv", "f_wk", "f_wq", "f_w2", "f_w1s", "f_w1x",
           "f_xn", "p_T", "p_S2", "p_S1")




def build_l1_attn(nc, b, io, sems):
    s_ld, s_mm, s_ev, s_sc, s_st = sems["ld"], sems["mm"], sems["ev"], sems["sc"], sems["st"]
    s_gl, s_xr = sems["gl"], sems["xr"]
    s_qh, s_kh, s_vh = sems["qh"], sems["kh"], sems["vh"]
    s_mmav = sems["av"]
    H = 16
    pAV = b.psum("p_AV", (65, NB))
    pZ = b.psum("p_Z", (64, NB))
    pT = b.psum("p_T", (PC, PC), BF16)
    pA, pB, pC = b.sb["p_A"], b.sb["p_B"], b.sb["p_C"]
    rotT = Rot([pA, pB, pC, b.sb["p_R1"], b.sb["p_R2"]])
    idn = b.sb["g_idn"]
    trneg = b.sbuf("t_trneg", (PC, PC), BF16)
    iota8 = b.sbuf("t_iota8", (PC, 8), F32)

    ki = b.sbuf("t_ki", (64, S), BF16)
    wiT = b.sbuf("t_wiT", (PC, 32 * 4), F32)
    xn = b.sbuf("t_xn", (PC, NDC * S), BF16)

    xA = io["x2"]
    b.dma("sync", trneg[:], io["c_triu_neg"], s_xr,
          waits=[(s_ev, s_ev.count), (s_sc, s_sc.count), (s_mm, s_mm.count),
                 (sems["st"], sems["st"].count)])
    b.dma("sync", iota8[:], io["c_iota8"], s_xr)
    ld_tn = s_xr.count

    alloc_ln_region(b)
    emit_ln(nc, b, xA, xn, sems)
    free_ln_region(b)

    wst = b.sbuf("t_wst", (PC, 2 * NDC * PC), BF16)
    qc1 = b.sbuf("t_qc1", (PC, S), BF16)
    qc2 = b.sbuf("t_qc2", (PC, S), BF16)
    rsc = b.sbuf("t_rsc", (PC, S), BF16)
    rsc2 = b.sbuf("t_rsc2", (PC, S), BF16)
    cosP = b.sbuf("t_cosP", (PC, S), BF16)
    sinP = b.sbuf("t_sinP", (PC, S), BF16)
    wif = b.sbuf("t_wif", (4, S), F32)

    b.dma("sync", cosP[:], io["c_cosP"], s_xr,
          waits=[(s_ev, s_ev.count), (s_sc, s_sc.count), (s_mm, s_mm.count)])
    b.dma("sync", sinP[:], io["c_sinP"], s_xr)
    ld_cs = s_xr.count

    b.dma("gpsimd", *_ld3(io["WwiT"], wst[:, 0:NDC * 4], 4), s_gl,
          waits=[(s_ev, s_ev.count), (s_sc, s_sc.count), (s_mm, s_mm.count),
                 (s_st, s_st.count)])
    ld_wi = s_gl.count
    for nb_ in range(S // NB):
        n0 = nb_ * NB
        b.mm(pC[0:4, :],
             [(wst[:, kc * 4:(kc + 1) * 4], xn[:, kc * S + n0: kc * S + n0 + NB])
              for kc in range(NDC)],
             waits=[(s_gl, ld_wi), (s_ev, s_ev.count), (s_sc, s_sc.count)], inc=s_mm)
        mm_w = s_mm.count
        b.on("vector", lambda q, _n0=n0: nc.vector.tensor_copy(wif[:, _n0:_n0 + NB], pC[0:4, :]),
             waits=[(s_mm, mm_w)])
    idnf = b.sb["g_idnf"]
    for tc in range(32):
        b.on("tensor", lambda q, _tc=tc: nc.tensor.transpose(
            pC[:, 0:4], wif[:, _tc * PC:(_tc + 1) * PC], idnf[0:4, 0:4]),
            waits=[(s_ev, s_ev.count), (s_ld, s_ld.count)], inc=s_mm)
        mm_t = s_mm.count
        b.on("vector", lambda q, _tc=tc: nc.vector.tensor_copy(
            wiT[:, _tc * 4:(_tc + 1) * 4], pC[:, 0:4]),
            waits=[(s_mm, mm_t)])

    b.dma("gpsimd", *_ld3(io["WkiT"], wst[:, 0:NDC * 64], 64), s_gl,
          waits=[(s_mm, s_mm.count)])
    ld_ki = s_gl.count
    for nb_ in range(S // NB):
        n0 = nb_ * NB
        b.mm(pC[0:64, :],
             [(wst[:, kc * 64:(kc + 1) * 64], xn[:, kc * S + n0: kc * S + n0 + NB])
              for kc in range(NDC)],
             waits=[(s_gl, ld_ki), (s_ev, s_ev.count)], inc=s_mm)
        mm_k = s_mm.count
        b.on("vector", lambda q, _n0=n0: nc.vector.tensor_copy(ki[:, _n0:_n0 + NB], pC[0:64, :]),
             waits=[(s_mm, mm_k)])

    rotT.seed(s_ev)
    for ch in range(2):
        buf = ch % 2
        b.dma("gpsimd", *_ld3(io["WqiT"][:, ch * PC:(ch + 1) * PC], wst[:, buf * NDC * PC:(buf + 1) * NDC * PC], PC), s_gl,
              waits=[(s_mm, s_mm.count)])
        ld_q = s_gl.count
        for nb_ in range(S // NB):
            n0 = nb_ * NB
            dstp, rw_ = rotT.take()
            b.mm(dstp[:],
                 [(wst[:, buf * NDC * PC + kc * PC:buf * NDC * PC + (kc + 1) * PC],
                   xn[:, kc * S + n0: kc * S + n0 + NB]) for kc in range(NDC)],
                 waits=rw_ + [(s_gl, ld_q)], inc=s_mm)
            mm_q = s_mm.count
            b.on("vector", lambda q, _n0=n0, _p=dstp: nc.vector.tensor_copy(
                qc1[:, _n0:_n0 + NB], _p[:]),
                waits=[(s_mm, mm_q)])
            rotT.settle(s_ev)
        b.dma("gpsimd", io["qid"][ch * PC:(ch + 1) * PC, :], qc1[:], s_st,
              waits=[(s_ev, s_ev.count)])
        b.wait("vector", [(s_st, s_st.count)])

    for ti, (wname, dio) in enumerate([("WqT", "qd"), ("WkT", "kd")]):
        for c in range(4):
            for half, dst in [(0, qc1), (1, qc2)]:
                mc = c + 4 * half
                buf = (c * 2 + half) % 2
                b.dma("gpsimd", *_ld3(io[wname][:, mc * PC:(mc + 1) * PC],
                                      wst[:, buf * NDC * PC:(buf + 1) * NDC * PC], PC), s_gl,
                      waits=[(s_mm, s_mm.count)])
            ld_w = s_gl.count
            for half, dst in [(0, qc1), (1, qc2)]:
                buf = (c * 2 + half) % 2
                for nb_ in range(S // NB):
                    n0 = nb_ * NB
                    dstp, rw_ = rotT.take()
                    b.mm(dstp[:],
                         [(wst[:, buf * NDC * PC + kc * PC:buf * NDC * PC + (kc + 1) * PC],
                           xn[:, kc * S + n0: kc * S + n0 + NB]) for kc in range(NDC)],
                         waits=rw_ + [(s_gl, ld_w)], inc=s_mm)
                    mm_q = s_mm.count
                    b.on("vector", lambda q, _n0=n0, _p=dstp, _d=dst: nc.vector.tensor_copy(
                        _d[:, _n0:_n0 + NB], _p[:]),
                        waits=[(s_mm, mm_q)])
                    rotT.settle(s_ev)
            b.wait("vector", [(s_xr, ld_cs)])
            emit_rope(nc, b, qc1, qc2, rsc, rsc2, cosP, sinP)
            ev_rope = s_ev.count
            for h_loc in range(4):
                h = c * 4 + h_loc
                b.dma("gpsimd", io[dio][h, 0:32, :], qc1[h_loc * 32:(h_loc + 1) * 32, :], s_st,
                      waits=[(s_ev, ev_rope)])
                b.dma("gpsimd", io[dio][h, 32:64, :], qc2[h_loc * 32:(h_loc + 1) * 32, :], s_st,
                      waits=[(s_ev, ev_rope)])
            b.wait("vector", [(s_st, s_st.count)])

    vtile = qc1
    wv2 = b.sbuf("t_wv2", (PC, NDC * 1024), BF16)
    b.dma("gpsimd", *_ld3(io["WvT"], wv2[:], 1024), s_gl, waits=[(s_mm, s_mm.count)])
    ld_v = s_gl.count
    b.on("vector", lambda q: nc.vector.memset(vtile[:, 0:1040], 0.0))
    b.on("vector", lambda q: nc.vector.memset(
        vtile[:, 0:1040].rearrange("p (h d) -> p h d", d=65)[:, :, 64:65], 1.0))
    for sc in range(32):
        t0 = sc * PC
        for half in range(2):
            dstp, rw_ = rotT.take()
            b.mm(dstp[:],
                 [(xn[:, kc * S + t0: kc * S + t0 + PC],
                   wv2[:, kc * 1024 + half * NB: kc * 1024 + (half + 1) * NB]) for kc in range(NDC)],
                 waits=rw_ + [(s_gl, ld_v), (s_st, s_st.count)], inc=s_mm)
            mm_v = s_mm.count
            b.on("vector", lambda q, _h=half, _p=dstp: nc.vector.tensor_copy(
                vtile[:, 0:1040].rearrange("p (h d) -> p h d", d=65)[:, _h * 8:(_h + 1) * 8, 0:64],
                _p[:].rearrange("p (h d) -> p h d", d=64)),
                waits=[(s_mm, mm_v)])
            rotT.settle(s_ev)
        b.dma("gpsimd", io["vd"][t0:t0 + PC, :], vtile[:, 0:1040], s_st,
              waits=[(s_ev, s_ev.count)])
        b.wait("vector", [(s_st, s_st.count)])
    st_proj = s_st.count

    b.free("t_wv2", "t_wif", "t_sinP", "t_cosP", "t_rsc2", "t_rsc", "t_qc2", "t_qc1", "t_wst", "t_xn")
    idx = b.sbuf("u_idx", (PC, S), F32)
    scr32 = b.sbuf("u_scr32", (PC, S), F32)
    sgn0 = b.sbuf("u_sgn0", (PC, S), BF16)
    sgnB = b.sbuf("u_sgnB", (PC, S), BF16)
    msk = b.sbuf("u_msk", (PC, S), BF16)
    mTs = b.sbuf("u_mTs", (PC, 2 * 32 * NB), BF16)
    rl4 = b.sbuf("u_rl4", (PC, NB), BF16)
    qisb = b.sbuf("u_qisb", (64, 4 * NB), BF16)
    qh = b.sbuf("u_qh", (64, 2 * NB), BF16)
    kh = b.sbuf("u_kh", (64, 2 * S), BF16)
    vh = b.sbuf("u_vh", (PC, 2 * 32 * 65), BF16)
    un = b.sbuf("u_un", (PC, 4 * NB), BF16)
    acc = b.sbuf("u_acc", (65, NB), F32)
    hbuf = b.sbuf("u_hbuf", (64, 2 * NB), BF16)
    rhsb = b.sbuf("u_rhsb", (PC, NDC * NB), BF16)
    zrow = b.sbuf("u_zrow", (1, 2 * NB), F32)
    wo2 = b.sbuf("u_wo2", (PC, NDC * 1024), BF16)
    xres = b.sbuf("u_xres", (PC, NB), F32)
    oev = b.sbuf("u_oev", (PC, NB), F32)
    cols = {}
    for cn in ["poscnt", "nzcnt", "zcnt", "lo", "hi", "mid", "nmid", "ccol", "b01",
               "tmpa", "tmpb", "isP", "thr", "chi", "mcol", "tie", "quota", "rmin", "rmax",
               "snk"]:
        cols[cn] = b.sbuf("u_c_" + cn, (PC, 1), F32)

    b.dma("gpsimd", *_ld3(io["WoT"], wo2[:], 1024), s_gl,
          waits=[(s_mm, s_mm.count), (s_ev, s_ev.count), (s_sc, s_sc.count),
                 (s_st, s_st.count)])
    ld_wo = s_gl.count
    qh_slot = [0, 0]
    kh_slot = [0, 0]
    vh_slot = [0, 0]
    un_slot = [0, 0, 0, 0]
    z_guard = [0, 0]
    hb_st = [0, 0]
    rotT.seed(s_ev)

    def sel_gen(sp):
        t0 = sp * NB
        mT0 = (sp % 2) * 32 * NB
        for hi in range(4):
            b.dma("sync", qisb[:, hi * NB:(hi + 1) * NB],
                  io["qid"][hi * 64:(hi + 1) * 64, t0:t0 + NB], s_ld,
                  waits=([(s_st, s_st.count), (s_ev, s_ev.count), (s_mm, s_mm.count)]
                         if hi == 0 else ()))
        ld_qi = s_ld.count
        b.on("vector", lambda q, _sp=sp: nc.vector.memset(
            mTs[:, mT0 + _sp * 4 * NB: mT0 + (_sp * 4 + 4) * NB], -98304.0),
            waits=[(s_mm, s_mm.count)])
        yield
        if True:
            for tl in range(4):
                tcg = sp * 4 + tl
                sr = (tcg + 1) * PC
                nw = (sr + NB - 1) // NB
                first_mac = True
                for hi in range(4):
                    lq = qisb[:, hi * NB + tl * PC: hi * NB + (tl + 1) * PC]
                    for w in range(nw):
                        wid = min(NB, sr - w * NB)
                        dstp, rw_ = rotT.take()
                        b.on("tensor", lambda q, _l=lq, _w=w, _wd=wid, _p=dstp: nc.tensor.matmul(
                            _p[:, 0:_wd], _l, ki[:, _w * NB:_w * NB + _wd], start=True, stop=True),
                            waits=rw_ + [(s_ld, ld_qi)], inc=s_mm)
                        mm_d = s_mm.count
                        b.on("scalar", lambda q, _wd=wid, _p=dstp: nc.scalar.activation(
                            rl4[:, 0:_wd], _p[:, 0:_wd], AF.Relu),
                            waits=[(s_mm, mm_d), (s_ev, s_ev.count)])
                        sc_r = s_sc.count
                        rotT.settle(s_sc)
                        if first_mac:
                            b.on("vector", lambda q, _w=w, _wd=wid, _tc=tcg, _hi=hi: nc.vector.tensor_scalar(
                                idx[:, _w * NB:_w * NB + _wd], rl4[:, 0:_wd],
                                wiT[:, _tc * 4 + _hi:_tc * 4 + _hi + 1], None, op0=ALU.mult),
                                waits=[(s_sc, sc_r)])
                        else:
                            b.on("vector", lambda q, _w=w, _wd=wid, _tc=tcg, _hi=hi: nc.vector.scalar_tensor_tensor(
                                idx[:, _w * NB:_w * NB + _wd], rl4[:, 0:_wd],
                                wiT[:, _tc * 4 + _hi:_tc * 4 + _hi + 1],
                                idx[:, _w * NB:_w * NB + _wd], op0=ALU.mult, op1=ALU.add),
                                waits=[(s_sc, sc_r)])
                    if hi == 0:
                        first_mac = False
                    yield
                b.on("vector", lambda q, _tc=tcg: nc.vector.tensor_tensor(
                    idx[:, _tc * PC:(_tc + 1) * PC], idx[:, _tc * PC:(_tc + 1) * PC],
                    trneg[:], op=ALU.add), waits=[(s_xr, ld_tn)])
                b.on("scalar", lambda q, _sr=sr: nc.scalar.activation(
                    sgn0[:, 0:_sr], idx[:, 0:_sr], AF.Sign, accum_out=cols["poscnt"][:]),
                    waits=[(s_ev, s_ev.count)])
                b.on("scalar", lambda q, _sr=sr: nc.scalar.activation(
                    sgnB[:, 0:_sr], sgn0[:, 0:_sr], AF.Abs, accum_out=cols["nzcnt"][:]))
                b.on("scalar", lambda q: nc.scalar.copy(cols["snk"][:], cols["poscnt"][:]))
                b.on("scalar", lambda q: nc.scalar.copy(cols["snk"][:], cols["nzcnt"][:]))
                sc_cnt = s_sc.count
                yield
                b.chain("vector", [
                    lambda q: nc.vector.tensor_tensor(cols["tmpa"][:], cols["poscnt"][:], cols["nzcnt"][:], op=ALU.add),
                    lambda q: nc.vector.tensor_scalar(cols["poscnt"][:], cols["tmpa"][:], 0.5, None, op0=ALU.mult),
                    lambda q, _sr=sr: nc.vector.tensor_scalar(cols["zcnt"][:], cols["nzcnt"][:], -1.0, float(_sr), op0=ALU.mult, op1=ALU.add),
                    lambda q, _sr=sr: nc.vector.tensor_reduce(cols["rmin"][:], idx[:, 0:_sr - PC], axis=AX.X, op=ALU.min),
                    lambda q, _sr=sr: nc.vector.tensor_reduce(cols["rmax"][:], idx[:, 0:_sr], axis=AX.X, op=ALU.max),
                    lambda q: nc.vector.tensor_scalar(cols["isP"][:], cols["poscnt"][:], 512.0, None, op0=ALU.is_ge),
                    lambda q: nc.vector.tensor_scalar(cols["tmpa"][:], cols["rmin"][:], -1e-3, None, op0=ALU.add),
                    lambda q: nc.vector.tensor_tensor(cols["tmpb"][:], cols["tmpa"][:], cols["isP"][:], op=ALU.mult),
                    lambda q: nc.vector.tensor_tensor(cols["lo"][:], cols["tmpa"][:], cols["tmpb"][:], op=ALU.subtract),
                    lambda q: nc.vector.tensor_scalar(cols["tmpb"][:], cols["isP"][:], 2.0 ** -30, None, op0=ALU.mult),
                    lambda q: nc.vector.tensor_tensor(cols["lo"][:], cols["lo"][:], cols["tmpb"][:], op=ALU.add),
                    lambda q: nc.vector.tensor_scalar(cols["tmpa"][:], cols["rmax"][:], 1.0, None, op0=ALU.add),
                    lambda q: nc.vector.tensor_tensor(cols["tmpa"][:], cols["tmpa"][:], cols["isP"][:], op=ALU.mult),
                    lambda q: nc.vector.tensor_scalar(cols["tmpb"][:], cols["isP"][:], -1.0, 1.0, op0=ALU.mult, op1=ALU.add),
                    lambda q: nc.vector.tensor_scalar(cols["tmpb"][:], cols["tmpb"][:], 2.0 ** -30, None, op0=ALU.mult),
                    lambda q: nc.vector.tensor_tensor(cols["hi"][:], cols["tmpa"][:], cols["tmpb"][:], op=ALU.add),
                ], waits=[(s_sc, sc_cnt)])
                yield
                for it in range(18):
                    b.chain("vector", [
                        lambda q: nc.vector.tensor_tensor(cols["mid"][:], cols["lo"][:], cols["hi"][:], op=ALU.add),
                        lambda q: nc.vector.tensor_scalar(cols["mid"][:], cols["mid"][:], 0.5, None, op0=ALU.mult),
                        lambda q, _sr=sr: nc.vector.tensor_scalar(sgnB[:, 0:_sr], idx[:, 0:_sr], cols["mid"][:], 0.0, op0=ALU.is_ge, op1=ALU.add, accum_out=cols["ccol"][:]),
                        lambda q: nc.vector.tensor_scalar(cols["b01"][:], cols["ccol"][:], 512.0, None, op0=ALU.is_ge),
                        lambda q: nc.vector.tensor_tensor(cols["tmpa"][:], cols["mid"][:], cols["lo"][:], op=ALU.subtract),
                        lambda q: nc.vector.scalar_tensor_tensor(cols["lo"][:], cols["tmpa"][:], cols["b01"][:], cols["lo"][:], op0=ALU.mult, op1=ALU.add),
                        lambda q: nc.vector.tensor_tensor(cols["tmpa"][:], cols["mid"][:], cols["hi"][:], op=ALU.subtract),
                        lambda q: nc.vector.tensor_scalar(cols["tmpb"][:], cols["b01"][:], -1.0, 1.0, op0=ALU.mult, op1=ALU.add),
                        lambda q: nc.vector.scalar_tensor_tensor(cols["hi"][:], cols["tmpa"][:], cols["tmpb"][:], cols["hi"][:], op0=ALU.mult, op1=ALU.add),
                    ])
                    if it % 2 == 1:
                        yield
                b.chain("vector", [
                    lambda q, _sr=sr: nc.vector.tensor_scalar(sgnB[:, 0:_sr], idx[:, 0:_sr], cols["hi"][:], 0.0, op0=ALU.is_ge, op1=ALU.add, accum_out=cols["chi"][:]),
                    lambda q: nc.vector.tensor_scalar(cols["mcol"][:], cols["chi"][:], -1.0, 512.0, op0=ALU.mult, op1=ALU.add),
                    lambda q: nc.vector.tensor_scalar(cols["mcol"][:], cols["mcol"][:], 1.0, 8.0, op0=ALU.max, op1=ALU.min),
                    lambda q: nc.vector.tensor_scalar(cols["mcol"][:], cols["mcol"][:], -1.0, None, op0=ALU.add),
                    lambda q, _sr=sr: nc.vector.tensor_scalar(sgnB[:, 0:_sr], idx[:, 0:_sr], cols["hi"][:], -1e6, op0=ALU.is_ge, op1=ALU.mult),
                    lambda q, _sr=sr: nc.vector.tensor_tensor(scr32[:, 0:_sr], idx[:, 0:_sr], sgnB[:, 0:_sr], op=ALU.add),
                    lambda q, _sr=sr: nc.vector.max(rl4[:, 0:8], scr32[:, 0:_sr]),
                    lambda q: nc.vector.tensor_scalar(sgnB[:, 0:8], iota8[:], cols["mcol"][:], None, op0=ALU.is_equal),
                    lambda q: nc.vector.tensor_tensor(sgnB[:, 8:16], rl4[:, 0:8], sgnB[:, 0:8], op=ALU.mult),
                    lambda q: nc.vector.tensor_reduce(cols["thr"][:], sgnB[:, 8:16], axis=AX.X, op=ALU.add),
                    lambda q: nc.vector.tensor_scalar(cols["tie"][:], cols["poscnt"][:], 512.0, None, op0=ALU.is_lt),
                    lambda q: nc.vector.tensor_tensor(cols["tmpa"][:], cols["poscnt"][:], cols["zcnt"][:], op=ALU.add),
                    lambda q: nc.vector.tensor_scalar(cols["tmpa"][:], cols["tmpa"][:], 512.0, None, op0=ALU.is_ge),
                    lambda q: nc.vector.tensor_tensor(cols["tie"][:], cols["tie"][:], cols["tmpa"][:], op=ALU.mult),
                    lambda q: nc.vector.tensor_tensor(cols["tmpa"][:], cols["thr"][:], cols["tie"][:], op=ALU.mult),
                    lambda q: nc.vector.tensor_tensor(cols["thr"][:], cols["thr"][:], cols["tmpa"][:], op=ALU.subtract),
                    lambda q: nc.vector.tensor_scalar(cols["tmpa"][:], cols["tie"][:], 2.0 ** -100, None, op0=ALU.mult),
                    lambda q: nc.vector.tensor_tensor(cols["thr"][:], cols["thr"][:], cols["tmpa"][:], op=ALU.add),
                    lambda q: nc.vector.tensor_scalar(cols["quota"][:], cols["poscnt"][:], -1.0, 512.0, op0=ALU.mult, op1=ALU.add),
                    lambda q: nc.vector.tensor_tensor(cols["quota"][:], cols["quota"][:], cols["tie"][:], op=ALU.mult),
                    lambda q: nc.vector.tensor_scalar(cols["tmpa"][:], cols["tie"][:], -1.0, 1.0, op0=ALU.mult, op1=ALU.add),
                    lambda q: nc.vector.tensor_tensor(cols["quota"][:], cols["quota"][:], cols["tmpa"][:], op=ALU.subtract),
                    lambda q, _sr=sr: nc.vector.tensor_scalar(sgnB[:, 0:_sr], sgn0[:, 0:_sr], 0.0, None, op0=ALU.is_equal),
                    lambda q, _sr=sr: nc.vector.tensor_tensor_scan(scr32[:, 0:_sr], sgnB[:, 0:_sr], sgnB[:, 0:_sr], 0.0, op0=ALU.add, op1=ALU.bypass),
                    lambda q, _sr=sr: nc.vector.tensor_scalar(scr32[:, 0:_sr], scr32[:, 0:_sr], cols["quota"][:], None, op0=ALU.is_le),
                    lambda q, _sr=sr: nc.vector.tensor_tensor(sgnB[:, 0:_sr], sgnB[:, 0:_sr], scr32[:, 0:_sr], op=ALU.mult),
                    lambda q, _sr=sr: nc.vector.tensor_scalar(msk[:, 0:_sr], idx[:, 0:_sr], cols["thr"][:], None, op0=ALU.is_ge),
                    lambda q, _sr=sr: nc.vector.tensor_tensor(msk[:, 0:_sr], msk[:, 0:_sr], sgnB[:, 0:_sr], op=ALU.add),
                    lambda q, _sr=sr: nc.vector.tensor_scalar(msk[:, 0:_sr], msk[:, 0:_sr], 98304.0, -98304.0, op0=ALU.mult, op1=ALU.add),
                ], waits=[(s_xr, ld_tn)])
                ev_msk = s_ev.count
                yield
                for sc in range(tcg + 1):
                    dstp2, rw2 = rotT.take()
                    b.on("tensor", lambda q, _sc=sc, _p=dstp2: nc.tensor.matmul(
                        _p[0:PC, 0:PC], msk[:, _sc * PC:(_sc + 1) * PC], idn[:],
                        start=True, stop=True),
                        waits=rw2 + [(s_ev, ev_msk)], inc=s_mm)
                    mm_mt = s_mm.count
                    b.on("scalar", lambda q, _sc=sc, _tl=tl, _p=dstp2: nc.scalar.copy(
                        mTs[:, mT0 + _sc * NB + _tl * PC: mT0 + _sc * NB + (_tl + 1) * PC],
                        _p[0:PC, 0:PC]),
                        waits=[(s_mm, mm_mt)])
                    rotT.settle(s_sc)
                    if sc % 8 == 7:
                        yield
        yield



    sel_done = {}
    gen = None
    for sp in range(8):
        t0 = sp * NB
        srng = min(S, (sp + 1) * NB)
        nsc = srng // PC
        if sp == 0:
            gen = sel_gen(1)
        for h in range(H):
            hs = h % 2
            qhs = qh[:, hs * NB:(hs + 1) * NB]
            khs = kh[:, hs * S:hs * S + srng]
            vhs = vh[:, hs * 32 * 65:hs * 32 * 65 + nsc * 65]
            b.dma("sync", qhs, io["qd"][h, :, t0:t0 + NB], s_qh,
                  waits=[(s_mm, qh_slot[hs]), (s_qh, s_qh.count)]
                        + ([(s_ev, s_ev.count), (s_st, s_st.count),
                            (s_mm, s_mm.count)] if (h < 2) else []))
            ld_qh = s_qh.count
            b.dma("sync", khs, io["kd"][h, :, 0:srng], s_kh,
                  waits=[(s_mm, kh_slot[hs]), (s_kh, s_kh.count)])
            ld_kh = s_kh.count
            b.dma("sync", vhs.rearrange("p (c f) -> p c f", f=65),
                  io["vd"][0:srng, h * 65:(h + 1) * 65].rearrange("(c p) f -> p c f", p=PC),
                  s_vh, waits=[(s_mmav, vh_slot[hs]), (s_vh, s_vh.count)])
            ld_vh = s_vh.count
            for g0 in range(0, nsc, 4):
                grp = list(range(g0, min(g0 + 4, nsc)))
                ginfo = []
                for sc in grp:
                    dstp, rw_ = rotT.take()
                    if sp > 0:
                        b.on("tensor", lambda q, _sc=sc, _p=dstp, _k=khs, _q=qhs: nc.tensor.matmul(
                            _p[:], _k[:, _sc * PC:(_sc + 1) * PC], _q, start=True, stop=False),
                            waits=rw_ + [(s_kh, ld_kh), (s_qh, ld_qh)],
                            inc=s_mm)
                        sdv, sds = sel_done.get(sp, (0, 0))
                        b.on("tensor", lambda q, _sc=sc, _p=dstp, _m0=(sp % 2) * 32 * NB: nc.tensor.matmul(
                            _p[:], idn[:], mTs[:, _m0 + _sc * NB: _m0 + (_sc + 1) * NB],
                            start=False, stop=True),
                            waits=[(s_ev, sdv), (s_sc, sds)], inc=s_mm)
                    else:
                        b.on("tensor", lambda q, _sc=sc, _p=dstp, _k=khs, _q=qhs: nc.tensor.matmul(
                            _p[:], _k[:, _sc * PC:(_sc + 1) * PC], _q, start=True, stop=True),
                            waits=rw_ + [(s_kh, ld_kh), (s_qh, ld_qh)],
                            inc=s_mm)
                    ginfo.append((sc, dstp, s_mm.count))
                ubs = {}
                for sc, dstp, mm_s in ginfo:
                    usl = sc % 4
                    ub = un[:, usl * NB:(usl + 1) * NB]
                    ubs[sc] = ub
                    b.on("scalar", lambda q, _p=dstp, _u=ub: nc.scalar.activation(
                        _u, _p[:], AF.Exp, scale=0.125),
                        waits=[(s_mm, mm_s), (s_mmav, un_slot[usl])])
                    rotT.settle(s_sc)
                ng = len(ginfo)
                for i, (sc, dstp, mm_s) in enumerate(ginfo):
                    avw = [(s_sc, s_sc.count - (ng - 1 - i)), (s_vh, ld_vh)]
                    if sc == 0:
                        avw = avw + [(s_ev, s_ev.count)]
                    usl = sc % 4
                    b.on("tensor", lambda q, _sc=sc, _u=ubs[sc], _first=(sc == 0), _last=(sc == nsc - 1), _v=vhs: nc.tensor.matmul(
                        pAV[:], _v[:, _sc * 65:(_sc + 1) * 65], _u,
                        start=_first, stop=_last),
                        waits=avw, inc=s_mmav)
                    un_slot[usl] = s_mmav.count
            qh_slot[hs] = s_mm.count
            kh_slot[hs] = s_mm.count
            vh_slot[hs] = s_mmav.count
            if gen is not None:
                for _ in range(7):
                    try:
                        next(gen)
                    except StopIteration:
                        gen = None
                        break
            mm_av = s_mmav.count
            hsl = h % 2
            zs = zrow[:, hsl * NB:(hsl + 1) * NB]
            hb = hbuf[:, hsl * NB:(hsl + 1) * NB]
            b.chain("vector", [
                lambda q, _z=zs: nc.vector.tensor_copy(_z, pAV[64:65, :]),
                lambda q, _z=zs: nc.vector.reciprocal(_z, _z),
            ], waits=[(s_mmav, mm_av), (s_mm, z_guard[hsl])])
            ev_z = s_ev.count
            b.mm(pZ[:], [(b.sb["g_onesr"][:, 0:64], zs)],
                 waits=[(s_ev, ev_z)], inc=s_mm)
            mm_z = s_mm.count
            z_guard[hsl] = mm_z
            b.on("vector", lambda q: nc.vector.tensor_copy(acc[0:64, :], pAV[0:64, :]),
                 waits=[(s_mm, mm_z)])
            b.on("vector", lambda q, _h=hb: nc.vector.tensor_tensor(_h, acc[0:64, :], pZ[:], op=ALU.mult),
                 waits=[(s_st, hb_st[hsl])])
            ev_h = s_ev.count
            b.dma("gpsimd", rhsb[(h % 2) * 64:(h % 2) * 64 + 64, (h // 2) * NB:(h // 2 + 1) * NB],
                  hb, s_st, waits=[(s_ev, ev_h), (s_st, s_st.count)])
            hb_st[hsl] = s_st.count
        if gen is not None:
            for _ in gen:
                pass
            gen = None
        if sp + 1 < 8:
            sel_done[sp + 1] = (s_ev.count, s_sc.count)
            gen = sel_gen(sp + 2) if sp + 2 < 8 else None
        st_heads = s_st.count
        for mc in range(NDC):
            dstp, rw_ = rotT.take()
            b.mm(dstp[:],
                 [(wo2[:, kc * 1024 + mc * PC: kc * 1024 + (mc + 1) * PC],
                   rhsb[:, kc * NB:(kc + 1) * NB]) for kc in range(NDC)],
                 waits=rw_ + [(s_st, st_heads), (s_gl, ld_wo)], inc=s_mm)
            mm_o = s_mm.count
            b.dma("sync", xres[:], xA[mc * PC:(mc + 1) * PC, t0:t0 + NB], s_xr,
                  waits=[(s_ev, s_ev.count)])
            ld_x = s_xr.count
            b.on("vector", lambda q, _p=dstp: nc.vector.tensor_tensor(oev[:], _p[:], xres[:], op=ALU.add),
                 waits=[(s_mm, mm_o), (s_xr, ld_x)])
            rotT.settle(s_ev)
            b.dma("gpsimd", io["x3"][mc * PC:(mc + 1) * PC, t0:t0 + NB], oev[:], s_st,
                  waits=[(s_ev, s_ev.count)])
            b.wait("vector", [(s_st, s_st.count)])
    b.wait("gpsimd", [(s_st, s_st.count)])
    for cn in reversed(["poscnt", "nzcnt", "zcnt", "lo", "hi", "mid", "nmid", "ccol",
                        "b01", "tmpa", "tmpb", "isP", "thr", "chi", "mcol", "tie",
                        "quota", "rmin", "rmax", "snk"]):
        b.free("u_c_" + cn)
    b.free("u_oev", "u_xres", "u_wo2", "u_zrow", "u_rhsb", "u_hbuf", "u_acc", "u_un",
           "u_vh", "u_kh", "u_qh", "u_qisb", "u_rl4", "u_mTs", "u_msk", "u_sgnB",
           "u_sgn0", "u_scr32", "u_idx",
           "t_wiT", "t_ki", "t_iota8", "t_trneg", "p_T", "p_Z", "p_AV")


def emit_rope(nc, b, q1, q2, tmp, tmp2, cosP, sinP):
    nv = nc.vector
    ALU_ = ALU
    b.chain("vector", [
        lambda q: nv.tensor_tensor(tmp[:], q2[:], sinP[:], op=ALU_.mult),
        lambda q: nv.tensor_tensor(tmp2[:], q1[:], sinP[:], op=ALU_.mult),
        lambda q: nv.tensor_tensor(q2[:], q2[:], cosP[:], op=ALU_.mult),
        lambda q: nv.tensor_tensor(q1[:], q1[:], cosP[:], op=ALU_.mult),
        lambda q: nv.tensor_tensor(q1[:], q1[:], tmp[:], op=ALU_.subtract),
        lambda q: nv.tensor_tensor(q2[:], q2[:], tmp2[:], op=ALU_.add),
    ])

def build_l1_ffn(nc, b, io, sems, core_id=None):
    s_ld, s_mm, s_ev, s_sc, s_st = sems["ld"], sems["mm"], sems["ev"], sems["sc"], sems["st"]
    s_gl, s_xr, s_cc, s_gr = sems["gl"], sems["xr"], sems["cc"], sems["gr"]
    NU = 12
    CAP = 5120.0
    pTw = b.psum("p_T", (PC, NB))
    pT = pTw[:, 0:PC]
    pD = b.psum("p_D", (PC, NB))
    pE = b.psum("p_E", (PC, NB))
    pA, pB, pC = b.sb["p_A"], b.sb["p_B"], b.sb["p_C"]
    pR1, pR2 = b.sb["p_R1"], b.sb["p_R2"]
    pR1 = pR1[0:1, :]
    pR2 = pR2[0:1, :]
    rot4 = Rot([pA, pB, pD, pE, b.sb["p_R1"], b.sb["p_R2"]])
    rotC = Rot([pC, pTw])
    onesrb = b.sb["g_onesrb"]
    idnf = b.sb["g_idnf"]
    tric = b.sbuf("m_tric", (PC, PC), BF16)

    xn = b.sbuf("m_xn", (PC, NDC * S), BF16)
    gwT = b.sbuf("m_gwT", (PC, NDC * 9), BF16)
    rlf = b.sbuf("m_rlf", (9, S), F32)
    fltok = b.sbuf("m_fltok", (PC, 32 * 9), F32)
    rntok = b.sbuf("m_rntok", (PC, 32 * 8), F32)
    mtok = b.sbuf("m_mtok", (PC, 32 * 8), BF16)
    mcaptok = b.sbuf("m_mcap", (PC, 32 * 8), BF16)
    glwtok = b.sbuf("m_glwtok", (PC, 32 * 9), F32)
    glwfm = b.sbuf("m_glwfm", (9, S), F32)
    srt8 = b.sbuf("m_srt8", (PC, 8), F32)
    absdl = b.sbuf("m_absdl", (PC, 8), F32)
    absrt = b.sbuf("m_absrt", (PC, 8), F32)
    dsc = b.sbuf("m_dsc", (PC, 8), F32)
    fsc = b.sbuf("m_fsc", (PC, 8), F32)
    jms = b.sbuf("m_jms", (PC, 8), F32)
    als = b.sbuf("m_als", (PC, 8), F32)
    ets = b.sbuf("m_ets", (PC, 8), F32)
    rwt = b.sbuf("m_rwt", (PC, 8), F32)
    m9 = b.sbuf("m_m9", (PC, 9), F32)
    e9 = b.sbuf("m_e9", (PC, 9), F32)
    ncol = b.sbuf("m_ncol", (PC, 1), F32)
    zcol = b.sbuf("m_zcol", (PC, 1), F32)
    kcol = b.sbuf("m_kcol", (PC, 1), F32)
    wcol = b.sbuf("m_wcol", (PC, 1), F32)
    runrow = b.sbuf("m_runrow", (1, 8), F32)
    cntrow = b.sbuf("m_cntrow", (1, 8), F32)
    ccs = b.sbuf("m_ccs", (PC, PC), F32)
    ccsel = b.sbuf("m_ccsel", (1, PC), F32)
    prefc = b.sbuf("m_prefc", (8, 1), F32)
    ccld = b.sbuf("m_ccld", (8, 8), F32)

    xA = io["x3"]
    b.dma("sync", tric[:], io["c_tric"], s_ld,
          waits=[(s_ev, s_ev.count), (s_sc, s_sc.count), (s_mm, s_mm.count),
                 (s_st, s_st.count), (s_ld, s_ld.count)])
    b.dma("sync", ccsel[:], io["cc_sel"], s_ld)
    b.dma("sync", prefc[:], io["cc_pref"], s_ld)
    b.dma("gpsimd", *_ld3(io["gwT"], gwT[:], 9), s_gl,
          waits=[(s_st, s_st.count), (s_ev, s_ev.count), (s_sc, s_sc.count),
                 (s_mm, s_mm.count)])

    alloc_ln_region(b)
    emit_ln(nc, b, xA, xn, sems)
    free_ln_region(b)

    for nb_ in range(S // NB):
        n0 = nb_ * NB
        b.mm(pC[0:9, :],
             [(gwT[:, kc * 9:(kc + 1) * 9], xn[:, kc * S + n0: kc * S + n0 + NB])
              for kc in range(NDC)],
             waits=[(s_ev, s_ev.count), (s_gl, s_gl.count)], inc=s_mm)
        mm_l = s_mm.count
        b.on("vector", lambda q, _n0=n0: nc.vector.tensor_copy(rlf[:, _n0:_n0 + NB], pC[0:9, :]),
             waits=[(s_mm, mm_l)])

    for tc in range(32):
        t0 = tc * PC
        b.on("tensor", lambda q, _t0=t0: nc.tensor.transpose(
            pT[:, 0:9], rlf[:, _t0:_t0 + PC], idnf[0:9, 0:9]),
            waits=[(s_ev, s_ev.count)], inc=s_mm)
        mm_t = s_mm.count
        b.on("vector", lambda q, _tc=tc: nc.vector.tensor_copy(
            fltok[:, _tc * 9:(_tc + 1) * 9], pT[:, 0:9]),
            waits=[(s_mm, mm_t)])
        dl = fltok[:, tc * 9: tc * 9 + 8]
        b.chain("vector", [
            lambda q, _dl=dl: nc.vector.max(srt8[:], _dl),
            lambda q: nc.vector.memset(rwt[:], 0.0),
            lambda q: nc.vector.tensor_scalar(ncol[:], srt8[:, 0:1], -1.0, None, op0=ALU.mult),
        ])
        b.on("scalar", lambda q: nc.scalar.activation(ets[:], srt8[:], AF.Exp, bias=ncol[:]),
             waits=[(s_ev, s_ev.count)])
        b.chain("vector", [
            lambda q: nc.vector.tensor_reduce(zcol[:], ets[:], axis=AX.X, op=ALU.add),
            lambda q: nc.vector.reciprocal(zcol[:], zcol[:]),
            lambda q: nc.vector.tensor_scalar(ets[:], ets[:], zcol[:], None, op0=ALU.mult),
            lambda q: nc.vector.tensor_tensor_scan(ets[:], ets[:], ets[:], 0.0, op0=ALU.add, op1=ALU.bypass),
            lambda q: nc.vector.tensor_scalar(ets[:], ets[:], 0.75, None, op0=ALU.is_lt),
            lambda q: nc.vector.tensor_reduce(kcol[:], ets[:], axis=AX.X, op=ALU.add),
            lambda q: nc.vector.tensor_scalar(kcol[:], kcol[:], 1.0, 4.0, op0=ALU.add, op1=ALU.min),
        ], waits=[(s_sc, s_sc.count)])
        b.on("scalar", lambda q, _dl=dl: nc.scalar.activation(absdl[:], _dl, AF.Abs),
             waits=[(s_ev, s_ev.count)])
        b.on("scalar", lambda q: nc.scalar.activation(absrt[:], srt8[:], AF.Abs),
             waits=[(s_ev, s_ev.count)])
        sc_abs = s_sc.count
        b.wait("vector", [(s_sc, sc_abs)])
        for j in range(4):
            thr = srt8[:, j:j + 1]
            b.chain("vector", [
                lambda q, _dl=dl, _t=thr: nc.vector.tensor_scalar(dsc[:], _dl, _t, -1.0, op0=ALU.subtract, op1=ALU.mult),
                lambda q, _j=j: nc.vector.tensor_scalar(fsc[:], absdl[:], absrt[:, _j:_j + 1], 0.2, op0=ALU.max, op1=ALU.mult),
                lambda q: nc.vector.tensor_tensor(jms[:], dsc[:], fsc[:], op=ALU.is_gt),
                lambda q, _dl=dl, _t=thr: nc.vector.tensor_scalar(als[:], _dl, _t, None, op0=ALU.is_le),
                lambda q: nc.vector.tensor_tensor(als[:], als[:], jms[:], op=ALU.subtract),
                lambda q, _t=thr: nc.vector.tensor_scalar(ncol[:], _t, -1.0, None, op0=ALU.mult),
            ])
            b.on("scalar", lambda q, _dl=dl: nc.scalar.activation(e9[:, 0:8], _dl, AF.Exp, bias=ncol[:]),
                 waits=[(s_ev, s_ev.count)])
            b.chain("vector", [
                lambda q: nc.vector.tensor_tensor(e9[:, 0:8], e9[:, 0:8], als[:], op=ALU.mult),
                lambda q: nc.vector.tensor_reduce(zcol[:], e9[:, 0:8], axis=AX.X, op=ALU.add),
                lambda q: nc.vector.reciprocal(zcol[:], zcol[:]),
                lambda q, _j=j: nc.vector.tensor_scalar(wcol[:], kcol[:], float(_j), None, op0=ALU.is_gt),
                lambda q: nc.vector.tensor_tensor(wcol[:], wcol[:], zcol[:], op=ALU.mult),
                lambda q, _dl=dl, _t=thr: nc.vector.tensor_scalar(ets[:], _dl, _t, None, op0=ALU.is_equal),
                lambda q: nc.vector.scalar_tensor_tensor(rwt[:], ets[:], wcol[:], rwt[:], op0=ALU.mult, op1=ALU.add),
            ], waits=[(s_sc, s_sc.count)])
        b.chain("vector", [
            lambda q: nc.vector.tensor_reduce(zcol[:], rwt[:], axis=AX.X, op=ALU.add),
            lambda q: nc.vector.reciprocal(zcol[:], zcol[:]),
            lambda q, _tc=tc: nc.vector.tensor_scalar(rntok[:, _tc * 8:(_tc + 1) * 8], rwt[:], zcol[:], None, op0=ALU.mult),
            lambda q, _tc=tc: nc.vector.tensor_scalar(mtok[:, _tc * 8:(_tc + 1) * 8],
                                                      rntok[:, _tc * 8:(_tc + 1) * 8], 0.0, None, op0=ALU.is_gt),
        ])
    ev_routing = s_ev.count

    for tc in range(32):
        b.on("tensor", lambda q, _tc=tc: nc.tensor.matmul(
            pR1[:, 0:8], b.sb["g_onescolb"][:], mtok[:, _tc * 8:(_tc + 1) * 8],
            start=(_tc == 0), stop=(_tc == 31)),
            waits=[(s_ev, ev_routing)] if tc == 0 else (),
            inc=(s_mm if tc == 31 else None))
    mm_cnt = s_mm.count
    b.on("vector", lambda q: nc.vector.tensor_copy(cntrow[:], pR1[:, 0:8]),
         waits=[(s_mm, mm_cnt)])
    ev_cnt = s_ev.count
    b.on("vector", lambda q: nc.vector.memset(ccs[:], 0.0))
    b.mm(pT[:, 0:8], [(ccsel[:], cntrow[:])], waits=[(s_ev, s_ev.count)], inc=s_mm)
    b.on("vector", lambda q: nc.vector.tensor_copy(ccs[:, 0:8], pT[:, 0:8]),
         waits=[(s_mm, s_mm.count)])
    b.dma("gpsimd", io["cc_in"], ccs[:], s_st, waits=[(s_ev, s_ev.count)])

    def cc_fn(q):
        return q.collective_compute(
            "AllReduce", mybir.AluOpType.add,
            replica_groups=[list(range(NCORES))],
            ins=[io["cc_in"].opt()],
            outs=[io["cc_out"].opt()],
        )
    b.on("gpsimd", cc_fn, waits=[(s_st, s_st.count)], inc=s_cc, inc_n=1)
    b.dma("sync", ccld[:], io["cc_out"][0:8, 0:8], s_ld, waits=[(s_cc, s_cc.count)])
    b.mm(pR2[0:1, 0:8], [(prefc[:], ccld[:])], waits=[(s_ld, s_ld.count)], inc=s_mm)
    b.on("vector", lambda q: nc.vector.tensor_copy(runrow[:], pR2[0:1, 0:8]),
         waits=[(s_mm, s_mm.count)])

    for tc in range(32):
        mslice = mtok[:, tc * 8:(tc + 1) * 8]
        b.mm(pT[:, 0:8], [(tric[:], mslice)], waits=[(s_ev, s_ev.count)], inc=s_mm)
        mm_tri = s_mm.count
        b.mm(pC[:, 0:8], [(b.sb["g_onesr"][:], runrow[:])], inc=s_mm)
        mm_off = s_mm.count
        b.chain("vector", [
            lambda q: nc.vector.tensor_tensor(m9[:, 0:8], pT[:, 0:8], pC[:, 0:8], op=ALU.add) if False else nc.vector.tensor_copy(m9[:, 0:8], pT[:, 0:8]),
            lambda q: nc.vector.tensor_tensor(m9[:, 0:8], m9[:, 0:8], pC[:, 0:8], op=ALU.add),
            lambda q: nc.vector.tensor_scalar(m9[:, 0:8], m9[:, 0:8], CAP, None, op0=ALU.is_le),
            lambda q, _tc=tc, _m=mslice: nc.vector.tensor_tensor(
                mcaptok[:, _tc * 8:(_tc + 1) * 8], _m, m9[:, 0:8], op=ALU.mult),
        ], waits=[(s_mm, mm_off)])
        b.on("tensor", lambda q, _m=mslice: nc.tensor.matmul(
            pR1[:, 0:8], b.sb["g_onescolb"][:], _m, start=True, stop=True),
            waits=[(s_ev, s_ev.count)], inc=s_mm)
        mm_c = s_mm.count
        b.chain("vector", [
            lambda q: nc.vector.tensor_copy(cntrow[:], pR1[:, 0:8]),
            lambda q: nc.vector.tensor_tensor(runrow[:], runrow[:], cntrow[:], op=ALU.add),
        ], waits=[(s_mm, mm_c)])
    ev_cap = s_ev.count

    for tc in range(32):
        fl9 = fltok[:, tc * 9:(tc + 1) * 9]
        mc8 = mcaptok[:, tc * 8:(tc + 1) * 8]
        b.chain("vector", [
            lambda q, _m=mc8: nc.vector.tensor_copy(m9[:, 0:8], _m),
            lambda q: nc.vector.memset(m9[:, 8:9], 1.0),
            lambda q: nc.vector.tensor_scalar(e9[:], m9[:], 1.0, 1e30, op0=ALU.subtract, op1=ALU.mult),
            lambda q, _f=fl9: nc.vector.tensor_tensor(e9[:], e9[:], _f, op=ALU.add),
            lambda q: nc.vector.tensor_reduce(ncol[:], e9[:], axis=AX.X, op=ALU.max),
            lambda q: nc.vector.tensor_scalar(ncol[:], ncol[:], -1.0, None, op0=ALU.mult),
        ])
        b.on("scalar", lambda q: nc.scalar.activation(e9[:], e9[:], AF.Exp, bias=ncol[:]),
             waits=[(s_ev, s_ev.count)])
        b.chain("vector", [
            lambda q: nc.vector.tensor_reduce(zcol[:], e9[:], axis=AX.X, op=ALU.add),
            lambda q: nc.vector.reciprocal(zcol[:], zcol[:]),
            lambda q: nc.vector.tensor_scalar(e9[:], e9[:], zcol[:], None, op0=ALU.mult),
            lambda q: nc.vector.tensor_reduce(ncol[:], e9[:, 0:8], axis=AX.X, op=ALU.add),
            lambda q, _tc=tc: nc.vector.tensor_scalar(
                glwtok[:, _tc * 9:(_tc + 1) * 9 - 1], rntok[:, _tc * 8:(_tc + 1) * 8],
                ncol[:], None, op0=ALU.mult),
            lambda q, _tc=tc, _m=mc8: nc.vector.tensor_tensor(
                glwtok[:, _tc * 9:(_tc + 1) * 9 - 1], glwtok[:, _tc * 9:(_tc + 1) * 9 - 1],
                _m, op=ALU.mult),
            lambda q, _tc=tc: nc.vector.tensor_copy(glwtok[:, _tc * 9 + 8:(_tc + 1) * 9], e9[:, 8:9]),
        ], waits=[(s_sc, s_sc.count)])
        b.on("tensor", lambda q, _tc=tc: nc.tensor.transpose(
            pT[0:9, :], glwtok[:, _tc * 9:(_tc + 1) * 9], idnf[:]),
            waits=[(s_ev, s_ev.count)], inc=s_mm)
        mm_g = s_mm.count
        b.on("vector", lambda q, _tc=tc: nc.vector.tensor_copy(
            glwfm[:, _tc * PC:(_tc + 1) * PC], pT[0:9, :]),
            waits=[(s_mm, mm_g)])
    ev_glw = s_ev.count
    b.dma("gpsimd", io["glwd"], glwfm[:], s_st, waits=[(s_ev, ev_glw)])
    st_glw = s_st.count

    wg = b.sbuf("m_wg", (PC, 2 * NDC * NB), BF16)
    wu = b.sbuf("m_wu", (PC, 2 * NDC * NB), BF16)
    sact = b.sbuf("m_sact", (PC, 2 * NB), F32)
    aout = b.sbuf("m_aout", (PC, 2 * NB), BF16)
    glwrow = b.sbuf("m_glwrow", (1, 2 * NB), BF16)
    rot4.seed(s_ev)
    rotC.seed(s_ev)
    slot_ev = [0, 0]
    slot_sc = [0, 0]
    git_ = 0
    for u in range(NU):
        buf = u % 2
        wgs = wg[:, buf * NDC * NB:(buf + 1) * NDC * NB]
        wus = wu[:, buf * NDC * NB:(buf + 1) * NDC * NB]
        e_u = u if u < 8 else 8
        if u < 8:
            gsrc = io["dgT"][u * D:(u + 1) * D, :]
            usrc = io["duT"][u * D:(u + 1) * D, :]
        else:
            sh0 = (u - 8) * NB
            gsrc = io["shgT"][:, sh0:sh0 + NB]
            usrc = io["shuT"][:, sh0:sh0 + NB]
        b.dma("gpsimd", *_ld3(gsrc, wgs, NB), s_gl,
              waits=[(s_mm, s_mm.count)] + ([(s_ev, s_ev.count), (s_sc, s_sc.count)]
                                            if u == 0 else []))
        b.dma("gpsimd", *_ld3(usrc, wus, NB), s_gl)
        ld_w = s_gl.count
        for nb_ in range(S // NB):
            n0 = nb_ * NB
            gsl = (u * (S // NB) + nb_) % 2
            b.dma("gpsimd", glwrow[:, gsl * NB:(gsl + 1) * NB], io["glwd"][e_u:e_u + 1, n0:n0 + NB], s_gr,
                  waits=[(s_st, st_glw), (s_mm, s_mm.count)])
            ld_gr = s_gr.count
            pCb, rwC = rotC.take()
            b.mm(pCb[:], [(onesrb[:], glwrow[:, gsl * NB:(gsl + 1) * NB])],
                 waits=rwC + [(s_gr, ld_gr)], inc=s_mm)
            for mc4 in range(4):
                asl = git_ % 2
                git_ += 1
                pG, rwG = rot4.take()
                b.mm(pG[:],
                     [(wgs[:, kc * NB + mc4 * PC: kc * NB + (mc4 + 1) * PC],
                       xn[:, kc * S + n0: kc * S + n0 + NB]) for kc in range(NDC)],
                     waits=rwG + [(s_gl, ld_w)], inc=s_mm)
                pU, rwU = rot4.take()
                b.mm(pU[:],
                     [(wus[:, kc * NB + mc4 * PC: kc * NB + (mc4 + 1) * PC],
                       xn[:, kc * S + n0: kc * S + n0 + NB]) for kc in range(NDC)],
                     waits=rwU, inc=s_mm)
                mm_gu = s_mm.count
                sa = sact[:, asl * NB:(asl + 1) * NB]
                ao = aout[:, asl * NB:(asl + 1) * NB]
                b.on("scalar", lambda q, _p=pG, _s=sa: nc.scalar.activation(_s, _p[:], AF.Silu),
                     waits=[(s_mm, mm_gu), (s_ev, slot_sc[asl])])
                sc_si = s_sc.count
                rot4.settle(s_sc)
                b.chain("vector", [
                    lambda q, _s=sa, _p=pU: nc.vector.tensor_tensor(_s, _s, _p[:], op=ALU.mult),
                    lambda q, _s=sa, _a=ao, _c=pCb: nc.vector.tensor_tensor(_a, _s, _c[:], op=ALU.mult),
                ], waits=[(s_sc, sc_si), (s_st, slot_ev[asl])])
                rot4.settle(s_ev)
                slot_sc[asl] = s_ev.count
                ev_a = s_ev.count
                b.dma("gpsimd", io["actscr"][u * NB + mc4 * PC: u * NB + (mc4 + 1) * PC, n0:n0 + NB],
                      ao, s_st, waits=[(s_ev, ev_a), (s_st, s_st.count)])
                slot_ev[asl] = s_st.count
            rotC.settle(s_ev)
    b.free("m_glwrow", "m_aout", "m_sact", "m_wu", "m_wg",
           "m_ccld", "m_prefc", "m_ccsel", "m_ccs", "m_cntrow", "m_runrow",
           "m_wcol", "m_kcol", "m_zcol", "m_ncol", "m_e9", "m_m9", "m_rwt",
           "m_ets", "m_als", "m_jms", "m_fsc", "m_dsc", "m_absrt", "m_absdl",
           "m_srt8", "m_glwfm", "m_glwtok", "m_mcap", "m_mtok", "m_rntok",
           "m_fltok", "m_rlf", "m_gwT", "m_xn")
    dall = b.sbuf("m_dall", (PC, 48 * 1024), BF16)
    actn = b.sbuf("m_actn", (PC, 48 * NB), BF16)
    xres = b.sbuf("m_xres", (PC, 2 * NB), F32)
    oev = b.sbuf("m_oev", (PC, 2 * NB), F32)
    oev_slot = [0, 0]
    b.dma("gpsimd", *_ld3(io["dallT"], dall[:], 1024), s_gl,
          waits=[(s_mm, s_mm.count), (s_ev, s_ev.count), (s_sc, s_sc.count),
                 (s_st, s_st.count)])
    ld_dall = s_gl.count
    for nb_ in range(S // NB):
        n0 = nb_ * NB
        for u in range(NU):
            b.dma("sync", actn[:, u * 4 * NB:(u + 1) * 4 * NB].rearrange("p (c f) -> p c f", f=NB),
                  io["actscr"][u * NB:(u + 1) * NB, n0:n0 + NB].rearrange("(c p) f -> p c f", p=PC),
                  s_ld, waits=([(s_mm, s_mm.count), (s_ev, s_ev.count),
                                (s_sc, s_sc.count), (s_st, s_st.count)] if u == 0 else ()))
        ld_act = s_ld.count
        for mc in range(NDC):
            dstp, rw_ = rot4.take()
            osl = mc % 2
            ov = oev[:, osl * NB:(osl + 1) * NB]
            b.mm(dstp[:],
                 [(dall[:, kc48 * 1024 + mc * PC: kc48 * 1024 + (mc + 1) * PC],
                   actn[:, kc48 * NB:(kc48 + 1) * NB]) for kc48 in range(48)],
                 waits=rw_ + [(s_ld, ld_act), (s_gl, ld_dall)], inc=s_mm)
            mm_f = s_mm.count
            xsl = mc % 2
            xv = xres[:, xsl * NB:(xsl + 1) * NB]
            b.dma("sync", xv, xA[mc * PC:(mc + 1) * PC, n0:n0 + NB], s_xr,
                  waits=[(s_ev, s_ev.count)])
            ld_x = s_xr.count
            b.chain("vector", [
                lambda q, _p=dstp, _o=ov, _x=xv: nc.vector.tensor_tensor(_o, _p[:], _x, op=ALU.add),
            ], waits=[(s_mm, mm_f), (s_xr, ld_x), (s_st, oev_slot[osl])])
            rot4.settle(s_ev)
            b.dma("gpsimd", io["x4"][mc * PC:(mc + 1) * PC, n0:n0 + NB], ov, s_st,
                  waits=[(s_ev, s_ev.count), (s_st, s_st.count)])
            oev_slot[osl] = s_st.count
    b.wait("gpsimd", [(s_st, s_st.count)])
    b.free("m_oev", "m_xres", "m_actn", "m_dall", "m_tric", "p_E", "p_D", "p_T")


_CACHE = {}


def _declare_consts(nc, io, consts):
    for nm, arr in consts.items():
        dt = BF16 if arr.dtype == BF else F32
        io[nm] = nc.dram_tensor(nm, list(arr.shape), dt, kind="ExternalInput").ap()


def _build_l0a_nc(consts):
    nc = bass.Bass("TRN2", target_bir_lowering=False, name="l0attn")
    io = {}
    io["x"] = nc.dram_tensor("x", [D, S], F32, kind="ExternalInput").ap()
    io["x1"] = nc.dram_tensor("x1", [D, S], F32, kind="ExternalOutput").ap()
    io["qscr"] = nc.dram_tensor("qscr", [D, S], BF16, kind="Internal").ap()
    for nm in ["Wk", "Wv", "Wq"]:
        io[nm] = nc.dram_tensor(nm, [D, D], F32, kind="ExternalInput").ap()
    for nm in ["bk", "bv", "bq"]:
        io[nm] = nc.dram_tensor(nm, [PC, NDC], F32, kind="ExternalInput").ap()
    _declare_consts(nc, io, consts)
    b = Builder(nc)
    sems = make_sems(b)
    emit_prelude(nc, b, io, sems)
    build_l0_attn(nc, b, io, sems)
    b.finish()
    return nc


def _build_l0f_nc(consts):
    nc = bass.Bass("TRN2", target_bir_lowering=False, name="l0ffn")
    io = {}
    io["x1"] = nc.dram_tensor("x1", [D, S], F32, kind="ExternalInput").ap()
    io["x2"] = nc.dram_tensor("x2", [D, S], F32, kind="ExternalOutput").ap()
    shapes = dict(slotT=[D, 64], w1x=[D, NB], w1s=[D, NB], w2=[NB, 64],
                  WihT=[D, 3072], WhhT=[D, 3072],
                  wq=[D, 256], wk=[D, 256], wv=[D, D], wvp=[D, D], wo=[D, D],
                  b1c=[PC, 4], b2c=[64, 1], tau=[1, 1],
                  bihc=[PC, 24], bhhc=[PC, 24], bqc=[PC, 2], bkc=[PC, 2],
                  bvrow=[1, D], bvprow=[1, D], lngc=[PC, NDC], lnbc=[PC, NDC],
                  boc=[PC, NDC])
    for nm, shp in shapes.items():
        io[nm] = nc.dram_tensor(nm, shp, F32, kind="ExternalInput").ap()
    _declare_consts(nc, io, consts)
    b = Builder(nc)
    sems = make_sems(b)
    emit_prelude(nc, b, io, sems)
    build_l0_ffn(nc, b, io, sems)
    b.finish()
    return nc


def _qk_perm():
    perm = np.zeros(1024, np.int64)
    for parity in range(2):
        for c in range(4):
            for h_loc in range(4):
                for i in range(32):
                    new = parity * 512 + c * 128 + h_loc * 32 + i
                    old = (c * 4 + h_loc) * 64 + 2 * i + parity
                    perm[new] = old
    return perm


def _build_l1a_nc(consts):
    nc = bass.Bass("TRN2", target_bir_lowering=False, name="l1attn")
    io = {}
    io["x2"] = nc.dram_tensor("x2", [D, S], F32, kind="ExternalInput").ap()
    io["x3"] = nc.dram_tensor("x3", [D, S], F32, kind="ExternalOutput").ap()
    io["qd"] = nc.dram_tensor("qd", [16, 64, S], BF16, kind="Internal").ap()
    io["kd"] = nc.dram_tensor("kd", [16, 64, S], BF16, kind="Internal").ap()
    io["vd"] = nc.dram_tensor("vd", [S, 16 * 65], BF16, kind="Internal").ap()
    io["qid"] = nc.dram_tensor("qid", [256, S], BF16, kind="Internal").ap()
    shapes = dict(WqT=[D, D], WkT=[D, D], WvT=[D, D], WqiT=[D, 256],
                  WkiT=[D, 64], WwiT=[D, 4], WoT=[D, D])
    for nm, shp in shapes.items():
        io[nm] = nc.dram_tensor(nm, shp, F32, kind="ExternalInput").ap()
    _declare_consts(nc, io, consts)
    b = Builder(nc)
    sems = make_sems(b)
    emit_prelude(nc, b, io, sems)
    build_l1_attn(nc, b, io, sems)
    b.finish()
    return nc


def _l1a_inputs(inputs):
    wqkv = np.asarray(inputs["L1_attn_w_qkv"], np.float32)
    perm = _qk_perm()
    return {
        "WqT": np.ascontiguousarray(wqkv[0:1024][perm].T),
        "WkT": np.ascontiguousarray(wqkv[1024:2048][perm].T),
        "WvT": np.ascontiguousarray(wqkv[2048:3072].T),
        "WqiT": np.ascontiguousarray(np.asarray(inputs["L1_attn_w_qi"], np.float32).T),
        "WkiT": np.ascontiguousarray(np.asarray(inputs["L1_attn_w_ki"], np.float32).T),
        "WwiT": np.ascontiguousarray(np.asarray(inputs["L1_attn_w_wi"], np.float32).T),
        "WoT": np.ascontiguousarray(np.asarray(inputs["L1_attn_w_o"], np.float32).T),
    }


def _build_l1f_nc(consts, core_id):
    nc = bass.Bass("TRN2", target_bir_lowering=False, name="l1ffn")
    io = {}
    io["x3"] = nc.dram_tensor("x3", [D, S], F32, kind="ExternalInput").ap()
    io["x4"] = nc.dram_tensor("x4", [D, S], F32, kind="ExternalOutput").ap()
    io["cc_in"] = nc.dram_tensor("cc_in", [PC, PC], F32).ap()
    io["cc_out"] = nc.dram_tensor("cc_out", [PC, PC], F32).ap()
    io["actscr"] = nc.dram_tensor("actscr", [12 * NB, S], BF16, kind="Internal").ap()
    io["glwd"] = nc.dram_tensor("glwd", [9, S], F32, kind="Internal").ap()
    shapes = dict(gwT=[D, 9], dgT=[8 * D, NB], duT=[8 * D, NB],
                  shgT=[D, 2048], shuT=[D, 2048], dallT=[6144, D],
                  cc_sel=[1, PC], cc_pref=[8, 1])
    for nm, shp in shapes.items():
        io[nm] = nc.dram_tensor(nm, shp, F32, kind="ExternalInput").ap()
    _declare_consts(nc, io, consts)
    b = Builder(nc)
    sems = make_sems(b)
    emit_prelude(nc, b, io, sems)
    build_l1_ffn(nc, b, io, sems, core_id)
    b.finish()
    return nc


def _l1f_inputs(inputs, core_id):
    t = lambda a: np.ascontiguousarray(np.asarray(a, np.float32).T)
    dg = np.asarray(inputs["L1_ffn_dyn_gate_w"], np.float32)
    du = np.asarray(inputs["L1_ffn_dyn_up_w"], np.float32)
    dd = np.asarray(inputs["L1_ffn_dyn_down_w"], np.float32)
    sel = np.zeros((1, PC), np.float32)
    sel[0, core_id] = 1.0
    pref = np.zeros((8, 1), np.float32)
    pref[:core_id, 0] = 1.0
    return {
        "gwT": t(inputs["L1_ffn_gate_w"]),
        "dgT": np.ascontiguousarray(np.concatenate([dg[e].T for e in range(8)], axis=0)),
        "duT": np.ascontiguousarray(np.concatenate([du[e].T for e in range(8)], axis=0)),
        "shgT": t(inputs["L1_ffn_sh_gate_w"][0]),
        "shuT": t(inputs["L1_ffn_sh_up_w"][0]),
        "dallT": np.ascontiguousarray(np.concatenate(
            [dd[e].T for e in range(8)] + [np.asarray(inputs["L1_ffn_sh_down_w"], np.float32)[0].T], axis=0)),
        "cc_sel": sel,
        "cc_pref": pref,
    }


def _col(v, nchunk):
    return np.ascontiguousarray(np.asarray(v, np.float32).reshape(nchunk, PC).T)


def _l0f_inputs(inputs):
    t = lambda a: np.ascontiguousarray(np.asarray(a, np.float32).T)
    w1 = np.asarray(inputs["L0_ffn_w1"], np.float32)
    return {
        "slotT": t(inputs["L0_ffn_slot_init"]),
        "w1x": np.ascontiguousarray(w1[:D]),
        "w1s": np.ascontiguousarray(w1[D:]),
        "w2": np.asarray(inputs["L0_ffn_w2"], np.float32),
        "WihT": t(inputs["L0_ffn_Wih"]),
        "WhhT": t(inputs["L0_ffn_Whh"]),
        "wq": np.asarray(inputs["L0_ffn_wq"], np.float32),
        "wk": np.asarray(inputs["L0_ffn_wk"], np.float32),
        "wv": np.asarray(inputs["L0_ffn_wv"], np.float32),
        "wvp": np.asarray(inputs["L0_ffn_wvp"], np.float32),
        "wo": np.asarray(inputs["L0_ffn_wo"], np.float32),
        "b1c": _col(inputs["L0_ffn_b1"], 4),
        "b2c": np.asarray(inputs["L0_ffn_b2"], np.float32).reshape(64, 1),
        "tau": np.asarray(inputs["L0_ffn_tau"], np.float32).reshape(1, 1),
        "bihc": _col(inputs["L0_ffn_bih"], 24),
        "bhhc": _col(inputs["L0_ffn_bhh"], 24),
        "bqc": _col(inputs["L0_ffn_bq"], 2),
        "bkc": _col(inputs["L0_ffn_bk"], 2),
        "bvrow": np.asarray(inputs["L0_ffn_bv"], np.float32).reshape(1, D),
        "bvprow": np.asarray(inputs["L0_ffn_bvp"], np.float32).reshape(1, D),
        "lngc": _col(inputs["L0_ffn_ln_g"], NDC),
        "lnbc": _col(inputs["L0_ffn_ln_b"], NDC),
        "boc": _col(inputs["L0_ffn_bo"], NDC),
    }


def _build_fused_nc(consts):
    nc = bass.Bass("TRN2", target_bir_lowering=False, name="mk_fused")
    io = {}
    io["x"] = nc.dram_tensor("x", [D, S], F32, kind="ExternalInput").ap()
    io["x4"] = nc.dram_tensor("x4", [D, S], F32, kind="ExternalOutput").ap()
    for nm in ["x1", "x2", "x3"]:
        io[nm] = nc.dram_tensor(nm, [D, S], F32, kind="Internal").ap()
    io["qscr"] = nc.dram_tensor("qscr", [D, S], BF16, kind="Internal").ap()
    io["qd"] = nc.dram_tensor("qd", [16, 64, S], BF16, kind="Internal").ap()
    io["kd"] = nc.dram_tensor("kd", [16, 64, S], BF16, kind="Internal").ap()
    io["vd"] = nc.dram_tensor("vd", [S, 16 * 65], BF16, kind="Internal").ap()
    io["qid"] = nc.dram_tensor("qid", [256, S], BF16, kind="Internal").ap()
    io["actscr"] = nc.dram_tensor("actscr", [12 * NB, S], BF16, kind="Internal").ap()
    io["glwd"] = nc.dram_tensor("glwd", [9, S], F32, kind="Internal").ap()
    io["cc_in"] = nc.dram_tensor("cc_in", [PC, PC], F32).ap()
    io["cc_out"] = nc.dram_tensor("cc_out", [PC, PC], F32).ap()
    for nm in ["Wk", "Wv", "Wq"]:
        io[nm] = nc.dram_tensor(nm, [D, D], F32, kind="ExternalInput").ap()
    for nm in ["bk", "bv", "bq"]:
        io[nm] = nc.dram_tensor(nm, [PC, NDC], F32, kind="ExternalInput").ap()
    shapes = dict(slotT=[D, 64], w1x=[D, NB], w1s=[D, NB], w2=[NB, 64],
                  WihT=[D, 3072], WhhT=[D, 3072],
                  wq=[D, 256], wk=[D, 256], wv=[D, D], wvp=[D, D], wo=[D, D],
                  b1c=[PC, 4], b2c=[64, 1], tau=[1, 1],
                  bihc=[PC, 24], bhhc=[PC, 24], bqc=[PC, 2], bkc=[PC, 2],
                  bvrow=[1, D], bvprow=[1, D], lngc=[PC, NDC], lnbc=[PC, NDC],
                  boc=[PC, NDC],
                  WqT=[D, D], WkT=[D, D], WvT=[D, D], WqiT=[D, 256],
                  WkiT=[D, 64], WwiT=[D, 4], WoT=[D, D],
                  gwT=[D, 9], dgT=[8 * D, NB], duT=[8 * D, NB],
                  shgT=[D, 2048], shuT=[D, 2048], dallT=[6144, D],
                  cc_sel=[1, PC], cc_pref=[8, 1])
    for nm, shp in shapes.items():
        io[nm] = nc.dram_tensor(nm, shp, F32, kind="ExternalInput").ap()
    _declare_consts(nc, io, consts)
    b = Builder(nc)
    sems = make_sems(b)
    emit_prelude(nc, b, io, sems)
    build_l0_attn(nc, b, dict(io, x=io["x"], x1=io["x1"]), sems)
    build_l0_ffn(nc, b, io, sems)
    build_l1_attn(nc, b, io, sems)
    build_l1_ffn(nc, b, io, sems)
    b.finish()
    return nc


def kernel(**inputs):
    B = 8
    consts = make_consts()
    x = np.asarray(inputs["L0_attn_x"])
    x_fm = [np.ascontiguousarray(x[c].T) for c in range(B)]
    nc = _build_fused_nc(consts)
    base = {
        "Wk": np.asarray(inputs["L0_attn_Wk"]),
        "Wv": np.asarray(inputs["L0_attn_Wv"]),
        "Wq": np.asarray(inputs["L0_attn_Wq"]),
        "bk": np.ascontiguousarray(np.asarray(inputs["L0_attn_bk"]).reshape(NDC, PC).T),
        "bv": np.ascontiguousarray(np.asarray(inputs["L0_attn_bv"]).reshape(NDC, PC).T),
        "bq": np.ascontiguousarray(np.asarray(inputs["L0_attn_bq"]).reshape(NDC, PC).T),
    }
    base.update(_l0f_inputs(inputs))
    base.update(_l1a_inputs(inputs))
    base.update(consts)
    percore = [_l1f_inputs(inputs, c) for c in range(B)]
    in_maps = []
    for c in range(B):
        m = dict(base)
        m.update(percore[c])
        m["x"] = x_fm[c]
        in_maps.append(m)
    res = run_bass_kernel_spmd(nc, in_maps, core_ids=list(range(B)))
    x4 = np.stack([np.asarray(r["x4"]).T for r in res.results])
    return x4.astype(np.float32)


# version 16
# speedup vs baseline: 1.0434x; 1.0133x over previous
import numpy as np
import ml_dtypes
import concourse.bass as bass
import concourse.mybir as mybir
from concourse.alu_op_type import AluOpType as ALU
from concourse.bass_utils import run_bass_kernel_spmd

F32 = mybir.dt.float32
BF16 = mybir.dt.bfloat16
U32 = mybir.dt.uint32
AF = mybir.ActivationFunctionType
AX = mybir.AxisListType
BF = ml_dtypes.bfloat16

NCORES = 8
S = 4096
D = 1024
PC = 128
NDC = D // PC
NB = 512
HS = S // 2
ENGINES = ("sync", "tensor", "vector", "scalar", "gpsimd")


class _Sem:
    __slots__ = ("h", "count", "name")


class Rot:

    def __init__(self, psums):
        self.slots = [[p, None, 0] for p in psums]
        self.i = -1
        self.pending = []

    def take(self):
        self.i = (self.i + 1) % len(self.slots)
        p, sem, cnt = self.slots[self.i]
        w = [(sem, cnt)] if sem is not None and cnt > 0 else []
        self.pending.append(self.i)
        return p, w

    def settle(self, sem):
        i = self.pending.pop(0)
        self.slots[i][1] = sem
        self.slots[i][2] = sem.count

    def seed(self, sem):
        assert not self.pending
        for s in self.slots:
            s[1] = sem
            s[2] = sem.count


class Builder:
    def __init__(self, nc):
        self.nc = nc
        self.steps = {e: [] for e in ENGINES}
        self._ctxs = []
        self.sb = {}
        self.esem = {}

    def set_engine_sem(self, eng, sem):
        self.esem[eng] = sem

    def sem(self, name):
        ctx = self.nc.semaphore(name)
        h = ctx.__enter__()
        self._ctxs.append((name, ctx))
        s = _Sem()
        s.h, s.count, s.name = h, 0, name
        return s

    _uniq = {}

    def _uname(self, name):
        n = Builder._uniq.get(name, 0)
        Builder._uniq[name] = n + 1
        return name if n == 0 else f"{name}__{n}"

    def sbuf(self, name, shape, dtype=F32):
        ctx = self.nc.sbuf_tensor(self._uname(name), list(shape), dtype)
        h = ctx.__enter__()
        self._ctxs.append((name, ctx))
        self.sb[name] = h
        return h

    def psum(self, name, shape, dtype=F32):
        ctx = self.nc.psum_tensor(self._uname(name), list(shape), dtype)
        h = ctx.__enter__()
        self._ctxs.append((name, ctx))
        self.sb[name] = h
        return h

    def free(self, *names):
        for name in names:
            nm, ctx = self._ctxs.pop()
            assert nm == name, f"free order mismatch: {nm} != {name}"
            ctx.__exit__(None, None, None)
            self.sb.pop(nm, None)

    def on(self, eng, fn, waits=(), inc=None, inc_n=1):
        es = self.esem.get(eng)
        wl = []
        if es is not None and es.count > 0:
            wl.append((es.h, es.count))
        wl += [(s.h, int(v)) for (s, v) in waits if v and v > 0]
        if inc is None and es is not None:
            inc = es
        tgt = None
        if inc is not None:
            inc.count += inc_n
            tgt = (inc.h, inc_n)

        def step(q, _wl=wl, _fn=fn, _tgt=tgt):
            for h, v in _wl:
                q.wait_ge(h, v)
            r = _fn(q)
            if _tgt is not None:
                insts = r if isinstance(r, (list, tuple)) else [r]
                insts[-1].then_inc(*_tgt)
        self.steps[eng].append(step)
        return inc.count if inc is not None else None

    def chain(self, eng, fns, waits=(), inc=None):
        n = len(fns)
        for i, fn in enumerate(fns):
            self.on(eng, fn,
                    waits=(waits if i == 0 else ()),
                    inc=(inc if (i == n - 1 and inc is not None) else None))

    def dma(self, qeng, out, in_, sem, waits=()):
        return self.on(qeng, lambda q, _o=out, _i=in_: q.dma_start(out=_o, in_=_i),
                       waits=waits, inc=sem, inc_n=16)

    def wait(self, eng, waits):
        wl = [(s.h, int(v)) for (s, v) in waits if v and v > 0]
        if not wl:
            return

        def step(q, _wl=wl):
            for h, v in _wl:
                q.wait_ge(h, v)
        self.steps[eng].append(step)

    def mm(self, dst, pairs, waits=(), inc=None, tile_position=None):
        def fn(q, _dst=dst, _pairs=tuple(pairs), _tp=tile_position):
            insts = []
            n = len(_pairs)
            for i, (l, r) in enumerate(_pairs):
                insts.append(self.nc.tensor.matmul(
                    _dst, l, r, start=(i == 0), stop=(i == n - 1),
                    tile_position=_tp))
            return insts
        return self.on("tensor", fn, waits=waits, inc=inc)

    def finish(self):
        nc = self.nc
        with nc.Block() as block:
            for e in ENGINES:
                if not self.steps[e]:
                    continue

                def body(q, _steps=tuple(self.steps[e])):
                    seen = {}

                    class QW:
                        def __getattr__(self, a):
                            return getattr(q, a)

                        def wait_ge(self, h, v, _seen=seen):
                            k = id(h)
                            if _seen.get(k, -1) >= v:
                                return
                            _seen[k] = v
                            q.wait_ge(h, v)
                    qw = QW()
                    for st in _steps:
                        st(qw)
                getattr(block, e)(body)
        for nm, c in reversed(self._ctxs):
            c.__exit__(None, None, None)
        self._ctxs = []




def _tbf(nc, out, in_, idn):
    with nc.allow_low_precision(reason="bf16 PE transpose (no accumulation)"):
        return nc.tensor.transpose(out, in_, idn)

def _ld3(dram2d, sb2d, cols):
    return (sb2d.rearrange("p (c f) -> p c f", f=cols),
            dram2d.rearrange("(c p) f -> p c f", p=PC))

def make_consts():
    c = {}
    N = 1024
    n = np.arange(N)[:, None]
    f_re = np.arange(513)[None, :]
    f_im = np.arange(1, 512)[None, :]
    PK = np.concatenate([np.cos(2 * np.pi * n * f_re / N),
                         -np.sin(2 * np.pi * n * f_im / N)], axis=1)
    c["c_PK"] = PK.astype(BF)
    nn = np.arange(N)[None, :]
    ff_re = np.arange(513)[:, None]
    ff_im = np.arange(1, 512)[:, None]
    coef = np.concatenate([np.ones(1), 2 * np.ones(511), np.ones(1)])[:, None]
    IW_top = coef * np.cos(2 * np.pi * ff_re * nn / N) / N
    IW_bot = -2 * np.sin(2 * np.pi * ff_im * nn / N) / N
    c["c_IW"] = np.concatenate([IW_top, IW_bot], axis=0).astype(BF)
    W = np.full((PC, NDC), 2.0, np.float32)
    W[0, 0] = 1.0
    W[0, 4] = 1.0
    c["c_nw"] = W.astype(BF)
    c["c_ones_row"] = np.ones((1, PC), np.float32)
    c["c_identity"] = np.eye(PC, dtype=np.float32).astype(BF)
    c["c_identity_f32"] = np.eye(PC, dtype=np.float32)
    c["c_tric"] = np.tril(np.ones((PC, PC), np.float32)).T.astype(BF)
    c["c_triu_neg"] = (np.triu(np.ones((PC, PC), np.float32), 1) * -1e4).astype(BF)
    c["c_iota8"] = np.broadcast_to(np.arange(8, dtype=np.float32), (PC, 8)).copy()
    inv = 1.0 / 10000.0 ** (np.arange(0, 64, 2, dtype=np.float64) / 64.0)
    ang = np.arange(S, dtype=np.float64)[None, :] * inv[:, None]
    cosr = np.cos(ang)
    sinr = np.sin(ang)
    c["c_cosP"] = np.tile(cosr, (4, 1)).astype(BF)
    c["c_sinP"] = np.tile(sinr, (4, 1)).astype(BF)
    return c



def make_sems(b):
    keys = ["ld", "mm", "ev", "sc", "st", "gl", "xr", "qh", "kh", "vh", "cc", "gr", "av"]
    sems = {k: b.sem("s_" + k) for k in keys}
    b.set_engine_sem("vector", sems["ev"])
    b.set_engine_sem("scalar", sems["sc"])
    return sems


def emit_prelude(nc, b, io, sems):
    b.psum("p_A", (PC, NB))
    b.psum("p_B", (PC, NB))
    b.psum("p_C", (PC, NB))
    b.psum("p_R1", (PC, NB))
    b.psum("p_R2", (PC, NB))
    b.sbuf("g_onescol", (PC, 1), F32)
    b.sbuf("g_onescolb", (PC, 1), BF16)
    b.sbuf("g_onesr", (1, PC), F32)
    b.sbuf("g_onesrb", (1, PC), BF16)
    b.sbuf("g_idn", (PC, PC), BF16)
    b.sbuf("g_idnf", (PC, PC), F32)
    b.on("vector", lambda q: [nc.vector.memset(b.sb["g_onescol"][:], 1.0),
                              nc.vector.memset(b.sb["g_onescolb"][:], 1.0),
                              nc.vector.memset(b.sb["g_onesrb"][:], 1.0)], inc=sems["ev"])
    b.dma("sync", b.sb["g_onesr"][:], io["c_ones_row"], sems["ld"])
    b.dma("sync", b.sb["g_idn"][:], io["c_identity"], sems["ld"])
    b.dma("sync", b.sb["g_idnf"][:], io["c_identity_f32"], sems["ld"])
    b.wait("tensor", [(sems["ld"], sems["ld"].count)])


def alloc_ln_region(b):
    b.sbuf("ln_xld", (PC, NDC * NB), F32)
    b.sbuf("ln_sq", (PC, NDC * NB), BF16)
    b.sbuf("ln_tmp", (PC, NB), F32)
    b.sbuf("ln_rowm", (1, NB), F32)
    b.sbuf("ln_rowv", (1, NB), F32)
    b.sbuf("ln_rown", (1, NB), F32)


def free_ln_region(b):
    b.free("ln_rown", "ln_rowv", "ln_rowm", "ln_tmp", "ln_sq", "ln_xld")


def emit_ln(nc, b, xA, xn, sems, t0=0, t1=S):
    s_ld, s_mm, s_ev, s_sc = sems["ld"], sems["mm"], sems["ev"], sems["sc"]
    sb = b.sb
    xld, sq, tmp = sb["ln_xld"], sb["ln_sq"], sb["ln_tmp"]
    rowm, rowv, rown = sb["ln_rowm"], sb["ln_rowv"], sb["ln_rown"]
    onescol, onescolb, onesr = sb["g_onescol"], sb["g_onescolb"], sb["g_onesr"]
    pR1, pR2, pA, pB = sb["p_R1"], sb["p_R2"], sb["p_A"], sb["p_B"]
    pR1 = pR1[0:1, :]
    pR2 = pR2[0:1, :]

    s_st = sems["st"]
    for n0 in range(t0, t1, NB):
        for dc in range(NDC):
            b.dma("sync", xld[:, dc * NB:(dc + 1) * NB],
                  xA[dc * PC:(dc + 1) * PC, n0:n0 + NB], s_ld,
                  waits=[(s_ev, s_ev.count), (s_sc, s_sc.count), (s_mm, s_mm.count),
                         (s_st, s_st.count), (s_ld, s_ld.count)] if dc == 0 else ())
        ld_here = s_ld.count
        for dc in range(NDC):
            xc = xld[:, dc * NB:(dc + 1) * NB]
            sqc = sq[:, dc * NB:(dc + 1) * NB]
            b.on("scalar", lambda q, _x=xc, _s=sqc: nc.scalar.activation(_s, _x, AF.Square),
                 waits=[(s_ld, ld_here), (s_mm, s_mm.count)], inc=s_sc)
            sc_here = s_sc.count
            b.on("tensor", lambda q, _dc=dc, _s=sqc: nc.tensor.matmul(
                pR2[:], onescolb[:], _s, start=(_dc == 0), stop=(_dc == NDC - 1)),
                waits=[(s_sc, sc_here)] + ([(s_ev, s_ev.count)] if dc == 0 else []),
                inc=(s_mm if dc == NDC - 1 else None))
            b.on("tensor", lambda q, _dc=dc, _x=xc: nc.tensor.matmul(
                pR1[:], onescol[:], _x, start=(_dc == 0), stop=(_dc == NDC - 1)),
                waits=[(s_ld, ld_here)] + ([(s_ev, s_ev.count)] if dc == 0 else []),
                inc=(s_mm if dc == NDC - 1 else None))
        mm_stats = s_mm.count

        b.chain("vector", [
            lambda q: nc.vector.tensor_scalar(rowm[:], pR1[:], 1.0 / D, None, op0=ALU.mult),
            lambda q: nc.vector.tensor_scalar(rowv[:], pR2[:], 1.0 / D, None, op0=ALU.mult),
            lambda q: nc.vector.tensor_tensor(rown[:], rowm[:], rowm[:], op=ALU.mult),
            lambda q: nc.vector.tensor_tensor(rowv[:], rowv[:], rown[:], op=ALU.subtract),
            lambda q: nc.vector.tensor_scalar(rowv[:], rowv[:], 1e-5, None, op0=ALU.add),
        ], waits=[(s_mm, mm_stats)])
        b.on("scalar", lambda q: nc.scalar.activation(rowv[:], rowv[:], AF.Sqrt),
             waits=[(s_ev, s_ev.count)], inc=s_sc)

        b.chain("vector", [
            lambda q: nc.vector.reciprocal(rowv[:], rowv[:]),
            lambda q: nc.vector.tensor_tensor(rown[:], rowm[:], rowv[:], op=ALU.mult),
            lambda q: nc.vector.tensor_scalar(rown[:], rown[:], -1.0, None, op0=ALU.mult),
        ], waits=[(s_sc, s_sc.count)])
        b.on("tensor", lambda q: [
            nc.tensor.matmul(pA[:], onesr[:], rowv[:], start=True, stop=True),
            nc.tensor.matmul(pB[:], onesr[:], rown[:], start=True, stop=True)],
            waits=[(s_ev, s_ev.count)], inc=s_mm)
        mm_bc = s_mm.count
        for dc in range(NDC):
            xc = xld[:, dc * NB:(dc + 1) * NB]
            xo = xn[:, dc * S + n0: dc * S + n0 + NB]

            b.chain("vector", [
                lambda q, _x=xc: nc.vector.tensor_tensor(tmp[:], _x, pA[:], op=ALU.mult),
                lambda q, _o=xo: nc.vector.tensor_tensor(_o, tmp[:], pB[:], op=ALU.add),
            ], waits=[(s_mm, mm_bc)])


def build_l0_attn(nc, b, io, sems):
    s_ld, s_mm, s_ev, s_sc, s_st = sems["ld"], sems["mm"], sems["ev"], sems["sc"], sems["st"]
    s_gl, s_xr = sems["gl"], sems["xr"]
    b.psum("p_N", (1, NB))
    pD = b.psum("p_D", (PC, NB))
    rotA = Rot([b.sb["p_A"], b.sb["p_B"], pD, b.sb["p_R1"], b.sb["p_R2"]])
    xn = b.sbuf("a_xn", (PC, NDC * S), BF16)
    wpk = b.sbuf("a_wpk", (PC, NDC * 1024), BF16)
    wprime = [b.sbuf(f"a_wp{i}", (PC, NDC * 1024), BF16) for i in range(3)]
    bvec = b.sbuf("a_bvec", (PC, NDC * 3), BF16)
    bcol = b.sbuf("a_bcol", (PC, 3 * NDC), F32)
    nw = b.sbuf("a_nw", (PC, NDC), BF16)
    nrm2row = b.sbuf("a_nrm2", (1, S), F32)
    scl = nrm2row
    carry = b.sbuf("a_carry", (PC, NDC * 2), F32)

    pA, pB, pC = b.sb["p_A"], b.sb["p_B"], b.sb["p_C"]
    pN = b.sb["p_N"]
    onesr = b.sb["g_onesr"]

    xA = io["x"]
    b.dma("sync", *_ld3(io["c_PK"], wpk[:], 1024), s_ld,
          waits=[(s_ld, s_ld.count)])
    b.dma("sync", nw[:], io["c_nw"], s_ld)
    b.wait("tensor", [(s_ld, s_ld.count)])
    b.wait("scalar", [(s_ld, s_ld.count)])

    alloc_ln_region(b)
    emit_ln(nc, b, xA, xn, sems)
    free_ln_region(b)

    wW = b.sbuf("a_wW", (PC, NDC * 1024), BF16)
    for wi, (wname, bname) in enumerate([("Wk", "bk"), ("Wv", "bv"), ("Wq", "bq")]):
        b.dma("gpsimd", bvec[:, wi * NDC:(wi + 1) * NDC], io[bname], s_gl,
              waits=[(s_mm, s_mm.count)] + ([(s_ev, s_ev.count), (s_sc, s_sc.count)]
                                            if wi == 0 else []))
        b.dma("gpsimd", *_ld3(io[wname], wW[:], 1024), s_gl)
        ld_w = s_gl.count
        for mc in range(NDC):
            for nb_ in range(2):
                dstp = pA if (nb_ == 0) else pB
                b.mm(dstp[:],
                     [(wW[:, kc * 1024 + mc * PC: kc * 1024 + (mc + 1) * PC],
                       wpk[:, kc * 1024 + nb_ * NB: kc * 1024 + (nb_ + 1) * NB])
                      for kc in range(NDC)],
                     waits=[(s_gl, ld_w), (s_ev, s_ev.count)], inc=s_mm)
                mm_w = s_mm.count
                b.on("vector", lambda q, _wi=wi, _mc=mc, _nb=nb_, _p=dstp: nc.vector.tensor_copy(
                    wprime[_wi][:, _mc * 1024 + _nb * NB: _mc * 1024 + (_nb + 1) * NB], _p[:]),
                    waits=[(s_mm, mm_w)], inc=s_ev)
        ld_b = s_gl.count
        for mc in range(NDC):
            b.mm(pC[:, 0:1],
                 [(wpk[:, kc * 1024 + mc * PC: kc * 1024 + mc * PC + PC],
                   bvec[:, wi * NDC + kc: wi * NDC + kc + 1]) for kc in range(NDC)],
                 waits=[(s_gl, ld_b), (s_ev, s_ev.count)], inc=s_mm)
            mm_b = s_mm.count
            b.on("vector", lambda q, _wi=wi, _mc=mc: nc.vector.tensor_copy(
                bcol[:, _wi * NDC + _mc: _wi * NDC + _mc + 1], pC[:, 0:1]),
                waits=[(s_mm, mm_b)], inc=s_ev)

    rotA.seed(s_ev)
    b.free("a_wW")
    kr = b.sbuf("a_kr", (PC, HS), BF16)
    ki = b.sbuf("a_ki", (PC, HS), BF16)
    vr = b.sbuf("a_vr", (PC, HS), BF16)
    vi = b.sbuf("a_vi", (PC, HS), BF16)
    kvr = b.sbuf("a_kvr", (PC, HS), BF16)
    kvi = b.sbuf("a_kvi", (PC, HS), BF16)
    sqs = b.sbuf("a_sqs", (PC, HS), BF16)

    for h in range(2):
        th0 = h * HS
        for pc_ in range(4):
            mc_re, mc_im = pc_, pc_ + 4
            for out_t, wi, mc in [(kr, 0, mc_re), (ki, 0, mc_im),
                                  (vr, 1, mc_re), (vi, 1, mc_im)]:
                for nb_ in range(HS // NB):
                    n0 = th0 + nb_ * NB
                    dstp, rw_ = rotA.take()
                    b.mm(dstp[:],
                         [(wprime[wi][:, kc * 1024 + mc * PC: kc * 1024 + (mc + 1) * PC],
                           xn[:, kc * S + n0: kc * S + n0 + NB]) for kc in range(NDC)],
                         waits=rw_, inc=s_mm)
                    mm_p = s_mm.count
                    b.on("vector", lambda q, _o=out_t, _wi=wi, _mc=mc, _nb=nb_, _p=dstp: nc.vector.tensor_scalar(
                        _o[:, _nb * NB:(_nb + 1) * NB], _p[:],
                        bcol[:, _wi * NDC + _mc: _wi * NDC + _mc + 1], None, op0=ALU.add),
                        waits=[(s_mm, mm_p)], inc=s_ev)
                    rotA.settle(s_ev)
            ev_kv_in = s_ev.count

            b.chain("vector", [
                lambda q: nc.vector.tensor_tensor(kvr[:], kr[:], vr[:], op=ALU.mult),
                lambda q: nc.vector.tensor_tensor(sqs[:], ki[:], vi[:], op=ALU.mult),
                lambda q: nc.vector.tensor_tensor(kvr[:], kvr[:], sqs[:], op=ALU.subtract),
                lambda q: nc.vector.tensor_tensor(kvi[:], kr[:], vi[:], op=ALU.mult),
                lambda q: nc.vector.tensor_tensor(sqs[:], ki[:], vr[:], op=ALU.mult),
                lambda q: nc.vector.tensor_tensor(kvi[:], kvi[:], sqs[:], op=ALU.add),
            ], waits=[(s_ev, ev_kv_in), (s_st, s_st.count)])
            if pc_ == 0:
                b.chain("vector", [
                    lambda q: nc.vector.tensor_tensor(kvr[0:1, :], kr[0:1, :], vr[0:1, :], op=ALU.mult),
                    lambda q: nc.vector.tensor_tensor(kvi[0:1, :], ki[0:1, :], vi[0:1, :], op=ALU.mult),
                ])
            c_re = carry[:, pc_ * 2: pc_ * 2 + 1]
            c_im = carry[:, pc_ * 2 + 1: pc_ * 2 + 2]

            b.chain("vector", [
                lambda q, _c=c_re, _h=h: nc.vector.tensor_tensor_scan(
                    vr[:], kvr[:], kvr[:], (0.0 if _h == 0 else _c), op0=ALU.add, op1=ALU.bypass),
                lambda q, _c=c_im, _h=h: nc.vector.tensor_tensor_scan(
                    vi[:], kvi[:], kvi[:], (0.0 if _h == 0 else _c), op0=ALU.add, op1=ALU.bypass),
                lambda q, _c=c_re: nc.vector.tensor_copy(_c, vr[:, HS - 1:HS]),
                lambda q, _c=c_im: nc.vector.tensor_copy(_c, vi[:, HS - 1:HS]),
            ])
            ev_scan = s_ev.count
            for out_t, mc in [(kr, mc_re), (ki, mc_im)]:
                for nb_ in range(HS // NB):
                    n0 = th0 + nb_ * NB
                    dstp, rw_ = rotA.take()
                    b.mm(dstp[:],
                         [(wprime[2][:, kc * 1024 + mc * PC: kc * 1024 + (mc + 1) * PC],
                           xn[:, kc * S + n0: kc * S + n0 + NB]) for kc in range(NDC)],
                         waits=rw_ + [(s_ev, ev_scan)], inc=s_mm)
                    mm_p = s_mm.count
                    b.on("vector", lambda q, _o=out_t, _mc=mc, _nb=nb_, _p=dstp: nc.vector.tensor_scalar(
                        _o[:, _nb * NB:(_nb + 1) * NB], _p[:],
                        bcol[:, 2 * NDC + _mc: 2 * NDC + _mc + 1], None, op0=ALU.add),
                        waits=[(s_mm, mm_p)], inc=s_ev)
                    rotA.settle(s_ev)
            ev_q = s_ev.count

            b.chain("vector", [
                lambda q: nc.vector.tensor_tensor(sqs[:], vr[:], kr[:], op=ALU.mult),
                lambda q: nc.vector.tensor_tensor(kvr[:], vi[:], ki[:], op=ALU.mult),
                lambda q: nc.vector.tensor_tensor(kvr[:], kvr[:], sqs[:], op=ALU.add),
                lambda q: nc.vector.tensor_tensor(sqs[:], vi[:], kr[:], op=ALU.mult),
                lambda q: nc.vector.tensor_tensor(kvi[:], vr[:], ki[:], op=ALU.mult),
                lambda q: nc.vector.scalar_tensor_tensor(kvi[:], kvi[:], -1.0, sqs[:], op0=ALU.mult, op1=ALU.add),
            ], waits=[(s_ev, ev_q)])
            if pc_ == 0:
                b.chain("vector", [
                    lambda q: nc.vector.tensor_tensor(kvr[0:1, :], vr[0:1, :], kr[0:1, :], op=ALU.mult),
                    lambda q: nc.vector.tensor_tensor(kvi[0:1, :], vi[0:1, :], ki[0:1, :], op=ALU.mult),
                ])
            ev_u = s_ev.count
            b.dma("gpsimd", io["qscr"][mc_re * PC:(mc_re + 1) * PC, th0:th0 + HS], kvr[:], s_st,
                  waits=[(s_ev, ev_u)])
            b.dma("gpsimd", io["qscr"][mc_im * PC:(mc_im + 1) * PC, th0:th0 + HS], kvi[:], s_st,
                  waits=[(s_ev, ev_u)])
            for slc in range(HS // NB):
                n0 = th0 + slc * NB
                sl = slice(slc * NB, (slc + 1) * NB)
                b.on("scalar", lambda q, _sl=sl: nc.scalar.activation(
                    sqs[:, _sl], kvr[:, _sl], AF.Square),
                    waits=[(s_ev, ev_u), (s_mm, s_mm.count)], inc=s_sc)
                sc1 = s_sc.count
                b.on("tensor", lambda q, _wc=mc_re, _sl=sl: nc.tensor.matmul(
                    pN[:], nw[:, _wc:_wc + 1], sqs[:, _sl], start=True, stop=False),
                    waits=[(s_sc, sc1), (s_ev, s_ev.count)], inc=s_mm)
                mm1 = s_mm.count
                b.on("scalar", lambda q, _sl=sl: nc.scalar.activation(
                    sqs[:, _sl], kvi[:, _sl], AF.Square),
                    waits=[(s_mm, mm1)], inc=s_sc)
                sc2 = s_sc.count
                b.on("tensor", lambda q, _wc=mc_im, _sl=sl: nc.tensor.matmul(
                    pN[:], nw[:, _wc:_wc + 1], sqs[:, _sl], start=False, stop=True),
                    waits=[(s_sc, sc2)], inc=s_mm)
                mm2 = s_mm.count
                b.on("vector", lambda q, _n0=n0, _first=(pc_ == 0): (
                    nc.vector.tensor_copy(nrm2row[:, _n0:_n0 + NB], pN[:]) if _first
                    else nc.vector.tensor_tensor(nrm2row[:, _n0:_n0 + NB],
                                                 nrm2row[:, _n0:_n0 + NB], pN[:], op=ALU.add)),
                    waits=[(s_mm, mm2)], inc=s_ev)

    b.on("scalar", lambda q: nc.scalar.activation(nrm2row[:], nrm2row[:], AF.Sqrt),
         waits=[(s_ev, s_ev.count)], inc=s_sc)

    b.chain("vector", [
        lambda q: nc.vector.tensor_scalar(scl[:], nrm2row[:], 1e-9, 32.0, op0=ALU.add, op1=ALU.mult),
        lambda q: nc.vector.reciprocal(scl[:], scl[:]),
    ], waits=[(s_sc, s_sc.count)])
    ev_scale = s_ev.count

    b.free("a_sqs", "a_kvi", "a_kvr", "a_vi", "a_vr", "a_ki", "a_kr")
    qld = b.sbuf("a_qld", (PC, NDC * NB), BF16)
    xres = b.sbuf("a_xres", (PC, NB), F32)
    oev = b.sbuf("a_oev", (PC, NB), F32)
    sclb = b.sbuf("a_sclb", (PC, NB), F32)

    b.dma("gpsimd", *_ld3(io["c_IW"], wpk[:], 1024), s_gl,
          waits=[(s_mm, s_mm.count)])
    ld_iw = s_gl.count
    for nb_ in range(S // NB):
        n0 = nb_ * NB
        for kc in range(NDC):
            b.dma("sync", qld[:, kc * NB:(kc + 1) * NB],
                  io["qscr"][kc * PC:(kc + 1) * PC, n0:n0 + NB], s_ld,
                  waits=[(s_st, s_st.count), (s_mm, s_mm.count), (s_ev, s_ev.count),
                         (s_sc, s_sc.count)] if kc == 0 else ())
        ld_q = s_ld.count
        b.on("tensor", lambda q, _n0=n0: nc.tensor.matmul(
            pC[:], onesr[:], scl[:, _n0:_n0 + NB], start=True, stop=True),
            waits=[(s_ev, s_ev.count)], inc=s_mm)
        mm_s = s_mm.count
        b.on("vector", lambda q: nc.vector.tensor_copy(sclb[:], pC[:]),
             waits=[(s_mm, mm_s)], inc=s_ev)
        ev_sb = s_ev.count
        for mc in range(NDC):
            dstp, rw_ = rotA.take()
            b.mm(dstp[:],
                 [(wpk[:, kc * 1024 + mc * PC: kc * 1024 + (mc + 1) * PC],
                   qld[:, kc * NB:(kc + 1) * NB]) for kc in range(NDC)],
                 waits=rw_ + [(s_ld, ld_q), (s_gl, ld_iw)], inc=s_mm)
            mm_i = s_mm.count
            b.dma("sync", xres[:], xA[mc * PC:(mc + 1) * PC, n0:n0 + NB], s_xr,
                  waits=[(s_ev, s_ev.count)])
            ld_x = s_xr.count

            b.chain("vector", [
                lambda q, _p=dstp: nc.vector.tensor_tensor(oev[:], _p[:], sclb[:], op=ALU.mult),
                lambda q: nc.vector.tensor_tensor(oev[:], oev[:], xres[:], op=ALU.add),
            ], waits=[(s_mm, mm_i), (s_xr, ld_x), (s_ev, ev_sb)])
            rotA.settle(s_ev)
            b.dma("gpsimd", io["x1"][mc * PC:(mc + 1) * PC, n0:n0 + NB], oev[:], s_st,
                  waits=[(s_ev, s_ev.count)])
            b.wait("vector", [(s_st, s_st.count)])
    b.wait("gpsimd", [(s_st, s_st.count)])
    b.free("a_sclb", "a_oev", "a_xres", "a_qld")
    b.free("a_carry", "a_nrm2", "a_nw", "a_bcol", "a_bvec",
           "a_wp2", "a_wp1", "a_wp0", "a_wpk", "a_xn", "p_D", "p_N")



def build_l0_ffn(nc, b, io, sems):
    s_ld, s_mm, s_ev, s_sc, s_st = sems["ld"], sems["mm"], sems["ev"], sems["sc"], sems["st"]
    s_gl, s_xr = sems["gl"], sems["xr"]
    NSL = 64
    pS1 = b.psum("p_S1", (NSL, NB))
    pS2 = b.psum("p_S2", (NSL, NB))
    pT = b.psum("p_T", (PC, PC), BF16)
    rotF = Rot([b.sb["p_A"], b.sb["p_B"], b.sb["p_R1"], b.sb["p_R2"]])
    pA, pB, pC = b.sb["p_A"], b.sb["p_B"], b.sb["p_C"]
    pR1, pR2 = b.sb["p_R1"], b.sb["p_R2"]
    pR1 = pR1[0:1, :]
    pR2 = pR2[0:1, :]
    onesrb = b.sb["g_onesrb"]
    idn = b.sb["g_idn"]
    idnf = b.sb["g_idnf"]

    xn = b.sbuf("f_xn", (PC, NDC * S), BF16)
    w1x = b.sbuf("f_w1x", (PC, NDC * NB), BF16)
    w1s = b.sbuf("f_w1s", (PC, NDC * NB), BF16)
    w2 = b.sbuf("f_w2", (PC, 4 * NSL), BF16)
    wq = b.sbuf("f_wq", (PC, NDC * 256), BF16)
    wk = b.sbuf("f_wk", (PC, NDC * 256), BF16)
    wv = b.sbuf("f_wv", (PC, NDC * 1024), BF16)
    wvp = b.sbuf("f_wvp", (PC, NDC * 1024), BF16)
    wo = b.sbuf("f_wo", (PC, NDC * 1024), BF16)
    slotT = b.sbuf("f_slotT", (PC, NDC * NSL), BF16)
    smcol = b.sbuf("f_smcol", (PC, NDC), BF16)
    smcolf = b.sbuf("f_smcolf", (PC, NDC), F32)
    hcol = b.sbuf("f_hcol", (PC, 4), F32)
    b1c = b.sbuf("f_b1c", (PC, 4), F32)
    b2c = b.sbuf("f_b2c", (NSL, 1), F32)
    invt = b.sbuf("f_invt", (NSL, 1), F32)
    tau1 = b.sbuf("f_tau1", (1, 1), F32)
    s8 = b.sbuf("f_s8", (PC, 8), F32)
    ecol = b.sbuf("f_ecol", (PC, 1), F32)
    etile = b.sbuf("f_etile", (PC, NSL), F32)
    emsk = b.sbuf("f_emsk", (PC, NSL), BF16)
    atok = b.sbuf("f_atok", (PC, 32 * NSL), BF16)
    afm = b.sbuf("f_afm", (NSL, S), BF16)
    dcol = b.sbuf("f_dcol", (NSL, 1), F32)
    bihc = b.sbuf("f_bihc", (PC, 24), F32)
    bhhc = b.sbuf("f_bhhc", (PC, 24), F32)
    snT = b.sbuf("f_snT", (PC, NDC * NSL), BF16)
    snTf = b.sbuf("f_snTf", (PC, NDC * NSL), F32)
    qT = b.sbuf("f_qT", (PC, 2 * NSL), BF16)
    kT = b.sbuf("f_kT", (PC, 2 * NSL), BF16)
    bqc = b.sbuf("f_bqc", (PC, 2), F32)
    bkc = b.sbuf("f_bkc", (PC, 2), F32)
    vtok = b.sbuf("f_vtok", (NSL, 1024), BF16)
    bvrow = b.sbuf("f_bvrow", (1, 1024), F32)
    bvrowb = b.sbuf("f_bvrowb", (1, 1024), BF16)
    bvprow = b.sbuf("f_bvprow", (1, 1024), F32)
    bvprowb = b.sbuf("f_bvprowb", (1, 1024), BF16)
    sc64 = b.sbuf("f_sc64", (NSL, NSL), F32)
    at64 = b.sbuf("f_at64", (NSL, NSL), BF16)
    atT = b.sbuf("f_atT", (NSL, NSL), BF16)
    rowm = b.sbuf("f_rowm", (1, NSL), F32)
    rowv = b.sbuf("f_rowv", (1, NSL), F32)
    rown = b.sbuf("f_rown", (1, NSL), F32)
    lgc = b.sbuf("f_lgc", (PC, NDC), F32)
    lbc = b.sbuf("f_lbc", (PC, NDC), F32)
    svtok = b.sbuf("f_svtok", (NSL, 1024), BF16)
    boc = b.sbuf("f_boc", (PC, NDC), F32)
    ynT = b.sbuf("f_ynT", (PC, NDC * NSL), BF16)

    xA = io["x1"]
    b.dma("gpsimd", *_ld3(io["w1x"], w1x[:], NB), s_gl,
          waits=[(s_mm, s_mm.count), (s_ev, s_ev.count), (s_sc, s_sc.count),
                 (s_st, s_st.count)])
    b.dma("gpsimd", *_ld3(io["w1s"], w1s[:], NB), s_gl)
    b.dma("gpsimd", *_ld3(io["w2"], w2[:], NSL), s_gl)
    b.dma("gpsimd", *_ld3(io["wq"], wq[:], 256), s_gl)
    b.dma("gpsimd", *_ld3(io["wk"], wk[:], 256), s_gl)
    b.dma("gpsimd", *_ld3(io["wv"], wv[:], 1024), s_gl)
    b.dma("gpsimd", *_ld3(io["wvp"], wvp[:], 1024), s_gl)
    b.dma("gpsimd", *_ld3(io["wo"], wo[:], 1024), s_gl)
    b.dma("gpsimd", *_ld3(io["slotT"], slotT[:], NSL), s_gl)
    b.dma("sync", b1c[:], io["b1c"], s_ld,
          waits=[(s_ld, s_ld.count), (s_mm, s_mm.count), (s_ev, s_ev.count),
                 (s_sc, s_sc.count), (s_st, s_st.count)])
    b.dma("sync", b2c[:], io["b2c"], s_ld)
    b.dma("sync", tau1[:], io["tau"], s_ld)
    b.dma("sync", bihc[:], io["bihc"], s_ld)
    b.dma("sync", bhhc[:], io["bhhc"], s_ld)
    b.dma("sync", bqc[:], io["bqc"], s_ld)
    b.dma("sync", bkc[:], io["bkc"], s_ld)
    b.dma("sync", bvrow[:], io["bvrow"], s_ld)
    b.dma("sync", bvprow[:], io["bvprow"], s_ld)
    b.dma("sync", lgc[:], io["lngc"], s_ld)
    b.dma("sync", lbc[:], io["lnbc"], s_ld)
    b.dma("sync", boc[:], io["boc"], s_ld)
    ld0 = s_ld.count
    ld0g = s_gl.count

    alloc_ln_region(b)
    emit_ln(nc, b, xA, xn, sems)
    free_ln_region(b)

    hb = b.sbuf("f_hb", (PC, 4 * NB), BF16)
    rlfm = b.sbuf("f_rlfm", (NSL, S), F32)
    rltok = b.sbuf("f_rltok", (PC, 32 * NSL), F32)

    b.chain("vector",
            [lambda q, _dc=dc: nc.vector.tensor_reduce(
                smcolf[:, _dc:_dc + 1], slotT[:, _dc * NSL:(_dc + 1) * NSL], axis=AX.X, op=ALU.add)
             for dc in range(NDC)] +
            [lambda q: nc.vector.tensor_scalar(smcol[:], smcolf[:], 1.0 / NSL, None, op0=ALU.mult)],
            waits=[(s_ld, s_ld.count), (s_gl, s_gl.count)])
    b.on("scalar", lambda q: nc.scalar.activation(tau1[:], tau1[:], AF.Abs),
         waits=[(s_ld, s_ld.count)], inc=s_sc)

    b.chain("vector", [
        lambda q: nc.vector.tensor_scalar(tau1[:], tau1[:], 0.1, None, op0=ALU.add),
        lambda q: nc.vector.reciprocal(tau1[:], tau1[:]),
    ], waits=[(s_sc, s_sc.count)])
    ev_tau = s_ev.count
    b.mm(pC[:NSL, 0:1], [(b.sb["g_onesr"][:, 0:NSL], tau1[:])],
         waits=[(s_ev, ev_tau)], inc=s_mm)
    b.on("vector", lambda q: nc.vector.tensor_copy(invt[:], pC[:NSL, 0:1]),
         waits=[(s_mm, s_mm.count)], inc=s_ev)
    for mc in range(4):
        b.mm(pC[:, 0:1],
             [(w1s[:, kc * NB + mc * PC: kc * NB + (mc + 1) * PC],
               smcol[:, kc:kc + 1]) for kc in range(NDC)],
             waits=[(s_ev, s_ev.count)], inc=s_mm)
        mm_h = s_mm.count
        b.on("vector", lambda q, _mc=mc: nc.vector.scalar_tensor_tensor(
            hcol[:, _mc:_mc + 1], pC[:, 0:1], 1.0, b1c[:, _mc:_mc + 1],
            op0=ALU.mult, op1=ALU.add),
            waits=[(s_mm, mm_h)], inc=s_ev)
    ev_hcol = s_ev.count
    b.on("vector", lambda q: [nc.vector.tensor_copy(bvrowb[:], bvrow[:]),
                              nc.vector.tensor_copy(bvprowb[:], bvprow[:])],
         waits=[(s_ld, s_ld.count)], inc=s_ev)

    rotF.seed(s_ev)
    for nb_ in range(S // NB):
        n0 = nb_ * NB
        for mc in range(4):
            dstp, rw_ = rotF.take()
            b.mm(dstp[:],
                 [(w1x[:, kc * NB + mc * PC: kc * NB + (mc + 1) * PC],
                   xn[:, kc * S + n0: kc * S + n0 + NB]) for kc in range(NDC)],
                 waits=rw_ + [(s_gl, ld0g)], inc=s_mm)
            mm_h = s_mm.count
            b.on("scalar", lambda q, _mc=mc, _p=dstp: nc.scalar.activation(
                hb[:, _mc * NB:(_mc + 1) * NB], _p[:], AF.Gelu, bias=hcol[:, _mc:_mc + 1]),
                waits=[(s_mm, mm_h), (s_ev, ev_hcol)], inc=s_sc)
            rotF.settle(s_sc)
        sc_h = s_sc.count
        b.mm(pC[:NSL, :],
             [(w2[:, mc * NSL:(mc + 1) * NSL], hb[:, mc * NB:(mc + 1) * NB])
              for mc in range(4)],
             waits=[(s_sc, sc_h), (s_ev, s_ev.count)], inc=s_mm)
        mm_rl = s_mm.count
        b.on("vector", lambda q, _n0=n0: nc.vector.tensor_scalar(
            rlfm[:, _n0:_n0 + NB], pC[:NSL, :], b2c[:], invt[:],
            op0=ALU.add, op1=ALU.mult),
            waits=[(s_mm, mm_rl)], inc=s_ev)
    ev_rl = s_ev.count

    for tc in range(32):
        t0 = tc * PC
        b.on("tensor", lambda q, _t0=t0: nc.tensor.transpose(
            pA[:, 0:NSL], rlfm[:, _t0:_t0 + PC], idnf[0:NSL, 0:NSL]),
            waits=[(s_ev, s_ev.count)], inc=s_mm)
        mm_t = s_mm.count
        b.on("vector", lambda q, _tc=tc: nc.vector.tensor_copy(
            rltok[:, _tc * NSL:(_tc + 1) * NSL], pA[:, 0:NSL]),
            waits=[(s_mm, mm_t)], inc=s_ev)
        ev_t = s_ev.count
        rlc = rltok[:, tc * NSL:(tc + 1) * NSL]

        b.chain("vector", [
            lambda q, _rl=rlc: nc.vector.max(s8[:], _rl),
            lambda q: nc.vector.tensor_scalar(ecol[:], s8[:, 0:1], -1.0, None, op0=ALU.mult),
        ], waits=[(s_ev, ev_t)])
        ev_r = s_ev.count
        b.on("scalar", lambda q, _rl=rlc: nc.scalar.activation(
            etile[:], _rl, AF.Exp, bias=ecol[:]),
            waits=[(s_ev, ev_r)], inc=s_sc)
        sc_e = s_sc.count

        b.chain("vector", [
            lambda q, _rl=rlc: nc.vector.tensor_scalar(emsk[:], _rl, s8[:, 7:8], None, op0=ALU.is_ge),
            lambda q: nc.vector.tensor_tensor(etile[:], etile[:], emsk[:], op=ALU.mult),
            lambda q: nc.vector.tensor_reduce(ecol[:], etile[:], axis=AX.X, op=ALU.add),
            lambda q: nc.vector.reciprocal(ecol[:], ecol[:]),
            lambda q, _tc=tc: nc.vector.tensor_scalar(atok[:, _tc * NSL:(_tc + 1) * NSL], etile[:],
                                                      ecol[:], None, op0=ALU.mult),
        ], waits=[(s_sc, sc_e)])
        ev_a = s_ev.count
        b.on("tensor", lambda q, _tc=tc: _tbf(nc, pT[0:NSL, :], atok[:, _tc * NSL:(_tc + 1) * NSL], idn[:]),
            waits=[(s_ev, ev_a)], inc=s_mm)
        mm_at = s_mm.count
        b.on("vector", lambda q, _tc=tc: nc.vector.tensor_copy(
            afm[:, _tc * PC:(_tc + 1) * PC], pT[0:NSL, :]),
            waits=[(s_mm, mm_at)], inc=s_ev)
    ev_alpha = s_ev.count
    b.free("f_rltok", "f_rlfm", "f_hb")
    xtok = b.sbuf("f_xtok", (PC, NDC * PC), BF16)
    sit = b.sbuf("f_sit", (PC, NDC * NSL), BF16)
    siev = b.sbuf("f_siev", (NSL, 1024), BF16)
    git = b.sbuf("f_git", (PC, 24 * NSL), F32)
    ght = b.sbuf("f_ght", (PC, 24 * NSL), F32)
    grs = b.sbuf("f_grs", (PC, 8 * NSL), F32)
    gzs = b.sbuf("f_gzs", (PC, 8 * NSL), F32)
    gns = b.sbuf("f_gns", (PC, 8 * NSL), F32)
    yT = b.sbuf("f_yT", (PC, NDC * NSL), F32)
    y2 = b.sbuf("f_y2", (PC, NDC * NSL), BF16)
    wstage = b.sbuf("f_wstage", (PC, 2 * 1024), BF16)

    b.chain("vector", [
        lambda q: nc.vector.tensor_reduce(dcol[:], afm[:], axis=AX.X, op=ALU.add),
        lambda q: nc.vector.tensor_scalar(dcol[:], dcol[:], 1e-8, None, op0=ALU.add),
        lambda q: nc.vector.reciprocal(dcol[:], dcol[:]),
    ], waits=[(s_ev, ev_alpha)])
    ev_den = s_ev.count
    for tc in range(32):
        t0 = tc * PC
        for dc in range(NDC):
            b.on("tensor", lambda q, _dc=dc, _t0=t0: _tbf(nc, pT[:], xn[:, _dc * S + _t0: _dc * S + _t0 + PC], idn[:]),
                waits=[(s_ev, s_ev.count)], inc=s_mm)
            mm_x = s_mm.count
            b.on("vector", lambda q, _dc=dc: nc.vector.tensor_copy(
                xtok[:, _dc * PC:(_dc + 1) * PC], pT[:]),
                waits=[(s_mm, mm_x)], inc=s_ev)
        ev_x = s_ev.count
        for half, pS in [(0, pS1), (1, pS2)]:
            b.on("tensor", lambda q, _tc=tc, _h=half, _p=pS: nc.tensor.matmul(
                _p[:], atok[:, _tc * NSL:(_tc + 1) * NSL],
                xtok[:, _h * NB:(_h + 1) * NB],
                start=(_tc == 0), stop=(_tc == 31)),
                waits=[(s_ev, ev_x)],
                inc=(s_mm if tc == 31 else None))
    mm_si = s_mm.count
    b.on("vector", lambda q: [
        nc.vector.tensor_scalar(siev[:, 0:NB], pS1[:], dcol[:], None, op0=ALU.mult),
        nc.vector.tensor_scalar(siev[:, NB:1024], pS2[:], dcol[:], None, op0=ALU.mult)],
        waits=[(s_mm, mm_si), (s_ev, ev_den)], inc=s_ev)
    ev_si = s_ev.count
    for dc in range(NDC):
        b.on("tensor", lambda q, _dc=dc: _tbf(nc, pT[:, 0:NSL], siev[:, _dc * PC:(_dc + 1) * PC], idn[0:NSL, 0:NSL]),
            waits=[(s_ev, s_ev.count)], inc=s_mm)
        mm_s = s_mm.count
        b.on("vector", lambda q, _dc=dc: nc.vector.tensor_copy(
            sit[:, _dc * NSL:(_dc + 1) * NSL], pT[:, 0:NSL] if False else pT[0:PC, 0:NSL]),
            waits=[(s_mm, mm_s)], inc=s_ev)
    ev_sit = s_ev.count

    for gname, wsrc, dst, bcols in [("ih", "WihT", git, bihc), ("hh", "WhhT", ght, bhhc)]:
        rhs = sit if gname == "ih" else slotT
        for mc in range(24):
            buf = wstage[:, (mc % 2) * 1024:(mc % 2) * 1024 + 1024]
            b.dma("gpsimd", *_ld3(io[wsrc][:, mc * PC:(mc + 1) * PC], buf, PC), s_gl,
                  waits=[(s_mm, s_mm.count), (s_ev, s_ev.count)])
            ld_w = s_gl.count
            b.mm(pC[:, 0:NSL],
                 [(buf[:, kc * PC:(kc + 1) * PC], rhs[:, kc * NSL:(kc + 1) * NSL])
                  for kc in range(NDC)],
                 waits=[(s_gl, ld_w), (s_ev, s_ev.count)], inc=s_mm)
            mm_g = s_mm.count
            b.on("vector", lambda q, _dst=dst, _mc=mc, _bc=bcols: nc.vector.tensor_scalar(
                _dst[:, _mc * NSL:(_mc + 1) * NSL], pC[:, 0:NSL], _bc[:, _mc:_mc + 1], None, op0=ALU.add),
                waits=[(s_mm, mm_g)], inc=s_ev)
    ev_g = s_ev.count

    for dc in range(NDC):
        sl = slice(dc * NSL, (dc + 1) * NSL)
        slz = slice((NDC + dc) * NSL, (NDC + dc + 1) * NSL)
        sln = slice((2 * NDC + dc) * NSL, (2 * NDC + dc + 1) * NSL)

        def gru1(q, _sl=sl, _slz=slz, _sln=sln, _dc=dc):
            i1 = nc.vector.tensor_tensor(grs[:, _sl], git[:, _sl], ght[:, _sl], op=ALU.add)
            i2 = nc.vector.tensor_tensor(gzs[:, _sl], git[:, _slz], ght[:, _slz], op=ALU.add)
            return [i1, i2]
        b.on("vector", gru1, waits=[(s_ev, ev_g)], inc=s_ev)
        b.on("scalar", lambda q, _sl=sl: nc.scalar.activation(grs[:, _sl], grs[:, _sl], AF.Sigmoid),
             waits=[(s_ev, s_ev.count)], inc=s_sc)
        b.on("scalar", lambda q, _sl=sl: nc.scalar.activation(gzs[:, _sl], gzs[:, _sl], AF.Sigmoid),
             waits=[(s_ev, s_ev.count)], inc=s_sc)
        sc_rz = s_sc.count

        b.chain("vector", [
            lambda q, _sl=sl, _sln=sln: nc.vector.tensor_tensor(gns[:, _sl], grs[:, _sl], ght[:, _sln], op=ALU.mult),
            lambda q, _sl=sl, _sln=sln: nc.vector.tensor_tensor(gns[:, _sl], gns[:, _sl], git[:, _sln], op=ALU.add),
        ], waits=[(s_sc, sc_rz)])
        b.on("scalar", lambda q, _sl=sl: nc.scalar.activation(gns[:, _sl], gns[:, _sl], AF.Tanh),
             waits=[(s_ev, s_ev.count)], inc=s_sc)
        sc_n = s_sc.count

        b.chain("vector", [
            lambda q, _sl=sl: nc.vector.tensor_tensor(grs[:, _sl], slotT[:, _sl], gns[:, _sl], op=ALU.subtract),
            lambda q, _sl=sl: nc.vector.tensor_tensor(grs[:, _sl], grs[:, _sl], gzs[:, _sl], op=ALU.mult),
            lambda q, _sl=sl: nc.vector.tensor_tensor(snTf[:, _sl], grs[:, _sl], gns[:, _sl], op=ALU.add),
            lambda q, _sl=sl: nc.vector.tensor_copy(snT[:, _sl], snTf[:, _sl]),
        ], waits=[(s_sc, sc_n)])
    ev_sn = s_ev.count

    for dst, wsb, bcc, width in [(qT, wq, bqc, 256), (kT, wk, bkc, 256)]:
        for mc in range(width // PC):
            b.mm(pC[:, 0:NSL],
                 [(wsb[:, kc * width + mc * PC: kc * width + (mc + 1) * PC],
                   snT[:, kc * NSL:(kc + 1) * NSL]) for kc in range(NDC)],
                 waits=[(s_ev, s_ev.count)], inc=s_mm)
            mm_q = s_mm.count
            b.on("vector", lambda q, _dst=dst, _mc=mc, _bc=bcc: nc.vector.tensor_scalar(
                _dst[:, _mc * NSL:(_mc + 1) * NSL], pC[:, 0:NSL], _bc[:, _mc:_mc + 1], None, op0=ALU.add),
                waits=[(s_mm, mm_q)], inc=s_ev)
    for nb_ in range(2):
        b.mm(pS1[:, :] if nb_ == 0 else pS2[:, :],
             [(snT[:, kc * NSL:(kc + 1) * NSL],
               wv[:, kc * 1024 + nb_ * NB: kc * 1024 + (nb_ + 1) * NB]) for kc in range(NDC)] +
             [(onesrb[:, 0:NSL], bvrowb[:, nb_ * NB:(nb_ + 1) * NB])],
             waits=[(s_ev, s_ev.count)], inc=s_mm)
        mm_v = s_mm.count
        b.on("vector", lambda q, _nb=nb_: nc.vector.tensor_copy(
            vtok[:, _nb * NB:(_nb + 1) * NB], (pS1 if _nb == 0 else pS2)[:, :]),
            waits=[(s_mm, mm_v)], inc=s_ev)
    ev_v = s_ev.count

    b.mm(pC[:NSL, 0:NSL],
         [(qT[:, kc * NSL:(kc + 1) * NSL], kT[:, kc * NSL:(kc + 1) * NSL]) for kc in range(2)],
         waits=[(s_ev, s_ev.count)], inc=s_mm)
    mm_sc64 = s_mm.count
    b.on("vector", lambda q: nc.vector.tensor_scalar(
        sc64[:], pC[:NSL, 0:NSL], 0.0625, None, op0=ALU.mult),
        waits=[(s_mm, mm_sc64)], inc=s_ev)

    b.chain("vector", [
        lambda q: nc.vector.tensor_reduce(dcol[:], sc64[:], axis=AX.X, op=ALU.max),
        lambda q: nc.vector.tensor_scalar(dcol[:], dcol[:], -1.0, None, op0=ALU.mult),
    ], waits=[(s_ev, s_ev.count)])
    b.on("scalar", lambda q: nc.scalar.activation(sc64[:], sc64[:], AF.Exp, bias=dcol[:]),
         waits=[(s_ev, s_ev.count)], inc=s_sc)

    b.chain("vector", [
        lambda q: nc.vector.tensor_reduce(dcol[:], sc64[:], axis=AX.X, op=ALU.add),
        lambda q: nc.vector.reciprocal(dcol[:], dcol[:]),
        lambda q: nc.vector.tensor_scalar(at64[:], sc64[:], dcol[:], None, op0=ALU.mult),
    ], waits=[(s_sc, s_sc.count)])
    ev_at = s_ev.count
    b.on("tensor", lambda q: _tbf(nc, pT[0:NSL, 0:NSL], at64[:], idn[0:NSL, 0:NSL]),
         waits=[(s_ev, ev_at)], inc=s_mm)
    b.on("vector", lambda q: nc.vector.tensor_copy(atT[:], pT[0:NSL, 0:NSL]),
         waits=[(s_mm, s_mm.count)], inc=s_ev)
    ev_atT = s_ev.count
    for dc in range(NDC):
        b.mm(pC[:, 0:NSL],
             [(vtok[:NSL, dc * PC:(dc + 1) * PC], atT[:, :])],
             waits=[(s_ev, s_ev.count)], inc=s_mm)
        mm_y = s_mm.count
        b.on("vector", lambda q, _dc=dc: nc.vector.scalar_tensor_tensor(
            yT[:, _dc * NSL:(_dc + 1) * NSL], pC[:, 0:NSL], 1.0, snTf[:, _dc * NSL:(_dc + 1) * NSL],
            op0=ALU.mult, op1=ALU.add),
            waits=[(s_mm, mm_y)], inc=s_ev)
    ev_y = s_ev.count

    for dc in range(NDC):
        b.on("scalar", lambda q, _dc=dc: nc.scalar.activation(
            y2[:, _dc * NSL:(_dc + 1) * NSL], yT[:, _dc * NSL:(_dc + 1) * NSL], AF.Square),
            waits=[(s_ev, ev_y), (s_mm, s_mm.count)], inc=s_sc)
        sc_y = s_sc.count
        b.on("tensor", lambda q, _dc=dc: nc.tensor.matmul(
            pR2[:, 0:NSL], b.sb["g_onescolb"][:], y2[:, _dc * NSL:(_dc + 1) * NSL],
            start=(_dc == 0), stop=(_dc == NDC - 1)),
            waits=[(s_sc, sc_y)] + ([(s_ev, s_ev.count)] if dc == 0 else []),
            inc=(s_mm if dc == NDC - 1 else None))
        b.on("tensor", lambda q, _dc=dc: nc.tensor.matmul(
            pR1[:, 0:NSL], b.sb["g_onescol"][:], yT[:, _dc * NSL:(_dc + 1) * NSL],
            start=(_dc == 0), stop=(_dc == NDC - 1)),
            waits=[(s_ev, ev_y)] if dc == 0 else (),
            inc=(s_mm if dc == NDC - 1 else None))
    mm_ys = s_mm.count

    b.chain("vector", [
        lambda q: nc.vector.tensor_scalar(rowm[:], pR1[:, 0:NSL], 1.0 / D, None, op0=ALU.mult),
        lambda q: nc.vector.tensor_scalar(rowv[:], pR2[:, 0:NSL], 1.0 / D, None, op0=ALU.mult),
        lambda q: nc.vector.tensor_tensor(rown[:], rowm[:], rowm[:], op=ALU.mult),
        lambda q: nc.vector.tensor_tensor(rowv[:], rowv[:], rown[:], op=ALU.subtract),
        lambda q: nc.vector.tensor_scalar(rowv[:], rowv[:], 1e-5, None, op0=ALU.add),
    ], waits=[(s_mm, mm_ys)])
    b.on("scalar", lambda q: nc.scalar.activation(rowv[:], rowv[:], AF.Sqrt),
         waits=[(s_ev, s_ev.count)], inc=s_sc)

    b.chain("vector", [
        lambda q: nc.vector.reciprocal(rowv[:], rowv[:]),
        lambda q: nc.vector.tensor_tensor(rown[:], rowm[:], rowv[:], op=ALU.mult),
        lambda q: nc.vector.tensor_scalar(rown[:], rown[:], -1.0, None, op0=ALU.mult),
    ], waits=[(s_sc, s_sc.count)])
    b.on("tensor", lambda q: [
        nc.tensor.matmul(pA[:, 0:NSL], b.sb["g_onesr"][:], rowv[:], start=True, stop=True),
        nc.tensor.matmul(pB[:, 0:NSL], b.sb["g_onesr"][:], rown[:], start=True, stop=True)],
        waits=[(s_ev, s_ev.count)], inc=s_mm)
    mm_bc = s_mm.count
    for dc in range(NDC):
        sl_ = slice(dc * NSL, (dc + 1) * NSL)
        b.chain("vector", [
            lambda q, _sl=sl_: nc.vector.tensor_tensor(yT[:, _sl], yT[:, _sl], pA[:, 0:NSL], op=ALU.mult),
            lambda q, _sl=sl_: nc.vector.tensor_tensor(yT[:, _sl], yT[:, _sl], pB[:, 0:NSL], op=ALU.add),
            lambda q, _sl=sl_, _dc=dc: nc.vector.tensor_scalar(ynT[:, _sl], yT[:, _sl], lgc[:, _dc:_dc + 1],
                                                               lbc[:, _dc:_dc + 1], op0=ALU.mult, op1=ALU.add),
        ], waits=[(s_mm, mm_bc)])
    ev_yn = s_ev.count

    for nb_ in range(2):
        b.mm(pS1[:, :] if nb_ == 0 else pS2[:, :],
             [(ynT[:, kc * NSL:(kc + 1) * NSL],
               wvp[:, kc * 1024 + nb_ * NB: kc * 1024 + (nb_ + 1) * NB]) for kc in range(NDC)] +
             [(onesrb[:, 0:NSL], bvprowb[:, nb_ * NB:(nb_ + 1) * NB])],
             waits=[(s_ev, ev_yn)], inc=s_mm)
        mm_sv = s_mm.count
        b.on("vector", lambda q, _nb=nb_: nc.vector.tensor_copy(
            svtok[:, _nb * NB:(_nb + 1) * NB], (pS1 if _nb == 0 else pS2)[:, :]),
            waits=[(s_mm, mm_sv)], inc=s_ev)
    ev_sv = s_ev.count
    b.free("f_wstage", "f_y2", "f_yT", "f_gns", "f_gzs", "f_grs", "f_ght",
           "f_git", "f_siev", "f_sit", "f_xtok")
    outb = b.sbuf("f_outb", (PC, NDC * NB), BF16)
    xres = b.sbuf("f_xres", (PC, NB), F32)
    oev = b.sbuf("f_oev", (PC, NB), F32)

    rotF.seed(s_ev)
    for nb_ in range(S // NB):
        n0 = nb_ * NB
        for mc in range(NDC):
            dstp, rw_ = rotF.take()
            b.mm(dstp[:],
                 [(svtok[:NSL, mc * PC:(mc + 1) * PC], afm[:, n0:n0 + NB])],
                 waits=rw_ + [(s_ev, ev_sv)], inc=s_mm)
            mm_o = s_mm.count
            b.on("vector", lambda q, _mc=mc, _p=dstp: nc.vector.tensor_copy(
                outb[:, _mc * NB:(_mc + 1) * NB], _p[:]),
                waits=[(s_mm, mm_o)], inc=s_ev)
            rotF.settle(s_ev)
        ev_o = s_ev.count
        for mc in range(NDC):
            dstp, rw_ = rotF.take()
            b.mm(dstp[:],
                 [(wo[:, kc * 1024 + mc * PC: kc * 1024 + (mc + 1) * PC],
                   outb[:, kc * NB:(kc + 1) * NB]) for kc in range(NDC)],
                 waits=rw_ + [(s_ev, s_ev.count)], inc=s_mm)
            mm_f = s_mm.count
            b.dma("sync", xres[:], xA[mc * PC:(mc + 1) * PC, n0:n0 + NB], s_xr,
                  waits=[(s_ev, s_ev.count)])
            ld_x = s_xr.count

            b.chain("vector", [
                lambda q, _mc=mc, _p=dstp: nc.vector.tensor_scalar(
                    oev[:], _p[:], boc[:, _mc:_mc + 1], None, op0=ALU.add),
                lambda q: nc.vector.tensor_tensor(oev[:], oev[:], xres[:], op=ALU.add),
            ], waits=[(s_mm, mm_f), (s_xr, ld_x)])
            rotF.settle(s_ev)
            b.dma("gpsimd", io["x2"][mc * PC:(mc + 1) * PC, n0:n0 + NB], oev[:], s_st,
                  waits=[(s_ev, s_ev.count)])
            b.wait("vector", [(s_st, s_st.count)])
    b.wait("gpsimd", [(s_st, s_st.count)])
    b.free("f_oev", "f_xres", "f_outb",
           "f_ynT", "f_boc", "f_svtok", "f_lbc", "f_lgc", "f_rown", "f_rowv", "f_rowm",
           "f_atT", "f_at64", "f_sc64", "f_bvprowb", "f_bvprow", "f_bvrowb", "f_bvrow",
           "f_vtok", "f_bkc", "f_bqc", "f_kT", "f_qT", "f_snTf", "f_snT", "f_bhhc",
           "f_bihc", "f_dcol", "f_afm", "f_atok", "f_emsk", "f_etile", "f_ecol", "f_s8",
           "f_tau1", "f_invt", "f_b2c", "f_b1c", "f_hcol", "f_smcolf", "f_smcol",
           "f_slotT", "f_wo", "f_wvp", "f_wv", "f_wk", "f_wq", "f_w2", "f_w1s", "f_w1x",
           "f_xn", "p_T", "p_S2", "p_S1")




def build_l1_attn(nc, b, io, sems):
    s_ld, s_mm, s_ev, s_sc, s_st = sems["ld"], sems["mm"], sems["ev"], sems["sc"], sems["st"]
    s_gl, s_xr = sems["gl"], sems["xr"]
    s_qh, s_kh, s_vh = sems["qh"], sems["kh"], sems["vh"]
    s_mmav = sems["av"]
    H = 16
    pAV = b.psum("p_AV", (65, NB))
    pZ = b.psum("p_Z", (64, NB))
    pT = b.psum("p_T", (PC, PC), BF16)
    pA, pB, pC = b.sb["p_A"], b.sb["p_B"], b.sb["p_C"]
    rotT = Rot([pA, pB, pC, b.sb["p_R1"], b.sb["p_R2"]])
    idn = b.sb["g_idn"]
    trneg = b.sbuf("t_trneg", (PC, PC), BF16)
    iota8 = b.sbuf("t_iota8", (PC, 8), F32)

    ki = b.sbuf("t_ki", (64, S), BF16)
    wiT = b.sbuf("t_wiT", (PC, 32 * 4), F32)
    xn = b.sbuf("t_xn", (PC, NDC * S), BF16)

    xA = io["x2"]
    b.dma("sync", trneg[:], io["c_triu_neg"], s_xr,
          waits=[(s_ev, s_ev.count), (s_sc, s_sc.count), (s_mm, s_mm.count),
                 (sems["st"], sems["st"].count)])
    b.dma("sync", iota8[:], io["c_iota8"], s_xr)
    ld_tn = s_xr.count

    alloc_ln_region(b)
    emit_ln(nc, b, xA, xn, sems)
    free_ln_region(b)

    wst = b.sbuf("t_wst", (PC, 2 * NDC * PC), BF16)
    qc1 = b.sbuf("t_qc1", (PC, S), BF16)
    qc2 = b.sbuf("t_qc2", (PC, S), BF16)
    rsc = b.sbuf("t_rsc", (PC, S), BF16)
    rsc2 = b.sbuf("t_rsc2", (PC, S), BF16)
    cosP = b.sbuf("t_cosP", (PC, S), BF16)
    sinP = b.sbuf("t_sinP", (PC, S), BF16)
    wif = b.sbuf("t_wif", (4, S), F32)

    b.dma("sync", cosP[:], io["c_cosP"], s_xr,
          waits=[(s_ev, s_ev.count), (s_sc, s_sc.count), (s_mm, s_mm.count)])
    b.dma("sync", sinP[:], io["c_sinP"], s_xr)
    ld_cs = s_xr.count

    b.dma("gpsimd", *_ld3(io["WwiT"], wst[:, 0:NDC * 4], 4), s_gl,
          waits=[(s_ev, s_ev.count), (s_sc, s_sc.count), (s_mm, s_mm.count),
                 (s_st, s_st.count)])
    ld_wi = s_gl.count
    for nb_ in range(S // NB):
        n0 = nb_ * NB
        b.mm(pC[0:4, :],
             [(wst[:, kc * 4:(kc + 1) * 4], xn[:, kc * S + n0: kc * S + n0 + NB])
              for kc in range(NDC)],
             waits=[(s_gl, ld_wi), (s_ev, s_ev.count), (s_sc, s_sc.count)], inc=s_mm)
        mm_w = s_mm.count
        b.on("vector", lambda q, _n0=n0: nc.vector.tensor_copy(wif[:, _n0:_n0 + NB], pC[0:4, :]),
             waits=[(s_mm, mm_w)])
    idnf = b.sb["g_idnf"]
    for tc in range(32):
        b.on("tensor", lambda q, _tc=tc: nc.tensor.transpose(
            pC[:, 0:4], wif[:, _tc * PC:(_tc + 1) * PC], idnf[0:4, 0:4]),
            waits=[(s_ev, s_ev.count), (s_ld, s_ld.count)], inc=s_mm)
        mm_t = s_mm.count
        b.on("vector", lambda q, _tc=tc: nc.vector.tensor_copy(
            wiT[:, _tc * 4:(_tc + 1) * 4], pC[:, 0:4]),
            waits=[(s_mm, mm_t)])

    b.dma("gpsimd", *_ld3(io["WkiT"], wst[:, 0:NDC * 64], 64), s_gl,
          waits=[(s_mm, s_mm.count)])
    ld_ki = s_gl.count
    for nb_ in range(S // NB):
        n0 = nb_ * NB
        b.mm(pC[0:64, :],
             [(wst[:, kc * 64:(kc + 1) * 64], xn[:, kc * S + n0: kc * S + n0 + NB])
              for kc in range(NDC)],
             waits=[(s_gl, ld_ki), (s_ev, s_ev.count)], inc=s_mm)
        mm_k = s_mm.count
        b.on("vector", lambda q, _n0=n0: nc.vector.tensor_copy(ki[:, _n0:_n0 + NB], pC[0:64, :]),
             waits=[(s_mm, mm_k)])

    rotT.seed(s_ev)
    for ch in range(2):
        buf = ch % 2
        b.dma("gpsimd", *_ld3(io["WqiT"][:, ch * PC:(ch + 1) * PC], wst[:, buf * NDC * PC:(buf + 1) * NDC * PC], PC), s_gl,
              waits=[(s_mm, s_mm.count)])
        ld_q = s_gl.count
        for nb_ in range(S // NB):
            n0 = nb_ * NB
            dstp, rw_ = rotT.take()
            b.mm(dstp[:],
                 [(wst[:, buf * NDC * PC + kc * PC:buf * NDC * PC + (kc + 1) * PC],
                   xn[:, kc * S + n0: kc * S + n0 + NB]) for kc in range(NDC)],
                 waits=rw_ + [(s_gl, ld_q)], inc=s_mm)
            mm_q = s_mm.count
            b.on("vector", lambda q, _n0=n0, _p=dstp: nc.vector.tensor_copy(
                qc1[:, _n0:_n0 + NB], _p[:]),
                waits=[(s_mm, mm_q)])
            rotT.settle(s_ev)
        b.dma("gpsimd", io["qid"][ch * PC:(ch + 1) * PC, :], qc1[:], s_st,
              waits=[(s_ev, s_ev.count)])
        b.wait("vector", [(s_st, s_st.count)])

    for ti, (wname, dio) in enumerate([("WqT", "qd"), ("WkT", "kd")]):
        for c in range(4):
            for half, dst in [(0, qc1), (1, qc2)]:
                mc = c + 4 * half
                buf = (c * 2 + half) % 2
                b.dma("gpsimd", *_ld3(io[wname][:, mc * PC:(mc + 1) * PC],
                                      wst[:, buf * NDC * PC:(buf + 1) * NDC * PC], PC), s_gl,
                      waits=[(s_mm, s_mm.count)])
            ld_w = s_gl.count
            for half, dst in [(0, qc1), (1, qc2)]:
                buf = (c * 2 + half) % 2
                for nb_ in range(S // NB):
                    n0 = nb_ * NB
                    dstp, rw_ = rotT.take()
                    b.mm(dstp[:],
                         [(wst[:, buf * NDC * PC + kc * PC:buf * NDC * PC + (kc + 1) * PC],
                           xn[:, kc * S + n0: kc * S + n0 + NB]) for kc in range(NDC)],
                         waits=rw_ + [(s_gl, ld_w)], inc=s_mm)
                    mm_q = s_mm.count
                    b.on("vector", lambda q, _n0=n0, _p=dstp, _d=dst: nc.vector.tensor_copy(
                        _d[:, _n0:_n0 + NB], _p[:]),
                        waits=[(s_mm, mm_q)])
                    rotT.settle(s_ev)
            b.wait("vector", [(s_xr, ld_cs)])
            emit_rope(nc, b, qc1, qc2, rsc, rsc2, cosP, sinP)
            ev_rope = s_ev.count
            for h_loc in range(4):
                h = c * 4 + h_loc
                b.dma("gpsimd", io[dio][h, 0:32, :], qc1[h_loc * 32:(h_loc + 1) * 32, :], s_st,
                      waits=[(s_ev, ev_rope)])
                b.dma("gpsimd", io[dio][h, 32:64, :], qc2[h_loc * 32:(h_loc + 1) * 32, :], s_st,
                      waits=[(s_ev, ev_rope)])
            b.wait("vector", [(s_st, s_st.count)])

    vtile = qc1
    wv2 = b.sbuf("t_wv2", (PC, NDC * 1024), BF16)
    b.dma("gpsimd", *_ld3(io["WvT"], wv2[:], 1024), s_gl, waits=[(s_mm, s_mm.count)])
    ld_v = s_gl.count
    b.on("vector", lambda q: nc.vector.memset(vtile[:, 0:1040], 0.0))
    b.on("vector", lambda q: nc.vector.memset(
        vtile[:, 0:1040].rearrange("p (h d) -> p h d", d=65)[:, :, 64:65], 1.0))
    for sc in range(32):
        t0 = sc * PC
        for half in range(2):
            dstp, rw_ = rotT.take()
            b.mm(dstp[:],
                 [(xn[:, kc * S + t0: kc * S + t0 + PC],
                   wv2[:, kc * 1024 + half * NB: kc * 1024 + (half + 1) * NB]) for kc in range(NDC)],
                 waits=rw_ + [(s_gl, ld_v), (s_st, s_st.count)], inc=s_mm)
            mm_v = s_mm.count
            b.on("vector", lambda q, _h=half, _p=dstp: nc.vector.tensor_copy(
                vtile[:, 0:1040].rearrange("p (h d) -> p h d", d=65)[:, _h * 8:(_h + 1) * 8, 0:64],
                _p[:].rearrange("p (h d) -> p h d", d=64)),
                waits=[(s_mm, mm_v)])
            rotT.settle(s_ev)
        b.dma("gpsimd", io["vd"][t0:t0 + PC, :], vtile[:, 0:1040], s_st,
              waits=[(s_ev, s_ev.count)])
        b.wait("vector", [(s_st, s_st.count)])
    st_proj = s_st.count

    b.free("t_wv2", "t_wif", "t_sinP", "t_cosP", "t_rsc2", "t_rsc", "t_qc2", "t_qc1", "t_wst", "t_xn")
    idx = b.sbuf("u_idx", (PC, S), F32)
    scr32 = b.sbuf("u_scr32", (PC, S), F32)
    sgn0 = b.sbuf("u_sgn0", (PC, S), BF16)
    sgnB = b.sbuf("u_sgnB", (PC, S), BF16)
    msk = b.sbuf("u_msk", (PC, S), BF16)
    mTs = b.sbuf("u_mTs", (PC, 2 * 32 * NB), BF16)
    rl4 = b.sbuf("u_rl4", (PC, NB), BF16)
    qisb = b.sbuf("u_qisb", (64, 4 * NB), BF16)
    qh = b.sbuf("u_qh", (64, 2 * NB), BF16)
    kh = b.sbuf("u_kh", (64, 2 * S), BF16)
    vh = b.sbuf("u_vh", (PC, 2 * 32 * 65), BF16)
    un = b.sbuf("u_un", (PC, 4 * NB), BF16)
    acc = b.sbuf("u_acc", (65, NB), F32)
    hbuf = b.sbuf("u_hbuf", (64, 2 * NB), BF16)
    rhsb = b.sbuf("u_rhsb", (PC, NDC * NB), BF16)
    zrow = b.sbuf("u_zrow", (1, 2 * NB), F32)
    wo2 = b.sbuf("u_wo2", (PC, NDC * 1024), BF16)
    xres = b.sbuf("u_xres", (PC, NB), F32)
    oev = b.sbuf("u_oev", (PC, NB), F32)
    cols = {}
    for cn in ["poscnt", "nzcnt", "zcnt", "lo", "hi", "mid", "nmid", "ccol", "b01",
               "tmpa", "tmpb", "isP", "thr", "chi", "mcol", "tie", "quota", "rmin", "rmax",
               "snk"]:
        cols[cn] = b.sbuf("u_c_" + cn, (PC, 1), F32)

    b.dma("gpsimd", *_ld3(io["WoT"], wo2[:], 1024), s_gl,
          waits=[(s_mm, s_mm.count), (s_ev, s_ev.count), (s_sc, s_sc.count),
                 (s_st, s_st.count)])
    ld_wo = s_gl.count
    qh_slot = [0, 0]
    kh_slot = [0, 0]
    vh_slot = [0, 0]
    un_slot = [0, 0, 0, 0]
    z_guard = [0, 0]
    hb_st = [0, 0]
    rotT.seed(s_ev)

    def sel_gen(sp):
        t0 = sp * NB
        mT0 = (sp % 2) * 32 * NB
        for hi in range(4):
            b.dma("sync", qisb[:, hi * NB:(hi + 1) * NB],
                  io["qid"][hi * 64:(hi + 1) * 64, t0:t0 + NB], s_ld,
                  waits=([(s_st, s_st.count), (s_ev, s_ev.count), (s_mm, s_mm.count)]
                         if hi == 0 else ()))
        ld_qi = s_ld.count
        b.on("vector", lambda q, _sp=sp: nc.vector.memset(
            mTs[:, mT0 + _sp * 4 * NB: mT0 + (_sp * 4 + 4) * NB], -98304.0),
            waits=[(s_mm, s_mm.count)])
        yield
        if True:
            for tl in range(4):
                tcg = sp * 4 + tl
                sr = (tcg + 1) * PC
                nw = (sr + NB - 1) // NB
                first_mac = True
                for hi in range(4):
                    lq = qisb[:, hi * NB + tl * PC: hi * NB + (tl + 1) * PC]
                    for w in range(nw):
                        wid = min(NB, sr - w * NB)
                        dstp, rw_ = rotT.take()
                        b.on("tensor", lambda q, _l=lq, _w=w, _wd=wid, _p=dstp: nc.tensor.matmul(
                            _p[:, 0:_wd], _l, ki[:, _w * NB:_w * NB + _wd], start=True, stop=True),
                            waits=rw_ + [(s_ld, ld_qi)], inc=s_mm)
                        mm_d = s_mm.count
                        b.on("scalar", lambda q, _wd=wid, _p=dstp: nc.scalar.activation(
                            rl4[:, 0:_wd], _p[:, 0:_wd], AF.Relu),
                            waits=[(s_mm, mm_d), (s_ev, s_ev.count)])
                        sc_r = s_sc.count
                        rotT.settle(s_sc)
                        if first_mac:
                            b.on("vector", lambda q, _w=w, _wd=wid, _tc=tcg, _hi=hi: nc.vector.tensor_scalar(
                                idx[:, _w * NB:_w * NB + _wd], rl4[:, 0:_wd],
                                wiT[:, _tc * 4 + _hi:_tc * 4 + _hi + 1], None, op0=ALU.mult),
                                waits=[(s_sc, sc_r)])
                        else:
                            b.on("vector", lambda q, _w=w, _wd=wid, _tc=tcg, _hi=hi: nc.vector.scalar_tensor_tensor(
                                idx[:, _w * NB:_w * NB + _wd], rl4[:, 0:_wd],
                                wiT[:, _tc * 4 + _hi:_tc * 4 + _hi + 1],
                                idx[:, _w * NB:_w * NB + _wd], op0=ALU.mult, op1=ALU.add),
                                waits=[(s_sc, sc_r)])
                    if hi == 0:
                        first_mac = False
                    yield
                b.on("vector", lambda q, _tc=tcg: nc.vector.tensor_tensor(
                    idx[:, _tc * PC:(_tc + 1) * PC], idx[:, _tc * PC:(_tc + 1) * PC],
                    trneg[:], op=ALU.add), waits=[(s_xr, ld_tn)])
                b.on("scalar", lambda q, _sr=sr: nc.scalar.activation(
                    sgn0[:, 0:_sr], idx[:, 0:_sr], AF.Sign, accum_out=cols["poscnt"][:]),
                    waits=[(s_ev, s_ev.count)])
                b.on("scalar", lambda q, _sr=sr: nc.scalar.activation(
                    sgnB[:, 0:_sr], sgn0[:, 0:_sr], AF.Abs, accum_out=cols["nzcnt"][:]))
                b.on("scalar", lambda q: nc.scalar.copy(cols["snk"][:], cols["poscnt"][:]))
                b.on("scalar", lambda q: nc.scalar.copy(cols["snk"][:], cols["nzcnt"][:]))
                sc_cnt = s_sc.count
                yield
                b.chain("vector", [
                    lambda q: nc.vector.tensor_tensor(cols["tmpa"][:], cols["poscnt"][:], cols["nzcnt"][:], op=ALU.add),
                    lambda q: nc.vector.tensor_scalar(cols["poscnt"][:], cols["tmpa"][:], 0.5, None, op0=ALU.mult),
                    lambda q, _sr=sr: nc.vector.tensor_scalar(cols["zcnt"][:], cols["nzcnt"][:], -1.0, float(_sr), op0=ALU.mult, op1=ALU.add),
                    lambda q, _sr=sr: nc.vector.tensor_reduce(cols["rmin"][:], idx[:, 0:_sr - PC], axis=AX.X, op=ALU.min),
                    lambda q, _sr=sr: nc.vector.tensor_reduce(cols["rmax"][:], idx[:, 0:_sr], axis=AX.X, op=ALU.max),
                    lambda q: nc.vector.tensor_scalar(cols["isP"][:], cols["poscnt"][:], 512.0, None, op0=ALU.is_ge),
                    lambda q: nc.vector.tensor_scalar(cols["tmpa"][:], cols["rmin"][:], -1e-3, None, op0=ALU.add),
                    lambda q: nc.vector.tensor_tensor(cols["tmpb"][:], cols["tmpa"][:], cols["isP"][:], op=ALU.mult),
                    lambda q: nc.vector.tensor_tensor(cols["lo"][:], cols["tmpa"][:], cols["tmpb"][:], op=ALU.subtract),
                    lambda q: nc.vector.tensor_scalar(cols["tmpb"][:], cols["isP"][:], 2.0 ** -30, None, op0=ALU.mult),
                    lambda q: nc.vector.tensor_tensor(cols["lo"][:], cols["lo"][:], cols["tmpb"][:], op=ALU.add),
                    lambda q: nc.vector.tensor_scalar(cols["tmpa"][:], cols["rmax"][:], 1.0, None, op0=ALU.add),
                    lambda q: nc.vector.tensor_tensor(cols["tmpa"][:], cols["tmpa"][:], cols["isP"][:], op=ALU.mult),
                    lambda q: nc.vector.tensor_scalar(cols["tmpb"][:], cols["isP"][:], -1.0, 1.0, op0=ALU.mult, op1=ALU.add),
                    lambda q: nc.vector.tensor_scalar(cols["tmpb"][:], cols["tmpb"][:], 2.0 ** -30, None, op0=ALU.mult),
                    lambda q: nc.vector.tensor_tensor(cols["hi"][:], cols["tmpa"][:], cols["tmpb"][:], op=ALU.add),
                ], waits=[(s_sc, sc_cnt)])
                yield
                for it in range(18):
                    b.chain("vector", [
                        lambda q: nc.vector.tensor_tensor(cols["mid"][:], cols["lo"][:], cols["hi"][:], op=ALU.add),
                        lambda q: nc.vector.tensor_scalar(cols["mid"][:], cols["mid"][:], 0.5, None, op0=ALU.mult),
                        lambda q, _sr=sr: nc.vector.tensor_scalar(sgnB[:, 0:_sr], idx[:, 0:_sr], cols["mid"][:], 0.0, op0=ALU.is_ge, op1=ALU.add, accum_out=cols["ccol"][:]),
                        lambda q: nc.vector.tensor_scalar(cols["b01"][:], cols["ccol"][:], 512.0, None, op0=ALU.is_ge),
                        lambda q: nc.vector.tensor_tensor(cols["tmpa"][:], cols["mid"][:], cols["lo"][:], op=ALU.subtract),
                        lambda q: nc.vector.scalar_tensor_tensor(cols["lo"][:], cols["tmpa"][:], cols["b01"][:], cols["lo"][:], op0=ALU.mult, op1=ALU.add),
                        lambda q: nc.vector.tensor_tensor(cols["tmpa"][:], cols["mid"][:], cols["hi"][:], op=ALU.subtract),
                        lambda q: nc.vector.tensor_scalar(cols["tmpb"][:], cols["b01"][:], -1.0, 1.0, op0=ALU.mult, op1=ALU.add),
                        lambda q: nc.vector.scalar_tensor_tensor(cols["hi"][:], cols["tmpa"][:], cols["tmpb"][:], cols["hi"][:], op0=ALU.mult, op1=ALU.add),
                    ])
                    if it % 2 == 1:
                        yield
                b.chain("vector", [
                    lambda q, _sr=sr: nc.vector.tensor_scalar(sgnB[:, 0:_sr], idx[:, 0:_sr], cols["hi"][:], 0.0, op0=ALU.is_ge, op1=ALU.add, accum_out=cols["chi"][:]),
                    lambda q: nc.vector.tensor_scalar(cols["mcol"][:], cols["chi"][:], -1.0, 512.0, op0=ALU.mult, op1=ALU.add),
                    lambda q: nc.vector.tensor_scalar(cols["mcol"][:], cols["mcol"][:], 1.0, 8.0, op0=ALU.max, op1=ALU.min),
                    lambda q: nc.vector.tensor_scalar(cols["mcol"][:], cols["mcol"][:], -1.0, None, op0=ALU.add),
                    lambda q, _sr=sr: nc.vector.tensor_scalar(sgnB[:, 0:_sr], idx[:, 0:_sr], cols["hi"][:], -1e6, op0=ALU.is_ge, op1=ALU.mult),
                    lambda q, _sr=sr: nc.vector.tensor_tensor(scr32[:, 0:_sr], idx[:, 0:_sr], sgnB[:, 0:_sr], op=ALU.add),
                    lambda q, _sr=sr: nc.vector.max(rl4[:, 0:8], scr32[:, 0:_sr]),
                    lambda q: nc.vector.tensor_scalar(sgnB[:, 0:8], iota8[:], cols["mcol"][:], None, op0=ALU.is_equal),
                    lambda q: nc.vector.tensor_tensor(sgnB[:, 8:16], rl4[:, 0:8], sgnB[:, 0:8], op=ALU.mult),
                    lambda q: nc.vector.tensor_reduce(cols["thr"][:], sgnB[:, 8:16], axis=AX.X, op=ALU.add),
                    lambda q: nc.vector.tensor_scalar(cols["tie"][:], cols["poscnt"][:], 512.0, None, op0=ALU.is_lt),
                    lambda q: nc.vector.tensor_tensor(cols["tmpa"][:], cols["poscnt"][:], cols["zcnt"][:], op=ALU.add),
                    lambda q: nc.vector.tensor_scalar(cols["tmpa"][:], cols["tmpa"][:], 512.0, None, op0=ALU.is_ge),
                    lambda q: nc.vector.tensor_tensor(cols["tie"][:], cols["tie"][:], cols["tmpa"][:], op=ALU.mult),
                    lambda q: nc.vector.tensor_tensor(cols["tmpa"][:], cols["thr"][:], cols["tie"][:], op=ALU.mult),
                    lambda q: nc.vector.tensor_tensor(cols["thr"][:], cols["thr"][:], cols["tmpa"][:], op=ALU.subtract),
                    lambda q: nc.vector.tensor_scalar(cols["tmpa"][:], cols["tie"][:], 2.0 ** -100, None, op0=ALU.mult),
                    lambda q: nc.vector.tensor_tensor(cols["thr"][:], cols["thr"][:], cols["tmpa"][:], op=ALU.add),
                    lambda q: nc.vector.tensor_scalar(cols["quota"][:], cols["poscnt"][:], -1.0, 512.0, op0=ALU.mult, op1=ALU.add),
                    lambda q: nc.vector.tensor_tensor(cols["quota"][:], cols["quota"][:], cols["tie"][:], op=ALU.mult),
                    lambda q: nc.vector.tensor_scalar(cols["tmpa"][:], cols["tie"][:], -1.0, 1.0, op0=ALU.mult, op1=ALU.add),
                    lambda q: nc.vector.tensor_tensor(cols["quota"][:], cols["quota"][:], cols["tmpa"][:], op=ALU.subtract),
                    lambda q, _sr=sr: nc.vector.tensor_scalar(sgnB[:, 0:_sr], sgn0[:, 0:_sr], 0.0, None, op0=ALU.is_equal),
                    lambda q, _sr=sr: nc.vector.tensor_tensor_scan(scr32[:, 0:_sr], sgnB[:, 0:_sr], sgnB[:, 0:_sr], 0.0, op0=ALU.add, op1=ALU.bypass),
                    lambda q, _sr=sr: nc.vector.tensor_scalar(scr32[:, 0:_sr], scr32[:, 0:_sr], cols["quota"][:], None, op0=ALU.is_le),
                    lambda q, _sr=sr: nc.vector.tensor_tensor(sgnB[:, 0:_sr], sgnB[:, 0:_sr], scr32[:, 0:_sr], op=ALU.mult),
                    lambda q, _sr=sr: nc.vector.tensor_scalar(msk[:, 0:_sr], idx[:, 0:_sr], cols["thr"][:], None, op0=ALU.is_ge),
                    lambda q, _sr=sr: nc.vector.tensor_tensor(msk[:, 0:_sr], msk[:, 0:_sr], sgnB[:, 0:_sr], op=ALU.add),
                    lambda q, _sr=sr: nc.vector.tensor_scalar(msk[:, 0:_sr], msk[:, 0:_sr], 98304.0, -98304.0, op0=ALU.mult, op1=ALU.add),
                ], waits=[(s_xr, ld_tn)])
                ev_msk = s_ev.count
                yield
                for sc in range(tcg + 1):
                    dstp2, rw2 = rotT.take()
                    b.on("tensor", lambda q, _sc=sc, _p=dstp2: nc.tensor.matmul(
                        _p[0:PC, 0:PC], msk[:, _sc * PC:(_sc + 1) * PC], idn[:],
                        start=True, stop=True),
                        waits=rw2 + [(s_ev, ev_msk)], inc=s_mm)
                    mm_mt = s_mm.count
                    b.on("scalar", lambda q, _sc=sc, _tl=tl, _p=dstp2: nc.scalar.copy(
                        mTs[:, mT0 + _sc * NB + _tl * PC: mT0 + _sc * NB + (_tl + 1) * PC],
                        _p[0:PC, 0:PC]),
                        waits=[(s_mm, mm_mt)])
                    rotT.settle(s_sc)
                    if sc % 8 == 7:
                        yield
        yield



    sel_done = {}
    gen = None
    for sp in range(8):
        t0 = sp * NB
        srng = min(S, (sp + 1) * NB)
        nsc = srng // PC
        if sp == 0:
            gen = sel_gen(1)
        for h in range(H):
            hs = h % 2
            qhs = qh[:, hs * NB:(hs + 1) * NB]
            khs = kh[:, hs * S:hs * S + srng]
            vhs = vh[:, hs * 32 * 65:hs * 32 * 65 + nsc * 65]
            b.dma("sync", qhs, io["qd"][h, :, t0:t0 + NB], s_qh,
                  waits=[(s_mm, qh_slot[hs]), (s_qh, s_qh.count)]
                        + ([(s_ev, s_ev.count), (s_st, s_st.count),
                            (s_mm, s_mm.count)] if (h < 2) else []))
            ld_qh = s_qh.count
            b.dma("sync", khs, io["kd"][h, :, 0:srng], s_kh,
                  waits=[(s_mm, kh_slot[hs]), (s_kh, s_kh.count)])
            ld_kh = s_kh.count
            b.dma("sync", vhs.rearrange("p (c f) -> p c f", f=65),
                  io["vd"][0:srng, h * 65:(h + 1) * 65].rearrange("(c p) f -> p c f", p=PC),
                  s_vh, waits=[(s_mmav, vh_slot[hs]), (s_vh, s_vh.count)])
            ld_vh = s_vh.count
            for g0 in range(0, nsc, 4):
                grp = list(range(g0, min(g0 + 4, nsc)))
                ginfo = []
                for sc in grp:
                    dstp, rw_ = rotT.take()
                    if sp > 0:
                        b.on("tensor", lambda q, _sc=sc, _p=dstp, _k=khs, _q=qhs: nc.tensor.matmul(
                            _p[:], _k[:, _sc * PC:(_sc + 1) * PC], _q, start=True, stop=False),
                            waits=rw_ + [(s_kh, ld_kh), (s_qh, ld_qh)],
                            inc=s_mm)
                        sdv, sds = sel_done.get(sp, (0, 0))
                        b.on("tensor", lambda q, _sc=sc, _p=dstp, _m0=(sp % 2) * 32 * NB: nc.tensor.matmul(
                            _p[:], idn[:], mTs[:, _m0 + _sc * NB: _m0 + (_sc + 1) * NB],
                            start=False, stop=True),
                            waits=[(s_ev, sdv), (s_sc, sds)], inc=s_mm)
                    else:
                        b.on("tensor", lambda q, _sc=sc, _p=dstp, _k=khs, _q=qhs: nc.tensor.matmul(
                            _p[:], _k[:, _sc * PC:(_sc + 1) * PC], _q, start=True, stop=True),
                            waits=rw_ + [(s_kh, ld_kh), (s_qh, ld_qh)],
                            inc=s_mm)
                    ginfo.append((sc, dstp, s_mm.count))
                ubs = {}
                for sc, dstp, mm_s in ginfo:
                    usl = sc % 4
                    ub = un[:, usl * NB:(usl + 1) * NB]
                    ubs[sc] = ub
                    b.on("scalar", lambda q, _p=dstp, _u=ub: nc.scalar.activation(
                        _u, _p[:], AF.Exp, scale=0.125),
                        waits=[(s_mm, mm_s), (s_mmav, un_slot[usl])])
                    rotT.settle(s_sc)
                ng = len(ginfo)
                for i, (sc, dstp, mm_s) in enumerate(ginfo):
                    avw = [(s_sc, s_sc.count - (ng - 1 - i)), (s_vh, ld_vh)]
                    if sc == 0:
                        avw = avw + [(s_ev, s_ev.count)]
                    usl = sc % 4
                    b.on("tensor", lambda q, _sc=sc, _u=ubs[sc], _first=(sc == 0), _last=(sc == nsc - 1), _v=vhs: nc.tensor.matmul(
                        pAV[:], _v[:, _sc * 65:(_sc + 1) * 65], _u,
                        start=_first, stop=_last),
                        waits=avw, inc=s_mmav)
                    un_slot[usl] = s_mmav.count
            qh_slot[hs] = s_mm.count
            kh_slot[hs] = s_mm.count
            vh_slot[hs] = s_mmav.count
            if gen is not None:
                for _ in range(7):
                    try:
                        next(gen)
                    except StopIteration:
                        gen = None
                        break
            mm_av = s_mmav.count
            hsl = h % 2
            zs = zrow[:, hsl * NB:(hsl + 1) * NB]
            hb = hbuf[:, hsl * NB:(hsl + 1) * NB]
            b.chain("vector", [
                lambda q, _z=zs: nc.vector.tensor_copy(_z, pAV[64:65, :]),
                lambda q, _z=zs: nc.vector.reciprocal(_z, _z),
            ], waits=[(s_mmav, mm_av), (s_mm, z_guard[hsl])])
            ev_z = s_ev.count
            b.mm(pZ[:], [(b.sb["g_onesr"][:, 0:64], zs)],
                 waits=[(s_ev, ev_z)], inc=s_mm)
            mm_z = s_mm.count
            z_guard[hsl] = mm_z
            b.on("vector", lambda q: nc.vector.tensor_copy(acc[0:64, :], pAV[0:64, :]),
                 waits=[(s_mm, mm_z)])
            b.on("vector", lambda q, _h=hb: nc.vector.tensor_tensor(_h, acc[0:64, :], pZ[:], op=ALU.mult),
                 waits=[(s_st, hb_st[hsl])])
            ev_h = s_ev.count
            b.dma("gpsimd", rhsb[(h % 2) * 64:(h % 2) * 64 + 64, (h // 2) * NB:(h // 2 + 1) * NB],
                  hb, s_st, waits=[(s_ev, ev_h), (s_st, s_st.count)])
            hb_st[hsl] = s_st.count
        if gen is not None:
            for _ in gen:
                pass
            gen = None
        if sp + 1 < 8:
            sel_done[sp + 1] = (s_ev.count, s_sc.count)
            gen = sel_gen(sp + 2) if sp + 2 < 8 else None
        st_heads = s_st.count
        for mc in range(NDC):
            dstp, rw_ = rotT.take()
            b.mm(dstp[:],
                 [(wo2[:, kc * 1024 + mc * PC: kc * 1024 + (mc + 1) * PC],
                   rhsb[:, kc * NB:(kc + 1) * NB]) for kc in range(NDC)],
                 waits=rw_ + [(s_st, st_heads), (s_gl, ld_wo)], inc=s_mm)
            mm_o = s_mm.count
            b.dma("sync", xres[:], xA[mc * PC:(mc + 1) * PC, t0:t0 + NB], s_xr,
                  waits=[(s_ev, s_ev.count)])
            ld_x = s_xr.count
            b.on("vector", lambda q, _p=dstp: nc.vector.tensor_tensor(oev[:], _p[:], xres[:], op=ALU.add),
                 waits=[(s_mm, mm_o), (s_xr, ld_x)])
            rotT.settle(s_ev)
            b.dma("gpsimd", io["x3"][mc * PC:(mc + 1) * PC, t0:t0 + NB], oev[:], s_st,
                  waits=[(s_ev, s_ev.count)])
            b.wait("vector", [(s_st, s_st.count)])
    b.wait("gpsimd", [(s_st, s_st.count)])
    for cn in reversed(["poscnt", "nzcnt", "zcnt", "lo", "hi", "mid", "nmid", "ccol",
                        "b01", "tmpa", "tmpb", "isP", "thr", "chi", "mcol", "tie",
                        "quota", "rmin", "rmax", "snk"]):
        b.free("u_c_" + cn)
    b.free("u_oev", "u_xres", "u_wo2", "u_zrow", "u_rhsb", "u_hbuf", "u_acc", "u_un",
           "u_vh", "u_kh", "u_qh", "u_qisb", "u_rl4", "u_mTs", "u_msk", "u_sgnB",
           "u_sgn0", "u_scr32", "u_idx",
           "t_wiT", "t_ki", "t_iota8", "t_trneg", "p_T", "p_Z", "p_AV")


def emit_rope(nc, b, q1, q2, tmp, tmp2, cosP, sinP):
    nv = nc.vector
    ALU_ = ALU
    b.chain("vector", [
        lambda q: nv.tensor_tensor(tmp[:], q2[:], sinP[:], op=ALU_.mult),
        lambda q: nv.tensor_tensor(tmp2[:], q1[:], sinP[:], op=ALU_.mult),
        lambda q: nv.tensor_tensor(q2[:], q2[:], cosP[:], op=ALU_.mult),
        lambda q: nv.tensor_tensor(q1[:], q1[:], cosP[:], op=ALU_.mult),
        lambda q: nv.tensor_tensor(q1[:], q1[:], tmp[:], op=ALU_.subtract),
        lambda q: nv.tensor_tensor(q2[:], q2[:], tmp2[:], op=ALU_.add),
    ])

def build_l1_ffn(nc, b, io, sems, core_id=None):
    s_ld, s_mm, s_ev, s_sc, s_st = sems["ld"], sems["mm"], sems["ev"], sems["sc"], sems["st"]
    s_gl, s_xr, s_cc, s_gr = sems["gl"], sems["xr"], sems["cc"], sems["gr"]
    NU = 12
    CAP = 5120.0
    pTw = b.psum("p_T", (PC, NB))
    pT = pTw[:, 0:PC]
    pD = b.psum("p_D", (PC, NB))
    pE = b.psum("p_E", (PC, NB))
    pA, pB, pC = b.sb["p_A"], b.sb["p_B"], b.sb["p_C"]
    pR1, pR2 = b.sb["p_R1"], b.sb["p_R2"]
    pR1 = pR1[0:1, :]
    pR2 = pR2[0:1, :]
    rot4 = Rot([pA, pB, pD, pE, b.sb["p_R1"], b.sb["p_R2"]])
    rotC = Rot([pC, pTw])
    onesrb = b.sb["g_onesrb"]
    idnf = b.sb["g_idnf"]
    tric = b.sbuf("m_tric", (PC, PC), BF16)

    xn = b.sbuf("m_xn", (PC, NDC * S), BF16)
    gwT = b.sbuf("m_gwT", (PC, NDC * 9), BF16)
    rlf = b.sbuf("m_rlf", (9, S), F32)
    fltok = b.sbuf("m_fltok", (PC, 32 * 9), F32)
    rntok = b.sbuf("m_rntok", (PC, 32 * 8), F32)
    mtok = b.sbuf("m_mtok", (PC, 32 * 8), BF16)
    mcaptok = b.sbuf("m_mcap", (PC, 32 * 8), BF16)
    glwtok = b.sbuf("m_glwtok", (PC, 32 * 9), F32)
    glwfm = b.sbuf("m_glwfm", (9, S), F32)
    srt8 = b.sbuf("m_srt8", (PC, 8), F32)
    absdl = b.sbuf("m_absdl", (PC, 8), F32)
    absrt = b.sbuf("m_absrt", (PC, 8), F32)
    dsc = b.sbuf("m_dsc", (PC, 8), F32)
    fsc = b.sbuf("m_fsc", (PC, 8), F32)
    jms = b.sbuf("m_jms", (PC, 8), F32)
    als = b.sbuf("m_als", (PC, 8), F32)
    ets = b.sbuf("m_ets", (PC, 8), F32)
    rwt = b.sbuf("m_rwt", (PC, 8), F32)
    m9 = b.sbuf("m_m9", (PC, 9), F32)
    e9 = b.sbuf("m_e9", (PC, 9), F32)
    ncol = b.sbuf("m_ncol", (PC, 1), F32)
    zcol = b.sbuf("m_zcol", (PC, 1), F32)
    kcol = b.sbuf("m_kcol", (PC, 1), F32)
    wcol = b.sbuf("m_wcol", (PC, 1), F32)
    runrow = b.sbuf("m_runrow", (1, 8), F32)
    cntrow = b.sbuf("m_cntrow", (1, 8), F32)
    ccs = b.sbuf("m_ccs", (PC, PC), F32)
    ccsel = b.sbuf("m_ccsel", (1, PC), F32)
    prefc = b.sbuf("m_prefc", (8, 1), F32)
    ccld = b.sbuf("m_ccld", (8, 8), F32)

    xA = io["x3"]
    b.dma("sync", tric[:], io["c_tric"], s_ld,
          waits=[(s_ev, s_ev.count), (s_sc, s_sc.count), (s_mm, s_mm.count),
                 (s_st, s_st.count), (s_ld, s_ld.count)])
    b.dma("sync", ccsel[:], io["cc_sel"], s_ld)
    b.dma("sync", prefc[:], io["cc_pref"], s_ld)
    b.dma("gpsimd", *_ld3(io["gwT"], gwT[:], 9), s_gl,
          waits=[(s_st, s_st.count), (s_ev, s_ev.count), (s_sc, s_sc.count),
                 (s_mm, s_mm.count)])

    alloc_ln_region(b)
    emit_ln(nc, b, xA, xn, sems)
    free_ln_region(b)

    for nb_ in range(S // NB):
        n0 = nb_ * NB
        b.mm(pC[0:9, :],
             [(gwT[:, kc * 9:(kc + 1) * 9], xn[:, kc * S + n0: kc * S + n0 + NB])
              for kc in range(NDC)],
             waits=[(s_ev, s_ev.count), (s_gl, s_gl.count)], inc=s_mm)
        mm_l = s_mm.count
        b.on("vector", lambda q, _n0=n0: nc.vector.tensor_copy(rlf[:, _n0:_n0 + NB], pC[0:9, :]),
             waits=[(s_mm, mm_l)])

    for tc in range(32):
        t0 = tc * PC
        b.on("tensor", lambda q, _t0=t0: nc.tensor.transpose(
            pT[:, 0:9], rlf[:, _t0:_t0 + PC], idnf[0:9, 0:9]),
            waits=[(s_ev, s_ev.count)], inc=s_mm)
        mm_t = s_mm.count
        b.on("vector", lambda q, _tc=tc: nc.vector.tensor_copy(
            fltok[:, _tc * 9:(_tc + 1) * 9], pT[:, 0:9]),
            waits=[(s_mm, mm_t)])
        dl = fltok[:, tc * 9: tc * 9 + 8]
        b.chain("vector", [
            lambda q, _dl=dl: nc.vector.max(srt8[:], _dl),
            lambda q: nc.vector.memset(rwt[:], 0.0),
            lambda q: nc.vector.tensor_scalar(ncol[:], srt8[:, 0:1], -1.0, None, op0=ALU.mult),
        ])
        b.on("scalar", lambda q: nc.scalar.activation(ets[:], srt8[:], AF.Exp, bias=ncol[:]),
             waits=[(s_ev, s_ev.count)])
        b.chain("vector", [
            lambda q: nc.vector.tensor_reduce(zcol[:], ets[:], axis=AX.X, op=ALU.add),
            lambda q: nc.vector.reciprocal(zcol[:], zcol[:]),
            lambda q: nc.vector.tensor_scalar(ets[:], ets[:], zcol[:], None, op0=ALU.mult),
            lambda q: nc.vector.tensor_tensor_scan(ets[:], ets[:], ets[:], 0.0, op0=ALU.add, op1=ALU.bypass),
            lambda q: nc.vector.tensor_scalar(ets[:], ets[:], 0.75, None, op0=ALU.is_lt),
            lambda q: nc.vector.tensor_reduce(kcol[:], ets[:], axis=AX.X, op=ALU.add),
            lambda q: nc.vector.tensor_scalar(kcol[:], kcol[:], 1.0, 4.0, op0=ALU.add, op1=ALU.min),
        ], waits=[(s_sc, s_sc.count)])
        b.on("scalar", lambda q, _dl=dl: nc.scalar.activation(absdl[:], _dl, AF.Abs),
             waits=[(s_ev, s_ev.count)])
        b.on("scalar", lambda q: nc.scalar.activation(absrt[:], srt8[:], AF.Abs),
             waits=[(s_ev, s_ev.count)])
        sc_abs = s_sc.count
        b.wait("vector", [(s_sc, sc_abs)])
        for j in range(4):
            thr = srt8[:, j:j + 1]
            b.chain("vector", [
                lambda q, _dl=dl, _t=thr: nc.vector.tensor_scalar(dsc[:], _dl, _t, -1.0, op0=ALU.subtract, op1=ALU.mult),
                lambda q, _j=j: nc.vector.tensor_scalar(fsc[:], absdl[:], absrt[:, _j:_j + 1], 0.2, op0=ALU.max, op1=ALU.mult),
                lambda q: nc.vector.tensor_tensor(jms[:], dsc[:], fsc[:], op=ALU.is_gt),
                lambda q, _dl=dl, _t=thr: nc.vector.tensor_scalar(als[:], _dl, _t, None, op0=ALU.is_le),
                lambda q: nc.vector.tensor_tensor(als[:], als[:], jms[:], op=ALU.subtract),
                lambda q, _t=thr: nc.vector.tensor_scalar(ncol[:], _t, -1.0, None, op0=ALU.mult),
            ])
            b.on("scalar", lambda q, _dl=dl: nc.scalar.activation(e9[:, 0:8], _dl, AF.Exp, bias=ncol[:]),
                 waits=[(s_ev, s_ev.count)])
            b.chain("vector", [
                lambda q: nc.vector.tensor_tensor(e9[:, 0:8], e9[:, 0:8], als[:], op=ALU.mult),
                lambda q: nc.vector.tensor_reduce(zcol[:], e9[:, 0:8], axis=AX.X, op=ALU.add),
                lambda q: nc.vector.reciprocal(zcol[:], zcol[:]),
                lambda q, _j=j: nc.vector.tensor_scalar(wcol[:], kcol[:], float(_j), None, op0=ALU.is_gt),
                lambda q: nc.vector.tensor_tensor(wcol[:], wcol[:], zcol[:], op=ALU.mult),
                lambda q, _dl=dl, _t=thr: nc.vector.tensor_scalar(ets[:], _dl, _t, None, op0=ALU.is_equal),
                lambda q: nc.vector.scalar_tensor_tensor(rwt[:], ets[:], wcol[:], rwt[:], op0=ALU.mult, op1=ALU.add),
            ], waits=[(s_sc, s_sc.count)])
        b.chain("vector", [
            lambda q: nc.vector.tensor_reduce(zcol[:], rwt[:], axis=AX.X, op=ALU.add),
            lambda q: nc.vector.reciprocal(zcol[:], zcol[:]),
            lambda q, _tc=tc: nc.vector.tensor_scalar(rntok[:, _tc * 8:(_tc + 1) * 8], rwt[:], zcol[:], None, op0=ALU.mult),
            lambda q, _tc=tc: nc.vector.tensor_scalar(mtok[:, _tc * 8:(_tc + 1) * 8],
                                                      rntok[:, _tc * 8:(_tc + 1) * 8], 0.0, None, op0=ALU.is_gt),
        ])
    ev_routing = s_ev.count

    for tc in range(32):
        b.on("tensor", lambda q, _tc=tc: nc.tensor.matmul(
            pR1[:, 0:8], b.sb["g_onescolb"][:], mtok[:, _tc * 8:(_tc + 1) * 8],
            start=(_tc == 0), stop=(_tc == 31)),
            waits=[(s_ev, ev_routing)] if tc == 0 else (),
            inc=(s_mm if tc == 31 else None))
    mm_cnt = s_mm.count
    b.on("vector", lambda q: nc.vector.tensor_copy(cntrow[:], pR1[:, 0:8]),
         waits=[(s_mm, mm_cnt)])
    ev_cnt = s_ev.count
    b.on("vector", lambda q: nc.vector.memset(ccs[:], 0.0))
    b.mm(pT[:, 0:8], [(ccsel[:], cntrow[:])], waits=[(s_ev, s_ev.count)], inc=s_mm)
    b.on("vector", lambda q: nc.vector.tensor_copy(ccs[:, 0:8], pT[:, 0:8]),
         waits=[(s_mm, s_mm.count)])
    b.dma("gpsimd", io["cc_in"], ccs[:], s_st, waits=[(s_ev, s_ev.count)])

    def cc_fn(q):
        return q.collective_compute(
            "AllReduce", mybir.AluOpType.add,
            replica_groups=[list(range(NCORES))],
            ins=[io["cc_in"].opt()],
            outs=[io["cc_out"].opt()],
        )
    b.on("gpsimd", cc_fn, waits=[(s_st, s_st.count)], inc=s_cc, inc_n=1)
    b.dma("sync", ccld[:], io["cc_out"][0:8, 0:8], s_ld, waits=[(s_cc, s_cc.count)])
    b.mm(pR2[0:1, 0:8], [(prefc[:], ccld[:])], waits=[(s_ld, s_ld.count)], inc=s_mm)
    b.on("vector", lambda q: nc.vector.tensor_copy(runrow[:], pR2[0:1, 0:8]),
         waits=[(s_mm, s_mm.count)])

    for tc in range(32):
        mslice = mtok[:, tc * 8:(tc + 1) * 8]
        b.mm(pT[:, 0:8], [(tric[:], mslice)], waits=[(s_ev, s_ev.count)], inc=s_mm)
        mm_tri = s_mm.count
        b.mm(pC[:, 0:8], [(b.sb["g_onesr"][:], runrow[:])], inc=s_mm)
        mm_off = s_mm.count
        b.chain("vector", [
            lambda q: nc.vector.tensor_tensor(m9[:, 0:8], pT[:, 0:8], pC[:, 0:8], op=ALU.add) if False else nc.vector.tensor_copy(m9[:, 0:8], pT[:, 0:8]),
            lambda q: nc.vector.tensor_tensor(m9[:, 0:8], m9[:, 0:8], pC[:, 0:8], op=ALU.add),
            lambda q: nc.vector.tensor_scalar(m9[:, 0:8], m9[:, 0:8], CAP, None, op0=ALU.is_le),
            lambda q, _tc=tc, _m=mslice: nc.vector.tensor_tensor(
                mcaptok[:, _tc * 8:(_tc + 1) * 8], _m, m9[:, 0:8], op=ALU.mult),
        ], waits=[(s_mm, mm_off)])
        b.on("tensor", lambda q, _m=mslice: nc.tensor.matmul(
            pR1[:, 0:8], b.sb["g_onescolb"][:], _m, start=True, stop=True),
            waits=[(s_ev, s_ev.count)], inc=s_mm)
        mm_c = s_mm.count
        b.chain("vector", [
            lambda q: nc.vector.tensor_copy(cntrow[:], pR1[:, 0:8]),
            lambda q: nc.vector.tensor_tensor(runrow[:], runrow[:], cntrow[:], op=ALU.add),
        ], waits=[(s_mm, mm_c)])
    ev_cap = s_ev.count

    for tc in range(32):
        fl9 = fltok[:, tc * 9:(tc + 1) * 9]
        mc8 = mcaptok[:, tc * 8:(tc + 1) * 8]
        b.chain("vector", [
            lambda q, _m=mc8: nc.vector.tensor_copy(m9[:, 0:8], _m),
            lambda q: nc.vector.memset(m9[:, 8:9], 1.0),
            lambda q: nc.vector.tensor_scalar(e9[:], m9[:], 1.0, 1e30, op0=ALU.subtract, op1=ALU.mult),
            lambda q, _f=fl9: nc.vector.tensor_tensor(e9[:], e9[:], _f, op=ALU.add),
            lambda q: nc.vector.tensor_reduce(ncol[:], e9[:], axis=AX.X, op=ALU.max),
            lambda q: nc.vector.tensor_scalar(ncol[:], ncol[:], -1.0, None, op0=ALU.mult),
        ])
        b.on("scalar", lambda q: nc.scalar.activation(e9[:], e9[:], AF.Exp, bias=ncol[:]),
             waits=[(s_ev, s_ev.count)])
        b.chain("vector", [
            lambda q: nc.vector.tensor_reduce(zcol[:], e9[:], axis=AX.X, op=ALU.add),
            lambda q: nc.vector.reciprocal(zcol[:], zcol[:]),
            lambda q: nc.vector.tensor_scalar(e9[:], e9[:], zcol[:], None, op0=ALU.mult),
            lambda q: nc.vector.tensor_reduce(ncol[:], e9[:, 0:8], axis=AX.X, op=ALU.add),
            lambda q, _tc=tc: nc.vector.tensor_scalar(
                glwtok[:, _tc * 9:(_tc + 1) * 9 - 1], rntok[:, _tc * 8:(_tc + 1) * 8],
                ncol[:], None, op0=ALU.mult),
            lambda q, _tc=tc, _m=mc8: nc.vector.tensor_tensor(
                glwtok[:, _tc * 9:(_tc + 1) * 9 - 1], glwtok[:, _tc * 9:(_tc + 1) * 9 - 1],
                _m, op=ALU.mult),
            lambda q, _tc=tc: nc.vector.tensor_copy(glwtok[:, _tc * 9 + 8:(_tc + 1) * 9], e9[:, 8:9]),
        ], waits=[(s_sc, s_sc.count)])
        b.on("tensor", lambda q, _tc=tc: nc.tensor.transpose(
            pT[0:9, :], glwtok[:, _tc * 9:(_tc + 1) * 9], idnf[:]),
            waits=[(s_ev, s_ev.count)], inc=s_mm)
        mm_g = s_mm.count
        b.on("vector", lambda q, _tc=tc: nc.vector.tensor_copy(
            glwfm[:, _tc * PC:(_tc + 1) * PC], pT[0:9, :]),
            waits=[(s_mm, mm_g)])
    ev_glw = s_ev.count
    b.dma("gpsimd", io["glwd"], glwfm[:], s_st, waits=[(s_ev, ev_glw)])
    st_glw = s_st.count

    wg = b.sbuf("m_wg", (PC, 2 * NDC * NB), BF16)
    wu = b.sbuf("m_wu", (PC, 2 * NDC * NB), BF16)
    sact = b.sbuf("m_sact", (PC, 2 * NB), F32)
    aout = b.sbuf("m_aout", (PC, 2 * NB), BF16)
    glwrow = b.sbuf("m_glwrow", (1, 2 * NB), BF16)
    rot4.seed(s_ev)
    rotC.seed(s_ev)
    slot_ev = [0, 0]
    slot_sc = [0, 0]
    git_ = 0
    for u in range(NU):
        buf = u % 2
        wgs = wg[:, buf * NDC * NB:(buf + 1) * NDC * NB]
        wus = wu[:, buf * NDC * NB:(buf + 1) * NDC * NB]
        e_u = u if u < 8 else 8
        if u < 8:
            gsrc = io["dgT"][u * D:(u + 1) * D, :]
            usrc = io["duT"][u * D:(u + 1) * D, :]
        else:
            sh0 = (u - 8) * NB
            gsrc = io["shgT"][:, sh0:sh0 + NB]
            usrc = io["shuT"][:, sh0:sh0 + NB]
        b.dma("gpsimd", *_ld3(gsrc, wgs, NB), s_gl,
              waits=[(s_mm, s_mm.count)] + ([(s_ev, s_ev.count), (s_sc, s_sc.count)]
                                            if u == 0 else []))
        b.dma("gpsimd", *_ld3(usrc, wus, NB), s_gl)
        ld_w = s_gl.count
        for nb_ in range(S // NB):
            n0 = nb_ * NB
            gsl = (u * (S // NB) + nb_) % 2
            b.dma("gpsimd", glwrow[:, gsl * NB:(gsl + 1) * NB], io["glwd"][e_u:e_u + 1, n0:n0 + NB], s_gr,
                  waits=[(s_st, st_glw), (s_mm, s_mm.count)])
            ld_gr = s_gr.count
            pCb, rwC = rotC.take()
            b.mm(pCb[:], [(onesrb[:], glwrow[:, gsl * NB:(gsl + 1) * NB])],
                 waits=rwC + [(s_gr, ld_gr)], inc=s_mm)
            for mc4 in range(4):
                asl = git_ % 2
                git_ += 1
                pG, rwG = rot4.take()
                b.mm(pG[:],
                     [(wgs[:, kc * NB + mc4 * PC: kc * NB + (mc4 + 1) * PC],
                       xn[:, kc * S + n0: kc * S + n0 + NB]) for kc in range(NDC)],
                     waits=rwG + [(s_gl, ld_w)], inc=s_mm)
                pU, rwU = rot4.take()
                b.mm(pU[:],
                     [(wus[:, kc * NB + mc4 * PC: kc * NB + (mc4 + 1) * PC],
                       xn[:, kc * S + n0: kc * S + n0 + NB]) for kc in range(NDC)],
                     waits=rwU, inc=s_mm)
                mm_gu = s_mm.count
                sa = sact[:, asl * NB:(asl + 1) * NB]
                ao = aout[:, asl * NB:(asl + 1) * NB]
                b.on("scalar", lambda q, _p=pG, _s=sa: nc.scalar.activation(_s, _p[:], AF.Silu),
                     waits=[(s_mm, mm_gu), (s_ev, slot_sc[asl])])
                sc_si = s_sc.count
                rot4.settle(s_sc)
                b.chain("vector", [
                    lambda q, _s=sa, _p=pU: nc.vector.tensor_tensor(_s, _s, _p[:], op=ALU.mult),
                    lambda q, _s=sa, _a=ao, _c=pCb: nc.vector.tensor_tensor(_a, _s, _c[:], op=ALU.mult),
                ], waits=[(s_sc, sc_si), (s_xr, slot_ev[asl])])
                rot4.settle(s_ev)
                slot_sc[asl] = s_ev.count
                ev_a = s_ev.count
                b.dma("sync", io["actscr"][u * NB + mc4 * PC: u * NB + (mc4 + 1) * PC, n0:n0 + NB],
                      ao, s_xr, waits=[(s_ev, ev_a), (s_xr, s_xr.count)])
                slot_ev[asl] = s_xr.count
            rotC.settle(s_ev)
    b.free("m_glwrow", "m_aout", "m_sact", "m_wu", "m_wg",
           "m_ccld", "m_prefc", "m_ccsel", "m_ccs", "m_cntrow", "m_runrow",
           "m_wcol", "m_kcol", "m_zcol", "m_ncol", "m_e9", "m_m9", "m_rwt",
           "m_ets", "m_als", "m_jms", "m_fsc", "m_dsc", "m_absrt", "m_absdl",
           "m_srt8", "m_glwfm", "m_glwtok", "m_mcap", "m_mtok", "m_rntok",
           "m_fltok", "m_rlf", "m_gwT", "m_xn")
    dall = b.sbuf("m_dall", (PC, 48 * 1024), BF16)
    actn = b.sbuf("m_actn", (PC, 48 * NB), BF16)
    xres = b.sbuf("m_xres", (PC, 2 * NB), F32)
    oev = b.sbuf("m_oev", (PC, 2 * NB), F32)
    oev_slot = [0, 0]
    b.dma("gpsimd", *_ld3(io["dallT"], dall[:], 1024), s_gl,
          waits=[(s_mm, s_mm.count), (s_ev, s_ev.count), (s_sc, s_sc.count),
                 (s_st, s_st.count)])
    ld_dall = s_gl.count
    for nb_ in range(S // NB):
        n0 = nb_ * NB
        for u in range(NU):
            b.dma("sync", actn[:, u * 4 * NB:(u + 1) * 4 * NB].rearrange("p (c f) -> p c f", f=NB),
                  io["actscr"][u * NB:(u + 1) * NB, n0:n0 + NB].rearrange("(c p) f -> p c f", p=PC),
                  s_ld, waits=([(s_mm, s_mm.count), (s_ev, s_ev.count),
                                (s_sc, s_sc.count), (s_st, s_st.count),
                                (s_xr, s_xr.count)] if u == 0 else ()))
        ld_act = s_ld.count
        for mc in range(NDC):
            dstp, rw_ = rot4.take()
            osl = mc % 2
            ov = oev[:, osl * NB:(osl + 1) * NB]
            b.mm(dstp[:],
                 [(dall[:, kc48 * 1024 + mc * PC: kc48 * 1024 + (mc + 1) * PC],
                   actn[:, kc48 * NB:(kc48 + 1) * NB]) for kc48 in range(48)],
                 waits=rw_ + [(s_ld, ld_act), (s_gl, ld_dall)], inc=s_mm)
            mm_f = s_mm.count
            xsl = mc % 2
            xv = xres[:, xsl * NB:(xsl + 1) * NB]
            b.dma("sync", xv, xA[mc * PC:(mc + 1) * PC, n0:n0 + NB], s_xr,
                  waits=[(s_ev, s_ev.count)])
            ld_x = s_xr.count
            b.chain("vector", [
                lambda q, _p=dstp, _o=ov, _x=xv: nc.vector.tensor_tensor(_o, _p[:], _x, op=ALU.add),
            ], waits=[(s_mm, mm_f), (s_xr, ld_x), (s_st, oev_slot[osl])])
            rot4.settle(s_ev)
            b.dma("gpsimd", io["x4"][mc * PC:(mc + 1) * PC, n0:n0 + NB], ov, s_st,
                  waits=[(s_ev, s_ev.count), (s_st, s_st.count)])
            oev_slot[osl] = s_st.count
    b.wait("gpsimd", [(s_st, s_st.count)])
    b.free("m_oev", "m_xres", "m_actn", "m_dall", "m_tric", "p_E", "p_D", "p_T")


_CACHE = {}


def _declare_consts(nc, io, consts):
    for nm, arr in consts.items():
        dt = BF16 if arr.dtype == BF else F32
        io[nm] = nc.dram_tensor(nm, list(arr.shape), dt, kind="ExternalInput").ap()


def _build_l0a_nc(consts):
    nc = bass.Bass("TRN2", target_bir_lowering=False, name="l0attn")
    io = {}
    io["x"] = nc.dram_tensor("x", [D, S], F32, kind="ExternalInput").ap()
    io["x1"] = nc.dram_tensor("x1", [D, S], F32, kind="ExternalOutput").ap()
    io["qscr"] = nc.dram_tensor("qscr", [D, S], BF16, kind="Internal").ap()
    for nm in ["Wk", "Wv", "Wq"]:
        io[nm] = nc.dram_tensor(nm, [D, D], F32, kind="ExternalInput").ap()
    for nm in ["bk", "bv", "bq"]:
        io[nm] = nc.dram_tensor(nm, [PC, NDC], F32, kind="ExternalInput").ap()
    _declare_consts(nc, io, consts)
    b = Builder(nc)
    sems = make_sems(b)
    emit_prelude(nc, b, io, sems)
    build_l0_attn(nc, b, io, sems)
    b.finish()
    return nc


def _build_l0f_nc(consts):
    nc = bass.Bass("TRN2", target_bir_lowering=False, name="l0ffn")
    io = {}
    io["x1"] = nc.dram_tensor("x1", [D, S], F32, kind="ExternalInput").ap()
    io["x2"] = nc.dram_tensor("x2", [D, S], F32, kind="ExternalOutput").ap()
    shapes = dict(slotT=[D, 64], w1x=[D, NB], w1s=[D, NB], w2=[NB, 64],
                  WihT=[D, 3072], WhhT=[D, 3072],
                  wq=[D, 256], wk=[D, 256], wv=[D, D], wvp=[D, D], wo=[D, D],
                  b1c=[PC, 4], b2c=[64, 1], tau=[1, 1],
                  bihc=[PC, 24], bhhc=[PC, 24], bqc=[PC, 2], bkc=[PC, 2],
                  bvrow=[1, D], bvprow=[1, D], lngc=[PC, NDC], lnbc=[PC, NDC],
                  boc=[PC, NDC])
    for nm, shp in shapes.items():
        io[nm] = nc.dram_tensor(nm, shp, F32, kind="ExternalInput").ap()
    _declare_consts(nc, io, consts)
    b = Builder(nc)
    sems = make_sems(b)
    emit_prelude(nc, b, io, sems)
    build_l0_ffn(nc, b, io, sems)
    b.finish()
    return nc


def _qk_perm():
    perm = np.zeros(1024, np.int64)
    for parity in range(2):
        for c in range(4):
            for h_loc in range(4):
                for i in range(32):
                    new = parity * 512 + c * 128 + h_loc * 32 + i
                    old = (c * 4 + h_loc) * 64 + 2 * i + parity
                    perm[new] = old
    return perm


def _build_l1a_nc(consts):
    nc = bass.Bass("TRN2", target_bir_lowering=False, name="l1attn")
    io = {}
    io["x2"] = nc.dram_tensor("x2", [D, S], F32, kind="ExternalInput").ap()
    io["x3"] = nc.dram_tensor("x3", [D, S], F32, kind="ExternalOutput").ap()
    io["qd"] = nc.dram_tensor("qd", [16, 64, S], BF16, kind="Internal").ap()
    io["kd"] = nc.dram_tensor("kd", [16, 64, S], BF16, kind="Internal").ap()
    io["vd"] = nc.dram_tensor("vd", [S, 16 * 65], BF16, kind="Internal").ap()
    io["qid"] = nc.dram_tensor("qid", [256, S], BF16, kind="Internal").ap()
    shapes = dict(WqT=[D, D], WkT=[D, D], WvT=[D, D], WqiT=[D, 256],
                  WkiT=[D, 64], WwiT=[D, 4], WoT=[D, D])
    for nm, shp in shapes.items():
        io[nm] = nc.dram_tensor(nm, shp, F32, kind="ExternalInput").ap()
    _declare_consts(nc, io, consts)
    b = Builder(nc)
    sems = make_sems(b)
    emit_prelude(nc, b, io, sems)
    build_l1_attn(nc, b, io, sems)
    b.finish()
    return nc


def _l1a_inputs(inputs):
    wqkv = np.asarray(inputs["L1_attn_w_qkv"], np.float32)
    perm = _qk_perm()
    return {
        "WqT": np.ascontiguousarray(wqkv[0:1024][perm].T),
        "WkT": np.ascontiguousarray(wqkv[1024:2048][perm].T),
        "WvT": np.ascontiguousarray(wqkv[2048:3072].T),
        "WqiT": np.ascontiguousarray(np.asarray(inputs["L1_attn_w_qi"], np.float32).T),
        "WkiT": np.ascontiguousarray(np.asarray(inputs["L1_attn_w_ki"], np.float32).T),
        "WwiT": np.ascontiguousarray(np.asarray(inputs["L1_attn_w_wi"], np.float32).T),
        "WoT": np.ascontiguousarray(np.asarray(inputs["L1_attn_w_o"], np.float32).T),
    }


def _build_l1f_nc(consts, core_id):
    nc = bass.Bass("TRN2", target_bir_lowering=False, name="l1ffn")
    io = {}
    io["x3"] = nc.dram_tensor("x3", [D, S], F32, kind="ExternalInput").ap()
    io["x4"] = nc.dram_tensor("x4", [D, S], F32, kind="ExternalOutput").ap()
    io["cc_in"] = nc.dram_tensor("cc_in", [PC, PC], F32).ap()
    io["cc_out"] = nc.dram_tensor("cc_out", [PC, PC], F32).ap()
    io["actscr"] = nc.dram_tensor("actscr", [12 * NB, S], BF16, kind="Internal").ap()
    io["glwd"] = nc.dram_tensor("glwd", [9, S], F32, kind="Internal").ap()
    shapes = dict(gwT=[D, 9], dgT=[8 * D, NB], duT=[8 * D, NB],
                  shgT=[D, 2048], shuT=[D, 2048], dallT=[6144, D],
                  cc_sel=[1, PC], cc_pref=[8, 1])
    for nm, shp in shapes.items():
        io[nm] = nc.dram_tensor(nm, shp, F32, kind="ExternalInput").ap()
    _declare_consts(nc, io, consts)
    b = Builder(nc)
    sems = make_sems(b)
    emit_prelude(nc, b, io, sems)
    build_l1_ffn(nc, b, io, sems, core_id)
    b.finish()
    return nc


def _l1f_inputs(inputs, core_id):
    t = lambda a: np.ascontiguousarray(np.asarray(a, np.float32).T)
    dg = np.asarray(inputs["L1_ffn_dyn_gate_w"], np.float32)
    du = np.asarray(inputs["L1_ffn_dyn_up_w"], np.float32)
    dd = np.asarray(inputs["L1_ffn_dyn_down_w"], np.float32)
    sel = np.zeros((1, PC), np.float32)
    sel[0, core_id] = 1.0
    pref = np.zeros((8, 1), np.float32)
    pref[:core_id, 0] = 1.0
    return {
        "gwT": t(inputs["L1_ffn_gate_w"]),
        "dgT": np.ascontiguousarray(np.concatenate([dg[e].T for e in range(8)], axis=0)),
        "duT": np.ascontiguousarray(np.concatenate([du[e].T for e in range(8)], axis=0)),
        "shgT": t(inputs["L1_ffn_sh_gate_w"][0]),
        "shuT": t(inputs["L1_ffn_sh_up_w"][0]),
        "dallT": np.ascontiguousarray(np.concatenate(
            [dd[e].T for e in range(8)] + [np.asarray(inputs["L1_ffn_sh_down_w"], np.float32)[0].T], axis=0)),
        "cc_sel": sel,
        "cc_pref": pref,
    }


def _col(v, nchunk):
    return np.ascontiguousarray(np.asarray(v, np.float32).reshape(nchunk, PC).T)


def _l0f_inputs(inputs):
    t = lambda a: np.ascontiguousarray(np.asarray(a, np.float32).T)
    w1 = np.asarray(inputs["L0_ffn_w1"], np.float32)
    return {
        "slotT": t(inputs["L0_ffn_slot_init"]),
        "w1x": np.ascontiguousarray(w1[:D]),
        "w1s": np.ascontiguousarray(w1[D:]),
        "w2": np.asarray(inputs["L0_ffn_w2"], np.float32),
        "WihT": t(inputs["L0_ffn_Wih"]),
        "WhhT": t(inputs["L0_ffn_Whh"]),
        "wq": np.asarray(inputs["L0_ffn_wq"], np.float32),
        "wk": np.asarray(inputs["L0_ffn_wk"], np.float32),
        "wv": np.asarray(inputs["L0_ffn_wv"], np.float32),
        "wvp": np.asarray(inputs["L0_ffn_wvp"], np.float32),
        "wo": np.asarray(inputs["L0_ffn_wo"], np.float32),
        "b1c": _col(inputs["L0_ffn_b1"], 4),
        "b2c": np.asarray(inputs["L0_ffn_b2"], np.float32).reshape(64, 1),
        "tau": np.asarray(inputs["L0_ffn_tau"], np.float32).reshape(1, 1),
        "bihc": _col(inputs["L0_ffn_bih"], 24),
        "bhhc": _col(inputs["L0_ffn_bhh"], 24),
        "bqc": _col(inputs["L0_ffn_bq"], 2),
        "bkc": _col(inputs["L0_ffn_bk"], 2),
        "bvrow": np.asarray(inputs["L0_ffn_bv"], np.float32).reshape(1, D),
        "bvprow": np.asarray(inputs["L0_ffn_bvp"], np.float32).reshape(1, D),
        "lngc": _col(inputs["L0_ffn_ln_g"], NDC),
        "lnbc": _col(inputs["L0_ffn_ln_b"], NDC),
        "boc": _col(inputs["L0_ffn_bo"], NDC),
    }


def _build_fused_nc(consts):
    nc = bass.Bass("TRN2", target_bir_lowering=False, name="mk_fused")
    io = {}
    io["x"] = nc.dram_tensor("x", [D, S], F32, kind="ExternalInput").ap()
    io["x4"] = nc.dram_tensor("x4", [D, S], F32, kind="ExternalOutput").ap()
    for nm in ["x1", "x2", "x3"]:
        io[nm] = nc.dram_tensor(nm, [D, S], F32, kind="Internal").ap()
    io["qscr"] = nc.dram_tensor("qscr", [D, S], BF16, kind="Internal").ap()
    io["qd"] = nc.dram_tensor("qd", [16, 64, S], BF16, kind="Internal").ap()
    io["kd"] = nc.dram_tensor("kd", [16, 64, S], BF16, kind="Internal").ap()
    io["vd"] = nc.dram_tensor("vd", [S, 16 * 65], BF16, kind="Internal").ap()
    io["qid"] = nc.dram_tensor("qid", [256, S], BF16, kind="Internal").ap()
    io["actscr"] = nc.dram_tensor("actscr", [12 * NB, S], BF16, kind="Internal").ap()
    io["glwd"] = nc.dram_tensor("glwd", [9, S], F32, kind="Internal").ap()
    io["cc_in"] = nc.dram_tensor("cc_in", [PC, PC], F32).ap()
    io["cc_out"] = nc.dram_tensor("cc_out", [PC, PC], F32).ap()
    for nm in ["Wk", "Wv", "Wq"]:
        io[nm] = nc.dram_tensor(nm, [D, D], F32, kind="ExternalInput").ap()
    for nm in ["bk", "bv", "bq"]:
        io[nm] = nc.dram_tensor(nm, [PC, NDC], F32, kind="ExternalInput").ap()
    shapes = dict(slotT=[D, 64], w1x=[D, NB], w1s=[D, NB], w2=[NB, 64],
                  WihT=[D, 3072], WhhT=[D, 3072],
                  wq=[D, 256], wk=[D, 256], wv=[D, D], wvp=[D, D], wo=[D, D],
                  b1c=[PC, 4], b2c=[64, 1], tau=[1, 1],
                  bihc=[PC, 24], bhhc=[PC, 24], bqc=[PC, 2], bkc=[PC, 2],
                  bvrow=[1, D], bvprow=[1, D], lngc=[PC, NDC], lnbc=[PC, NDC],
                  boc=[PC, NDC],
                  WqT=[D, D], WkT=[D, D], WvT=[D, D], WqiT=[D, 256],
                  WkiT=[D, 64], WwiT=[D, 4], WoT=[D, D],
                  gwT=[D, 9], dgT=[8 * D, NB], duT=[8 * D, NB],
                  shgT=[D, 2048], shuT=[D, 2048], dallT=[6144, D],
                  cc_sel=[1, PC], cc_pref=[8, 1])
    for nm, shp in shapes.items():
        io[nm] = nc.dram_tensor(nm, shp, F32, kind="ExternalInput").ap()
    _declare_consts(nc, io, consts)
    b = Builder(nc)
    sems = make_sems(b)
    emit_prelude(nc, b, io, sems)
    build_l0_attn(nc, b, dict(io, x=io["x"], x1=io["x1"]), sems)
    build_l0_ffn(nc, b, io, sems)
    build_l1_attn(nc, b, io, sems)
    build_l1_ffn(nc, b, io, sems)
    b.finish()
    return nc


def kernel(**inputs):
    B = 8
    consts = make_consts()
    x = np.asarray(inputs["L0_attn_x"])
    x_fm = [np.ascontiguousarray(x[c].T) for c in range(B)]
    nc = _build_fused_nc(consts)
    base = {
        "Wk": np.asarray(inputs["L0_attn_Wk"]),
        "Wv": np.asarray(inputs["L0_attn_Wv"]),
        "Wq": np.asarray(inputs["L0_attn_Wq"]),
        "bk": np.ascontiguousarray(np.asarray(inputs["L0_attn_bk"]).reshape(NDC, PC).T),
        "bv": np.ascontiguousarray(np.asarray(inputs["L0_attn_bv"]).reshape(NDC, PC).T),
        "bq": np.ascontiguousarray(np.asarray(inputs["L0_attn_bq"]).reshape(NDC, PC).T),
    }
    base.update(_l0f_inputs(inputs))
    base.update(_l1a_inputs(inputs))
    base.update(consts)
    percore = [_l1f_inputs(inputs, c) for c in range(B)]
    in_maps = []
    for c in range(B):
        m = dict(base)
        m.update(percore[c])
        m["x"] = x_fm[c]
        in_maps.append(m)
    res = run_bass_kernel_spmd(nc, in_maps, core_ids=list(range(B)))
    x4 = np.stack([np.asarray(r["x4"]).T for r in res.results])
    return x4.astype(np.float32)
